# version 06
# speedup vs baseline: 1.8890x; 1.0099x over previous
import numpy as np
import ml_dtypes
import concourse.bass as bass
import concourse.mybir as mybir
from concourse.bass_utils import run_bass_kernel_spmd
from concourse.bass_core_rust import IndirectOffsetOnAxis

F32 = mybir.dt.float32
BF16 = mybir.dt.bfloat16
I32 = mybir.dt.int32
AF = mybir.ActivationFunctionType
ALU = mybir.AluOpType
AX = mybir.AxisListType

T, C, H, N = 2048, 1024, 16, 64
NT = T // 128
NPC = C // 128
LCH = 128
NCH = T // LCH
BLK = 512
NBLK = T // BLK
LAM = float(np.exp(-0.5))
EPS_LN = 1e-5
EPS_GN = 0.00064

DEBUG_SUB = None

ENGINES = ("t", "v", "s", "g", "q")
ENG_MAP = {"t": "tensor", "v": "vector", "s": "scalar", "g": "gpsimd", "q": "sync"}


class Prog:

    def __init__(self, nc):
        self.nc = nc
        self.ops = {e: [] for e in ENGINES}
        self.cnt = {e: 0 for e in ENGINES}
        self.chcnt = {}
        self.last_w = {}
        self.readers = {}

    def _dep(self, stamp, deps):
        kind, ident, val = stamp
        key = (kind, ident)
        if kind == "c":
            val = self.chcnt[ident]
        deps[key] = max(deps.get(key, 0), val)

    def op(self, e, emit, reads=(), writes=(), ch=None):
        deps = dict(getattr(self, "bar", {}))
        for k in reads:
            w = self.last_w.get(k)
            if w is not None:
                self._dep(w, deps)
        for k in writes:
            w = self.last_w.get(k)
            if w is not None:
                self._dep(w, deps)
            for r in self.readers.get(k, ()):
                self._dep(r, deps)
        if ch is None:
            self.cnt[e] += 1
            stamp = ("e", e, self.cnt[e])
        else:
            self.chcnt[ch] = self.chcnt.get(ch, 0) + 16
            stamp = ("c", ch, self.chcnt[ch])
        self.ops[e].append((emit, deps, ch))
        for k in reads:
            self.readers.setdefault(k, []).append(stamp)
        for k in writes:
            self.last_w[k] = stamp
            self.readers[k] = []
        return stamp

    def barrier(self):
        self.bar = {("e", e): self.cnt[e] for e in ENGINES if self.cnt[e]}
        self.bar.update({("c", c): v for c, v in self.chcnt.items()})

    def replay(self):
        nc = self.nc
        import contextlib
        stack = contextlib.ExitStack()
        esems = {e: stack.enter_context(nc.semaphore(f"pe_{e}")) for e in ENGINES}
        csems = {c: stack.enter_context(nc.semaphore(f"pc_{i}"))
                 for i, c in enumerate(self.chcnt)}

        def semof(key):
            kind, ident = key
            return esems[ident] if kind == "e" else csems[ident]

        with nc.Block() as block:
            for e in ENGINES:
                if not self.ops[e]:
                    continue

                def body(eng, e=e, oplist=self.ops[e]):
                    waited = {}
                    for emit, deps, ch in oplist:
                        for key, v in sorted(deps.items(), key=lambda kv: str(kv[0])):
                            if waited.get(key, 0) < v:
                                eng.wait_ge(semof(key), v)
                                waited[key] = v
                        inst = emit(eng)
                        inst.then_inc(esems[e] if ch is None else csems[ch],
                                      1 if ch is None else 16)
                getattr(block, ENG_MAP[e])(body)
        with nc.Block() as block:
            def fin(eng):
                for e in ENGINES:
                    if self.cnt[e]:
                        eng.wait_ge(esems[e], self.cnt[e])
                for c, v in self.chcnt.items():
                    eng.wait_ge(csems[c], v)
            block.sync(fin)
        stack.close()


class Arena:

    def __init__(self, nc, name, cols, dtype):
        self.t = nc.alloc_sbuf_tensor(name, [128, cols], dtype)
        self.name = name
        self.cols = cols
        self.free_at = 0
        self.marks = {}

    def alloc(self, n):
        a = self.free_at
        assert a + n <= self.cols, f"{self.name} overflow: {a}+{n} > {self.cols}"
        self.free_at += n
        return a

    def mark(self, tag):
        self.marks[tag] = self.free_at

    def reset(self, tag):
        self.free_at = self.marks[tag]


def ceil_div(a, b):
    return (a + b - 1) // b



def build_program(debug_stop=None):
    nc = bass.Bass("TRN2", target_bir_lowering=False)
    p = Prog(nc)

    di = {}

    def inp(name, shape, dtype=F32):
        di[name] = nc.dram_tensor(name, list(shape), dtype, kind="ExternalInput").ap()
        return di[name]

    inp("X", [T, C])
    for nm in ["x_r", "x_w", "x_k", "x_v", "x_a", "x_g", "w0",
               "a0", "k_k", "k_a", "ffn_x_k", "gn_w", "gn_b"]:
        inp("p_" + nm, [128, NPC])
    for nm in ["ln_pre_w", "ln_pre_b", "ln_attn_w", "ln_attn_b", "ln_ffn_w",
               "ln_ffn_b"]:
        inp("row_" + nm, [1, C])
    inp("r_k", [128, 8])
    for nm, shp in [("W_r", [C, C]), ("W_k", [C, C]), ("W_v", [C, C]),
                    ("W_o", [C, C]), ("w1", [C, 64]), ("w2", [64, C]),
                    ("a1", [C, 64]), ("a2", [64, C]), ("g1", [C, 128]),
                    ("g2", [128, C]), ("W_ffn1", [C, 4 * C]), ("W_ffn2", [4 * C, C])]:
        inp(nm, shp)
    inp("gate_wT", [C, 8])
    inp("w_gT", [8, C, 4096])
    inp("w_uT", [8, C, 4096])
    inp("w_dn", [8, 4096, C])
    for nm, shp in [("Wq", [C, C]), ("Wdown", [C, 256]), ("Wup", [256, 2 * C]),
                    ("Wout", [C, C]), ("Wr1", [C, 8])]:
        inp(nm, shp)
    inp("bq", [1, C])
    inp("bdown", [128, 2])
    inp("bup", [1, 2 * C])
    inp("bout", [1, C])
    inp("br1", [8, 1])
    inp("f_W1r", [32768, 1024])
    inp("f_b1", [128, 8 * 32])
    inp("f_W2", [8, 4096, C])
    inp("f_b2", [1, 8 * C])
    inp("gamma", [1, C])
    inp("beta", [1, C])

    OUT = nc.dram_tensor("OUT", [T, C], F32, kind="ExternalOutput").ap()

    ds = {}
    for nm, shape, dt in [("xn_bf", [T, C], BF16), ("xnaT", [C, T], BF16),
                          ("x0f", [T, C], F32), ("xmid", [T, C], F32),
                          ("sattn", [T, C], F32),
                          ("g0a", [T, C // 2], F32), ("g0b", [T, C // 2], F32),
                          ("x1", [T, C], F32), ("xn2f", [T, C], F32),
                          ("wm", [8, T], F32),
                          ("idxw0", [4097, 2], F32), ("idxw1", [4097, 2], F32),
                          ("idxw2", [4097, 2], F32), ("idxw3", [4097, 2], F32),
                          ("idxw4", [4097, 2], F32), ("idxw5", [4097, 2], F32),
                          ("idxw6", [4097, 2], F32), ("idxw7", [4097, 2], F32),
                          ("q_tok", [T, C], BF16), ("k_tok", [T, C], BF16),
                          ("v_tok", [T, C], BF16), ("o_grp", [T * 16, N], BF16),
                          ("s2", [T, C], F32)]:
        ds[nm] = nc.dram_tensor("scr_" + nm, list(shape), dt, kind="Internal").ap()

    ident_b = nc.inline_tensor(np.eye(128, dtype=ml_dtypes.bfloat16), "ident_b")
    id64_2 = nc.inline_tensor(
        np.concatenate([np.eye(64), np.eye(64)], 0).astype(ml_dtypes.bfloat16), "id64_2")
    mus_np = (np.arange(128)[:, None] < np.arange(128)[None, :]).astype(np.float32)
    mui_np = (np.arange(128)[:, None] <= np.arange(128)[None, :]).astype(np.float32)
    c_mus = nc.inline_tensor(mus_np.astype(ml_dtypes.bfloat16), "c_mus")
    c_mui = nc.inline_tensor(mui_np.astype(ml_dtypes.bfloat16), "c_mui")
    c_mls = nc.inline_tensor(mus_np.T.copy().astype(ml_dtypes.bfloat16), "c_mls")
    c_idf = nc.inline_tensor(np.eye(128, dtype=np.float32), "c_idf")
    ones_col = nc.inline_tensor(np.ones((1, 128), np.float32), "ones_col")
    ones_col_b = nc.inline_tensor(np.ones((1, 128), ml_dtypes.bfloat16), "ones_col_b")
    ind2 = np.zeros((128, 2), np.float32)
    ind2[0:64, 0] = 1.0
    ind2[64:128, 1] = 1.0
    c_ind2 = nc.inline_tensor(ind2.astype(ml_dtypes.bfloat16), "c_ind2")
    ind2T_ext = np.zeros((128, 128), np.float32)
    for rb in (0, 32, 64):
        ind2T_ext[rb + 0, 0:64] = 1.0
        ind2T_ext[rb + 1, 64:128] = 1.0
    c_ind2T = nc.inline_tensor(ind2T_ext.astype(ml_dtypes.bfloat16), "c_ind2T")
    c_ind2Tf = nc.inline_tensor(ind2T_ext, "c_ind2Tf")
    invf = (1.0 / 10000.0 ** (np.arange(0, 64, 2) / 64.0)).astype(np.float32)
    c_invf = nc.inline_tensor(np.broadcast_to(invf, (128, 32)).copy(), "c_invf")

    AF32 = Arena(nc, "AF32", 18000, F32)
    AB16 = Arena(nc, "AB16", 66944, BF16)
    sb_par = nc.alloc_sbuf_tensor("sb_par", [128, 32 * 20], F32)
    sb_id = nc.alloc_sbuf_tensor("sb_id", [128, 128], BF16)
    sb_id64 = nc.alloc_sbuf_tensor("sb_id64", [128, 64], BF16)
    sb_msk = nc.alloc_sbuf_tensor("sb_msk", [128, 3 * 128], BF16)
    sb_ones = nc.alloc_sbuf_tensor("sb_ones", [1, 128], F32)
    sb_onesb = nc.alloc_sbuf_tensor("sb_onesb", [1, 128], BF16)
    sb_ind2 = nc.alloc_sbuf_tensor("sb_ind2", [128, 2], BF16)
    sb_ind2T = nc.alloc_sbuf_tensor("sb_ind2T", [128, 128], BF16)
    sb_ind2Tf = nc.alloc_sbuf_tensor("sb_ind2Tf", [128, 128], F32)
    sb_invf = nc.alloc_sbuf_tensor("sb_invf", [128, 32], F32)
    sb_idf = nc.alloc_sbuf_tensor("sb_idf", [128, 128], F32)
    sb_gw = nc.alloc_sbuf_tensor("sb_gw", [128, 64], F32)

    PS = [nc.alloc_psum_tensor(f"PS{i}", [128, 512], F32) for i in range(6)]
    PB = [nc.alloc_psum_tensor(f"PBb{i}", [128, 1024], BF16) for i in range(2)]

    ctx = dict(nc=nc, p=p, di=di, ds=ds, OUT=OUT, AF32=AF32, AB16=AB16,
               sb_idf=sb_idf, sb_gw=sb_gw,
               sb_par=sb_par, sb_id=sb_id, sb_id64=sb_id64, sb_msk=sb_msk,
               sb_ones=sb_ones, sb_onesb=sb_onesb, sb_ind2=sb_ind2,
               sb_ind2T=sb_ind2T, sb_ind2Tf=sb_ind2Tf, sb_invf=sb_invf, PS=PS, PB=PB)

    p.op("q", lambda e: e.dma_start(out=sb_id[:], in_=ident_b.ap()), writes=["sb_id"], ch="lc")
    p.op("q", lambda e: e.dma_start(out=sb_id64[:], in_=id64_2.ap()), writes=["sb_id64"], ch="lc")
    p.op("q", lambda e: e.dma_start(out=sb_msk[:, 0:128], in_=c_mus.ap()), writes=["msk_us"], ch="lc")
    p.op("q", lambda e: e.dma_start(out=sb_msk[:, 128:256], in_=c_mui.ap()), writes=["msk_ui"], ch="lc")
    p.op("q", lambda e: e.dma_start(out=sb_msk[:, 256:384], in_=c_mls.ap()), writes=["msk_ls"], ch="lc")
    p.op("q", lambda e: e.dma_start(out=sb_ones[:], in_=ones_col.ap()), writes=["sb_ones"], ch="lc")
    p.op("q", lambda e: e.dma_start(out=sb_onesb[:], in_=ones_col_b.ap()), writes=["sb_onesb"], ch="lc")
    p.op("q", lambda e: e.dma_start(out=sb_ind2[:], in_=c_ind2.ap()), writes=["sb_ind2"], ch="lc")
    p.op("q", lambda e: e.dma_start(out=sb_ind2T[:], in_=c_ind2T.ap()), writes=["sb_ind2T"], ch="lc")
    p.op("q", lambda e: e.dma_start(out=sb_ind2Tf[:], in_=c_ind2Tf.ap()), writes=["sb_ind2Tf"], ch="lc")
    p.op("q", lambda e: e.dma_start(out=sb_invf[:], in_=c_invf.ap()), writes=["sb_invf"], ch="lc")
    p.op("q", lambda e: e.dma_start(out=sb_idf[:], in_=c_idf.ap()), writes=["sb_idf"], ch="lc")

    emit_phase01(ctx)
    if debug_stop == "p1":
        p.replay()
        nc.finalize()
        return nc
    emit_phase2345(ctx)
    if debug_stop == "p5":
        p.replay()
        nc.finalize()
        return nc
    emit_l0ffn(ctx)
    emit_combine0(ctx)
    if debug_stop == "p6":
        p.replay()
        nc.finalize()
        return nc
    emit_layer1(ctx)
    p.replay()
    nc.finalize()
    return nc



def ln_tok(ctx, key_in, xin_ap, scratch_a, out_ap, key_out, stat_a, wb=None,
           eps=EPS_LN, tag="", scratch_key="a_sq", stat_key="a_st"):
    p = ctx["p"]
    kS = scratch_key
    p.op("s", lambda e: e.activation(out=scratch_a, in_=xin_ap, func=AF.Square),
         reads=[key_in], writes=[kS])
    p.op("v", lambda e: e.tensor_reduce(out=stat_a[:, 0:1], in_=xin_ap, axis=AX.X, op=ALU.add),
         reads=[key_in], writes=[(stat_key, 0)])
    p.op("v", lambda e: e.tensor_reduce(out=stat_a[:, 1:2], in_=scratch_a, axis=AX.X, op=ALU.add),
         reads=[kS], writes=[(stat_key, 1)])
    p.op("v", lambda e: e.tensor_scalar_mul(stat_a[:, 2:3], stat_a[:, 0:1], 1.0 / C),
         reads=[(stat_key, 0)], writes=[(stat_key, 2)])
    p.op("s", lambda e: e.activation(out=stat_a[:, 3:4], in_=stat_a[:, 2:3], func=AF.Square),
         reads=[(stat_key, 2)], writes=[(stat_key, 3)])
    p.op("v", lambda e: e.tensor_scalar(stat_a[:, 4:5], stat_a[:, 1:2],
                                        1.0 / C, stat_a[:, 3:4],
                                        op0=ALU.mult, op1=ALU.subtract),
         reads=[(stat_key, 1), (stat_key, 3)], writes=[(stat_key, 4)])
    p.op("v", lambda e: e.tensor_scalar_add(stat_a[:, 7:8], stat_a[:, 4:5], float(eps)),
         reads=[(stat_key, 4)], writes=[(stat_key, 7)])
    p.op("s", lambda e: e.activation(out=stat_a[:, 5:6], in_=stat_a[:, 7:8], func=AF.Ln),
         reads=[(stat_key, 7)], writes=[(stat_key, 5)])
    p.op("s", lambda e: e.activation(out=stat_a[:, 6:7], in_=stat_a[:, 5:6],
                                     func=AF.Exp, scale=-0.5),
         reads=[(stat_key, 5)], writes=[(stat_key, 6)])
    p.op("v", lambda e: e.tensor_scalar(out_ap, xin_ap, stat_a[:, 2:3], stat_a[:, 6:7],
                                        op0=ALU.subtract, op1=ALU.mult),
         reads=[key_in, (stat_key, 2), (stat_key, 6)], writes=[key_out])
    if wb is not None:
        w_ap, b_ap, keyw, keyb = wb
        p.op("v", lambda e: e.tensor_mul(out_ap, out_ap, w_ap),
             reads=[key_out, keyw], writes=[key_out])
        p.op("v", lambda e: e.tensor_add(out_ap, out_ap, b_ap),
             reads=[key_out, keyb], writes=[key_out])


def bcast_row(ctx, row_ap, key_row, psum_idx, out_ap, key_out, n):
    p, PS, sb_ones = ctx["p"], ctx["PS"], ctx["sb_ones"]
    ps = PS[psum_idx]
    assert n <= 512
    p.op("t", lambda e: e.matmul(ps[:, 0:n], sb_ones[:], row_ap, start=True, stop=True),
         reads=[key_row, "sb_ones"], writes=[("PSb", psum_idx)])
    p.op("v", lambda e: e.tensor_copy(out=out_ap, in_=ps[:, 0:n]),
         reads=[("PSb", psum_idx)], writes=[key_out])



def emit_phase01(ctx):
    nc, p, di, ds = ctx["nc"], ctx["p"], ctx["di"], ctx["ds"]
    AF32, AB16, PS, PB = ctx["AF32"], ctx["AB16"], ctx["PS"], ctx["PB"]
    sb_id = ctx["sb_id"]
    av, ab = AF32.t, AB16.t

    sb_par = ctx["sb_par"]
    ctx["par_off"] = {}
    off = 0
    for nm in ["x_r", "x_w", "x_k", "x_v", "x_a", "x_g", "w0",
               "a0", "k_k", "k_a", "ffn_x_k", "gn_w", "gn_b"]:
        ctx["par_off"][nm] = off
        p.op("q", lambda e, nm=nm, o=off: e.dma_start(out=sb_par[:, o:o + NPC], in_=di["p_" + nm]),
             writes=[("par", nm)], ch="lpar")
        off += NPC
    ctx["par_off"]["r_k"] = off
    p.op("q", lambda e, o=off: e.dma_start(out=sb_par[:, o:o + 8], in_=di["r_k"]),
         writes=[("par", "r_k")], ch="lpar")
    off += 8
    ctx["par_off"]["fb1"] = off
    p.op("q", lambda e, o=off: e.dma_start(out=sb_par[:, o:o + 256], in_=di["f_b1"]),
         writes=[("par", "fb1")], ch="lpar")
    off += 256
    ctx["par_off"]["ka1m"] = off
    oka = ctx["par_off"]["k_a"]
    p.op("v", lambda e, o=off, oka=oka: e.tensor_scalar(sb_par[:, o:o + NPC],
                                                        sb_par[:, oka:oka + NPC],
                                                        -1.0, 1.0, op0=ALU.mult, op1=ALU.add),
         reads=[("par", "k_a")], writes=[("par", "ka1m")])
    off += NPC

    a_lnb = AB16.alloc(6 * 1024)
    ctx["a_lnb"] = a_lnb
    for i, nm in enumerate(["ln_pre_w", "ln_pre_b", "ln_attn_w", "ln_attn_b",
                            "ln_ffn_w", "ln_ffn_b"]):
        p.op("g", lambda e, nm=nm, i=i: e.dma_start(
            out=ab[:, a_lnb + i * 1024:a_lnb + (i + 1) * 1024],
            in_=di["row_" + nm].to_broadcast([128, C])),
            writes=[("lnb", nm)], ch="llnb")

    sb_gw = ctx["sb_gw"]
    for c in range(NPC):
        p.op("q", lambda e, c=c: e.dma_start(out=sb_gw[:, c * 8:(c + 1) * 8],
                                             in_=di["gate_wT"][c * 128:(c + 1) * 128, :]),
             writes=[("gwT", c)], ch="lgw")

    a_logits = AF32.alloc(T)
    logitsT = av[:, a_logits:a_logits + T]
    ctx["logitsT"] = logitsT
    AF32.mark("keep")
    AB16.mark("keep")

    a_x = AF32.alloc(1024)
    a_sq = AF32.alloc(1024)
    a_xn = AF32.alloc(1024)
    a_x0 = AF32.alloc(1024)
    a_st = AF32.alloc(16)
    b_xn = AB16.alloc(1024)
    b_tr = AB16.alloc(128)
    a_tr = AF32.alloc(128)
    lnb = ctx["a_lnb"]
    sb_idf = ctx["sb_idf"]

    for i in range(NT):
        p.op("q", lambda e, i=i: e.dma_start(out=av[:, a_x:a_x + 1024],
                                             in_=di["X"][i * 128:(i + 1) * 128, :]),
             writes=["a_x"], ch="ldx")
        ln_tok(ctx, "a_x", av[:, a_x:a_x + 1024], av[:, a_sq:a_sq + 1024],
               av[:, a_xn:a_xn + 1024], "a_xn", av[:, a_st:a_st + 16], tag="p1")
        p.op("v", lambda e: e.tensor_copy(out=ab[:, b_xn:b_xn + 1024], in_=av[:, a_xn:a_xn + 1024]),
             reads=["a_xn"], writes=["b_xn"])
        p.op("q", lambda e, i=i: e.dma_start(out=ds["xn_bf"][i * 128:(i + 1) * 128, :],
                                             in_=ab[:, b_xn:b_xn + 1024]),
             reads=["b_xn"], writes=[("xnbd", i)], ch="stxn")
        for c in range(NPC):
            p.op("t", lambda e, c=c: e.transpose(PS[4][:, 0:128],
                                                 av[:, a_xn + c * 128:a_xn + (c + 1) * 128],
                                                 sb_idf[:]),
                 reads=["a_xn", "sb_idf", ("PS", 4)], writes=[("PS", 4)])
            p.op("v", lambda e: e.tensor_copy(out=av[:, a_tr:a_tr + 128], in_=PS[4][:, 0:128]),
                 reads=[("PS", 4)], writes=["a_tr"])
            p.op("t", lambda e, c=c: e.matmul(PS[5][0:8, 0:128],
                                              sb_gw[:, c * 8:(c + 1) * 8],
                                              av[:, a_tr:a_tr + 128],
                                              start=(c == 0), stop=(c == NPC - 1)),
                 reads=[("gwT", c), "a_tr", "PS5"], writes=["PS5"])
        p.op("v", lambda e, i=i: e.tensor_copy(out=logitsT[0:8, i * 128:(i + 1) * 128],
                                               in_=PS[5][0:8, 0:128]),
             reads=["PS5"], writes=[("lgT", i)])

        ln_tok(ctx, "a_xn", av[:, a_xn:a_xn + 1024], av[:, a_sq:a_sq + 1024],
               av[:, a_x0:a_x0 + 1024], "a_x0", av[:, a_st:a_st + 16],
               wb=(ab[:, lnb:lnb + 1024], ab[:, lnb + 1024:lnb + 2048],
                   ("lnb", "ln_pre_w"), ("lnb", "ln_pre_b")), tag="p1")
        p.op("q", lambda e, i=i: e.dma_start(out=ds["x0f"][i * 128:(i + 1) * 128, :],
                                             in_=av[:, a_x0:a_x0 + 1024]),
             reads=["a_x0"], writes=[("x0d", i)], ch="stx0")
        ln_tok(ctx, "a_x0", av[:, a_x0:a_x0 + 1024], av[:, a_sq:a_sq + 1024],
               av[:, a_xn:a_xn + 1024], "a_xn", av[:, a_st:a_st + 16],
               wb=(ab[:, lnb + 2048:lnb + 3072], ab[:, lnb + 3072:lnb + 4096],
                   ("lnb", "ln_attn_w"), ("lnb", "ln_attn_b")), tag="p1")
        p.op("v", lambda e: e.tensor_copy(out=ab[:, b_xn:b_xn + 1024], in_=av[:, a_xn:a_xn + 1024]),
             reads=["a_xn"], writes=["b_xn"])
        for c in range(NPC):
            p.op("t", lambda e, c=c: e.transpose(PB[0][:, 0:128],
                                                 ab[:, b_xn + c * 128:b_xn + (c + 1) * 128],
                                                 sb_id[:]),
                 reads=["b_xn", "sb_id"], writes=["PB0"])
            p.op("v", lambda e: e.tensor_copy(out=ab[:, b_tr:b_tr + 128], in_=PB[0][:, 0:128]),
                 reads=["PB0"], writes=["b_tr"])
            p.op("q", lambda e, c=c, i=i: e.dma_start(
                out=ds["xnaT"][c * 128:(c + 1) * 128, i * 128:(i + 1) * 128],
                in_=ab[:, b_tr:b_tr + 128]),
                reads=["b_tr"], writes=[("xnaTd", c, i)], ch="stxT")


def emit_phase2345(ctx):
    nc, p, di, ds = ctx["nc"], ctx["p"], ctx["di"], ctx["ds"]
    AF32, AB16, PS, PB = ctx["AF32"], ctx["AB16"], ctx["PS"], ctx["PB"]
    av, ab = AF32.t, AB16.t
    par, po = ctx["sb_par"], ctx["par_off"]
    sb_id, sb_id64 = ctx["sb_id"], ctx["sb_id64"]
    sb_msk, sb_ind2, sb_ind2T = ctx["sb_msk"], ctx["sb_ind2"], ctx["sb_ind2T"]
    lnb = ctx["a_lnb"]

    p.barrier()
    AF32.reset("keep")
    AB16.reset("keep")

    def parap(nm, c, n=1):
        o = po[nm]
        return par[:, o + c:o + c + n]

    b_xa = AB16.alloc(8 * 514)
    b_d1 = AB16.alloc(8 * 512)
    b_var = AB16.alloc(8 * 512)
    b_wA = AB16.alloc(2048)
    b_wB = AB16.alloc(2048)
    b_sm = AB16.alloc(3072)
    o_w1, o_a1, o_g1 = b_sm, b_sm + 512, b_sm + 1024
    o_lo2 = b_sm + 2048
    b_rT = AB16.alloc(8 * 512)
    b_kT = AB16.alloc(8 * 512)
    b_aS = AB16.alloc(8 * 512)
    b_kkn = AB16.alloc(8 * 512)
    b_gg = AB16.alloc(8 * 512)
    b_vt = AB16.alloc(4 * 1024)
    b_vT = AB16.alloc(8 * 512)
    b_der = AB16.alloc(4 * 1024)
    b_xfT = AB16.alloc(8 * 514)
    b_mid = b_wB
    b_BTs = [AB16.alloc(7 * 128) for _ in range(2)]
    b_Bc = [AB16.alloc(2 * 128) for _ in range(2)]
    b_KTm = [AB16.alloc(128) for _ in range(2)]
    b_Pbm = [AB16.alloc(128) for _ in range(2)]
    b_Pkm = [AB16.alloc(128) for _ in range(2)]
    b_Wc = [AB16.alloc(2 * 64) for _ in range(2)]
    b_Btok = [AB16.alloc(64) for _ in range(2)]
    b_Ktok = [AB16.alloc(64) for _ in range(2)]
    b_stbf = AB16.alloc(8 * 64)

    a_sig = AF32.alloc(8 * 512)
    a_gsc = AF32.alloc(516)
    a_gr = AF32.alloc(8 * 512)
    a_xm = AF32.alloc(4 * 1024)
    a_scr = AF32.alloc(1024)
    a_scr2 = AF32.alloc(1024)
    a_st = AF32.alloc(16)
    a_state = AF32.alloc(8 * 64)
    a_stsc = AF32.alloc(8 * 64)
    a_gcol = AF32.alloc(8)

    wbufs = [(b_wA, "chWA", "b_wA"), (b_wB, "chWB", "b_wB")]
    nwb = [0]

    def load_w(dram_ap, rows, cols, tag):
        boff, chn, keyn = wbufs[nwb[0] % 2]
        nwb[0] += 1
        nch = rows // 128
        assert nch * cols <= 2048
        src = dram_ap.rearrange("(c p) j -> p c j", p=128)
        p.op("g", lambda e, boff=boff, nch=nch, cols=cols, src=src: e.dma_start(
            out=ab[:, boff:boff + nch * cols].rearrange("p (c j) -> p c j", c=nch), in_=src),
            reads=[keyn], writes=[keyn], ch=chn)
        return boff, keyn

    p.op("g", lambda e: e.dma_start(out=ab[:, o_w1:o_w1 + 512].rearrange("p (c j) -> p c j", c=8),
                                    in_=di["w1"].rearrange("(c p) j -> p c j", p=128)),
         writes=["sm_w1"], ch="lsm")
    p.op("g", lambda e: e.dma_start(out=ab[:, o_a1:o_a1 + 512].rearrange("p (c j) -> p c j", c=8),
                                    in_=di["a1"].rearrange("(c p) j -> p c j", p=128)),
         writes=["sm_a1"], ch="lsm")
    p.op("g", lambda e: e.dma_start(out=ab[:, o_g1:o_g1 + 1024].rearrange("p (c j) -> p c j", c=8),
                                    in_=di["g1"].rearrange("(c p) j -> p c j", p=128)),
         writes=["sm_g1"], ch="lsm")

    p.op("v", lambda e: e.memset(av[:, a_state:a_state + 512], 0.0),
         writes=[("state", pcx, hx) for pcx in range(8) for hx in range(2)])
    p.op("v", lambda e: e.memset(ab[:, b_stbf:b_stbf + 512], 0.0), writes=["stbf"])
    p.op("v", lambda e: e.memset(ab[:, b_xfT:b_xfT + 8 * 514]
                                 .rearrange("p (c w) -> p c w", w=514)[:, :, 0:1], 0.0),
         writes=["b_xfT"])

    def xa_view(c, c0, n):
        return ab[:, b_xa + c * 514 + c0: b_xa + c * 514 + c0 + n]

    def blk8(base, c, n=512):
        return ab[:, base + c * 512: base + c * 512 + n]

    def proj_var(xi_nm):
        for c in range(8):
            p.op("v", lambda e, c=c, xi_nm=xi_nm: e.scalar_tensor_tensor(
                out=blk8(b_var, c), in0=blk8(b_d1, c), scalar=parap(xi_nm, c),
                in1=xa_view(c, 1, 512), op0=ALU.mult, op1=ALU.add),
                reads=["b_d1", "b_xa", ("par", xi_nm)], writes=["b_var"])

    def mm_feature_qtr(dram_w, out_base, outkey, tag):
        for q in range(4):
            wo, wk = load_w(dram_w[:, q * 256:(q + 1) * 256], C, 256, (tag, q))
            for jj in range(2):
                j = q * 2 + jj
                for c in range(8):
                    p.op("t", lambda e, wo=wo, jj=jj, c=c: e.matmul(
                        PS[0][:, 0:512],
                        ab[:, wo + c * 256 + jj * 128: wo + c * 256 + (jj + 1) * 128],
                        blk8(b_var, c), start=(c == 0), stop=(c == 7)),
                        reads=[wk, "b_var", ("PS", 0)], writes=[("PS", 0)])
                p.op("v", lambda e, j=j: e.tensor_copy(out=blk8(out_base, j), in_=PS[0][:, 0:512]),
                     reads=[("PS", 0)], writes=[outkey])

    nblk = 1 if DEBUG_SUB else NBLK
    for b in range(nblk):
        t0 = b * BLK
        if b == 0:
            for c in range(8):
                p.op("v", lambda e, c=c: e.memset(xa_view(c, 0, 1), 0.0), writes=["b_xa"])
        else:
            for c in range(8):
                p.op("q", lambda e, c=c, t0=t0: e.dma_start(
                    out=xa_view(c, 0, 1), in_=ds["xnaT"][c * 128:(c + 1) * 128, t0 - 1:t0],
                    allow_slow_non_contiguous=True),
                    reads=["b_xa"], writes=["b_xa"], ch="lxa")
        for c in range(8):
            p.op("q", lambda e, c=c, t0=t0: e.dma_start(
                out=xa_view(c, 1, 512), in_=ds["xnaT"][c * 128:(c + 1) * 128, t0:t0 + 512]),
                reads=["b_xa"] + [("xnaTd", c, i) for i in range(4 * b, 4 * b + 4)],
                writes=["b_xa"], ch="lxa")
        for c in range(8):
            p.op("v", lambda e, c=c: e.tensor_sub(blk8(b_d1, c), xa_view(c, 0, 512), xa_view(c, 1, 512)),
                 reads=["b_xa"], writes=["b_d1"])

        p.op("g", lambda e: e.dma_start(out=ab[0:64, o_lo2:o_lo2 + 1024], in_=di["w2"]),
             reads=["sm_lo2"], writes=["sm_lo2"], ch="lsm2")
        proj_var("x_w")
        for c in range(8):
            p.op("t", lambda e, c=c: e.matmul(PS[2][0:64, 0:512],
                                              ab[:, o_w1 + c * 64:o_w1 + (c + 1) * 64],
                                              blk8(b_var, c), start=(c == 0), stop=(c == 7)),
                 reads=["sm_w1", "b_var", ("PS", 2)], writes=[("PS", 2)])
        p.op("s", lambda e: e.activation(out=ab[0:64, b_mid:b_mid + 512], in_=PS[2][0:64, 0:512],
                                         func=AF.Tanh),
             reads=[("PS", 2)], writes=["b_wB"])
        for j in range(8):
            p.op("t", lambda e, j=j: e.matmul(PS[2][:, 0:512],
                                              ab[0:64, o_lo2 + j * 128:o_lo2 + (j + 1) * 128],
                                              ab[0:64, b_mid:b_mid + 512],
                                              start=True, stop=True),
                 reads=["sm_lo2", "b_wB", ("PS", 2)], writes=[("PS", 2)])
            p.op("s", lambda e, j=j: e.activation(out=av[:, a_sig + j * 512:a_sig + (j + 1) * 512],
                                                  in_=PS[2][:, 0:512], func=AF.Sigmoid,
                                                  bias=parap("w0", j)),
                 reads=[("PS", 2), ("par", "w0")], writes=["a_sig"])

        p.op("g", lambda e: e.dma_start(out=ab[0:64, o_lo2:o_lo2 + 1024], in_=di["a2"]),
             reads=["sm_lo2"], writes=["sm_lo2"], ch="lsm2")
        proj_var("x_a")
        for c in range(8):
            p.op("t", lambda e, c=c: e.matmul(PS[2][0:64, 0:512],
                                              ab[:, o_a1 + c * 64:o_a1 + (c + 1) * 64],
                                              blk8(b_var, c), start=(c == 0), stop=(c == 7)),
                 reads=["sm_a1", "b_var", ("PS", 2)], writes=[("PS", 2)])
        p.op("v", lambda e: e.tensor_copy(out=ab[0:64, b_mid:b_mid + 512], in_=PS[2][0:64, 0:512]),
             reads=[("PS", 2)], writes=["b_wB"])
        for j in range(8):
            p.op("t", lambda e, j=j: e.matmul(PS[2][:, 0:512],
                                              ab[0:64, o_lo2 + j * 128:o_lo2 + (j + 1) * 128],
                                              ab[0:64, b_mid:b_mid + 512],
                                              start=True, stop=True),
                 reads=["sm_lo2", "b_wB", ("PS", 2)], writes=[("PS", 2)])
            p.op("s", lambda e, j=j: e.activation(out=blk8(b_aS, j), in_=PS[2][:, 0:512],
                                                  func=AF.Sigmoid, bias=parap("a0", j)),
                 reads=[("PS", 2), ("par", "a0")], writes=["b_aS"])

        p.op("g", lambda e: e.dma_start(out=ab[:, o_lo2:o_lo2 + 1024], in_=di["g2"]),
             reads=["sm_lo2"], writes=["sm_lo2"], ch="lsm2")
        proj_var("x_g")
        for c in range(8):
            p.op("t", lambda e, c=c: e.matmul(PS[2][:, 0:512],
                                              ab[:, o_g1 + c * 128:o_g1 + (c + 1) * 128],
                                              blk8(b_var, c), start=(c == 0), stop=(c == 7)),
                 reads=["sm_g1", "b_var", ("PS", 2)], writes=[("PS", 2)])
        p.op("s", lambda e: e.activation(out=ab[:, b_mid:b_mid + 512], in_=PS[2][:, 0:512],
                                         func=AF.Sigmoid),
             reads=[("PS", 2)], writes=["b_wB"])
        for j in range(8):
            p.op("t", lambda e, j=j: e.matmul(PS[2][:, 0:512],
                                              ab[:, o_lo2 + j * 128:o_lo2 + (j + 1) * 128],
                                              ab[:, b_mid:b_mid + 512],
                                              start=True, stop=True),
                 reads=["sm_lo2", "b_wB", ("PS", 2)], writes=[("PS", 2)])
            p.op("v", lambda e, j=j: e.tensor_copy(out=blk8(b_gg, j), in_=PS[2][:, 0:512]),
                 reads=[("PS", 2)], writes=["b_gg"])

        proj_var("x_k")
        mm_feature_qtr(di["W_k"], b_kT, "b_kT", ("Wk", b))
        for c in range(8):
            p.op("v", lambda e, c=c: e.tensor_scalar_mul(av[:, a_scr:a_scr + 512],
                                                         blk8(b_kT, c), parap("k_k", c)),
                 reads=["b_kT", ("par", "k_k")], writes=["a_scr"])
            p.op("v", lambda e, c=c: e.tensor_copy(out=blk8(b_kkn, c), in_=av[:, a_scr:a_scr + 512]),
                 reads=["a_scr"], writes=["b_kkn"])
            p.op("s", lambda e: e.activation(out=ab[:, b_var:b_var + 512],
                                             in_=av[:, a_scr:a_scr + 512], func=AF.Square),
                 reads=["a_scr"], writes=["b_var"])
            p.op("t", lambda e, c=c: e.matmul(PS[3][0:2, 0:512], sb_ind2[:],
                                              ab[:, b_var:b_var + 512], start=True, stop=True),
                 reads=["sb_ind2", "b_var", ("PS", 3)], writes=[("PS", 3)])
            p.op("v", lambda e, c=c: e.tensor_copy(out=av[0:2, a_xm + c * 512:a_xm + (c + 1) * 512],
                                                   in_=PS[3][0:2, 0:512]),
                 reads=[("PS", 3)], writes=["a_xm"])
        p.op("v", lambda e: e.tensor_scalar_max(av[0:2, a_xm:a_xm + 4096],
                                                av[0:2, a_xm:a_xm + 4096], 1e-24),
             reads=["a_xm"], writes=["a_xm"])
        p.op("s", lambda e: e.activation(out=av[0:2, a_xm:a_xm + 4096],
                                         in_=av[0:2, a_xm:a_xm + 4096], func=AF.Ln),
             reads=["a_xm"], writes=["a_xm"])
        p.op("s", lambda e: e.activation(out=av[0:2, a_xm:a_xm + 4096],
                                         in_=av[0:2, a_xm:a_xm + 4096], func=AF.Exp, scale=-0.5),
             reads=["a_xm"], writes=["a_xm"])
        for c in range(8):
            p.op("t", lambda e, c=c: e.matmul(PS[3][:, 0:512], ctx["sb_ind2Tf"][0:2, :],
                                              av[0:2, a_xm + c * 512:a_xm + (c + 1) * 512],
                                              start=True, stop=True),
                 reads=["a_xm", "sb_ind2Tf", ("PS", 3)], writes=[("PS", 3)])
            p.op("v", lambda e, c=c: e.tensor_mul(blk8(b_kkn, c), blk8(b_kkn, c), PS[3][:, 0:512]),
                 reads=["b_kkn", ("PS", 3)], writes=["b_kkn"])
        for c in range(8):
            p.op("v", lambda e, c=c: e.tensor_scalar(av[:, a_scr:a_scr + 512], blk8(b_aS, c),
                                                     parap("k_a", c), parap("ka1m", c),
                                                     op0=ALU.mult, op1=ALU.add),
                 reads=["b_aS", ("par", "k_a"), ("par", "ka1m")], writes=["a_scr"])
            p.op("v", lambda e, c=c: e.tensor_mul(blk8(b_kT, c), blk8(b_kT, c), av[:, a_scr:a_scr + 512]),
                 reads=["b_kT", "a_scr"], writes=["b_kT"])

        proj_var("x_r")
        mm_feature_qtr(di["W_r"], b_rT, "b_rT", ("Wr", b))

        proj_var("x_v")
        for q in range(4):
            wo, wk = load_w(di["W_v"][:, q * 256:(q + 1) * 256], C, 256, ("Wv", b, q))
            for ts in range(4):
                for hh in range(2):
                    jcol = q * 256 + hh * 128
                    for c in range(8):
                        p.op("t", lambda e, ts=ts, hh=hh, c=c, wo=wo: e.matmul(
                            PS[0][:, 0:128 + 0] if False else PS[0][:, hh * 128: hh * 128 + 128],
                            blk8(b_var, c)[:, ts * 128:(ts + 1) * 128],
                            ab[:, wo + c * 256 + hh * 128: wo + c * 256 + (hh + 1) * 128],
                            start=(c == 0), stop=(c == 7)),
                            reads=[wk, "b_var", ("PS", 0)], writes=[("PS", 0)])
                    p.op("v", lambda e, ts=ts, jcol=jcol, hh=hh: e.tensor_copy(
                        out=ab[:, b_vt + ts * 1024 + jcol: b_vt + ts * 1024 + jcol + 128],
                        in_=PS[0][:, hh * 128: hh * 128 + 128]),
                        reads=[("PS", 0)], writes=["b_vt"])
        for ts in range(4):
            for c in range(8):
                p.op("t", lambda e, ts=ts, c=c: e.transpose(
                    PB[0][:, 0:128],
                    ab[:, b_vt + ts * 1024 + c * 128: b_vt + ts * 1024 + (c + 1) * 128],
                    sb_id[:]),
                    reads=["b_vt", "sb_id"], writes=["PB0"])
                p.op("v", lambda e, ts=ts, c=c: e.tensor_copy(
                    out=blk8(b_vT, c)[:, ts * 128:(ts + 1) * 128], in_=PB[0][:, 0:128]),
                    reads=["PB0"], writes=["b_vT"])

        for c in range(8):
            p.op("v", lambda e: e.memset(av[:, a_gsc:a_gsc + 1], 0.0), writes=["a_gsc"])
            p.op("v", lambda e, c=c: e.tensor_tensor_scan(
                out=av[:, a_gsc + 1:a_gsc + 513], data0=av[:, a_sig + c * 512:a_sig + (c + 1) * 512],
                data1=av[:, a_gsc:a_gsc + 1].to_broadcast([128, 512]),
                initial=0.0, op0=ALU.add, op1=ALU.add),
                reads=["a_sig", "a_gsc"], writes=["a_gsc"])
            p.op("v", lambda e, c=c: e.tensor_tensor(
                out=av[:, a_gr + c * 512:a_gr + (c + 1) * 512].rearrange("p (k l) -> p k l", l=LCH),
                in0=av[:, a_gsc + 1:a_gsc + 513].rearrange("p (k l) -> p k l", l=LCH),
                in1=av[:, a_gsc:a_gsc + 512].rearrange("p (k l) -> p k l", l=LCH)[:, :, 0:1]
                .to_broadcast([128, 4, LCH]),
                op=ALU.subtract),
                reads=["a_gsc"], writes=["a_gr"])

        if DEBUG_SUB == "p2":
            continue
        WSUB = DEBUG_SUB if DEBUG_SUB in ("w1", "w2", "w3", "w4", "w5") else None
        gr3 = av[:, a_gr:a_gr + 4096].rearrange("p (c t) -> p c t", t=512)
        sg3 = av[:, a_sig:a_sig + 4096].rearrange("p (c t) -> p c t", t=512)
        kkn3 = ab[:, b_kkn:b_kkn + 4096].rearrange("p (c t) -> p c t", t=512)
        aS3 = ab[:, b_aS:b_aS + 4096].rearrange("p (c t) -> p c t", t=512)
        rT3 = ab[:, b_rT:b_rT + 4096].rearrange("p (c t) -> p c t", t=512)
        kT3 = ab[:, b_kT:b_kT + 4096].rearrange("p (c t) -> p c t", t=512)
        o_rt, o_kt, o_at, o_bt = b_der, b_der + 1024, b_der + 2048, b_der + 3072
        for lc in range(4):
            tl = lc * LCH
            sl = lambda base: ab[:, base:base + 1024].rearrange("p (c t) -> p c t", t=128)
            p.op("s", lambda e, tl=tl: e.activation(
                out=av[:, a_gcol:a_gcol + 8],
                in_=gr3[:, :, tl + LCH - 1:tl + LCH], func=AF.Exp, scale=-LAM),
                reads=["a_gr"], writes=["a_gcol"])
            p.op("s", lambda e, tl=tl: e.activation(out=av[:, a_scr:a_scr + 1024]
                                                    .rearrange("p (c t) -> p c t", t=128),
                                                    in_=gr3[:, :, tl:tl + LCH],
                                                    func=AF.Exp, scale=-LAM),
                 reads=["a_gr"], writes=["a_scr"])
            p.op("v", lambda e, tl=tl: e.tensor_mul(sl(o_rt), rT3[:, :, tl:tl + LCH],
                                                    av[:, a_scr:a_scr + 1024]
                                                    .rearrange("p (c t) -> p c t", t=128)),
                 reads=["b_rT", "a_scr"], writes=["b_der"])
            p.op("v", lambda e, tl=tl: e.scalar_tensor_tensor(
                out=av[:, a_scr2:a_scr2 + 1024].rearrange("p (c t) -> p c t", t=128),
                in0=gr3[:, :, tl:tl + LCH], scalar=-1.0, in1=sg3[:, :, tl:tl + LCH],
                op0=ALU.mult, op1=ALU.add),
                reads=["a_gr", "a_sig"], writes=["a_scr2"])
            p.op("s", lambda e: e.activation(out=av[:, a_scr2:a_scr2 + 1024],
                                             in_=av[:, a_scr2:a_scr2 + 1024],
                                             func=AF.Exp, scale=LAM),
                 reads=["a_scr2"], writes=["a_scr2"])
            p.op("v", lambda e, tl=tl: e.scalar_tensor_tensor(
                out=sl(o_at), in0=kkn3[:, :, tl:tl + LCH], scalar=-1.0,
                in1=av[:, a_scr2:a_scr2 + 1024].rearrange("p (c t) -> p c t", t=128),
                op0=ALU.mult, op1=ALU.mult),
                reads=["b_kkn", "a_scr2"], writes=["b_der"])
            p.op("s", lambda e, tl=tl: e.activation(out=av[:, a_scr:a_scr + 1024]
                                                    .rearrange("p (c t) -> p c t", t=128),
                                                    in_=gr3[:, :, tl:tl + LCH],
                                                    func=AF.Exp, scale=LAM),
                 reads=["a_gr", "b_der"], writes=["a_scr"])
            p.op("v", lambda e, tl=tl: e.tensor_mul(sl(o_kt), kT3[:, :, tl:tl + LCH],
                                                    av[:, a_scr:a_scr + 1024]
                                                    .rearrange("p (c t) -> p c t", t=128)),
                 reads=["b_kT", "a_scr"], writes=["b_der"])
            p.op("v", lambda e, tl=tl: e.tensor_mul(
                av[:, a_scr2:a_scr2 + 1024].rearrange("p (c t) -> p c t", t=128),
                kkn3[:, :, tl:tl + LCH], aS3[:, :, tl:tl + LCH]),
                reads=["b_kkn", "b_aS"], writes=["a_scr2"])
            p.op("v", lambda e: e.tensor_mul(ab[:, o_bt:o_bt + 1024],
                                             av[:, a_scr2:a_scr2 + 1024]
                                             .bitcast(F32) if False else av[:, a_scr2:a_scr2 + 1024],
                                             av[:, a_scr:a_scr + 1024]),
                 reads=["a_scr2", "a_scr"], writes=["b_der"])
            for pc in range(8):
                for hh in range(2):
                    half = hh * 64
                    h = pc * 2 + hh
                    rv = ab[half:half + 64, o_rt + pc * 128:o_rt + (pc + 1) * 128]
                    kv_ = ab[half:half + 64, o_kt + pc * 128:o_kt + (pc + 1) * 128]
                    avw = ab[half:half + 64, o_at + pc * 128:o_at + (pc + 1) * 128]
                    bvw = ab[half:half + 64, o_bt + pc * 128:o_bt + (pc + 1) * 128]
                    rd_pre = ["b_der"]
                    vch = ab[:, b_vt + lc * 1024 + h * 64: b_vt + lc * 1024 + (h + 1) * 64]
                    st_hv = av[half:half + 64, a_state + pc * 64:a_state + (pc + 1) * 64]
                    stbf_hv = ab[half:half + 64, b_stbf + pc * 64:b_stbf + (pc + 1) * 64]
                    stsc_hv = av[half:half + 64, a_stsc + pc * 64:a_stsc + (pc + 1) * 64]
                    emit_wkv_head(ctx, locals())
            p.op("v", lambda e: e.tensor_copy(out=ab[:, b_stbf:b_stbf + 512],
                                              in_=av[:, a_state:a_state + 512]),
                 reads=[("state", pcx, hx) for pcx in range(8) for hx in range(2)],
                 writes=["stbf"])

        if DEBUG_SUB in ("p3", "w0", "w1", "w2", "w3", "w4", "w5"):
            continue
        emit_opost(ctx, locals())
        if DEBUG_SUB == "p4a":
            continue
        emit_l0tail(ctx, locals())


def emit_wkv_head(ctx, L):
    p, PS, PB = ctx["p"], ctx["PS"], ctx["PB"]
    ab, av = ctx["AB16"].t, ctx["AF32"].t
    sb_msk, sb_id64 = ctx["sb_msk"], ctx["sb_id64"]
    rv, kv_, avw, bvw = L["rv"], L["kv_"], L["avw"], L["bvw"]
    rd_pre, vch = L["rd_pre"], L["vch"]
    st_hv, stbf_hv, stsc_hv = L["st_hv"], L["stbf_hv"], L["stsc_hv"]
    half, pc = L["half"], L["pc"]
    hh = L["hh"]
    par = (pc * 2 + hh) % 2
    b_BTs = L["b_BTs"][par]
    b_Bc = L["b_Bc"][par]
    b_KTm = L["b_KTm"][par]
    b_Pbm = L["b_Pbm"][par]
    b_Pkm = L["b_Pkm"][par]
    b_Wc = L["b_Wc"][par]
    b_Btok = L["b_Btok"][par]
    b_Ktok = L["b_Ktok"][par]
    b_d1, a_gcol = L["b_d1"], L["a_gcol"]
    tl = L["tl"]
    m_us, m_ui, m_ls = sb_msk[:, 0:128], sb_msk[:, 128:256], sb_msk[:, 256:384]
    if DEBUG_SUB == "w0":
        return

    PSd = PS[0 + par]
    PSq = PS[2 + par]
    PSo = PS[4 + par]
    PBt = PB[0]
    pbs = par * 256
    kst = ("state", pc, hh)
    kssc = ("stsc", pc, hh)
    kmap = {"PSd": ("PS", par), "PSq": ("PS", 2 + par), "PSo": ("PS", 4 + par),
            "PBt": "PB0"}

    def K(nm):
        return kmap.get(nm, (nm, par))

    p.op("t", lambda e: e.matmul(PSd[:, 0:128], bvw, avw, start=True, stop=True),
         reads=rd_pre + [K("PSd")], writes=[K("PSd")])
    p.op("t", lambda e: e.matmul(PSd[:, 128:256], avw, bvw, start=True, stop=True),
         reads=rd_pre + [K("PSd")], writes=[K("PSd")])
    p.op("t", lambda e: e.matmul(PSd[:, 256:384], kv_, avw, start=True, stop=True),
         reads=rd_pre + [K("PSd")], writes=[K("PSd")])
    p.op("t", lambda e: e.matmul(PSd[:, 384:512], bvw, rv, start=True, stop=True),
         reads=rd_pre + [K("PSd")], writes=[K("PSd")])
    p.op("t", lambda e: e.matmul(PSq[:, 0:128], kv_, rv, start=True, stop=True),
         reads=rd_pre + [K("PSq")], writes=[K("PSq")])
    p.op("v", lambda e: e.tensor_mul(ab[:, b_BTs:b_BTs + 128], PSd[:, 0:128], m_us),
         reads=[K("PSd"), "msk_us"], writes=[K("BT0")])
    p.op("v", lambda e: e.tensor_mul(ab[:, b_Bc:b_Bc + 128], PSd[:, 128:256], m_ls),
         reads=[K("PSd"), "msk_ls"], writes=[K("B0")])
    p.op("v", lambda e: e.tensor_mul(ab[:, b_KTm:b_KTm + 128], PSd[:, 256:384], m_us),
         reads=[K("PSd"), "msk_us"], writes=[K("KTm")])
    p.op("v", lambda e: e.tensor_mul(ab[:, b_Pbm:b_Pbm + 128], PSd[:, 384:512], m_ui),
         reads=[K("PSd"), "msk_ui"], writes=[K("Pbm")])
    p.op("s", lambda e: e.activation(out=ab[:, b_Pkm:b_Pkm + 128], in_=PSq[:, 0:128], func=AF.Copy),
         reads=[K("PSq")], writes=[K("Pkm")])
    p.op("v", lambda e: e.tensor_mul(ab[:, b_Pkm:b_Pkm + 128], ab[:, b_Pkm:b_Pkm + 128], m_ui),
         reads=[K("Pkm"), "msk_ui"], writes=[K("Pkm")])

    if DEBUG_SUB == "w1":
        return
    for lv in range(6):
        btp = ab[:, b_BTs + lv * 128:b_BTs + (lv + 1) * 128]
        bp = ab[:, b_Bc + (lv % 2) * 128:b_Bc + (lv % 2) * 128 + 128]
        bn = ab[:, b_Bc + ((lv + 1) % 2) * 128:b_Bc + ((lv + 1) % 2) * 128 + 128]
        p.op("t", lambda e, bp=bp, btp=btp: e.matmul(PSq[:, 128:256], bp, btp, start=True, stop=True),
             reads=[K(f"BT{lv}"), K(f"B{lv}"), K("PSq")], writes=[K("PSq")])
        p.op("t", lambda e, btp=btp, bp=bp: e.matmul(PSq[:, 256:384], btp, bp, start=True, stop=True),
             reads=[K(f"BT{lv}"), K(f"B{lv}"), K("PSq")], writes=[K("PSq")])
        p.op("v", lambda e, lv=lv: e.tensor_copy(out=ab[:, b_BTs + (lv + 1) * 128:b_BTs + (lv + 2) * 128],
                                                 in_=PSq[:, 128:256]),
             reads=[K("PSq")], writes=[K(f"BT{lv + 1}")])
        p.op("v", lambda e, bn=bn: e.tensor_copy(out=bn, in_=PSq[:, 256:384]),
             reads=[K("PSq")], writes=[K(f"B{lv + 1}")])

    if DEBUG_SUB == "w2":
        return
    p.op("t", lambda e: e.matmul(PSq[:, 384:448], avw, stbf_hv, start=True, stop=False),
         reads=rd_pre + ["stbf", K("PSq")], writes=[K("PSq")])
    p.op("t", lambda e: e.matmul(PSq[:, 384:448], ab[:, b_KTm:b_KTm + 128], vch, start=False, stop=True),
         reads=[K("KTm"), "b_vt", K("PSq")], writes=[K("PSq")])
    p.op("v", lambda e: e.tensor_copy(out=ab[:, b_Wc:b_Wc + 64], in_=PSq[:, 384:448]),
         reads=[K("PSq")], writes=[K("W0")])
    for i in range(7):
        wcur = ab[:, b_Wc + (i % 2) * 64:b_Wc + (i % 2) * 64 + 64]
        wnext = ab[:, b_Wc + ((i + 1) % 2) * 64:b_Wc + ((i + 1) % 2) * 64 + 64]
        bti = ab[:, b_BTs + i * 128:b_BTs + (i + 1) * 128]
        p.op("t", lambda e, bti=bti, wcur=wcur: e.matmul(PSq[:, 448:512], bti, wcur, start=True, stop=True),
             reads=[K(f"BT{i}"), K(f"W{i}"), K("PSq")], writes=[K("PSq")])
        p.op("v", lambda e, wnext=wnext, wcur=wcur: e.tensor_add(wnext, PSq[:, 448:512], wcur),
             reads=[K("PSq"), K(f"W{i}")], writes=[K(f"W{i + 1}")])
    U = ab[:, b_Wc + 64:b_Wc + 128]

    if DEBUG_SUB == "w3":
        return
    pov = PSo[half:half + 64, 0:128]
    p.op("t", lambda e, pov=pov, half=half: e.matmul(pov, stbf_hv, rv, start=True, stop=False,
                                                     tile_position=(half, half)),
         reads=["stbf", K("PSo")] + rd_pre, writes=[K("PSo")])
    p.op("t", lambda e, pov=pov, half=half: e.matmul(pov, U, ab[:, b_Pbm:b_Pbm + 128],
                                                     start=False, stop=False, tile_position=(0, half)),
         reads=[K("W7"), K("Pbm"), K("PSo")], writes=[K("PSo")])
    p.op("t", lambda e, pov=pov, half=half: e.matmul(pov, vch, ab[:, b_Pkm:b_Pkm + 128],
                                                     start=False, stop=True, tile_position=(0, half)),
         reads=["b_vt", K("Pkm"), K("PSo")], writes=[K("PSo")])
    p.op("v", lambda e, pov=pov: e.tensor_copy(
        out=ab[half:half + 64, b_d1 + pc * 512 + tl: b_d1 + pc * 512 + tl + LCH], in_=pov),
        reads=[K("PSo")], writes=["b_d1"])

    if DEBUG_SUB == "w4":
        return
    p.op("t", lambda e: e.transpose(PBt[:, pbs:pbs + 64], bvw, sb_id64[half:half + 64, :]),
         reads=rd_pre + ["sb_id64", K("PBt")], writes=[K("PBt")])
    p.op("v", lambda e: e.tensor_copy(out=ab[:, b_Btok:b_Btok + 64], in_=PBt[:, pbs:pbs + 64]),
         reads=[K("PBt")], writes=[K("Btok")])
    p.op("t", lambda e: e.transpose(PBt[:, pbs + 64:pbs + 128], kv_, sb_id64[half:half + 64, :]),
         reads=rd_pre + ["sb_id64", K("PBt")], writes=[K("PBt")])
    p.op("v", lambda e: e.tensor_copy(out=ab[:, b_Ktok:b_Ktok + 64], in_=PBt[:, pbs + 64:pbs + 128]),
         reads=[K("PBt")], writes=[K("Ktok")])

    if DEBUG_SUB == "w5":
        return
    pstv = PSo[half:half + 64, 128:192]
    p.op("t", lambda e, pstv=pstv, half=half: e.matmul(pstv, ab[:, b_Btok:b_Btok + 64], U,
                                                       start=True, stop=False, tile_position=(0, half)),
         reads=[K("Btok"), K("W7"), K("PSo")], writes=[K("PSo")])
    p.op("t", lambda e, pstv=pstv, half=half: e.matmul(pstv, ab[:, b_Ktok:b_Ktok + 64], vch,
                                                       start=False, stop=True, tile_position=(0, half)),
         reads=[K("Ktok"), "b_vt", K("PSo")], writes=[K("PSo")])
    p.op("v", lambda e, pstv=pstv: e.tensor_add(stsc_hv, pstv, st_hv),
         reads=[K("PSo"), kst], writes=[kssc])
    p.op("v", lambda e: e.tensor_scalar_mul(st_hv, stsc_hv,
                                            av[half:half + 64, a_gcol + pc:a_gcol + pc + 1]),
         reads=[kssc, "a_gcol"], writes=[kst])


def emit_opost(ctx, L):
    p, PS = ctx["p"], ctx["PS"]
    ab, av = ctx["AB16"].t, ctx["AF32"].t
    sb_ind2, sb_ind2T = ctx["sb_ind2"], ctx["sb_ind2T"]
    par, po = ctx["sb_par"], ctx["par_off"]
    b_d1, b_var, b_rT, b_kT, b_vT, b_gg, b_kkn = (L[k] for k in
        ["b_d1", "b_var", "b_rT", "b_kT", "b_vT", "b_gg", "b_kkn"])
    a_scr, a_scr2, a_sig, a_gr, a_xm = (L[k] for k in ["a_scr", "a_scr2", "a_sig", "a_gr", "a_xm"])
    st2 = av[0:2, a_sig:a_sig + 4096]
    st2b = av[0:2, a_gr:a_gr + 4096]
    st3 = av[0:2, a_xm:a_xm + 4096]

    def blk8(base, c, n=512):
        return ab[:, base + c * 512: base + c * 512 + n]

    def parap(nm, c, n=1):
        o = po[nm]
        return par[:, o + c:o + c + n]

    for c in range(8):
        p.op("t", lambda e, c=c: e.matmul(PS[3][0:2, 0:512], sb_ind2[:], blk8(b_d1, c),
                                          start=True, stop=True),
             reads=["sb_ind2", "b_d1", ("PS", 3)], writes=[("PS", 3)])
        p.op("v", lambda e, c=c: e.tensor_copy(out=st2[:, c * 512:(c + 1) * 512],
                                               in_=PS[3][0:2, 0:512]),
             reads=[("PS", 3)], writes=["a_sig"])
        p.op("s", lambda e, c=c: e.activation(out=ab[:, b_var:b_var + 512], in_=blk8(b_d1, c),
                                              func=AF.Square),
             reads=["b_d1"], writes=["b_var"])
        p.op("t", lambda e, c=c: e.matmul(PS[3][0:2, 0:512], sb_ind2[:],
                                          ab[:, b_var:b_var + 512], start=True, stop=True),
             reads=["sb_ind2", "b_var", ("PS", 3)], writes=[("PS", 3)])
        p.op("v", lambda e, c=c: e.tensor_copy(out=st2b[:, c * 512:(c + 1) * 512],
                                               in_=PS[3][0:2, 0:512]),
             reads=[("PS", 3)], writes=["a_gr"])
    p.op("v", lambda e: e.tensor_scalar_mul(st2[:], st2[:], 1.0 / 64),
         reads=["a_sig"], writes=["a_sig"])
    p.op("s", lambda e: e.activation(out=st3[:], in_=st2[:], func=AF.Square),
         reads=["a_sig"], writes=["a_xm"])
    p.op("v", lambda e: e.tensor_scalar_mul(st2b[:], st2b[:], 1.0 / 64),
         reads=["a_gr"], writes=["a_gr"])
    p.op("v", lambda e: e.tensor_sub(st2b[:], st2b[:], st3[:]),
         reads=["a_gr", "a_xm"], writes=["a_gr"])
    p.op("v", lambda e: e.tensor_scalar_add(st2b[:], st2b[:], float(EPS_GN)),
         reads=["a_gr"], writes=["a_gr"])
    p.op("s", lambda e: e.activation(out=st2b[:], in_=st2b[:], func=AF.Ln),
         reads=["a_gr"], writes=["a_gr"])
    p.op("s", lambda e: e.activation(out=st2b[:], in_=st2b[:], func=AF.Exp, scale=-0.5),
         reads=["a_gr"], writes=["a_gr"])
    rko = po["r_k"]
    for c in range(8):
        p.op("v", lambda e, c=c: e.scalar_tensor_tensor(
            out=ab[:, b_var:b_var + 512], in0=blk8(b_rT, c), scalar=par[:, rko + c:rko + c + 1],
            in1=blk8(b_kT, c), op0=ALU.mult, op1=ALU.mult),
            reads=["b_rT", "b_kT", ("par", "r_k")], writes=["b_var"])
        p.op("t", lambda e, c=c: e.matmul(PS[3][0:2, 0:512], sb_ind2[:],
                                          ab[:, b_var:b_var + 512], start=True, stop=True),
             reads=["sb_ind2", "b_var", ("PS", 3)], writes=[("PS", 3)])
        p.op("v", lambda e, c=c: e.tensor_copy(out=st3[:, c * 512:(c + 1) * 512],
                                               in_=PS[3][0:2, 0:512]),
             reads=[("PS", 3)], writes=["a_xm"])
    for c in range(8):
        p.op("t", lambda e, c=c: e.matmul(PS[3][:, 0:512], ctx["sb_ind2Tf"][0:2, :],
                                          st2[:, c * 512:(c + 1) * 512], start=True, stop=True),
             reads=["a_sig", "sb_ind2Tf", ("PS", 3)], writes=[("PS", 3)])
        p.op("t", lambda e, c=c: e.matmul(PS[4][:, 0:512], ctx["sb_ind2Tf"][0:2, :],
                                          st2b[:, c * 512:(c + 1) * 512], start=True, stop=True),
             reads=["a_gr", "sb_ind2Tf", ("PS", 4)], writes=[("PS", 4)])
        p.op("v", lambda e, c=c: e.tensor_sub(av[:, a_scr:a_scr + 512], blk8(b_d1, c), PS[3][:, 0:512]),
             reads=["b_d1", ("PS", 3)], writes=["a_scr"])
        p.op("v", lambda e, c=c: e.tensor_mul(av[:, a_scr:a_scr + 512], av[:, a_scr:a_scr + 512],
                                              PS[4][:, 0:512]),
             reads=["a_scr", ("PS", 4)], writes=["a_scr"])
        p.op("v", lambda e, c=c: e.tensor_scalar(av[:, a_scr:a_scr + 512], av[:, a_scr:a_scr + 512],
                                                 parap("gn_w", c), parap("gn_b", c),
                                                 op0=ALU.mult, op1=ALU.add),
             reads=["a_scr", ("par", "gn_w"), ("par", "gn_b")], writes=["a_scr"])
        p.op("t", lambda e, c=c: e.matmul(PS[3][:, 0:512], ctx["sb_ind2Tf"][0:2, :],
                                          st3[:, c * 512:(c + 1) * 512], start=True, stop=True),
             reads=["a_xm", "sb_ind2Tf", ("PS", 3)], writes=[("PS", 3)])
        p.op("v", lambda e, c=c: e.tensor_mul(av[:, a_scr2:a_scr2 + 512], PS[3][:, 0:512],
                                              blk8(b_vT, c)),
             reads=[("PS", 3), "b_vT"], writes=["a_scr2"])
        p.op("v", lambda e, c=c: e.tensor_add(av[:, a_scr:a_scr + 512], av[:, a_scr:a_scr + 512],
                                              av[:, a_scr2:a_scr2 + 512]),
             reads=["a_scr", "a_scr2"], writes=["a_scr"])
        p.op("v", lambda e, c=c: e.tensor_mul(blk8(b_kkn, c), av[:, a_scr:a_scr + 512],
                                              blk8(b_gg, c)),
             reads=["a_scr", "b_gg"], writes=["b_kkn"])


def emit_l0tail(ctx, L):
    nc, p, di, ds, PS, PB = ctx["nc"], ctx["p"], ctx["di"], ctx["ds"], ctx["PS"], ctx["PB"]
    ab, av = ctx["AB16"].t, ctx["AF32"].t
    par, po = ctx["sb_par"], ctx["par_off"]
    sb_id = ctx["sb_id"]
    lnb = ctx["a_lnb"]
    b = L["b"]
    b_kkn, b_var, b_xfT, b_xa, b_d1 = L["b_kkn"], L["b_var"], L["b_xfT"], L["b_xa"], L["b_d1"]
    a_xm, a_scr, a_scr2, a_st = L["a_xm"], L["a_scr"], L["a_scr2"], L["a_st"]
    load_w = L["load_w"]

    def blk8(base, c, n=512):
        return ab[:, base + c * 512: base + c * 512 + n]

    def parap(nm, c, n=1):
        o = po[nm]
        return par[:, o + c:o + c + n]

    def xf_view(c, c0, n):
        return ab[:, b_xfT + c * 514 + c0: b_xfT + c * 514 + c0 + n]

    for q in range(4):
        wo, wk = load_w(di["W_o"][:, q * 256:(q + 1) * 256], C, 256, ("Wo", b, q))
        for ts in range(4):
            for c in range(8):
                p.op("t", lambda e, ts=ts, c=c, wo=wo, q=q: e.matmul(
                    PS[q % 2][:, 0:256],
                    blk8(b_kkn, c)[:, ts * 128:(ts + 1) * 128],
                    ab[:, wo + c * 256: wo + (c + 1) * 256],
                    start=(c == 0), stop=(c == 7)),
                    reads=[wk, "b_kkn", ("PS", q % 2)], writes=[("PS", q % 2)])
            p.op("v", lambda e, ts=ts, q=q: e.tensor_copy(
                out=av[:, a_xm + ts * 1024 + q * 256: a_xm + ts * 1024 + (q + 1) * 256],
                in_=PS[q % 2][:, 0:256]),
                reads=[("PS", q % 2)], writes=["a_xm"])
    for ts in range(4):
        gi = b * 4 + ts
        xm = av[:, a_xm + ts * 1024: a_xm + (ts + 1) * 1024]
        p.op("q", lambda e, gi=gi: e.dma_start(out=av[:, a_scr:a_scr + 1024],
                                               in_=ds["x0f"][gi * 128:(gi + 1) * 128, :]),
             reads=[("x0d", gi), "a_scr"], writes=["a_scr"], ch="lx0")
        p.op("v", lambda e, xm=xm: e.tensor_add(xm, xm, av[:, a_scr:a_scr + 1024]),
             reads=["a_xm", "a_scr"], writes=["a_xm"])
        ln_tok(ctx, "a_xm", xm, av[:, a_scr:a_scr + 1024], av[:, a_scr2:a_scr2 + 1024],
               "a_scr2", av[:, a_st:a_st + 16],
               wb=(ab[:, lnb + 4096:lnb + 5120], ab[:, lnb + 5120:lnb + 6144],
                   ("lnb", "ln_ffn_w"), ("lnb", "ln_ffn_b")), scratch_key="a_scr")
        p.op("v", lambda e: e.tensor_copy(out=ab[:, b_var:b_var + 1024],
                                          in_=av[:, a_scr2:a_scr2 + 1024]),
             reads=["a_scr2"], writes=["b_var"])
        for c in range(8):
            p.op("t", lambda e, c=c: e.transpose(PB[0][:, 0:128],
                                                 ab[:, b_var + c * 128:b_var + (c + 1) * 128],
                                                 sb_id[:]),
                 reads=["b_var", "sb_id", "PB0"], writes=["PB0"])
            p.op("v", lambda e, c=c, ts=ts: e.tensor_copy(
                out=xf_view(c, 1 + ts * 128, 128), in_=PB[0][:, 0:128]),
                reads=["PB0"], writes=["b_xfT"])
    for c in range(8):
        p.op("v", lambda e, c=c: e.tensor_sub(av[:, a_scr:a_scr + 512],
                                              xf_view(c, 0, 512), xf_view(c, 1, 512)),
             reads=["b_xfT"], writes=["a_scr"])
        p.op("v", lambda e, c=c: e.scalar_tensor_tensor(
            out=blk8(b_d1, c), in0=av[:, a_scr:a_scr + 512], scalar=parap("ffn_x_k", c),
            in1=xf_view(c, 1, 512), op0=ALU.mult, op1=ALU.add),
            reads=["a_scr", "b_xfT", ("par", "ffn_x_k")], writes=["b_d1"])
    p.op("v", lambda e: e.tensor_copy(
        out=ab[:, b_xfT:b_xfT + 8 * 514].rearrange("p (c w) -> p c w", w=514)[:, :, 0:1],
        in_=ab[:, b_xfT:b_xfT + 8 * 514].rearrange("p (c w) -> p c w", w=514)[:, :, 512:513]),
        reads=["b_xfT"], writes=["b_xfT"])

    for ts in range(4):
        gi = b * 4 + ts
        for q in range(16):
            w1o, w1k = load_w(di["W_ffn1"][:, q * 256:(q + 1) * 256], C, 256, ("f1", b, ts, q))
            for fc in range(2):
                fg = q * 2 + fc
                for c in range(8):
                    p.op("t", lambda e, fc=fc, c=c, w1o=w1o, ts=ts: e.matmul(
                        PS[2][:, 0:128],
                        ab[:, w1o + c * 256 + fc * 128: w1o + c * 256 + (fc + 1) * 128],
                        blk8(b_d1, c)[:, ts * 128:(ts + 1) * 128],
                        start=(c == 0), stop=(c == 7)),
                        reads=[w1k, "b_d1", ("PS", 2)], writes=[("PS", 2)])
                p.op("s", lambda e: e.activation(out=av[:, a_scr:a_scr + 128], in_=PS[2][:, 0:128],
                                                 func=AF.Relu),
                     reads=[("PS", 2)], writes=["a_scr"])
                p.op("s", lambda e, fg=fg: e.activation(
                    out=ab[:, b_xa + fg * 128: b_xa + (fg + 1) * 128],
                    in_=av[:, a_scr:a_scr + 128], func=AF.Square),
                    reads=["a_scr"], writes=["b_xa"])
        for q in range(8):
            w2o, w2k = load_w(di["W_ffn2"][q * 512:(q + 1) * 512, :], 512, C // 4, ("f2x", b, ts, q)) \
                if False else (None, None)
        first = [True, True]
        for fs in range(8):
            for half in range(2):
                w2o, w2k = load_w(di["W_ffn2"][fs * 512:(fs + 1) * 512, half * 512:(half + 1) * 512],
                                  512, 512, ("f2", b, ts, fs, half))
                for fc in range(4):
                    fg = fs * 4 + fc
                    p.op("t", lambda e, fg=fg, fc=fc, half=half, w2o=w2o, fs=fs: e.matmul(
                        PS[4 + half][:, 0:512],
                        ab[:, b_xa + fg * 128: b_xa + (fg + 1) * 128],
                        ab[:, w2o + fc * 512: w2o + (fc + 1) * 512],
                        start=(fg == 0), stop=(fg == 31)),
                        reads=[w2k, "b_xa", ("PS", 4 + half)], writes=[("PS", 4 + half)])
        xm = av[:, a_xm + ts * 1024: a_xm + (ts + 1) * 1024]
        for half in range(2):
            p.op("v", lambda e, half=half, xm=xm: e.tensor_add(
                av[:, a_scr2 + half * 512:a_scr2 + half * 512 + 512], PS[4 + half][:, 0:512],
                xm[:, half * 512:half * 512 + 512]),
                reads=[("PS", 4 + half), "a_xm"], writes=["a_scr2"])
        p.op("q", lambda e, gi=gi: e.dma_start(out=ds["sattn"][gi * 128:(gi + 1) * 128, :],
                                               in_=av[:, a_scr2:a_scr2 + 1024]),
             reads=["a_scr2"], writes=[("sad", gi)], ch="stsa")


def emit_l0ffn(ctx):
    nc, p, di, ds = ctx["nc"], ctx["p"], ctx["di"], ctx["ds"]
    AF32, AB16, PS, PB = ctx["AF32"], ctx["AB16"], ctx["PS"], ctx["PB"]
    av, ab = AF32.t, AB16.t
    sb_id, sb_idf = ctx["sb_id"], ctx["sb_idf"]
    logitsT = ctx["logitsT"]

    p.barrier()
    AF32.reset("keep")
    AB16.reset("keep")

    a_msk = AF32.alloc(2048)
    a_aux = AF32.alloc(2048)
    a_pos = AF32.alloc(2048)
    a_bis = AF32.alloc(32)
    a_ptt = AF32.alloc(16 * 16)
    a_tv = AF32.alloc(16)
    a_pay = AF32.alloc(2)
    a_sidx = AF32.alloc(8)
    a_scr = AF32.alloc(1024)
    a_isc = AF32.alloc(8)
    a_pay8 = AF32.alloc(16)

    a_scr2 = AF32.alloc(1024)
    i_idx = nc.alloc_sbuf_tensor("i_idx", [128, 8], I32)
    b_xin = AB16.alloc(1024)
    b_xinT = AB16.alloc(8 * 512)
    b_wA = AB16.alloc(2048)
    b_wA2 = AB16.alloc(2048)
    b_wB = AB16.alloc(1024)
    b_wB2 = AB16.alloc(1024)
    b_xin2 = AB16.alloc(1024)
    b_hf = AB16.alloc(32 * 512)
    b_h1 = AB16.alloc(512)

    lo = av[0:8, a_bis + 0:a_bis + 1]
    hi = av[0:8, a_bis + 1:a_bis + 2]
    mid = av[0:8, a_bis + 2:a_bis + 3]
    cnt = av[0:8, a_bis + 3:a_bis + 4]
    sel1 = av[0:8, a_bis + 4:a_bis + 5]
    sel1i = av[0:8, a_bis + 5:a_bis + 6]
    dd = av[0:8, a_bis + 6:a_bis + 7]
    vk = av[0:8, a_bis + 7:a_bis + 8]
    need = av[0:8, a_bis + 9:a_bis + 10]
    aff = logitsT

    for i in range(NT):
        p.op("t", lambda e, i=i: e.transpose(PS[4][:, 0:8], logitsT[0:8, i * 128:(i + 1) * 128],
                                             sb_idf[0:8, 0:8]),
             reads=[("lgT", i), "sb_idf", ("PS", 4)], writes=[("PS", 4)])
        p.op("v", lambda e: e.tensor_copy(out=av[:, a_scr:a_scr + 8], in_=PS[4][:, 0:8]),
             reads=[("PS", 4)], writes=["a_scr"])
        p.op("v", lambda e: e.tensor_reduce(out=av[:, a_scr + 8:a_scr + 9],
                                            in_=av[:, a_scr:a_scr + 8], axis=AX.X,
                                            op=ALU.max, negate=True),
             reads=["a_scr"], writes=["a_scr"])
        p.op("s", lambda e: e.activation(out=av[:, a_scr:a_scr + 8], in_=av[:, a_scr:a_scr + 8],
                                         func=AF.Exp, bias=av[:, a_scr + 8:a_scr + 9],
                                         accum_out=av[:, a_scr + 9:a_scr + 10]),
             reads=["a_scr"], writes=["a_scr"])
        p.op("s", lambda e: e.activation(out=av[:, a_scr + 10:a_scr + 11],
                                         in_=av[:, a_scr + 9:a_scr + 10], func=AF.Ln),
             reads=["a_scr"], writes=["a_scr"])
        p.op("s", lambda e: e.activation(out=av[:, a_scr + 8:a_scr + 9],
                                         in_=av[:, a_scr + 10:a_scr + 11], func=AF.Exp, scale=-1.0),
             reads=["a_scr"], writes=["a_scr"])
        p.op("v", lambda e: e.tensor_scalar_mul(av[:, a_scr:a_scr + 8],
                                                av[:, a_scr:a_scr + 8],
                                                av[:, a_scr + 8:a_scr + 9]),
             reads=["a_scr"], writes=["a_scr"])
        p.op("t", lambda e, i=i: e.transpose(PS[4][0:8, 0:128], av[:, a_scr:a_scr + 8], sb_idf[:]),
             reads=["a_scr", "sb_idf", ("PS", 4)], writes=[("PS", 4)])
        p.op("v", lambda e, i=i: e.tensor_copy(out=aff[0:8, i * 128:(i + 1) * 128],
                                               in_=PS[4][0:8, 0:128]),
             reads=[("PS", 4)], writes=["aff"])

    p.op("v", lambda e: e.memset(lo, 0.0), writes=["bis"])
    p.op("v", lambda e: e.memset(hi, 1.0), writes=["bis"])
    for it in range(40):
        p.op("v", lambda e: e.tensor_add(mid, lo, hi), reads=["bis"], writes=["bis"])
        p.op("v", lambda e: e.tensor_scalar_mul(mid, mid, 0.5), reads=["bis"], writes=["bis"])
        p.op("v", lambda e: e.tensor_single_scalar(out=av[0:8, a_msk:a_msk + 2048],
                                                   in_=aff[0:8, :], scalar=mid, op=ALU.is_gt),
             reads=["aff", "bis"], writes=["a_msk"])
        p.op("v", lambda e: e.tensor_reduce(out=cnt, in_=av[0:8, a_msk:a_msk + 2048],
                                            axis=AX.X, op=ALU.add),
             reads=["a_msk"], writes=["bis"])
        p.op("v", lambda e: e.tensor_single_scalar(out=sel1, in_=cnt, scalar=512.0, op=ALU.is_ge),
             reads=["bis"], writes=["bis"])
        p.op("v", lambda e: e.tensor_scalar(sel1i, sel1, -1.0, 1.0, op0=ALU.mult, op1=ALU.add),
             reads=["bis"], writes=["bis"])
        p.op("v", lambda e: e.tensor_sub(dd, mid, lo), reads=["bis"], writes=["bis"])
        p.op("v", lambda e: e.scalar_tensor_tensor(out=lo, in0=dd, scalar=sel1, in1=lo,
                                                   op0=ALU.mult, op1=ALU.add),
             reads=["bis"], writes=["bis"])
        p.op("v", lambda e: e.tensor_sub(dd, mid, hi), reads=["bis"], writes=["bis"])
        p.op("v", lambda e: e.scalar_tensor_tensor(out=hi, in0=dd, scalar=sel1i, in1=hi,
                                                   op0=ALU.mult, op1=ALU.add),
             reads=["bis"], writes=["bis"])
    p.op("v", lambda e: e.tensor_single_scalar(out=av[0:8, a_msk:a_msk + 2048], in_=aff[0:8, :],
                                               scalar=hi, op=ALU.is_le),
         reads=["aff", "bis"], writes=["a_msk"])
    p.op("v", lambda e: e.tensor_mul(av[0:8, a_aux:a_aux + 2048], aff[0:8, :],
                                     av[0:8, a_msk:a_msk + 2048]),
         reads=["aff", "a_msk"], writes=["a_aux"])
    p.op("v", lambda e: e.tensor_reduce(out=vk, in_=av[0:8, a_aux:a_aux + 2048],
                                        axis=AX.X, op=ALU.max),
         reads=["a_aux"], writes=["bis"])
    p.op("v", lambda e: e.tensor_single_scalar(out=av[0:8, a_msk:a_msk + 2048], in_=aff[0:8, :],
                                               scalar=vk, op=ALU.is_gt),
         reads=["aff", "bis"], writes=["a_msk"])
    p.op("v", lambda e: e.tensor_reduce(out=cnt, in_=av[0:8, a_msk:a_msk + 2048],
                                        axis=AX.X, op=ALU.add),
         reads=["a_msk"], writes=["bis"])
    p.op("v", lambda e: e.tensor_scalar(need, cnt, -1.0, 512.0, op0=ALU.mult, op1=ALU.add),
         reads=["bis"], writes=["bis"])
    p.op("v", lambda e: e.tensor_single_scalar(out=av[0:8, a_aux:a_aux + 2048], in_=aff[0:8, :],
                                               scalar=vk, op=ALU.is_equal),
         reads=["aff", "bis"], writes=["a_aux"])
    p.op("v", lambda e: e.memset(av[0:8, a_pos:a_pos + 1], 0.0), writes=["a_pos"])
    p.op("v", lambda e: e.tensor_tensor_scan(out=av[0:8, a_pos:a_pos + 2048],
                                             data0=av[0:8, a_aux:a_aux + 2048],
                                             data1=av[0:8, a_pos:a_pos + 1].to_broadcast([8, 2048]),
                                             initial=0.0, op0=ALU.add, op1=ALU.add),
         reads=["a_aux", "a_pos"], writes=["a_pos"])
    p.op("v", lambda e: e.tensor_single_scalar(out=av[0:8, a_pos:a_pos + 2048],
                                               in_=av[0:8, a_pos:a_pos + 2048],
                                               scalar=need, op=ALU.is_le),
         reads=["a_pos", "bis"], writes=["a_pos"])
    p.op("v", lambda e: e.tensor_mul(av[0:8, a_aux:a_aux + 2048], av[0:8, a_aux:a_aux + 2048],
                                     av[0:8, a_pos:a_pos + 2048]),
         reads=["a_aux", "a_pos"], writes=["a_aux"])
    p.op("v", lambda e: e.tensor_add(av[0:8, a_msk:a_msk + 2048], av[0:8, a_msk:a_msk + 2048],
                                     av[0:8, a_aux:a_aux + 2048]),
         reads=["a_msk", "a_aux"], writes=["a_msk"])
    p.op("v", lambda e: e.tensor_mul(av[0:8, a_aux:a_aux + 2048], aff[0:8, :],
                                     av[0:8, a_msk:a_msk + 2048]),
         reads=["aff", "a_msk"], writes=["a_aux"])
    p.op("v", lambda e: e.memset(av[0:8, a_pos:a_pos + 1], 0.0), writes=["a_pos"])
    p.op("v", lambda e: e.tensor_tensor_scan(out=av[0:8, a_pos:a_pos + 2048],
                                             data0=av[0:8, a_msk:a_msk + 2048],
                                             data1=av[0:8, a_pos:a_pos + 1].to_broadcast([8, 2048]),
                                             initial=0.0, op0=ALU.add, op1=ALU.add),
         reads=["a_msk", "a_pos"], writes=["a_pos"])
    p.op("v", lambda e: e.tensor_scalar_add(av[0:8, a_pos:a_pos + 2048],
                                            av[0:8, a_pos:a_pos + 2048], -1.0),
         reads=["a_pos"], writes=["a_pos"])
    p.op("v", lambda e: e.tensor_mul(av[0:8, a_pos:a_pos + 2048], av[0:8, a_pos:a_pos + 2048],
                                     av[0:8, a_msk:a_msk + 2048]),
         reads=["a_pos", "a_msk"], writes=["a_pos"])
    p.op("v", lambda e: e.scalar_tensor_tensor(out=av[0:8, a_scr:a_scr + 1024] if False else
                                               av[0:8, a_msk:a_msk + 2048],
                                               in0=av[0:8, a_msk:a_msk + 2048], scalar=-4096.0,
                                               in1=av[0:8, a_pos:a_pos + 2048],
                                               op0=ALU.mult, op1=ALU.bypass),
         reads=["a_msk"], writes=["a_msk"]) if False else None
    p.op("v", lambda e: e.tensor_scalar(av[0:8, a_aux + 0:a_aux + 0] if False else
                                        av[0:8, a_msk:a_msk + 2048],
                                        av[0:8, a_msk:a_msk + 2048], -4096.0, 4096.0,
                                        op0=ALU.mult, op1=ALU.add),
         reads=["a_msk"], writes=["a_msk"])
    p.op("v", lambda e: e.tensor_add(av[0:8, a_pos:a_pos + 2048], av[0:8, a_pos:a_pos + 2048],
                                     av[0:8, a_msk:a_msk + 2048]),
         reads=["a_pos", "a_msk"], writes=["a_pos"])

    p.op("g", lambda e: e.iota(av[:, a_tv:a_tv + 16], [[128, 16]], base=0,
                               channel_multiplier=1, allow_small_or_imprecise_dtypes=True),
         writes=["a_tv"])

    i_sc = av[:, a_isc:a_isc + 8].bitcast(I32)
    for i in range(NT):
        ip = i % 2
        pto = a_ptt + ip * 16
        p.op("t", lambda e, i=i, ip=ip: e.transpose(PS[4 + ip][:, 0:8],
                                                    av[0:8, a_pos + i * 128:a_pos + (i + 1) * 128],
                                                    sb_idf[0:8, 0:8]),
             reads=["a_pos", "sb_idf", ("PS", 4 + ip)], writes=[("PS", 4 + ip)])
        p.op("v", lambda e, i=i, ip=ip, pto=pto: e.tensor_copy(out=av[:, pto:pto + 8],
                                                               in_=PS[4 + ip][:, 0:8]),
             reads=[("PS", 4 + ip), ("a_ptt", ip)], writes=[("a_ptt", ip)])
        p.op("t", lambda e, i=i, ip=ip: e.transpose(PS[4 + ip][:, 8:16],
                                                    av[0:8, a_aux + i * 128:a_aux + (i + 1) * 128],
                                                    sb_idf[0:8, 0:8]),
             reads=["a_aux", "sb_idf", ("PS", 4 + ip)], writes=[("PS", 4 + ip)])
        p.op("v", lambda e, i=i, ip=ip, pto=pto: e.tensor_copy(out=av[:, pto + 8:pto + 16],
                                                               in_=PS[4 + ip][:, 8:16]),
             reads=[("PS", 4 + ip), ("a_ptt", ip)], writes=[("a_ptt", ip)])
        for ee in range(8):
            p.op("v", lambda e, ee=ee, ip=ip, pto=pto: e.tensor_copy(
                out=i_sc[:, ee:ee + 1], in_=av[:, pto + ee:pto + ee + 1]),
                reads=[("a_ptt", ip), ("isc", ee)], writes=[("isc", ee)])
            p.op("v", lambda e, i=i, ee=ee: e.tensor_copy(
                out=av[:, a_pay8 + 2 * ee:a_pay8 + 2 * ee + 1],
                in_=av[:, a_tv + i:a_tv + i + 1]),
                reads=["a_tv", ("pay", ee)], writes=[("pay", ee)])
            p.op("v", lambda e, ee=ee, ip=ip, pto=pto: e.tensor_copy(
                out=av[:, a_pay8 + 2 * ee + 1:a_pay8 + 2 * ee + 2],
                in_=av[:, pto + 8 + ee:pto + 9 + ee]),
                reads=[("a_ptt", ip), ("pay", ee)], writes=[("pay", ee)])
            p.op("g", lambda e, ee=ee: e.indirect_dma_start(
                out=ds["idxw%d" % ee], out_offset=IndirectOffsetOnAxis(i_sc[:, ee:ee + 1], 0),
                in_=av[:, a_pay8 + 2 * ee:a_pay8 + 2 * ee + 2], in_offset=None),
                reads=[("isc", ee), ("pay", ee)], writes=[("idxd", ee)], ch="sc" + str(ee % 4))

    p.op("v", lambda e: e.memset(av[:, a_scr:a_scr + 1024], 0.0),
         writes=[("a_scr", 0), ("a_scr", 1)])
    for i in range(NT):
        p.op("q", lambda e, i=i: e.dma_start(out=ds["g0a"][i * 128:(i + 1) * 128, :],
                                             in_=av[:, a_scr:a_scr + 512]),
             reads=[("a_scr", 0)], writes=[("g0z", i)], ch="g0z")
        p.op("q", lambda e, i=i: e.dma_start(out=ds["g0b"][i * 128:(i + 1) * 128, :],
                                             in_=av[:, a_scr:a_scr + 512]),
             reads=[("a_scr", 0)], writes=[("g0z", i)], ch="g0z")

    for ee in range(8):
        p.op("q", lambda e, ee=ee: e.dma_start(
            out=av[:, a_sidx:a_sidx + 8].rearrange("p (g x) -> p g x", x=2),
            in_=ds["idxw%d" % ee][0:512, :].rearrange("(g p) x -> p g x", p=128)),
            reads=[("idxd", ee), "a_sidx"], writes=["a_sidx"], ch="lsidx")
        for g in range(4):
            p.op("v", lambda e, g=g: e.tensor_copy(out=i_idx[:, g:g + 1],
                                                   in_=av[:, a_sidx + 2 * g:a_sidx + 2 * g + 1]),
                 reads=["a_sidx"], writes=["i_idx"])
        for g in range(4):
            gp = g % 2
            xo = b_xin if gp == 0 else b_xin2
            p.op("g", lambda e, g=g, xo=xo, gp=gp: e.indirect_dma_start(
                out=ab[:, xo:xo + 1024], out_offset=None,
                in_=ds["xn_bf"], in_offset=IndirectOffsetOnAxis(i_idx[:, g:g + 1], 0)),
                reads=["i_idx", ("b_xin", gp)] + [("xnbd", i) for i in range(NT)],
                writes=[("b_xin", gp)], ch="gxin" + str(gp))
            for c in range(8):
                p.op("t", lambda e, c=c, xo=xo, gp=gp: e.transpose(
                    PB[gp][:, 0:128], ab[:, xo + c * 128:xo + (c + 1) * 128], sb_id[:]),
                    reads=[("b_xin", gp), "sb_id", "PB%d" % gp], writes=["PB%d" % gp])
                p.op("v", lambda e, c=c, g=g, gp=gp: e.tensor_copy(
                    out=ab[:, b_xinT + c * 512 + g * 128: b_xinT + c * 512 + (g + 1) * 128],
                    in_=PB[gp][:, 0:128]),
                    reads=["PB%d" % gp], writes=["b_xinT"])
        for fs in range(32):
            fp = fs % 2
            wo = b_wA if fp == 0 else b_wA2
            so = a_scr + fp * 512
            p.op("g", lambda e, ee=ee, fs=fs, wo=wo, fp=fp: e.dma_start(
                out=ab[:, wo:wo + 1024].rearrange("p (c j) -> p c j", c=8),
                in_=di["w_gT"][ee, :, fs * 128:(fs + 1) * 128].rearrange("(c p) j -> p c j", p=128)),
                reads=[("b_wA", fp)], writes=[("b_wA", fp)], ch="fWA" + str(fp))
            p.op("g", lambda e, ee=ee, fs=fs, wo=wo, fp=fp: e.dma_start(
                out=ab[:, wo + 1024:wo + 2048].rearrange("p (c j) -> p c j", c=8),
                in_=di["w_uT"][ee, :, fs * 128:(fs + 1) * 128].rearrange("(c p) j -> p c j", p=128)),
                reads=[("b_wA", fp)], writes=[("b_wA", fp)], ch="fWA" + str(fp))
            for c in range(8):
                p.op("t", lambda e, c=c, wo=wo, fp=fp: e.matmul(
                    PS[fp][:, 0:512],
                    ab[:, wo + c * 128:wo + (c + 1) * 128],
                    ab[:, b_xinT + c * 512:b_xinT + (c + 1) * 512],
                    start=(c == 0), stop=(c == 7)),
                    reads=[("b_wA", fp), "b_xinT", ("PS", fp)], writes=[("PS", fp)])
            for c in range(8):
                p.op("t", lambda e, c=c, wo=wo, fp=fp: e.matmul(
                    PS[2 + fp][:, 0:512],
                    ab[:, wo + 1024 + c * 128:wo + 1024 + (c + 1) * 128],
                    ab[:, b_xinT + c * 512:b_xinT + (c + 1) * 512],
                    start=(c == 0), stop=(c == 7)),
                    reads=[("b_wA", fp), "b_xinT", ("PS", 2 + fp)], writes=[("PS", 2 + fp)])
            p.op("s", lambda e, fp=fp, so=so: e.activation(out=av[:, so:so + 512],
                                                           in_=PS[fp][:, 0:512], func=AF.Silu),
                 reads=[("PS", fp), ("a_scr", fp)], writes=[("a_scr", fp)])
            p.op("v", lambda e, fs=fs, fp=fp, so=so: e.tensor_mul(
                ab[:, b_hf + fs * 512: b_hf + (fs + 1) * 512],
                av[:, so:so + 512], PS[2 + fp][:, 0:512]),
                reads=[("a_scr", fp), ("PS", 2 + fp)], writes=["b_hf"])
        for half in range(2):
            for fs in range(32):
                fp = fs % 2
                wo = b_wB if fp == 0 else b_wB2
                p.op("g", lambda e, ee=ee, fs=fs, half=half, wo=wo, fp=fp: e.dma_start(
                    out=ab[:, wo:wo + 512],
                    in_=di["w_dn"][ee, fs * 128:(fs + 1) * 128, half * 512:(half + 1) * 512]),
                    reads=[("b_wB", fp)], writes=[("b_wB", fp)], ch="fWB" + str(fp))
                for g in range(4):
                    p.op("t", lambda e, fs=fs, g=g, wo=wo, fp=fp: e.matmul(
                        PS[g][:, 0:512],
                        ab[:, b_hf + fs * 512 + g * 128: b_hf + fs * 512 + (g + 1) * 128],
                        ab[:, wo:wo + 512],
                        start=(fs == 0), stop=(fs == 31)),
                        reads=[("b_wB", fp), "b_hf", ("PS", g)], writes=[("PS", g)])
            for g in range(4):
                gp = g % 2
                so = a_scr + gp * 512
                p.op("v", lambda e, g=g, half=half, so=so, gp=gp: e.tensor_scalar_mul(
                    av[:, so:so + 512],
                    PS[g][:, 0:512],
                    av[:, a_sidx + 2 * g + 1:a_sidx + 2 * g + 2]),
                    reads=[("PS", g), "a_sidx", ("a_scr", gp)], writes=[("a_scr", gp)])
                p.op("g", lambda e, g=g, half=half, so=so, gp=gp: e.indirect_dma_start(
                    out=ds["g0a" if half == 0 else "g0b"],
                    out_offset=IndirectOffsetOnAxis(i_idx[:, g:g + 1], 0),
                    in_=av[:, so:so + 512], in_offset=None, compute_op=ALU.add),
                    reads=["i_idx", ("a_scr", gp), ("g0d",)] + [("g0z", i) for i in range(NT)],
                    writes=[("g0d",)], ch="g0sc")

def emit_combine0(ctx):
    nc, p, di, ds = ctx["nc"], ctx["p"], ctx["di"], ctx["ds"]
    AF32, AB16 = ctx["AF32"], ctx["AB16"]
    av = AF32.t
    p.barrier()
    AF32.reset("keep")
    AB16.reset("keep")
    a_x = AF32.alloc(1024)
    a_s = AF32.alloc(1024)
    a_g = AF32.alloc(1024)
    for i in range(NT):
        p.op("q", lambda e, i=i: e.dma_start(out=av[:, a_x:a_x + 1024],
                                             in_=di["X"][i * 128:(i + 1) * 128, :]),
             reads=["c0x"], writes=["c0x"], ch="c0x")
        p.op("q", lambda e, i=i: e.dma_start(out=av[:, a_s:a_s + 1024],
                                             in_=ds["sattn"][i * 128:(i + 1) * 128, :]),
             reads=["c0s", ("sad", i)], writes=["c0s"], ch="c0s")
        p.op("q", lambda e, i=i: e.dma_start(out=av[:, a_g:a_g + 512],
                                             in_=ds["g0a"][i * 128:(i + 1) * 128, :]),
             reads=["c0g", ("g0d",)], writes=["c0g"], ch="c0g")
        p.op("q", lambda e, i=i: e.dma_start(out=av[:, a_g + 512:a_g + 1024],
                                             in_=ds["g0b"][i * 128:(i + 1) * 128, :]),
             reads=["c0g", ("g0d",)], writes=["c0g"], ch="c0g")
        p.op("s", lambda e: e.activation(out=av[:, a_g:a_g + 1024], in_=av[:, a_g:a_g + 1024],
                                         func=AF.Silu),
             reads=["c0g"], writes=["c0g"])
        p.op("v", lambda e: e.tensor_mul(av[:, a_s:a_s + 1024], av[:, a_s:a_s + 1024],
                                         av[:, a_g:a_g + 1024]),
             reads=["c0s", "c0g"], writes=["c0s"])
        p.op("v", lambda e: e.tensor_add(av[:, a_x:a_x + 1024], av[:, a_x:a_x + 1024],
                                         av[:, a_s:a_s + 1024]),
             reads=["c0x", "c0s"], writes=["c0x"])
        p.op("q", lambda e, i=i: e.dma_start(out=ds["x1"][i * 128:(i + 1) * 128, :],
                                             in_=av[:, a_x:a_x + 1024]),
             reads=["c0x"], writes=[("x1d", i)], ch="stx1")


def emit_layer1(ctx):
    nc, p, di, ds, OUT = ctx["nc"], ctx["p"], ctx["di"], ctx["ds"], ctx["OUT"]
    AF32, AB16, PS, PB = ctx["AF32"], ctx["AB16"], ctx["PS"], ctx["PB"]
    av, ab = AF32.t, AB16.t
    sb_id, sb_idf, sb_onesb, sb_invf = ctx["sb_id"], ctx["sb_idf"], ctx["sb_onesb"], ctx["sb_invf"]

    p.barrier()
    AF32.reset("keep")
    AB16.reset("keep")

    meanrow = nc.dram_tensor("scr_meanrow", [1, C], F32, kind="Internal").ap()
    meanT_d = nc.dram_tensor("scr_meanT", [128, 8], F32, kind="Internal").ap()
    rwrow_d = nc.dram_tensor("scr_rwrow", [1, 8], F32, kind="Internal").ap()
    cw_d = nc.dram_tensor("scr_cw", [1, 8], F32, kind="Internal").ap()

    xn2T_d = nc.dram_tensor("scr_xn2T", [C, T], BF16, kind="Internal").ap()
    b_x2T = AB16.alloc(8 * 512)
    b_latT = AB16.alloc(2 * 2048)
    b_st = AB16.alloc(1024)
    b_rows = AB16.alloc(4 * 1024)
    b_b2 = AB16.alloc(1024)
    b_qg = AB16.alloc(65)
    b_kg = AB16.alloc(65)
    b_vg = AB16.alloc(65)
    b_qgT = AB16.alloc(512)
    b_kgT = AB16.alloc(512)
    b_pT = AB16.alloc(512)
    b_og = AB16.alloc(64)
    b_qg2 = AB16.alloc(65)
    b_kg2 = AB16.alloc(65)
    b_vg2 = AB16.alloc(65)
    b_og2 = AB16.alloc(64)
    b_hT = AB16.alloc(32 * 512)
    b_oT = b_hT
    b_g2 = AB16.alloc(16 * 1024)
    b_msk01 = AB16.alloc(512)
    b_wA = AB16.alloc(2048)
    b_wB = AB16.alloc(2048)
    b_gb = AB16.alloc(2048)

    a_x = AF32.alloc(1024)
    a_sq = AF32.alloc(1024)
    a_o = AF32.alloc(1024)
    a_st = AF32.alloc(16)
    a_sc = AF32.alloc(1024)
    a_cc = AF32.alloc(1024)
    a_tv = AF32.alloc(16)
    a_i32 = AF32.alloc(32)
    a_pi = AF32.alloc(1)
    a_mt = AF32.alloc(8)
    a_rw = AF32.alloc(32)
    a_cwb = AF32.alloc(12)
    i_fl = nc.alloc_sbuf_tensor("i_fl", [128, 32], I32)

    selW1 = nc.dram_tensor("scr_selW1", [8192, 1024], BF16, kind="Internal").ap()
    selW2 = nc.dram_tensor("scr_selW2", [8192, 1024], BF16, kind="Internal").ap()
    eidx_d = nc.dram_tensor("scr_eidx", [1, 2], F32, kind="Internal").ap()
    m12_d = nc.dram_tensor("scr_m12", [1, 18], F32, kind="Internal").ap()
    a_eid = AF32.alloc(4)
    a_gw = AF32.alloc(64)
    a_gwi = AF32.alloc(64)
    a_b1s = AF32.alloc(64)
    a_mb = AF32.alloc(25)
    if AB16.free_at % 2:
        AB16.alloc(1)
    b_gst = AB16.alloc(4096)
    i_gw = av[:, a_gwi:a_gwi + 64].bitcast(I32)
    sb_io8 = av[0:1, a_mb + 17:a_mb + 25]
    c_io8 = nc.inline_tensor(np.arange(8, dtype=np.float32).reshape(1, 8), "c_io8")
    p.op("q", lambda e: e.dma_start(out=sb_io8, in_=c_io8.ap()), writes=["a_mb"], ch="l1c")
    fsp = (np.arange(128).reshape(128, 1)
           + np.tile(np.arange(32) * 128, 2).reshape(1, 64)).astype(np.float32)
    c_fsp = nc.inline_tensor(fsp, "c_fsp")
    p.op("q", lambda e: e.dma_start(out=av[:, a_gw:a_gw + 64], in_=c_fsp.ap()),
         writes=["a_gw"], ch="l1c")

    maskg = np.zeros((128, 512), np.float32)
    for g in range(4):
        for t in range(8):
            maskg[t * 16:(t + 1) * 16, g * 128 + t * 16: g * 128 + (t + 1) * 16] = 1.0
    c_maskg = nc.inline_tensor(maskg.astype(ml_dtypes.bfloat16), "c_maskg")
    p.op("q", lambda e: e.dma_start(out=ab[:, b_msk01:b_msk01 + 512], in_=c_maskg.ap()),
         writes=["msk01"], ch="l1c")
    c_pi = nc.inline_tensor(np.full((128, 1), np.pi, np.float32), "c_pi")
    p.op("q", lambda e: e.dma_start(out=av[:, a_pi:a_pi + 1], in_=c_pi.ap()), writes=["a_pi"], ch="l1c")

    p.op("g", lambda e: e.dma_start(out=ab[0:1, b_rows:b_rows + 1024], in_=di["bq"]),
         writes=["rows"], ch="l1r")
    p.op("g", lambda e: e.dma_start(out=ab[0:1, b_rows + 1024:b_rows + 3072], in_=di["bup"]),
         writes=["rows"], ch="l1r")
    p.op("g", lambda e: e.dma_start(out=ab[0:1, b_rows + 3072:b_rows + 4096], in_=di["bout"]),
         writes=["rows"], ch="l1r")
    p.op("g", lambda e: e.dma_start(out=ab[:, b_gb:b_gb + 1024],
                                    in_=di["gamma"].to_broadcast([128, C])),
         writes=["b_gb"], ch="l1r")
    p.op("g", lambda e: e.dma_start(out=ab[:, b_gb + 1024:b_gb + 2048],
                                    in_=di["beta"].to_broadcast([128, C])),
         writes=["b_gb"], ch="l1r")
    sb_bd = nc.alloc_sbuf_tensor("sb_bd", [128, 2], F32)
    p.op("q", lambda e: e.dma_start(out=sb_bd[:], in_=di["bdown"]), writes=["sb_bd"], ch="l1c")
    sb_onesf = nc.alloc_sbuf_tensor("sb_onesf", [128, 1], F32)
    p.op("v", lambda e: e.memset(sb_onesf[:], 1.0), writes=["sb_onesf"])
    p.op("v", lambda e: e.memset(ab[:, b_vg + 64:b_vg + 65], 1.0), writes=[("b_vg1", 0)])
    p.op("v", lambda e: e.memset(ab[:, b_vg2 + 64:b_vg2 + 65], 1.0), writes=[("b_vg1", 1)])

    for i in range(NT):
        p.op("q", lambda e, i=i: e.dma_start(out=av[:, a_x:a_x + 1024],
                                             in_=ds["x1"][i * 128:(i + 1) * 128, :]),
             reads=["a_x", ("x1d", i)], writes=["a_x"], ch="l1x")
        ln_tok(ctx, "a_x", av[:, a_x:a_x + 1024], av[:, a_sq:a_sq + 1024],
               av[:, a_o:a_o + 1024], "a_o", av[:, a_st:a_st + 16])
        p.op("q", lambda e, i=i: e.dma_start(out=ds["xn2f"][i * 128:(i + 1) * 128, :],
                                             in_=av[:, a_o:a_o + 1024]),
             reads=["a_o"], writes=[("x2d", i)], ch="stx2")
        for half in range(2):
            p.op("t", lambda e, i=i, half=half: e.matmul(
                PS[4 + half][0:1, 0:512], sb_onesf[:],
                av[:, a_o + half * 512:a_o + half * 512 + 512],
                start=(i == 0), stop=(i == NT - 1)),
                reads=["a_o", "sb_onesf", ("PS", 4 + half)], writes=[("PS", 4 + half)])
        p.op("v", lambda e: e.tensor_copy(out=ab[:, b_st:b_st + 1024], in_=av[:, a_o:a_o + 1024]),
             reads=["a_o"], writes=["b_st"])
        for c in range(8):
            p.op("t", lambda e, c=c: e.transpose(PB[0][:, 0:128],
                                                 ab[:, b_st + c * 128:b_st + (c + 1) * 128],
                                                 sb_id[:]),
                 reads=["b_st", "sb_id", "PB0"], writes=["PB0"])
            p.op("v", lambda e, c=c, i=i: e.tensor_copy(
                out=ab[:, b_x2T + c * 512: b_x2T + c * 512 + 128],
                in_=PB[0][:, 0:128]),
                reads=["PB0"], writes=["b_x2T"])
            p.op("q", lambda e, c=c, i=i: e.dma_start(
                out=xn2T_d[c * 128:(c + 1) * 128, i * 128:(i + 1) * 128],
                in_=ab[:, b_x2T + c * 512: b_x2T + c * 512 + 128]),
                reads=["b_x2T"], writes=[("x2Td", c, i)], ch="stx2T")
    for half in range(2):
        p.op("v", lambda e, half=half: e.tensor_scalar_mul(
            av[0:1, a_sq + half * 512:a_sq + half * 512 + 512],
            PS[4 + half][0:1, 0:512], 1.0 / T),
            reads=[("PS", 4 + half)], writes=["a_sq"])
    p.op("q", lambda e: e.dma_start(out=meanrow, in_=av[0:1, a_sq:a_sq + 1024]),
         reads=["a_sq"], writes=["meand"], ch="stmn")
    p.op("q", lambda e: e.dma_start(out=av[:, a_mt:a_mt + 8],
                                    in_=meanrow.rearrange("o (c p) -> (o p) c", p=128),
                                    allow_slow_non_contiguous=True),
         reads=["meand"], writes=["a_mt"], ch="lmn")
    sb_wr = nc.alloc_sbuf_tensor("sb_wr", [128, 64], F32)
    sb_br = nc.alloc_sbuf_tensor("sb_br", [8, 1], F32)
    for c in range(8):
        p.op("q", lambda e, c=c: e.dma_start(out=sb_wr[:, c * 8:(c + 1) * 8],
                                             in_=di["Wr1"][c * 128:(c + 1) * 128, :]),
             writes=["sb_wr"], ch="l1c")
    p.op("q", lambda e: e.dma_start(out=sb_br[:], in_=di["br1"]), writes=["sb_br"], ch="l1c")
    for c in range(8):
        p.op("t", lambda e, c=c: e.matmul(PS[4][0:8, 0:1], sb_wr[:, c * 8:(c + 1) * 8],
                                          av[:, a_mt + c:a_mt + c + 1],
                                          start=(c == 0), stop=(c == 7)),
             reads=["a_mt", "sb_wr", ("PS", 4)], writes=[("PS", 4)])
    p.op("v", lambda e: e.tensor_add(av[0:8, a_rw:a_rw + 1], PS[4][0:8, 0:1], sb_br[:]),
         reads=[("PS", 4), "sb_br"], writes=["a_rw"])
    p.op("q", lambda e: e.dma_start(out=rwrow_d.rearrange("o e -> (o e)").unsqueeze(1),
                                    in_=av[0:8, a_rw:a_rw + 1], allow_slow_non_contiguous=True),
         reads=["a_rw"], writes=["rwd"], ch="strw")
    p.op("q", lambda e: e.dma_start(out=av[0:1, a_rw + 8:a_rw + 16], in_=rwrow_d),
         reads=["rwd"], writes=["a_rw"], ch="lrw")
    rwr = av[0:1, a_rw + 8:a_rw + 16]
    p.op("v", lambda e: e.tensor_reduce(out=av[0:1, a_rw + 16:a_rw + 17], in_=rwr, axis=AX.X,
                                        op=ALU.max, negate=True),
         reads=["a_rw"], writes=["a_rw"])
    p.op("s", lambda e: e.activation(out=rwr, in_=rwr, func=AF.Exp,
                                     bias=av[0:1, a_rw + 16:a_rw + 17],
                                     accum_out=av[0:1, a_rw + 17:a_rw + 18]),
         reads=["a_rw"], writes=["a_rw"])
    p.op("s", lambda e: e.activation(out=av[0:1, a_rw + 18:a_rw + 19],
                                     in_=av[0:1, a_rw + 17:a_rw + 18], func=AF.Ln),
         reads=["a_rw"], writes=["a_rw"])
    p.op("s", lambda e: e.activation(out=av[0:1, a_rw + 16:a_rw + 17],
                                     in_=av[0:1, a_rw + 18:a_rw + 19], func=AF.Exp, scale=-1.0),
         reads=["a_rw"], writes=["a_rw"])
    p.op("v", lambda e: e.tensor_scalar_mul(rwr, rwr, av[0:1, a_rw + 16:a_rw + 17]),
         reads=["a_rw"], writes=["a_rw"])
    p.op("v", lambda e: e.max(out=av[0:1, a_rw + 20:a_rw + 28], in_=rwr),
         reads=["a_rw"], writes=["a_rw"])
    v1 = av[0:1, a_rw + 20:a_rw + 21]
    v2 = av[0:1, a_rw + 21:a_rw + 22]
    p.op("v", lambda e: e.tensor_sub(av[0:1, a_rw + 28:a_rw + 29], v1, v2),
         reads=["a_rw"], writes=["a_rw"])
    p.op("s", lambda e: e.activation(out=av[0:1, a_rw + 29:a_rw + 30],
                                     in_=av[0:1, a_rw + 28:a_rw + 29], func=AF.Sigmoid),
         reads=["a_rw"], writes=["a_rw"])
    p.op("v", lambda e: e.tensor_scalar(av[0:1, a_rw + 30:a_rw + 31],
                                        av[0:1, a_rw + 29:a_rw + 30], -1.0, 1.0,
                                        op0=ALU.mult, op1=ALU.add),
         reads=["a_rw"], writes=["a_rw"])
    cwrow = av[0:1, a_rw + 0:a_rw + 8]
    m1 = av[0:1, a_rw + 31:a_rw + 32]
    tmp8a = av[0:1, a_rw - 0 + 8:a_rw + 16]
    t8 = AF32.alloc(32)
    m1r = av[0:1, t8:t8 + 8]
    m2r = av[0:1, t8 + 8:t8 + 16]
    rkr = av[0:1, t8 + 16:t8 + 24]
    p.op("v", lambda e: e.tensor_single_scalar(out=m1r, in_=rwr, scalar=v1, op=ALU.is_equal),
         reads=["a_rw"], writes=["t8"])
    p.op("v", lambda e: e.memset(av[0:1, t8 + 24:t8 + 25], 0.0), writes=["t8"])
    p.op("v", lambda e: e.tensor_tensor_scan(out=rkr, data0=m1r,
                                             data1=av[0:1, t8 + 24:t8 + 25].to_broadcast([1, 8]),
                                             initial=0.0, op0=ALU.add, op1=ALU.add),
         reads=["t8"], writes=["t8"])
    p.op("v", lambda e: e.tensor_single_scalar(out=rkr, in_=rkr, scalar=1.0, op=ALU.is_le),
         reads=["t8"], writes=["t8"])
    p.op("v", lambda e: e.tensor_mul(m1r, m1r, rkr), reads=["t8"], writes=["t8"])
    p.op("v", lambda e: e.tensor_single_scalar(out=m2r, in_=rwr, scalar=v2, op=ALU.is_equal),
         reads=["a_rw", "t8"], writes=["t8"])
    p.op("v", lambda e: e.scalar_tensor_tensor(out=rkr, in0=m1r, scalar=-1.0, in1=m2r,
                                               op0=ALU.mult, op1=ALU.add),
         reads=["t8"], writes=["t8"])
    p.op("v", lambda e: e.tensor_single_scalar(out=rkr, in_=rkr, scalar=0.5, op=ALU.is_ge),
         reads=["t8"], writes=["t8"])
    p.op("v", lambda e: e.memset(av[0:1, t8 + 24:t8 + 25], 0.0), writes=["t8"])
    p.op("v", lambda e: e.tensor_tensor_scan(out=m2r, data0=rkr,
                                             data1=av[0:1, t8 + 24:t8 + 25].to_broadcast([1, 8]),
                                             initial=0.0, op0=ALU.add, op1=ALU.add),
         reads=["t8"], writes=["t8"])
    p.op("v", lambda e: e.tensor_single_scalar(out=m2r, in_=m2r, scalar=1.0, op=ALU.is_le),
         reads=["t8"], writes=["t8"])
    p.op("v", lambda e: e.tensor_mul(m2r, m2r, rkr), reads=["t8"], writes=["t8"])
    p.op("v", lambda e: e.tensor_scalar_mul(av[0:1, t8 + 24:t8 + 32], m1r,
                                            av[0:1, a_rw + 29:a_rw + 30]),
         reads=["t8", "a_rw"], writes=["t8"])
    p.op("v", lambda e: e.scalar_tensor_tensor(out=cwrow, in0=m2r,
                                               scalar=av[0:1, a_rw + 30:a_rw + 31],
                                               in1=av[0:1, t8 + 24:t8 + 32],
                                               op0=ALU.mult, op1=ALU.add),
         reads=["t8", "a_rw"], writes=["a_rw"])
    p.op("q", lambda e: e.dma_start(out=cw_d, in_=cwrow), reads=["a_rw"], writes=["cwd"], ch="stcw")
    p.op("q", lambda e: e.dma_start(out=av[:, a_cwb:a_cwb + 8], in_=cw_d.to_broadcast([128, 8])),
         reads=["cwd"], writes=["a_cwb"], ch="lcw")
    p.op("v", lambda e: e.tensor_mul(av[0:1, t8 + 16:t8 + 24], m1r, sb_io8),
         reads=["t8", "a_mb"], writes=["t8"])
    p.op("v", lambda e: e.tensor_reduce(out=av[0:1, a_eid:a_eid + 1],
                                        in_=av[0:1, t8 + 16:t8 + 24], axis=AX.X, op=ALU.add),
         reads=["t8"], writes=["a_eid"])
    p.op("v", lambda e: e.tensor_mul(av[0:1, t8 + 16:t8 + 24], m2r, sb_io8),
         reads=["t8", "a_mb"], writes=["t8"])
    p.op("v", lambda e: e.tensor_reduce(out=av[0:1, a_eid + 1:a_eid + 2],
                                        in_=av[0:1, t8 + 16:t8 + 24], axis=AX.X, op=ALU.add),
         reads=["t8"], writes=["a_eid"])
    p.op("q", lambda e: e.dma_start(out=eidx_d, in_=av[0:1, a_eid:a_eid + 2]),
         reads=["a_eid"], writes=["eidxd"], ch="steidx")
    p.op("q", lambda e: e.dma_start(out=m12_d[0:1, 0:16], in_=av[0:1, t8:t8 + 16]),
         reads=["t8"], writes=["m12d"], ch="stm12")
    p.op("q", lambda e: e.dma_start(out=m12_d[0:1, 16:18], in_=av[0:1, a_rw + 29:a_rw + 31]),
         reads=["a_rw"], writes=["m12d"], ch="stm12")
    p.op("q", lambda e: e.dma_start(out=av[:, a_eid + 2:a_eid + 4],
                                    in_=eidx_d.to_broadcast([128, 2])),
         reads=["eidxd", "a_eid"], writes=["a_eid"], ch="leidx")
    p.op("q", lambda e: e.dma_start(out=av[:, a_mb:a_mb + 16],
                                    in_=m12_d[0:1, 0:16].to_broadcast([128, 16])),
         reads=["m12d", "a_mb"], writes=["a_mb"], ch="lmb")
    p.op("q", lambda e: e.dma_start(out=av[:, a_cwb + 8:a_cwb + 10],
                                    in_=m12_d[0:1, 16:18].to_broadcast([128, 2])),
         reads=["m12d", "a_cwb"], writes=["a_cwb"], ch="lcw")
    for ee2 in range(2):
        p.op("v", lambda e, ee2=ee2: e.scalar_tensor_tensor(
            out=av[:, a_gw + ee2 * 32:a_gw + (ee2 + 1) * 32],
            in0=av[:, a_eid + 2 + ee2:a_eid + 3 + ee2].to_broadcast([128, 32]),
            scalar=4096.0, in1=av[:, a_gw + ee2 * 32:a_gw + (ee2 + 1) * 32],
            op0=ALU.mult, op1=ALU.add),
            reads=["a_eid", "a_gw"], writes=["a_gw"])
    p.op("v", lambda e: e.tensor_copy(out=i_gw, in_=av[:, a_gw:a_gw + 64]),
         reads=["a_gw", "a_gwi"], writes=["a_gwi"])
    spar = ctx["sb_par"]
    fb1o = ctx["par_off"]["fb1"]
    for ee2 in range(2):
        for e8 in range(8):
            if e8 == 0:
                p.op("v", lambda e, ee2=ee2, e8=e8: e.tensor_scalar_mul(
                    av[:, a_b1s + ee2 * 32:a_b1s + (ee2 + 1) * 32],
                    spar[:, fb1o + e8 * 32:fb1o + (e8 + 1) * 32],
                    av[:, a_mb + ee2 * 8 + e8:a_mb + ee2 * 8 + e8 + 1]),
                    reads=[("par", "fb1"), "a_mb", "a_b1s"], writes=["a_b1s"])
            else:
                p.op("v", lambda e, ee2=ee2, e8=e8: e.scalar_tensor_tensor(
                    out=av[:, a_b1s + ee2 * 32:a_b1s + (ee2 + 1) * 32],
                    in0=spar[:, fb1o + e8 * 32:fb1o + (e8 + 1) * 32],
                    scalar=av[:, a_mb + ee2 * 8 + e8:a_mb + ee2 * 8 + e8 + 1],
                    in1=av[:, a_b1s + ee2 * 32:a_b1s + (ee2 + 1) * 32],
                    op0=ALU.mult, op1=ALU.add),
                    reads=[("par", "fb1"), "a_mb", "a_b1s"], writes=["a_b1s"])
    gstv = [ab[:, b_gst + 2048 * j:b_gst + 2048 * (j + 1)].bitcast(F32) for j in range(2)]
    gsrcs = [(di["f_W1r"], selW1, "sW1"),
             (di["f_W2"].rearrange("e f c -> (e f) c"), selW2, "sW2")]
    gcnt = 0
    for gsrc, gdst, gdkey in gsrcs:
        for ee2 in range(2):
            for k in range(32):
                par2 = gcnt % 2
                p.op("g", lambda e, gsrc=gsrc, ee2=ee2, k=k, par2=par2: e.indirect_dma_start(
                    out=gstv[par2], out_offset=None, in_=gsrc,
                    in_offset=IndirectOffsetOnAxis(i_gw[:, ee2 * 32 + k:ee2 * 32 + k + 1], 0)),
                    reads=["a_gwi", ("gst", par2)], writes=[("gst", par2)], ch="gg" + str(par2))
                p.op("g", lambda e, gdst=gdst, ee2=ee2, k=k, par2=par2: e.dma_start(
                    out=gdst[(ee2 * 32 + k) * 128:(ee2 * 32 + k + 1) * 128, :],
                    in_=gstv[par2]),
                    reads=[("gst", par2)], writes=[(gdkey, ee2 * 32 + k)], ch="gs" + str(par2))
                gcnt += 1

    p.op("g", lambda e: e.iota(av[:, a_tv:a_tv + 16], [[128, 16]], base=0,
                               channel_multiplier=1, allow_small_or_imprecise_dtypes=True),
         writes=["a_tv"])
    for i in range(NT):
        fcol = av[:, a_sq:a_sq + 32]
        p.op("v", lambda e, i=i: e.tensor_mul(fcol, av[:, a_tv + i:a_tv + i + 1]
                                              .to_broadcast([128, 32]), sb_invf[:]),
             reads=["a_tv", "sb_invf"], writes=["a_sq"])
        for which, base_off, phase in [("sin", a_sc, 0.0), ("cos", a_cc, 0.25)]:
            u = av[:, a_sq + 32:a_sq + 64]
            p.op("v", lambda e, phase=phase: e.tensor_scalar(u, fcol, 1.0 / (2 * np.pi), phase,
                                                             op0=ALU.mult, op1=ALU.add),
                 reads=["a_sq"], writes=["a_sq"])
            p.op("v", lambda e: e.tensor_copy(out=i_fl[:], in_=u), reads=["a_sq"], writes=["i_fl"])
            p.op("v", lambda e: e.tensor_copy(out=av[:, a_sq + 64:a_sq + 96], in_=i_fl[:]),
                 reads=["i_fl"], writes=["a_sq"])
            p.op("v", lambda e: e.tensor_sub(u, u, av[:, a_sq + 64:a_sq + 96]),
                 reads=["a_sq"], writes=["a_sq"])
            p.op("v", lambda e: e.tensor_single_scalar(out=av[:, a_sq + 64:a_sq + 96], in_=u,
                                                       scalar=0.0, op=ALU.is_lt),
                 reads=["a_sq"], writes=["a_sq"])
            p.op("v", lambda e: e.tensor_add(u, u, av[:, a_sq + 64:a_sq + 96]),
                 reads=["a_sq"], writes=["a_sq"])
            p.op("s", lambda e, base_off=base_off, i=i: e.activation(
                out=av[:, base_off + i * 32: base_off + (i + 1) * 32], in_=u, func=AF.Sin,
                scale=float(-2 * np.pi), bias=av[:, a_pi:a_pi + 1]),
                reads=["a_sq", "a_pi"], writes=["a_sc" if base_off == a_sc else "a_cc"])

    for tb in range(4):
        for c in range(8):
            p.op("q", lambda e, c=c, tb=tb: e.dma_start(
                out=ab[:, b_x2T + c * 512: b_x2T + (c + 1) * 512],
                in_=xn2T_d[c * 128:(c + 1) * 128, tb * 512:(tb + 1) * 512]),
                reads=["b_x2T"] + [("x2Td", c, i) for i in range(NT)],
                writes=["b_x2T"], ch="lx2w")
        for lc in range(2):
            p.op("g", lambda e, lc=lc: e.dma_start(
                out=ab[:, b_wA:b_wA + 1024].rearrange("p (c j) -> p c j", c=8),
                in_=di["Wdown"][:, lc * 128:(lc + 1) * 128].rearrange("(c p) j -> p c j", p=128)),
                reads=["b_wA"], writes=["b_wA"], ch="chWA")
            for c in range(8):
                p.op("t", lambda e, lc=lc, tb=tb, c=c: e.matmul(
                    PS[0][:, 0:512],
                    ab[:, b_wA + c * 128:b_wA + (c + 1) * 128],
                    ab[:, b_x2T + c * 512: b_x2T + (c + 1) * 512],
                    start=(c == 0), stop=(c == 7)),
                    reads=["b_wA", "b_x2T", ("PS", 0)], writes=[("PS", 0)])
            p.op("v", lambda e, lc=lc, tb=tb: e.tensor_scalar_add(
                ab[:, b_latT + lc * 2048 + tb * 512: b_latT + lc * 2048 + (tb + 1) * 512],
                PS[0][:, 0:512], sb_bd[:, lc:lc + 1]),
                reads=[("PS", 0), "sb_bd"], writes=["b_latT"])

    a_xq = AF32.alloc(4096)
    for tb in range(4):
        for c in range(8):
            p.op("q", lambda e, c=c, tb=tb: e.dma_start(
                out=ab[:, b_x2T + c * 512: b_x2T + (c + 1) * 512],
                in_=xn2T_d[c * 128:(c + 1) * 128, tb * 512:(tb + 1) * 512]),
                reads=["b_x2T"] + [("x2Td", c, i) for i in range(NT)],
                writes=["b_x2T"], ch="lx2w")
        for qr in range(4):
            qp = qr % 2
            wqo = b_wB if qp == 0 else b_wA
            wkey = "b_wB" if qp == 0 else "b_wA"
            wch = "chWB" if qp == 0 else "chWA"
            p.op("g", lambda e, qr=qr, wqo=wqo: e.dma_start(
                out=ab[:, wqo:wqo + 2048].rearrange("p (c j) -> p c j", c=8),
                in_=di["Wq"][:, qr * 256:(qr + 1) * 256].rearrange("(c p) j -> p c j", p=128)),
                reads=[wkey], writes=[wkey], ch=wch)
            for ii in range(4):
                bank = (qr * 4 + ii) % 2
                for c in range(8):
                    p.op("t", lambda e, ii=ii, c=c, wqo=wqo, bank=bank: e.matmul(
                        PS[bank][:, 0:256],
                        ab[:, b_x2T + c * 512 + ii * 128: b_x2T + c * 512 + ii * 128 + 128],
                        ab[:, wqo + c * 256:wqo + (c + 1) * 256],
                        start=(c == 0), stop=(c == 7)),
                        reads=[wkey, "b_x2T", ("PS", bank)], writes=[("PS", bank)])
                p.op("v", lambda e, qr=qr, ii=ii, bank=bank: e.tensor_copy(
                    out=av[:, a_xq + ii * 1024 + qr * 256:a_xq + ii * 1024 + (qr + 1) * 256],
                    in_=PS[bank][:, 0:256]),
                    reads=[("PS", bank), ("a_xq", ii)], writes=[("a_xq", ii)])
        for ii in range(4):
            i = tb * 4 + ii
            xo = a_xq + ii * 1024
            emit_rowadd(ctx, av[:, xo:xo + 1024], ab[0:1, b_rows:b_rows + 1024],
                        ("a_xq", ii), "rows", PS, p)
            emit_rope(ctx, p, av, xo, a_sc, a_cc, i, ("a_xq", ii))
            p.op("v", lambda e, xo=xo: e.tensor_copy(out=ab[:, b_st:b_st + 1024],
                                                     in_=av[:, xo:xo + 1024]),
                 reads=[("a_xq", ii), "b_st"], writes=["b_st"])
            p.op("q", lambda e, i=i: e.dma_start(out=ds["q_tok"][i * 128:(i + 1) * 128, :],
                                                 in_=ab[:, b_st:b_st + 1024]),
                 reads=["b_st"], writes=[("qd", i)], ch="stq")
        for part, dst, dkey in [(0, "k_tok", "kd"), (1, "v_tok", "vd")]:
            for half in range(2):
                qp = half
                wqo = b_wB if qp == 0 else b_wA
                wkey = "b_wB" if qp == 0 else "b_wA"
                wch = "chWB" if qp == 0 else "chWA"
                p.op("g", lambda e, part=part, half=half, wqo=wqo: e.dma_start(
                    out=ab[:, wqo:wqo + 1024].rearrange("p (c j) -> p c j", c=2),
                    in_=di["Wup"][:, part * 1024 + half * 512: part * 1024 + (half + 1) * 512]
                    .rearrange("(c p) j -> p c j", p=128)),
                    reads=[wkey], writes=[wkey], ch=wch)
                for ii in range(4):
                    i = tb * 4 + ii
                    bank = ii % 2
                    for lc in range(2):
                        p.op("t", lambda e, i=i, lc=lc, wqo=wqo, bank=bank: e.matmul(
                            PS[bank][:, 0:512],
                            ab[:, b_latT + lc * 2048 + i * 128: b_latT + lc * 2048 + (i + 1) * 128],
                            ab[:, wqo + lc * 512:wqo + (lc + 1) * 512],
                            start=(lc == 0), stop=(lc == 1)),
                            reads=[wkey, "b_latT", ("PS", bank)], writes=[("PS", bank)])
                    p.op("v", lambda e, half=half, ii=ii, bank=bank: e.tensor_copy(
                        out=av[:, a_xq + ii * 1024 + half * 512:a_xq + ii * 1024 + half * 512 + 512],
                        in_=PS[bank][:, 0:512]),
                        reads=[("PS", bank), ("a_xq", ii)], writes=[("a_xq", ii)])
            for ii in range(4):
                i = tb * 4 + ii
                xo = a_xq + ii * 1024
                emit_rowadd(ctx, av[:, xo:xo + 1024],
                            ab[0:1, b_rows + 1024 + part * 1024:b_rows + 1024 + (part + 1) * 1024],
                            ("a_xq", ii), "rows", PS, p)
                if part == 0:
                    emit_rope(ctx, p, av, xo, a_sc, a_cc, i, ("a_xq", ii))
                p.op("v", lambda e, xo=xo, ii=ii: e.tensor_copy(out=ab[:, b_st:b_st + 1024],
                                                                in_=av[:, xo:xo + 1024]),
                     reads=[("a_xq", ii), "b_st"], writes=["b_st"])
                p.op("q", lambda e, i=i, dst=dst, dkey=dkey: e.dma_start(
                    out=ds[dst][i * 128:(i + 1) * 128, :], in_=ab[:, b_st:b_st + 1024]),
                    reads=["b_st"], writes=[(dkey, i)], ch="st" + dst)

    a_ax = AF32.alloc(1024)
    a_asq = AF32.alloc(16)
    for gb in range(64):
        par = gb % 2
        qgo = b_qg if par == 0 else b_qg2
        kgo = b_kg if par == 0 else b_kg2
        vgo = b_vg if par == 0 else b_vg2
        ogo = b_og if par == 0 else b_og2
        qkb = 0 if par == 0 else 2
        avb = 1 if par == 0 else 3
        aoo = a_ax + par * 512
        sqo = a_asq + par * 8
        t0 = gb * 32
        for gg in range(4):
            tg = t0 + gg * 8
            p.op("q", lambda e, tg=tg, qgo=qgo, par=par: e.dma_start(
                out=ab[:, qgo:qgo + 64],
                in_=ds["q_tok"][tg:tg + 8, :].rearrange("t (h d) -> (t h) d", d=64)),
                reads=[("b_qg", par)] + [("qd", tg // 128)], writes=[("b_qg", par)],
                ch="lqg" + str(par))
            p.op("q", lambda e, tg=tg, kgo=kgo, par=par: e.dma_start(
                out=ab[:, kgo:kgo + 64],
                in_=ds["k_tok"][tg:tg + 8, :].rearrange("t (h d) -> (t h) d", d=64)),
                reads=[("b_kg", par)] + [("kd", tg // 128)], writes=[("b_kg", par)],
                ch="lkg" + str(par))
            p.op("t", lambda e, gg=gg, qgo=qgo, par=par: e.transpose(
                PB[0][0:64, gg * 128:(gg + 1) * 128], ab[:, qgo:qgo + 64], sb_id[:]),
                reads=[("b_qg", par), "sb_id", "PB0"], writes=["PB0"])
            p.op("v", lambda e, gg=gg, par=par: e.tensor_copy(
                out=ab[0:64, b_qgT + gg * 128:b_qgT + (gg + 1) * 128],
                in_=PB[0][0:64, gg * 128:(gg + 1) * 128]),
                reads=["PB0"], writes=["b_qgT"])
            p.op("t", lambda e, gg=gg, kgo=kgo, par=par: e.transpose(
                PB[1][0:64, gg * 128:(gg + 1) * 128], ab[:, kgo:kgo + 64], sb_id[:]),
                reads=[("b_kg", par), "sb_id", "PB1"], writes=["PB1"])
            p.op("v", lambda e, gg=gg, par=par: e.tensor_copy(
                out=ab[0:64, b_kgT + gg * 128:b_kgT + (gg + 1) * 128],
                in_=PB[1][0:64, gg * 128:(gg + 1) * 128]),
                reads=["PB1"], writes=["b_kgT"])
            p.op("t", lambda e, gg=gg, qkb=qkb: e.matmul(
                PS[qkb][:, gg * 128:(gg + 1) * 128],
                ab[0:64, b_kgT + gg * 128:b_kgT + (gg + 1) * 128],
                ab[0:64, b_qgT + gg * 128:b_qgT + (gg + 1) * 128],
                start=True, stop=True),
                reads=["b_kgT", "b_qgT", ("PS", qkb)], writes=[("PS", qkb)])
        p.op("s", lambda e, qkb=qkb, aoo=aoo, par=par: e.activation(
            out=av[:, aoo:aoo + 512], in_=PS[qkb][:, 0:512], func=AF.Exp, scale=0.125),
             reads=[("PS", qkb), ("a_o", par)], writes=[("a_o", par)])
        p.op("v", lambda e, aoo=aoo, par=par: e.tensor_mul(
            ab[:, b_pT:b_pT + 512], av[:, aoo:aoo + 512], ab[:, b_msk01:b_msk01 + 512]),
             reads=[("a_o", par), "msk01", "b_pT"], writes=["b_pT"])
        for gg in range(4):
            tg = t0 + gg * 8
            p.op("q", lambda e, tg=tg, vgo=vgo, par=par: e.dma_start(
                out=ab[:, vgo:vgo + 64],
                in_=ds["v_tok"][tg:tg + 8, :].rearrange("t (h d) -> (t h) d", d=64)),
                reads=[("b_vg", par)] + [("vd", tg // 128)], writes=[("b_vg", par)],
                ch="lvg" + str(par))
            p.op("t", lambda e, gg=gg, vgo=vgo, avb=avb, par=par: e.matmul(
                PS[avb][:, 0:65],
                ab[:, b_pT + gg * 128:b_pT + (gg + 1) * 128],
                ab[:, vgo:vgo + 65],
                start=True, stop=True),
                reads=["b_pT", ("b_vg", par), ("b_vg1", par), ("PS", avb)],
                writes=[("PS", avb)])
            p.op("v", lambda e, avb=avb, sqo=sqo, par=par: e.tensor_copy(
                out=av[:, sqo + 2:sqo + 3], in_=PS[avb][:, 64:65]),
                reads=[("PS", avb), ("a_sq", par)], writes=[("a_sq", par)])
            p.op("s", lambda e, sqo=sqo, par=par: e.activation(
                out=av[:, sqo:sqo + 1], in_=av[:, sqo + 2:sqo + 3], func=AF.Ln),
                reads=[("a_sq", par)], writes=[("a_sq", par)])
            p.op("s", lambda e, sqo=sqo, par=par: e.activation(
                out=av[:, sqo + 1:sqo + 2], in_=av[:, sqo:sqo + 1], func=AF.Exp, scale=-1.0),
                reads=[("a_sq", par)], writes=[("a_sq", par)])
            p.op("v", lambda e, avb=avb, ogo=ogo, sqo=sqo, par=par: e.tensor_scalar_mul(
                ab[:, ogo:ogo + 64], PS[avb][:, 0:64], av[:, sqo + 1:sqo + 2]),
                reads=[("PS", avb), ("a_sq", par), ("b_og", par)], writes=[("b_og", par)])
            p.op("q", lambda e, tg=tg, ogo=ogo, par=par: e.dma_start(
                out=ds["o_grp"][tg * 16:(tg + 8) * 16, :], in_=ab[:, ogo:ogo + 64]),
                reads=[("b_og", par)], writes=[("ogd", tg // 128)], ch="stog" + str(par))

    for i in range(NT):
        p.op("q", lambda e, i=i: e.dma_start(
            out=ab[:, b_st:b_st + 1024],
            in_=ds["o_grp"][i * 2048:(i + 1) * 2048, :].rearrange("(t h) d -> t (h d)", h=16)),
            reads=["b_st", ("ogd", i)], writes=["b_st"], ch="log")
        for c in range(8):
            p.op("t", lambda e, c=c: e.transpose(PB[0][:, 0:128],
                                                 ab[:, b_st + c * 128:b_st + (c + 1) * 128],
                                                 sb_id[:]),
                 reads=["b_st", "sb_id", "PB0"], writes=["PB0"])
            p.op("v", lambda e, c=c: e.tensor_copy(out=ab[:, b_oT + c * 512:b_oT + c * 512 + 128],
                                                   in_=PB[0][:, 0:128]),
                 reads=["PB0"], writes=["b_hT"])
        for qr in range(4):
            p.op("g", lambda e, qr=qr: e.dma_start(
                out=ab[:, b_wB:b_wB + 2048].rearrange("p (c j) -> p c j", c=8),
                in_=di["Wout"][:, qr * 256:(qr + 1) * 256].rearrange("(c p) j -> p c j", p=128)),
                reads=["b_wB"], writes=["b_wB"], ch="chWB")
            for c in range(8):
                p.op("t", lambda e, c=c: e.matmul(
                    PS[0][:, 0:256],
                    ab[:, b_oT + c * 512:b_oT + c * 512 + 128],
                    ab[:, b_wB + c * 256:b_wB + (c + 1) * 256],
                    start=(c == 0), stop=(c == 7)),
                    reads=["b_wB", "b_hT", ("PS", 0)], writes=[("PS", 0)])
            p.op("v", lambda e, qr=qr: e.tensor_copy(out=av[:, a_x + qr * 256:a_x + (qr + 1) * 256],
                                                     in_=PS[0][:, 0:256]),
                 reads=[("PS", 0)], writes=["a_x"])
        emit_rowadd(ctx, av[:, a_x:a_x + 1024], ab[0:1, b_rows + 3072:b_rows + 4096],
                    "a_x", "rows", PS, p)
        p.op("q", lambda e, i=i: e.dma_start(out=ds["s2"][i * 128:(i + 1) * 128, :],
                                             in_=av[:, a_x:a_x + 1024]),
             reads=["a_x"], writes=[("s2d", i)], ch="sts2")

    p.barrier()
    for ee in range(2):
        for tb in range(4):
            for c in range(8):
                p.op("q", lambda e, c=c, tb=tb: e.dma_start(
                    out=ab[:, b_x2T + c * 512: b_x2T + (c + 1) * 512],
                    in_=xn2T_d[c * 128:(c + 1) * 128, tb * 512:(tb + 1) * 512]),
                    reads=["b_x2T"] + [("x2Td", c, i) for i in range(NT)],
                    writes=["b_x2T"], ch="lx2w")
            for fs in range(32):
                wpar = fs % 2
                wo = b_wA + wpar * 1024
                p.op("q", lambda e, ee=ee, fs=fs, wo=wo: e.dma_start(
                    out=ab[:, wo:wo + 1024],
                    in_=selW1[(ee * 32 + fs) * 128:(ee * 32 + fs + 1) * 128, :]),
                    reads=[("b_wA", wpar), ("sW1", ee * 32 + fs)],
                    writes=[("b_wA", wpar)], ch="chWA" + str(wpar))
                for c in range(8):
                    p.op("t", lambda e, fs=fs, c=c, wo=wo, wpar=wpar: e.matmul(
                        PS[2 + wpar][:, 0:512],
                        ab[:, wo + c * 128:wo + (c + 1) * 128],
                        ab[:, b_x2T + c * 512:(b_x2T + (c + 1) * 512)],
                        start=(c == 0), stop=(c == 7)),
                        reads=[("b_wA", wpar), "b_x2T", ("PS", 2 + wpar)],
                        writes=[("PS", 2 + wpar)])
                p.op("s", lambda e, ee=ee, fs=fs, wpar=wpar: e.activation(
                    out=ab[:, b_hT + fs * 512:b_hT + (fs + 1) * 512], in_=PS[2 + wpar][:, 0:512],
                    func=AF.Gelu,
                    bias=av[:, a_b1s + ee * 32 + fs:a_b1s + ee * 32 + fs + 1]),
                    reads=[("PS", 2 + wpar), "a_b1s"], writes=["b_hT"])
            for fs in range(32):
                wpar = fs % 2
                wo = b_wB + wpar * 1024
                p.op("q", lambda e, ee=ee, fs=fs, wo=wo: e.dma_start(
                    out=ab[:, wo:wo + 1024],
                    in_=selW2[(ee * 32 + fs) * 128:(ee * 32 + fs + 1) * 128, :]),
                    reads=[("b_wB", wpar), ("sW2", ee * 32 + fs)],
                    writes=[("b_wB", wpar)], ch="chWB" + str(wpar))
                for tsu in range(4):
                    for half in range(2):
                        p.op("t", lambda e, fs=fs, tsu=tsu, half=half, wo=wo: e.matmul(
                            PS[tsu * 2 + half][:, 0:512] if tsu * 2 + half < 6 else
                            PB[tsu * 2 + half - 6].ap().bitcast(F32)[:, 0:512],
                            ab[:, b_hT + fs * 512 + tsu * 128: b_hT + fs * 512 + (tsu + 1) * 128],
                            ab[:, wo + half * 512:wo + (half + 1) * 512],
                            start=(fs == 0), stop=(fs == 31)),
                            reads=[("b_wB", wpar), "b_hT",
                                   ("PS", tsu * 2 + half) if tsu * 2 + half < 6 else "PB%d" % (tsu * 2 + half - 6)],
                            writes=[("PS", tsu * 2 + half) if tsu * 2 + half < 6 else "PB%d" % (tsu * 2 + half - 6)])
            for tsu in range(4):
                gi = tb * 4 + tsu
                for half in range(2):
                    pbank = (PS[tsu * 2 + half][:, 0:512] if tsu * 2 + half < 6 else
                             PB[tsu * 2 + half - 6].ap().bitcast(F32)[:, 0:512])
                    psxkey = ("PS", tsu * 2 + half) if tsu * 2 + half < 6 else "PB%d" % (tsu * 2 + half - 6)
                    if ee == 0:
                        p.op("v", lambda e, pbank=pbank, gi=gi, half=half: e.tensor_scalar_mul(
                            ab[:, b_g2 + gi * 1024 + half * 512: b_g2 + gi * 1024 + (half + 1) * 512],
                            pbank, av[:, a_cwb + 8:a_cwb + 9]),
                            reads=[psxkey, "a_cwb"], writes=["b_g2"])
                    else:
                        p.op("v", lambda e, pbank=pbank, gi=gi, half=half: e.scalar_tensor_tensor(
                            out=ab[:, b_g2 + gi * 1024 + half * 512: b_g2 + gi * 1024 + (half + 1) * 512],
                            in0=pbank, scalar=av[:, a_cwb + 9:a_cwb + 10],
                            in1=ab[:, b_g2 + gi * 1024 + half * 512: b_g2 + gi * 1024 + (half + 1) * 512],
                            op0=ALU.mult, op1=ALU.add),
                            reads=[psxkey, "a_cwb", "b_g2"], writes=["b_g2"])

    a_b2w = AF32.alloc(1024)
    p.op("q", lambda e: e.dma_start(out=av[0:8, a_mb + 16:a_mb + 17],
                                    in_=cw_d.rearrange("o e -> (o e)").unsqueeze(1),
                                    allow_slow_non_contiguous=True),
         reads=["cwd", "a_mb"], writes=["a_mb"], ch="lcwT")
    p.op("q", lambda e: e.dma_start(out=av[0:8, a_b2w:a_b2w + 1024],
                                    in_=di["f_b2"].rearrange("o (e c) -> (o e) c", e=8)),
         reads=["a_b2w"], writes=["a_b2w"], ch="lb2f")
    p.op("v", lambda e: e.tensor_scalar_mul(av[0:8, a_b2w:a_b2w + 1024],
                                            av[0:8, a_b2w:a_b2w + 1024],
                                            av[0:8, a_mb + 16:a_mb + 17]),
         reads=["a_b2w", "a_mb"], writes=["a_b2w"])
    p.op("v", lambda e: e.memset(av[0:8, a_gw:a_gw + 1], 1.0),
         reads=["a_gw"], writes=["a_gw"])
    for half in range(2):
        p.op("t", lambda e, half=half: e.matmul(PS[half][0:1, 0:512], av[0:8, a_gw:a_gw + 1],
                                                av[0:8, a_b2w + half * 512:a_b2w + (half + 1) * 512],
                                                start=True, stop=True),
             reads=["a_gw", "a_b2w", ("PS", half)], writes=[("PS", half)])
    for half in range(2):
        p.op("v", lambda e, half=half: e.tensor_copy(
            out=av[0:1, a_b2w + half * 512:a_b2w + (half + 1) * 512],
            in_=PS[half][0:1, 0:512]),
            reads=[("PS", half), "a_b2w"], writes=["a_b2w"])

    for i in range(NT):
        p.op("q", lambda e, i=i: e.dma_start(out=av[:, a_o:a_o + 1024],
                                             in_=ds["xn2f"][i * 128:(i + 1) * 128, :]),
             reads=["a_o", ("x2d", i)], writes=["a_o"], ch="lx2")
        p.op("v", lambda e, i=i: e.tensor_add(av[:, a_x:a_x + 1024],
                                              ab[:, b_g2 + i * 1024:b_g2 + (i + 1) * 1024],
                                              av[:, a_o:a_o + 1024]),
             reads=["b_g2", "a_o"], writes=["a_x"])
        emit_rowadd(ctx, av[:, a_x:a_x + 1024], av[0:1, a_b2w:a_b2w + 1024].bitcast(F32)
                    if False else av[0:1, a_b2w:a_b2w + 1024], "a_x", "a_b2w", PS, p, f32=True)
        ln_tok(ctx, "a_x", av[:, a_x:a_x + 1024], av[:, a_sq:a_sq + 1024],
               av[:, a_o:a_o + 1024], "a_o", av[:, a_st:a_st + 16],
               wb=(ab[:, b_gb:b_gb + 1024], ab[:, b_gb + 1024:b_gb + 2048],
                   "b_gb", "b_gb"))
        p.op("s", lambda e: e.activation(out=av[:, a_o:a_o + 1024], in_=av[:, a_o:a_o + 1024],
                                         func=AF.Silu),
             reads=["a_o"], writes=["a_o"])
        p.op("q", lambda e, i=i: e.dma_start(out=av[:, a_x:a_x + 1024],
                                             in_=ds["s2"][i * 128:(i + 1) * 128, :]),
             reads=["a_x", ("s2d", i)], writes=["a_x"], ch="ls2")
        p.op("v", lambda e: e.tensor_mul(av[:, a_o:a_o + 1024], av[:, a_o:a_o + 1024],
                                         av[:, a_x:a_x + 1024]),
             reads=["a_o", "a_x"], writes=["a_o"])
        p.op("q", lambda e, i=i: e.dma_start(out=av[:, a_x:a_x + 1024],
                                             in_=ds["x1"][i * 128:(i + 1) * 128, :]),
             reads=["a_x", ("x1d", i)], writes=["a_x"], ch="lx1b")
        p.op("v", lambda e: e.tensor_add(av[:, a_o:a_o + 1024], av[:, a_o:a_o + 1024],
                                         av[:, a_x:a_x + 1024]),
             reads=["a_o", "a_x"], writes=["a_o"])
        p.op("q", lambda e, i=i: e.dma_start(out=OUT[i * 128:(i + 1) * 128, :],
                                             in_=av[:, a_o:a_o + 1024]),
             reads=["a_o"], writes=[("outd", i)], ch="stout")


def emit_rowadd(ctx, dst_ap, row_ap, dst_key, row_key, PS, p, f32=False):
    ones = ctx["sb_ones"] if f32 else ctx["sb_onesb"]
    okey = "sb_ones" if f32 else "sb_onesb"
    for half in range(2):
        p.op("t", lambda e, half=half, ones=ones: e.matmul(
            PS[3][:, 0:512], ones[0:1, :],
            row_ap[0:1, half * 512:(half + 1) * 512], start=True, stop=True),
            reads=[row_key, okey, ("PS", 3)], writes=[("PS", 3)])
        p.op("v", lambda e, half=half: e.tensor_add(
            dst_ap[:, half * 512:(half + 1) * 512], dst_ap[:, half * 512:(half + 1) * 512],
            PS[3][:, 0:512]),
            reads=[dst_key, ("PS", 3)], writes=[dst_key])


def emit_rope(ctx, p, av, a_x, a_sc, a_cc, i, key):
    AF32 = ctx["AF32"]
    ALU_ = ALU
    if "_rope_scr" not in ctx:
        ctx["_rope_scr"] = AF32.alloc(1024)
        ctx["_rope_scr2"] = AF32.alloc(1024)
    a_r1, a_r2 = ctx["_rope_scr"], ctx["_rope_scr2"]
    x3 = av[:, a_x:a_x + 1024].rearrange("p (h d) -> p h d", d=64)
    r3 = av[:, a_r1:a_r1 + 1024].rearrange("p (h d) -> p h d", d=64)
    q3 = av[:, a_r2:a_r2 + 1024].rearrange("p (h d) -> p h d", d=64)
    sin = av[:, a_sc + i * 32:a_sc + (i + 1) * 32].unsqueeze(1).to_broadcast([128, 16, 32])
    cos = av[:, a_cc + i * 32:a_cc + (i + 1) * 32].unsqueeze(1).to_broadcast([128, 16, 32])
    p.op("v", lambda e: e.tensor_mul(r3[:, :, 0:32], x3[:, :, 0:32], cos),
         reads=[key, "a_cc"], writes=["_rope"])
    p.op("v", lambda e: e.tensor_mul(r3[:, :, 32:64], x3[:, :, 0:32], sin),
         reads=[key, "a_sc"], writes=["_rope"])
    p.op("v", lambda e: e.tensor_mul(q3[:, :, 0:32], x3[:, :, 32:64], sin),
         reads=[key, "a_sc"], writes=["_rope2"])
    p.op("v", lambda e: e.tensor_mul(q3[:, :, 32:64], x3[:, :, 32:64], cos),
         reads=[key, "a_cc"], writes=["_rope2"])
    p.op("v", lambda e: e.tensor_sub(x3[:, :, 0:32], r3[:, :, 0:32], q3[:, :, 0:32]),
         reads=["_rope", "_rope2", key], writes=[key])
    p.op("v", lambda e: e.tensor_add(x3[:, :, 32:64], r3[:, :, 32:64], q3[:, :, 32:64]),
         reads=["_rope", "_rope2", key], writes=[key])



def _prep_vec(v):
    return np.ascontiguousarray(np.asarray(v, np.float32).reshape(NPC, 128).T)


def make_in_maps(inputs):
    d = {k: np.asarray(v) for k, v in inputs.items()}
    base = {}
    pfx = "L0_attn_"
    for nm in ["x_r", "x_w", "x_k", "x_v", "x_a", "x_g", "w0",
               "a0", "k_k", "k_a", "ffn_x_k", "gn_w", "gn_b"]:
        base["p_" + nm] = _prep_vec(d[pfx + nm])
    for nm in ["ln_pre_w", "ln_pre_b", "ln_attn_w", "ln_attn_b", "ln_ffn_w",
               "ln_ffn_b"]:
        base["row_" + nm] = np.asarray(d[pfx + nm], np.float32).reshape(1, C)
    base["r_k"] = np.ascontiguousarray(
        d[pfx + "r_k"].reshape(8, 2, 64).transpose(1, 2, 0).reshape(128, 8))
    for nm in ["W_r", "W_k", "W_v", "W_o", "w1", "w2", "a1", "a2", "g1", "g2",
               "W_ffn1", "W_ffn2"]:
        base[nm] = np.ascontiguousarray(d[pfx + nm], np.float32)
    base["gate_wT"] = np.ascontiguousarray(d["L0_ffn_gate_w"].T.astype(np.float32))
    base["w_gT"] = np.ascontiguousarray(d["L0_ffn_w_gate"].transpose(0, 2, 1))
    base["w_uT"] = np.ascontiguousarray(d["L0_ffn_w_up"].transpose(0, 2, 1))
    base["w_dn"] = np.ascontiguousarray(d["L0_ffn_w_down"].transpose(0, 2, 1))
    base["Wq"] = np.ascontiguousarray(d["L1_attn_Wq"], np.float32)
    base["Wdown"] = np.ascontiguousarray(d["L1_attn_Wdown"], np.float32)
    perm = ([h * 128 + j for h in range(16) for j in range(64)] +
            [h * 128 + 64 + j for h in range(16) for j in range(64)])
    base["Wup"] = np.ascontiguousarray(d["L1_attn_Wup"][:, perm], np.float32)
    base["Wout"] = np.ascontiguousarray(d["L1_attn_Wout"], np.float32)
    base["bq"] = np.ascontiguousarray(d["L1_attn_bq"].reshape(1, C), np.float32)
    base["bdown"] = np.ascontiguousarray(d["L1_attn_bdown"].reshape(2, 128).T, np.float32)
    base["bup"] = np.ascontiguousarray(
        np.asarray(d["L1_attn_bup"], np.float32)[perm].reshape(1, 2 * C))
    base["bout"] = np.ascontiguousarray(d["L1_attn_bout"].reshape(1, C), np.float32)
    base["Wr1"] = np.ascontiguousarray(d["L1_ffn_Wr"], np.float32)
    base["br1"] = np.ascontiguousarray(d["L1_ffn_br"].reshape(8, 1), np.float32)
    base["f_W1r"] = np.ascontiguousarray(
        d["L1_ffn_W1"].reshape(8, 8, 128, 32, 128).transpose(0, 3, 2, 1, 4)
        .reshape(32768, 1024), np.float32)
    base["f_b1"] = np.ascontiguousarray(
        d["L1_ffn_b1"].reshape(8, 32, 128).transpose(2, 0, 1).reshape(128, 8 * 32))
    base["f_W2"] = np.ascontiguousarray(d["L1_ffn_W2"], np.float32)
    base["f_b2"] = np.ascontiguousarray(d["L1_ffn_b2"].reshape(1, 8 * C), np.float32)
    base["gamma"] = np.ascontiguousarray(d["L1_ffn_gamma"].reshape(1, C), np.float32)
    base["beta"] = np.ascontiguousarray(d["L1_ffn_beta"].reshape(1, C), np.float32)
    x = np.asarray(d["L0_attn_x"], np.float32)
    maps = []
    for b in range(8):
        m = dict(base)
        m["X"] = np.ascontiguousarray(x[b])
        maps.append(m)
    return maps


def kernel(**inputs):
    nc = build_program()
    in_maps = make_in_maps(inputs)
    res = run_bass_kernel_spmd(nc, in_maps, core_ids=list(range(8)))
    return np.stack([r["OUT"] for r in res.results], 0).astype(np.float32)


# version 07
# speedup vs baseline: 1.9222x; 1.0176x over previous
import numpy as np
import ml_dtypes
import concourse.bass as bass
import concourse.mybir as mybir
from concourse.bass_utils import run_bass_kernel_spmd
from concourse.bass_core_rust import IndirectOffsetOnAxis

F32 = mybir.dt.float32
BF16 = mybir.dt.bfloat16
I32 = mybir.dt.int32
AF = mybir.ActivationFunctionType
ALU = mybir.AluOpType
AX = mybir.AxisListType

T, C, H, N = 2048, 1024, 16, 64
NT = T // 128
NPC = C // 128
LCH = 128
NCH = T // LCH
BLK = 512
NBLK = T // BLK
LAM = float(np.exp(-0.5))
EPS_LN = 1e-5
EPS_GN = 0.00064

DEBUG_SUB = None

ENGINES = ("t", "v", "s", "g", "q")
ENG_MAP = {"t": "tensor", "v": "vector", "s": "scalar", "g": "gpsimd", "q": "sync"}


class Prog:

    def __init__(self, nc):
        self.nc = nc
        self.ops = {e: [] for e in ENGINES}
        self.cnt = {e: 0 for e in ENGINES}
        self.chcnt = {}
        self.last_w = {}
        self.readers = {}

    def _dep(self, stamp, deps):
        kind, ident, val = stamp
        key = (kind, ident)
        if kind == "c":
            val = self.chcnt[ident]
        deps[key] = max(deps.get(key, 0), val)

    def op(self, e, emit, reads=(), writes=(), ch=None):
        deps = dict(getattr(self, "bar", {}))
        for k in reads:
            w = self.last_w.get(k)
            if w is not None:
                self._dep(w, deps)
        for k in writes:
            w = self.last_w.get(k)
            if w is not None:
                self._dep(w, deps)
            for r in self.readers.get(k, ()):
                self._dep(r, deps)
        if ch is None:
            self.cnt[e] += 1
            stamp = ("e", e, self.cnt[e])
        else:
            self.chcnt[ch] = self.chcnt.get(ch, 0) + 16
            stamp = ("c", ch, self.chcnt[ch])
        self.ops[e].append((emit, deps, ch))
        for k in reads:
            self.readers.setdefault(k, []).append(stamp)
        for k in writes:
            self.last_w[k] = stamp
            self.readers[k] = []
        return stamp

    def barrier(self):
        self.bar = {("e", e): self.cnt[e] for e in ENGINES if self.cnt[e]}
        self.bar.update({("c", c): v for c, v in self.chcnt.items()})

    def replay(self):
        nc = self.nc
        import contextlib
        stack = contextlib.ExitStack()
        esems = {e: stack.enter_context(nc.semaphore(f"pe_{e}")) for e in ENGINES}
        csems = {c: stack.enter_context(nc.semaphore(f"pc_{i}"))
                 for i, c in enumerate(self.chcnt)}

        def semof(key):
            kind, ident = key
            return esems[ident] if kind == "e" else csems[ident]

        with nc.Block() as block:
            for e in ENGINES:
                if not self.ops[e]:
                    continue

                def body(eng, e=e, oplist=self.ops[e]):
                    waited = {}
                    for emit, deps, ch in oplist:
                        for key, v in sorted(deps.items(), key=lambda kv: str(kv[0])):
                            if waited.get(key, 0) < v:
                                eng.wait_ge(semof(key), v)
                                waited[key] = v
                        inst = emit(eng)
                        inst.then_inc(esems[e] if ch is None else csems[ch],
                                      1 if ch is None else 16)
                getattr(block, ENG_MAP[e])(body)
        with nc.Block() as block:
            def fin(eng):
                for e in ENGINES:
                    if self.cnt[e]:
                        eng.wait_ge(esems[e], self.cnt[e])
                for c, v in self.chcnt.items():
                    eng.wait_ge(csems[c], v)
            block.sync(fin)
        stack.close()


class Arena:

    def __init__(self, nc, name, cols, dtype):
        self.t = nc.alloc_sbuf_tensor(name, [128, cols], dtype)
        self.name = name
        self.cols = cols
        self.free_at = 0
        self.marks = {}

    def alloc(self, n):
        a = self.free_at
        assert a + n <= self.cols, f"{self.name} overflow: {a}+{n} > {self.cols}"
        self.free_at += n
        return a

    def mark(self, tag):
        self.marks[tag] = self.free_at

    def reset(self, tag):
        self.free_at = self.marks[tag]


def ceil_div(a, b):
    return (a + b - 1) // b



def build_program(debug_stop=None):
    nc = bass.Bass("TRN2", target_bir_lowering=False)
    p = Prog(nc)

    di = {}

    def inp(name, shape, dtype=F32):
        di[name] = nc.dram_tensor(name, list(shape), dtype, kind="ExternalInput").ap()
        return di[name]

    inp("X", [T, C])
    for nm in ["x_r", "x_w", "x_k", "x_v", "x_a", "x_g", "w0",
               "a0", "k_k", "k_a", "ffn_x_k", "gn_w", "gn_b"]:
        inp("p_" + nm, [128, NPC])
    for nm in ["ln_pre_w", "ln_pre_b", "ln_attn_w", "ln_attn_b", "ln_ffn_w",
               "ln_ffn_b"]:
        inp("row_" + nm, [1, C])
    inp("r_k", [128, 8])
    for nm, shp in [("W_r", [C, C]), ("W_k", [C, C]), ("W_v", [C, C]),
                    ("W_o", [C, C]), ("w1", [C, 64]), ("w2", [64, C]),
                    ("a1", [C, 64]), ("a2", [64, C]), ("g1", [C, 128]),
                    ("g2", [128, C]), ("W_ffn1", [C, 4 * C]), ("W_ffn2", [4 * C, C])]:
        inp(nm, shp)
    inp("gate_wT", [C, 8])
    inp("w_gT", [8, C, 4096])
    inp("w_uT", [8, C, 4096])
    inp("w_dn", [8, 4096, C])
    for nm, shp in [("Wq", [C, C]), ("Wdown", [C, 256]), ("Wup", [256, 2 * C]),
                    ("Wout", [C, C]), ("Wr1", [C, 8])]:
        inp(nm, shp)
    inp("bq", [1, C])
    inp("bdown", [128, 2])
    inp("bup", [1, 2 * C])
    inp("bout", [1, C])
    inp("br1", [8, 1])
    inp("f_W1r", [32768, 1024])
    inp("f_b1", [128, 8 * 32])
    inp("f_W2", [8, 4096, C])
    inp("f_b2", [1, 8 * C])
    inp("gamma", [1, C])
    inp("beta", [1, C])

    OUT = nc.dram_tensor("OUT", [T, C], F32, kind="ExternalOutput").ap()

    ds = {}
    for nm, shape, dt in [("xn_bf", [T, C], BF16), ("xnaT", [C, T], BF16),
                          ("x0f", [T, C], F32), ("xmid", [T, C], F32),
                          ("sattn", [T, C], F32),
                          ("g0a", [T, C // 2], F32), ("g0b", [T, C // 2], F32),
                          ("x1", [T, C], F32), ("xn2f", [T, C], F32),
                          ("wm", [8, T], F32),
                          ("idxw0", [4097, 2], F32), ("idxw1", [4097, 2], F32),
                          ("idxw2", [4097, 2], F32), ("idxw3", [4097, 2], F32),
                          ("idxw4", [4097, 2], F32), ("idxw5", [4097, 2], F32),
                          ("idxw6", [4097, 2], F32), ("idxw7", [4097, 2], F32),
                          ("q_tok", [T, C], BF16), ("k_tok", [T, C], BF16),
                          ("v_tok", [T, C], BF16), ("o_grp", [T * 16, N], BF16),
                          ("s2", [T, C], F32)]:
        ds[nm] = nc.dram_tensor("scr_" + nm, list(shape), dt, kind="Internal").ap()

    ident_b = nc.inline_tensor(np.eye(128, dtype=ml_dtypes.bfloat16), "ident_b")
    id64_2 = nc.inline_tensor(
        np.concatenate([np.eye(64), np.eye(64)], 0).astype(ml_dtypes.bfloat16), "id64_2")
    mus_np = (np.arange(128)[:, None] < np.arange(128)[None, :]).astype(np.float32)
    mui_np = (np.arange(128)[:, None] <= np.arange(128)[None, :]).astype(np.float32)
    c_mus = nc.inline_tensor(mus_np.astype(ml_dtypes.bfloat16), "c_mus")
    c_mui = nc.inline_tensor(mui_np.astype(ml_dtypes.bfloat16), "c_mui")
    c_mls = nc.inline_tensor(mus_np.T.copy().astype(ml_dtypes.bfloat16), "c_mls")
    c_idf = nc.inline_tensor(np.eye(128, dtype=np.float32), "c_idf")
    ones_col = nc.inline_tensor(np.ones((1, 128), np.float32), "ones_col")
    ones_col_b = nc.inline_tensor(np.ones((1, 128), ml_dtypes.bfloat16), "ones_col_b")
    ind2 = np.zeros((128, 2), np.float32)
    ind2[0:64, 0] = 1.0
    ind2[64:128, 1] = 1.0
    c_ind2 = nc.inline_tensor(ind2.astype(ml_dtypes.bfloat16), "c_ind2")
    ind2T_ext = np.zeros((128, 128), np.float32)
    for rb in (0, 32, 64):
        ind2T_ext[rb + 0, 0:64] = 1.0
        ind2T_ext[rb + 1, 64:128] = 1.0
    c_ind2T = nc.inline_tensor(ind2T_ext.astype(ml_dtypes.bfloat16), "c_ind2T")
    c_ind2Tf = nc.inline_tensor(ind2T_ext, "c_ind2Tf")
    invf = (1.0 / 10000.0 ** (np.arange(0, 64, 2) / 64.0)).astype(np.float32)
    c_invf = nc.inline_tensor(np.broadcast_to(invf, (128, 32)).copy(), "c_invf")

    AF32 = Arena(nc, "AF32", 18000, F32)
    AB16 = Arena(nc, "AB16", 66944, BF16)
    sb_par = nc.alloc_sbuf_tensor("sb_par", [128, 32 * 20], F32)
    sb_id = nc.alloc_sbuf_tensor("sb_id", [128, 128], BF16)
    sb_id64 = nc.alloc_sbuf_tensor("sb_id64", [128, 64], BF16)
    sb_msk = nc.alloc_sbuf_tensor("sb_msk", [128, 3 * 128], BF16)
    sb_ones = nc.alloc_sbuf_tensor("sb_ones", [1, 128], F32)
    sb_onesb = nc.alloc_sbuf_tensor("sb_onesb", [1, 128], BF16)
    sb_ind2 = nc.alloc_sbuf_tensor("sb_ind2", [128, 2], BF16)
    sb_ind2T = nc.alloc_sbuf_tensor("sb_ind2T", [128, 128], BF16)
    sb_ind2Tf = nc.alloc_sbuf_tensor("sb_ind2Tf", [128, 128], F32)
    sb_invf = nc.alloc_sbuf_tensor("sb_invf", [128, 32], F32)
    sb_idf = nc.alloc_sbuf_tensor("sb_idf", [128, 128], F32)
    sb_gw = nc.alloc_sbuf_tensor("sb_gw", [128, 64], F32)

    PS = [nc.alloc_psum_tensor(f"PS{i}", [128, 512], F32) for i in range(6)]
    PB = [nc.alloc_psum_tensor(f"PBb{i}", [128, 1024], BF16) for i in range(2)]

    ctx = dict(nc=nc, p=p, di=di, ds=ds, OUT=OUT, AF32=AF32, AB16=AB16,
               sb_idf=sb_idf, sb_gw=sb_gw,
               sb_par=sb_par, sb_id=sb_id, sb_id64=sb_id64, sb_msk=sb_msk,
               sb_ones=sb_ones, sb_onesb=sb_onesb, sb_ind2=sb_ind2,
               sb_ind2T=sb_ind2T, sb_ind2Tf=sb_ind2Tf, sb_invf=sb_invf, PS=PS, PB=PB)

    p.op("q", lambda e: e.dma_start(out=sb_id[:], in_=ident_b.ap()), writes=["sb_id"], ch="lc")
    p.op("q", lambda e: e.dma_start(out=sb_id64[:], in_=id64_2.ap()), writes=["sb_id64"], ch="lc")
    p.op("q", lambda e: e.dma_start(out=sb_msk[:, 0:128], in_=c_mus.ap()), writes=["msk_us"], ch="lc")
    p.op("q", lambda e: e.dma_start(out=sb_msk[:, 128:256], in_=c_mui.ap()), writes=["msk_ui"], ch="lc")
    p.op("q", lambda e: e.dma_start(out=sb_msk[:, 256:384], in_=c_mls.ap()), writes=["msk_ls"], ch="lc")
    p.op("q", lambda e: e.dma_start(out=sb_ones[:], in_=ones_col.ap()), writes=["sb_ones"], ch="lc")
    p.op("q", lambda e: e.dma_start(out=sb_onesb[:], in_=ones_col_b.ap()), writes=["sb_onesb"], ch="lc")
    p.op("q", lambda e: e.dma_start(out=sb_ind2[:], in_=c_ind2.ap()), writes=["sb_ind2"], ch="lc")
    p.op("q", lambda e: e.dma_start(out=sb_ind2T[:], in_=c_ind2T.ap()), writes=["sb_ind2T"], ch="lc")
    p.op("q", lambda e: e.dma_start(out=sb_ind2Tf[:], in_=c_ind2Tf.ap()), writes=["sb_ind2Tf"], ch="lc")
    p.op("q", lambda e: e.dma_start(out=sb_invf[:], in_=c_invf.ap()), writes=["sb_invf"], ch="lc")
    p.op("q", lambda e: e.dma_start(out=sb_idf[:], in_=c_idf.ap()), writes=["sb_idf"], ch="lc")

    emit_phase01(ctx)
    if debug_stop == "p1":
        p.replay()
        nc.finalize()
        return nc
    emit_phase2345(ctx)
    if debug_stop == "p5":
        p.replay()
        nc.finalize()
        return nc
    emit_l0ffn(ctx)
    emit_combine0(ctx)
    if debug_stop == "p6":
        p.replay()
        nc.finalize()
        return nc
    emit_layer1(ctx)
    p.replay()
    nc.finalize()
    return nc



def ln_tok(ctx, key_in, xin_ap, scratch_a, out_ap, key_out, stat_a, wb=None,
           eps=EPS_LN, tag="", scratch_key="a_sq", stat_key="a_st"):
    p = ctx["p"]
    kS = scratch_key
    p.op("s", lambda e: e.activation(out=scratch_a, in_=xin_ap, func=AF.Square),
         reads=[key_in], writes=[kS])
    p.op("v", lambda e: e.tensor_reduce(out=stat_a[:, 0:1], in_=xin_ap, axis=AX.X, op=ALU.add),
         reads=[key_in], writes=[(stat_key, 0)])
    p.op("v", lambda e: e.tensor_reduce(out=stat_a[:, 1:2], in_=scratch_a, axis=AX.X, op=ALU.add),
         reads=[kS], writes=[(stat_key, 1)])
    p.op("v", lambda e: e.tensor_scalar_mul(stat_a[:, 2:3], stat_a[:, 0:1], 1.0 / C),
         reads=[(stat_key, 0)], writes=[(stat_key, 2)])
    p.op("s", lambda e: e.activation(out=stat_a[:, 3:4], in_=stat_a[:, 2:3], func=AF.Square),
         reads=[(stat_key, 2)], writes=[(stat_key, 3)])
    p.op("v", lambda e: e.tensor_scalar(stat_a[:, 4:5], stat_a[:, 1:2],
                                        1.0 / C, stat_a[:, 3:4],
                                        op0=ALU.mult, op1=ALU.subtract),
         reads=[(stat_key, 1), (stat_key, 3)], writes=[(stat_key, 4)])
    p.op("v", lambda e: e.tensor_scalar_add(stat_a[:, 7:8], stat_a[:, 4:5], float(eps)),
         reads=[(stat_key, 4)], writes=[(stat_key, 7)])
    p.op("s", lambda e: e.activation(out=stat_a[:, 5:6], in_=stat_a[:, 7:8], func=AF.Ln),
         reads=[(stat_key, 7)], writes=[(stat_key, 5)])
    p.op("s", lambda e: e.activation(out=stat_a[:, 6:7], in_=stat_a[:, 5:6],
                                     func=AF.Exp, scale=-0.5),
         reads=[(stat_key, 5)], writes=[(stat_key, 6)])
    p.op("v", lambda e: e.tensor_scalar(out_ap, xin_ap, stat_a[:, 2:3], stat_a[:, 6:7],
                                        op0=ALU.subtract, op1=ALU.mult),
         reads=[key_in, (stat_key, 2), (stat_key, 6)], writes=[key_out])
    if wb is not None:
        w_ap, b_ap, keyw, keyb = wb
        p.op("v", lambda e: e.tensor_mul(out_ap, out_ap, w_ap),
             reads=[key_out, keyw], writes=[key_out])
        p.op("v", lambda e: e.tensor_add(out_ap, out_ap, b_ap),
             reads=[key_out, keyb], writes=[key_out])


def bcast_row(ctx, row_ap, key_row, psum_idx, out_ap, key_out, n):
    p, PS, sb_ones = ctx["p"], ctx["PS"], ctx["sb_ones"]
    ps = PS[psum_idx]
    assert n <= 512
    p.op("t", lambda e: e.matmul(ps[:, 0:n], sb_ones[:], row_ap, start=True, stop=True),
         reads=[key_row, "sb_ones"], writes=[("PSb", psum_idx)])
    p.op("v", lambda e: e.tensor_copy(out=out_ap, in_=ps[:, 0:n]),
         reads=[("PSb", psum_idx)], writes=[key_out])



def emit_phase01(ctx):
    nc, p, di, ds = ctx["nc"], ctx["p"], ctx["di"], ctx["ds"]
    AF32, AB16, PS, PB = ctx["AF32"], ctx["AB16"], ctx["PS"], ctx["PB"]
    sb_id = ctx["sb_id"]
    av, ab = AF32.t, AB16.t

    sb_par = ctx["sb_par"]
    ctx["par_off"] = {}
    off = 0
    for nm in ["x_r", "x_w", "x_k", "x_v", "x_a", "x_g", "w0",
               "a0", "k_k", "k_a", "ffn_x_k", "gn_w", "gn_b"]:
        ctx["par_off"][nm] = off
        p.op("q", lambda e, nm=nm, o=off: e.dma_start(out=sb_par[:, o:o + NPC], in_=di["p_" + nm]),
             writes=[("par", nm)], ch="lpar")
        off += NPC
    ctx["par_off"]["r_k"] = off
    p.op("q", lambda e, o=off: e.dma_start(out=sb_par[:, o:o + 8], in_=di["r_k"]),
         writes=[("par", "r_k")], ch="lpar")
    off += 8
    ctx["par_off"]["fb1"] = off
    p.op("q", lambda e, o=off: e.dma_start(out=sb_par[:, o:o + 256], in_=di["f_b1"]),
         writes=[("par", "fb1")], ch="lpar")
    off += 256
    ctx["par_off"]["ka1m"] = off
    oka = ctx["par_off"]["k_a"]
    p.op("v", lambda e, o=off, oka=oka: e.tensor_scalar(sb_par[:, o:o + NPC],
                                                        sb_par[:, oka:oka + NPC],
                                                        -1.0, 1.0, op0=ALU.mult, op1=ALU.add),
         reads=[("par", "k_a")], writes=[("par", "ka1m")])
    off += NPC

    a_lnb = AB16.alloc(6 * 1024)
    ctx["a_lnb"] = a_lnb
    for i, nm in enumerate(["ln_pre_w", "ln_pre_b", "ln_attn_w", "ln_attn_b",
                            "ln_ffn_w", "ln_ffn_b"]):
        p.op("g", lambda e, nm=nm, i=i: e.dma_start(
            out=ab[:, a_lnb + i * 1024:a_lnb + (i + 1) * 1024],
            in_=di["row_" + nm].to_broadcast([128, C])),
            writes=[("lnb", nm)], ch="llnb")

    sb_gw = ctx["sb_gw"]
    for c in range(NPC):
        p.op("q", lambda e, c=c: e.dma_start(out=sb_gw[:, c * 8:(c + 1) * 8],
                                             in_=di["gate_wT"][c * 128:(c + 1) * 128, :]),
             writes=[("gwT", c)], ch="lgw")

    a_logits = AF32.alloc(T)
    logitsT = av[:, a_logits:a_logits + T]
    ctx["logitsT"] = logitsT
    AF32.mark("keep")
    AB16.mark("keep")

    a_x = AF32.alloc(1024)
    a_sq = AF32.alloc(1024)
    a_xn = AF32.alloc(1024)
    a_x0 = AF32.alloc(1024)
    a_st = AF32.alloc(16)
    b_xn = AB16.alloc(1024)
    b_tr = AB16.alloc(128)
    a_tr = AF32.alloc(128)
    lnb = ctx["a_lnb"]
    sb_idf = ctx["sb_idf"]

    for i in range(NT):
        p.op("q", lambda e, i=i: e.dma_start(out=av[:, a_x:a_x + 1024],
                                             in_=di["X"][i * 128:(i + 1) * 128, :]),
             writes=["a_x"], ch="ldx")
        ln_tok(ctx, "a_x", av[:, a_x:a_x + 1024], av[:, a_sq:a_sq + 1024],
               av[:, a_xn:a_xn + 1024], "a_xn", av[:, a_st:a_st + 16], tag="p1")
        p.op("v", lambda e: e.tensor_copy(out=ab[:, b_xn:b_xn + 1024], in_=av[:, a_xn:a_xn + 1024]),
             reads=["a_xn"], writes=["b_xn"])
        p.op("q", lambda e, i=i: e.dma_start(out=ds["xn_bf"][i * 128:(i + 1) * 128, :],
                                             in_=ab[:, b_xn:b_xn + 1024]),
             reads=["b_xn"], writes=[("xnbd", i)], ch="stxn")
        for c in range(NPC):
            p.op("t", lambda e, c=c: e.transpose(PS[4][:, 0:128],
                                                 av[:, a_xn + c * 128:a_xn + (c + 1) * 128],
                                                 sb_idf[:]),
                 reads=["a_xn", "sb_idf", ("PS", 4)], writes=[("PS", 4)])
            p.op("v", lambda e: e.tensor_copy(out=av[:, a_tr:a_tr + 128], in_=PS[4][:, 0:128]),
                 reads=[("PS", 4)], writes=["a_tr"])
            p.op("t", lambda e, c=c: e.matmul(PS[5][0:8, 0:128],
                                              sb_gw[:, c * 8:(c + 1) * 8],
                                              av[:, a_tr:a_tr + 128],
                                              start=(c == 0), stop=(c == NPC - 1)),
                 reads=[("gwT", c), "a_tr", "PS5"], writes=["PS5"])
        p.op("v", lambda e, i=i: e.tensor_copy(out=logitsT[0:8, i * 128:(i + 1) * 128],
                                               in_=PS[5][0:8, 0:128]),
             reads=["PS5"], writes=[("lgT", i)])

        ln_tok(ctx, "a_xn", av[:, a_xn:a_xn + 1024], av[:, a_sq:a_sq + 1024],
               av[:, a_x0:a_x0 + 1024], "a_x0", av[:, a_st:a_st + 16],
               wb=(ab[:, lnb:lnb + 1024], ab[:, lnb + 1024:lnb + 2048],
                   ("lnb", "ln_pre_w"), ("lnb", "ln_pre_b")), tag="p1")
        p.op("q", lambda e, i=i: e.dma_start(out=ds["x0f"][i * 128:(i + 1) * 128, :],
                                             in_=av[:, a_x0:a_x0 + 1024]),
             reads=["a_x0"], writes=[("x0d", i)], ch="stx0")
        ln_tok(ctx, "a_x0", av[:, a_x0:a_x0 + 1024], av[:, a_sq:a_sq + 1024],
               av[:, a_xn:a_xn + 1024], "a_xn", av[:, a_st:a_st + 16],
               wb=(ab[:, lnb + 2048:lnb + 3072], ab[:, lnb + 3072:lnb + 4096],
                   ("lnb", "ln_attn_w"), ("lnb", "ln_attn_b")), tag="p1")
        p.op("v", lambda e: e.tensor_copy(out=ab[:, b_xn:b_xn + 1024], in_=av[:, a_xn:a_xn + 1024]),
             reads=["a_xn"], writes=["b_xn"])
        for c in range(NPC):
            p.op("t", lambda e, c=c: e.transpose(PB[0][:, 0:128],
                                                 ab[:, b_xn + c * 128:b_xn + (c + 1) * 128],
                                                 sb_id[:]),
                 reads=["b_xn", "sb_id"], writes=["PB0"])
            p.op("v", lambda e: e.tensor_copy(out=ab[:, b_tr:b_tr + 128], in_=PB[0][:, 0:128]),
                 reads=["PB0"], writes=["b_tr"])
            p.op("q", lambda e, c=c, i=i: e.dma_start(
                out=ds["xnaT"][c * 128:(c + 1) * 128, i * 128:(i + 1) * 128],
                in_=ab[:, b_tr:b_tr + 128]),
                reads=["b_tr"], writes=[("xnaTd", c, i)], ch="stxT")


def emit_phase2345(ctx):
    nc, p, di, ds = ctx["nc"], ctx["p"], ctx["di"], ctx["ds"]
    AF32, AB16, PS, PB = ctx["AF32"], ctx["AB16"], ctx["PS"], ctx["PB"]
    av, ab = AF32.t, AB16.t
    par, po = ctx["sb_par"], ctx["par_off"]
    sb_id, sb_id64 = ctx["sb_id"], ctx["sb_id64"]
    sb_msk, sb_ind2, sb_ind2T = ctx["sb_msk"], ctx["sb_ind2"], ctx["sb_ind2T"]
    lnb = ctx["a_lnb"]

    p.barrier()
    AF32.reset("keep")
    AB16.reset("keep")

    def parap(nm, c, n=1):
        o = po[nm]
        return par[:, o + c:o + c + n]

    b_xa = AB16.alloc(8 * 514)
    b_d1 = AB16.alloc(8 * 512)
    b_var = AB16.alloc(8 * 512)
    b_wA = AB16.alloc(2048)
    b_wB = AB16.alloc(2048)
    b_sm = AB16.alloc(3072)
    o_w1, o_a1, o_g1 = b_sm, b_sm + 512, b_sm + 1024
    o_lo2 = b_sm + 2048
    b_rT = AB16.alloc(8 * 512)
    b_kT = AB16.alloc(8 * 512)
    b_aS = AB16.alloc(8 * 512)
    b_kkn = AB16.alloc(8 * 512)
    b_gg = AB16.alloc(8 * 512)
    b_vt = AB16.alloc(4 * 1024)
    b_vT = AB16.alloc(8 * 512)
    b_der = AB16.alloc(4 * 1024)
    b_xfT = AB16.alloc(8 * 514)
    b_mid = b_wB
    b_BTs = [AB16.alloc(7 * 128) for _ in range(2)]
    b_Bc = [AB16.alloc(2 * 128) for _ in range(2)]
    b_KTm = [AB16.alloc(128) for _ in range(2)]
    b_Pbm = [AB16.alloc(128) for _ in range(2)]
    b_Pkm = [AB16.alloc(128) for _ in range(2)]
    b_Wc = [AB16.alloc(2 * 64) for _ in range(2)]
    b_Btok = [AB16.alloc(64) for _ in range(2)]
    b_Ktok = [AB16.alloc(64) for _ in range(2)]
    b_stbf = AB16.alloc(8 * 64)

    a_sig = AF32.alloc(8 * 512)
    a_gsc = AF32.alloc(516)
    a_gr = AF32.alloc(8 * 512)
    a_xm = AF32.alloc(4 * 1024)
    a_scr = AF32.alloc(1024)
    a_scr2 = AF32.alloc(1024)
    a_st = AF32.alloc(16)
    a_state = AF32.alloc(8 * 64)
    a_stsc = AF32.alloc(8 * 64)
    a_gcol = AF32.alloc(8)

    wbufs = [(b_wA, "chWA", "b_wA"), (b_wB, "chWB", "b_wB")]
    nwb = [0]

    def load_w(dram_ap, rows, cols, tag):
        boff, chn, keyn = wbufs[nwb[0] % 2]
        nwb[0] += 1
        nch = rows // 128
        assert nch * cols <= 2048
        src = dram_ap.rearrange("(c p) j -> p c j", p=128)
        p.op("g", lambda e, boff=boff, nch=nch, cols=cols, src=src: e.dma_start(
            out=ab[:, boff:boff + nch * cols].rearrange("p (c j) -> p c j", c=nch), in_=src),
            reads=[keyn], writes=[keyn], ch=chn)
        return boff, keyn

    p.op("g", lambda e: e.dma_start(out=ab[:, o_w1:o_w1 + 512].rearrange("p (c j) -> p c j", c=8),
                                    in_=di["w1"].rearrange("(c p) j -> p c j", p=128)),
         writes=["sm_w1"], ch="lsm")
    p.op("g", lambda e: e.dma_start(out=ab[:, o_a1:o_a1 + 512].rearrange("p (c j) -> p c j", c=8),
                                    in_=di["a1"].rearrange("(c p) j -> p c j", p=128)),
         writes=["sm_a1"], ch="lsm")
    p.op("g", lambda e: e.dma_start(out=ab[:, o_g1:o_g1 + 1024].rearrange("p (c j) -> p c j", c=8),
                                    in_=di["g1"].rearrange("(c p) j -> p c j", p=128)),
         writes=["sm_g1"], ch="lsm")

    p.op("v", lambda e: e.memset(av[:, a_state:a_state + 512], 0.0),
         writes=[("state", pcx, hx) for pcx in range(8) for hx in range(2)])
    p.op("v", lambda e: e.memset(ab[:, b_stbf:b_stbf + 512], 0.0), writes=["stbf"])
    p.op("v", lambda e: e.memset(ab[:, b_xfT:b_xfT + 8 * 514]
                                 .rearrange("p (c w) -> p c w", w=514)[:, :, 0:1], 0.0),
         writes=["b_xfT"])

    def xa_view(c, c0, n):
        return ab[:, b_xa + c * 514 + c0: b_xa + c * 514 + c0 + n]

    def blk8(base, c, n=512):
        return ab[:, base + c * 512: base + c * 512 + n]

    def proj_var(xi_nm):
        for c in range(8):
            p.op("v", lambda e, c=c, xi_nm=xi_nm: e.scalar_tensor_tensor(
                out=blk8(b_var, c), in0=blk8(b_d1, c), scalar=parap(xi_nm, c),
                in1=xa_view(c, 1, 512), op0=ALU.mult, op1=ALU.add),
                reads=["b_d1", "b_xa", ("par", xi_nm)], writes=["b_var"])

    def mm_feature_qtr(dram_w, out_base, outkey, tag):
        for q in range(4):
            wo, wk = load_w(dram_w[:, q * 256:(q + 1) * 256], C, 256, (tag, q))
            for jj in range(2):
                j = q * 2 + jj
                for c in range(8):
                    p.op("t", lambda e, wo=wo, jj=jj, c=c: e.matmul(
                        PS[0][:, 0:512],
                        ab[:, wo + c * 256 + jj * 128: wo + c * 256 + (jj + 1) * 128],
                        blk8(b_var, c), start=(c == 0), stop=(c == 7)),
                        reads=[wk, "b_var", ("PS", 0)], writes=[("PS", 0)])
                p.op("v", lambda e, j=j: e.tensor_copy(out=blk8(out_base, j), in_=PS[0][:, 0:512]),
                     reads=[("PS", 0)], writes=[outkey])

    nblk = 1 if DEBUG_SUB else NBLK
    for b in range(nblk):
        t0 = b * BLK
        if b == 0:
            for c in range(8):
                p.op("v", lambda e, c=c: e.memset(xa_view(c, 0, 1), 0.0), writes=["b_xa"])
        else:
            for c in range(8):
                p.op("q", lambda e, c=c, t0=t0: e.dma_start(
                    out=xa_view(c, 0, 1), in_=ds["xnaT"][c * 128:(c + 1) * 128, t0 - 1:t0],
                    allow_slow_non_contiguous=True),
                    reads=["b_xa"], writes=["b_xa"], ch="lxa")
        for c in range(8):
            p.op("q", lambda e, c=c, t0=t0: e.dma_start(
                out=xa_view(c, 1, 512), in_=ds["xnaT"][c * 128:(c + 1) * 128, t0:t0 + 512]),
                reads=["b_xa"] + [("xnaTd", c, i) for i in range(4 * b, 4 * b + 4)],
                writes=["b_xa"], ch="lxa")
        for c in range(8):
            p.op("v", lambda e, c=c: e.tensor_sub(blk8(b_d1, c), xa_view(c, 0, 512), xa_view(c, 1, 512)),
                 reads=["b_xa"], writes=["b_d1"])

        p.op("g", lambda e: e.dma_start(out=ab[0:64, o_lo2:o_lo2 + 1024], in_=di["w2"]),
             reads=["sm_lo2"], writes=["sm_lo2"], ch="lsm2")
        proj_var("x_w")
        for c in range(8):
            p.op("t", lambda e, c=c: e.matmul(PS[2][0:64, 0:512],
                                              ab[:, o_w1 + c * 64:o_w1 + (c + 1) * 64],
                                              blk8(b_var, c), start=(c == 0), stop=(c == 7)),
                 reads=["sm_w1", "b_var", ("PS", 2)], writes=[("PS", 2)])
        p.op("s", lambda e: e.activation(out=ab[0:64, b_mid:b_mid + 512], in_=PS[2][0:64, 0:512],
                                         func=AF.Tanh),
             reads=[("PS", 2)], writes=["b_wB"])
        for j in range(8):
            p.op("t", lambda e, j=j: e.matmul(PS[2][:, 0:512],
                                              ab[0:64, o_lo2 + j * 128:o_lo2 + (j + 1) * 128],
                                              ab[0:64, b_mid:b_mid + 512],
                                              start=True, stop=True),
                 reads=["sm_lo2", "b_wB", ("PS", 2)], writes=[("PS", 2)])
            p.op("s", lambda e, j=j: e.activation(out=av[:, a_sig + j * 512:a_sig + (j + 1) * 512],
                                                  in_=PS[2][:, 0:512], func=AF.Sigmoid,
                                                  bias=parap("w0", j)),
                 reads=[("PS", 2), ("par", "w0")], writes=["a_sig"])

        p.op("g", lambda e: e.dma_start(out=ab[0:64, o_lo2:o_lo2 + 1024], in_=di["a2"]),
             reads=["sm_lo2"], writes=["sm_lo2"], ch="lsm2")
        proj_var("x_a")
        for c in range(8):
            p.op("t", lambda e, c=c: e.matmul(PS[2][0:64, 0:512],
                                              ab[:, o_a1 + c * 64:o_a1 + (c + 1) * 64],
                                              blk8(b_var, c), start=(c == 0), stop=(c == 7)),
                 reads=["sm_a1", "b_var", ("PS", 2)], writes=[("PS", 2)])
        p.op("v", lambda e: e.tensor_copy(out=ab[0:64, b_mid:b_mid + 512], in_=PS[2][0:64, 0:512]),
             reads=[("PS", 2)], writes=["b_wB"])
        for j in range(8):
            p.op("t", lambda e, j=j: e.matmul(PS[2][:, 0:512],
                                              ab[0:64, o_lo2 + j * 128:o_lo2 + (j + 1) * 128],
                                              ab[0:64, b_mid:b_mid + 512],
                                              start=True, stop=True),
                 reads=["sm_lo2", "b_wB", ("PS", 2)], writes=[("PS", 2)])
            p.op("s", lambda e, j=j: e.activation(out=blk8(b_aS, j), in_=PS[2][:, 0:512],
                                                  func=AF.Sigmoid, bias=parap("a0", j)),
                 reads=[("PS", 2), ("par", "a0")], writes=["b_aS"])

        p.op("g", lambda e: e.dma_start(out=ab[:, o_lo2:o_lo2 + 1024], in_=di["g2"]),
             reads=["sm_lo2"], writes=["sm_lo2"], ch="lsm2")
        proj_var("x_g")
        for c in range(8):
            p.op("t", lambda e, c=c: e.matmul(PS[2][:, 0:512],
                                              ab[:, o_g1 + c * 128:o_g1 + (c + 1) * 128],
                                              blk8(b_var, c), start=(c == 0), stop=(c == 7)),
                 reads=["sm_g1", "b_var", ("PS", 2)], writes=[("PS", 2)])
        p.op("s", lambda e: e.activation(out=ab[:, b_mid:b_mid + 512], in_=PS[2][:, 0:512],
                                         func=AF.Sigmoid),
             reads=[("PS", 2)], writes=["b_wB"])
        for j in range(8):
            p.op("t", lambda e, j=j: e.matmul(PS[2][:, 0:512],
                                              ab[:, o_lo2 + j * 128:o_lo2 + (j + 1) * 128],
                                              ab[:, b_mid:b_mid + 512],
                                              start=True, stop=True),
                 reads=["sm_lo2", "b_wB", ("PS", 2)], writes=[("PS", 2)])
            p.op("v", lambda e, j=j: e.tensor_copy(out=blk8(b_gg, j), in_=PS[2][:, 0:512]),
                 reads=[("PS", 2)], writes=["b_gg"])

        proj_var("x_k")
        mm_feature_qtr(di["W_k"], b_kT, "b_kT", ("Wk", b))
        for c in range(8):
            p.op("v", lambda e, c=c: e.tensor_scalar_mul(av[:, a_scr:a_scr + 512],
                                                         blk8(b_kT, c), parap("k_k", c)),
                 reads=["b_kT", ("par", "k_k")], writes=["a_scr"])
            p.op("v", lambda e, c=c: e.tensor_copy(out=blk8(b_kkn, c), in_=av[:, a_scr:a_scr + 512]),
                 reads=["a_scr"], writes=["b_kkn"])
            p.op("s", lambda e: e.activation(out=ab[:, b_var:b_var + 512],
                                             in_=av[:, a_scr:a_scr + 512], func=AF.Square),
                 reads=["a_scr"], writes=["b_var"])
            p.op("t", lambda e, c=c: e.matmul(PS[3][0:2, 0:512], sb_ind2[:],
                                              ab[:, b_var:b_var + 512], start=True, stop=True),
                 reads=["sb_ind2", "b_var", ("PS", 3)], writes=[("PS", 3)])
            p.op("v", lambda e, c=c: e.tensor_copy(out=av[0:2, a_xm + c * 512:a_xm + (c + 1) * 512],
                                                   in_=PS[3][0:2, 0:512]),
                 reads=[("PS", 3)], writes=["a_xm"])
        p.op("v", lambda e: e.tensor_scalar_max(av[0:2, a_xm:a_xm + 4096],
                                                av[0:2, a_xm:a_xm + 4096], 1e-24),
             reads=["a_xm"], writes=["a_xm"])
        p.op("s", lambda e: e.activation(out=av[0:2, a_xm:a_xm + 4096],
                                         in_=av[0:2, a_xm:a_xm + 4096], func=AF.Ln),
             reads=["a_xm"], writes=["a_xm"])
        p.op("s", lambda e: e.activation(out=av[0:2, a_xm:a_xm + 4096],
                                         in_=av[0:2, a_xm:a_xm + 4096], func=AF.Exp, scale=-0.5),
             reads=["a_xm"], writes=["a_xm"])
        for c in range(8):
            p.op("t", lambda e, c=c: e.matmul(PS[3][:, 0:512], ctx["sb_ind2Tf"][0:2, :],
                                              av[0:2, a_xm + c * 512:a_xm + (c + 1) * 512],
                                              start=True, stop=True),
                 reads=["a_xm", "sb_ind2Tf", ("PS", 3)], writes=[("PS", 3)])
            p.op("v", lambda e, c=c: e.tensor_mul(blk8(b_kkn, c), blk8(b_kkn, c), PS[3][:, 0:512]),
                 reads=["b_kkn", ("PS", 3)], writes=["b_kkn"])
        for c in range(8):
            p.op("v", lambda e, c=c: e.tensor_scalar(av[:, a_scr:a_scr + 512], blk8(b_aS, c),
                                                     parap("k_a", c), parap("ka1m", c),
                                                     op0=ALU.mult, op1=ALU.add),
                 reads=["b_aS", ("par", "k_a"), ("par", "ka1m")], writes=["a_scr"])
            p.op("v", lambda e, c=c: e.tensor_mul(blk8(b_kT, c), blk8(b_kT, c), av[:, a_scr:a_scr + 512]),
                 reads=["b_kT", "a_scr"], writes=["b_kT"])

        proj_var("x_r")
        mm_feature_qtr(di["W_r"], b_rT, "b_rT", ("Wr", b))

        proj_var("x_v")
        for q in range(4):
            wo, wk = load_w(di["W_v"][:, q * 256:(q + 1) * 256], C, 256, ("Wv", b, q))
            for ts in range(4):
                for hh in range(2):
                    jcol = q * 256 + hh * 128
                    for c in range(8):
                        p.op("t", lambda e, ts=ts, hh=hh, c=c, wo=wo: e.matmul(
                            PS[0][:, 0:128 + 0] if False else PS[0][:, hh * 128: hh * 128 + 128],
                            blk8(b_var, c)[:, ts * 128:(ts + 1) * 128],
                            ab[:, wo + c * 256 + hh * 128: wo + c * 256 + (hh + 1) * 128],
                            start=(c == 0), stop=(c == 7)),
                            reads=[wk, "b_var", ("PS", 0)], writes=[("PS", 0)])
                    p.op("v", lambda e, ts=ts, jcol=jcol, hh=hh: e.tensor_copy(
                        out=ab[:, b_vt + ts * 1024 + jcol: b_vt + ts * 1024 + jcol + 128],
                        in_=PS[0][:, hh * 128: hh * 128 + 128]),
                        reads=[("PS", 0)], writes=["b_vt"])
        for ts in range(4):
            for c in range(8):
                p.op("t", lambda e, ts=ts, c=c: e.transpose(
                    PB[0][:, 0:128],
                    ab[:, b_vt + ts * 1024 + c * 128: b_vt + ts * 1024 + (c + 1) * 128],
                    sb_id[:]),
                    reads=["b_vt", "sb_id"], writes=["PB0"])
                p.op("v", lambda e, ts=ts, c=c: e.tensor_copy(
                    out=blk8(b_vT, c)[:, ts * 128:(ts + 1) * 128], in_=PB[0][:, 0:128]),
                    reads=["PB0"], writes=["b_vT"])

        for c in range(8):
            p.op("v", lambda e: e.memset(av[:, a_gsc:a_gsc + 1], 0.0), writes=["a_gsc"])
            p.op("v", lambda e, c=c: e.tensor_tensor_scan(
                out=av[:, a_gsc + 1:a_gsc + 513], data0=av[:, a_sig + c * 512:a_sig + (c + 1) * 512],
                data1=av[:, a_gsc:a_gsc + 1].to_broadcast([128, 512]),
                initial=0.0, op0=ALU.add, op1=ALU.add),
                reads=["a_sig", "a_gsc"], writes=["a_gsc"])
            p.op("v", lambda e, c=c: e.tensor_tensor(
                out=av[:, a_gr + c * 512:a_gr + (c + 1) * 512].rearrange("p (k l) -> p k l", l=LCH),
                in0=av[:, a_gsc + 1:a_gsc + 513].rearrange("p (k l) -> p k l", l=LCH),
                in1=av[:, a_gsc:a_gsc + 512].rearrange("p (k l) -> p k l", l=LCH)[:, :, 0:1]
                .to_broadcast([128, 4, LCH]),
                op=ALU.subtract),
                reads=["a_gsc"], writes=["a_gr"])

        if DEBUG_SUB == "p2":
            continue
        WSUB = DEBUG_SUB if DEBUG_SUB in ("w1", "w2", "w3", "w4", "w5") else None
        gr3 = av[:, a_gr:a_gr + 4096].rearrange("p (c t) -> p c t", t=512)
        sg3 = av[:, a_sig:a_sig + 4096].rearrange("p (c t) -> p c t", t=512)
        kkn3 = ab[:, b_kkn:b_kkn + 4096].rearrange("p (c t) -> p c t", t=512)
        aS3 = ab[:, b_aS:b_aS + 4096].rearrange("p (c t) -> p c t", t=512)
        rT3 = ab[:, b_rT:b_rT + 4096].rearrange("p (c t) -> p c t", t=512)
        kT3 = ab[:, b_kT:b_kT + 4096].rearrange("p (c t) -> p c t", t=512)
        o_rt, o_kt, o_at, o_bt = b_der, b_der + 1024, b_der + 2048, b_der + 3072
        for lc in range(4):
            tl = lc * LCH
            sl = lambda base: ab[:, base:base + 1024].rearrange("p (c t) -> p c t", t=128)
            p.op("s", lambda e, tl=tl: e.activation(
                out=av[:, a_gcol:a_gcol + 8],
                in_=gr3[:, :, tl + LCH - 1:tl + LCH], func=AF.Exp, scale=-LAM),
                reads=["a_gr"], writes=["a_gcol"])
            p.op("s", lambda e, tl=tl: e.activation(out=av[:, a_scr:a_scr + 1024]
                                                    .rearrange("p (c t) -> p c t", t=128),
                                                    in_=gr3[:, :, tl:tl + LCH],
                                                    func=AF.Exp, scale=-LAM),
                 reads=["a_gr"], writes=["a_scr"])
            p.op("v", lambda e, tl=tl: e.tensor_mul(sl(o_rt), rT3[:, :, tl:tl + LCH],
                                                    av[:, a_scr:a_scr + 1024]
                                                    .rearrange("p (c t) -> p c t", t=128)),
                 reads=["b_rT", "a_scr"], writes=["b_der"])
            p.op("v", lambda e, tl=tl: e.scalar_tensor_tensor(
                out=av[:, a_scr2:a_scr2 + 1024].rearrange("p (c t) -> p c t", t=128),
                in0=gr3[:, :, tl:tl + LCH], scalar=-1.0, in1=sg3[:, :, tl:tl + LCH],
                op0=ALU.mult, op1=ALU.add),
                reads=["a_gr", "a_sig"], writes=["a_scr2"])
            p.op("s", lambda e: e.activation(out=av[:, a_scr2:a_scr2 + 1024],
                                             in_=av[:, a_scr2:a_scr2 + 1024],
                                             func=AF.Exp, scale=LAM),
                 reads=["a_scr2"], writes=["a_scr2"])
            p.op("v", lambda e, tl=tl: e.scalar_tensor_tensor(
                out=sl(o_at), in0=kkn3[:, :, tl:tl + LCH], scalar=-1.0,
                in1=av[:, a_scr2:a_scr2 + 1024].rearrange("p (c t) -> p c t", t=128),
                op0=ALU.mult, op1=ALU.mult),
                reads=["b_kkn", "a_scr2"], writes=["b_der"])
            p.op("s", lambda e, tl=tl: e.activation(out=av[:, a_scr:a_scr + 1024]
                                                    .rearrange("p (c t) -> p c t", t=128),
                                                    in_=gr3[:, :, tl:tl + LCH],
                                                    func=AF.Exp, scale=LAM),
                 reads=["a_gr", "b_der"], writes=["a_scr"])
            p.op("v", lambda e, tl=tl: e.tensor_mul(sl(o_kt), kT3[:, :, tl:tl + LCH],
                                                    av[:, a_scr:a_scr + 1024]
                                                    .rearrange("p (c t) -> p c t", t=128)),
                 reads=["b_kT", "a_scr"], writes=["b_der"])
            p.op("v", lambda e, tl=tl: e.tensor_mul(
                av[:, a_scr2:a_scr2 + 1024].rearrange("p (c t) -> p c t", t=128),
                kkn3[:, :, tl:tl + LCH], aS3[:, :, tl:tl + LCH]),
                reads=["b_kkn", "b_aS"], writes=["a_scr2"])
            p.op("v", lambda e: e.tensor_mul(ab[:, o_bt:o_bt + 1024],
                                             av[:, a_scr2:a_scr2 + 1024]
                                             .bitcast(F32) if False else av[:, a_scr2:a_scr2 + 1024],
                                             av[:, a_scr:a_scr + 1024]),
                 reads=["a_scr2", "a_scr"], writes=["b_der"])
            for pc in range(8):
                for hh in range(2):
                    half = hh * 64
                    h = pc * 2 + hh
                    rv = ab[half:half + 64, o_rt + pc * 128:o_rt + (pc + 1) * 128]
                    kv_ = ab[half:half + 64, o_kt + pc * 128:o_kt + (pc + 1) * 128]
                    avw = ab[half:half + 64, o_at + pc * 128:o_at + (pc + 1) * 128]
                    bvw = ab[half:half + 64, o_bt + pc * 128:o_bt + (pc + 1) * 128]
                    rd_pre = ["b_der"]
                    vch = ab[:, b_vt + lc * 1024 + h * 64: b_vt + lc * 1024 + (h + 1) * 64]
                    st_hv = av[half:half + 64, a_state + pc * 64:a_state + (pc + 1) * 64]
                    stbf_hv = ab[half:half + 64, b_stbf + pc * 64:b_stbf + (pc + 1) * 64]
                    stsc_hv = av[half:half + 64, a_stsc + pc * 64:a_stsc + (pc + 1) * 64]
                    emit_wkv_head(ctx, locals())
            p.op("v", lambda e: e.tensor_copy(out=ab[:, b_stbf:b_stbf + 512],
                                              in_=av[:, a_state:a_state + 512]),
                 reads=[("state", pcx, hx) for pcx in range(8) for hx in range(2)],
                 writes=["stbf"])

        if DEBUG_SUB in ("p3", "w0", "w1", "w2", "w3", "w4", "w5"):
            continue
        emit_opost(ctx, locals())
        if DEBUG_SUB == "p4a":
            continue
        emit_l0tail(ctx, locals())


def emit_wkv_head(ctx, L):
    p, PS, PB = ctx["p"], ctx["PS"], ctx["PB"]
    ab, av = ctx["AB16"].t, ctx["AF32"].t
    sb_msk, sb_id64 = ctx["sb_msk"], ctx["sb_id64"]
    rv, kv_, avw, bvw = L["rv"], L["kv_"], L["avw"], L["bvw"]
    rd_pre, vch = L["rd_pre"], L["vch"]
    st_hv, stbf_hv, stsc_hv = L["st_hv"], L["stbf_hv"], L["stsc_hv"]
    half, pc = L["half"], L["pc"]
    hh = L["hh"]
    par = (pc * 2 + hh) % 2
    b_BTs = L["b_BTs"][par]
    b_Bc = L["b_Bc"][par]
    b_KTm = L["b_KTm"][par]
    b_Pbm = L["b_Pbm"][par]
    b_Pkm = L["b_Pkm"][par]
    b_Wc = L["b_Wc"][par]
    b_Btok = L["b_Btok"][par]
    b_Ktok = L["b_Ktok"][par]
    b_d1, a_gcol = L["b_d1"], L["a_gcol"]
    tl = L["tl"]
    m_us, m_ui, m_ls = sb_msk[:, 0:128], sb_msk[:, 128:256], sb_msk[:, 256:384]
    if DEBUG_SUB == "w0":
        return

    PSd = PS[0 + par]
    PSq = PS[2 + par]
    PSo = PS[4 + par]
    PBt = PB[0]
    pbs = par * 256
    kst = ("state", pc, hh)
    kssc = ("stsc", pc, hh)
    kmap = {"PSd": ("PS", par), "PSq": ("PS", 2 + par), "PSo": ("PS", 4 + par),
            "PBt": "PB0"}

    def K(nm):
        return kmap.get(nm, (nm, par))

    p.op("t", lambda e: e.matmul(PSd[:, 0:128], bvw, avw, start=True, stop=True),
         reads=rd_pre + [K("PSd")], writes=[K("PSd")])
    p.op("t", lambda e: e.matmul(PSd[:, 128:256], avw, bvw, start=True, stop=True),
         reads=rd_pre + [K("PSd")], writes=[K("PSd")])
    p.op("t", lambda e: e.matmul(PSd[:, 256:384], kv_, avw, start=True, stop=True),
         reads=rd_pre + [K("PSd")], writes=[K("PSd")])
    p.op("t", lambda e: e.matmul(PSd[:, 384:512], bvw, rv, start=True, stop=True),
         reads=rd_pre + [K("PSd")], writes=[K("PSd")])
    p.op("t", lambda e: e.matmul(PSq[:, 0:128], kv_, rv, start=True, stop=True),
         reads=rd_pre + [K("PSq")], writes=[K("PSq")])
    p.op("v", lambda e: e.tensor_mul(ab[:, b_BTs:b_BTs + 128], PSd[:, 0:128], m_us),
         reads=[K("PSd"), "msk_us"], writes=[K("BT0")])
    p.op("v", lambda e: e.tensor_mul(ab[:, b_Bc:b_Bc + 128], PSd[:, 128:256], m_ls),
         reads=[K("PSd"), "msk_ls"], writes=[K("B0")])
    p.op("v", lambda e: e.tensor_mul(ab[:, b_KTm:b_KTm + 128], PSd[:, 256:384], m_us),
         reads=[K("PSd"), "msk_us"], writes=[K("KTm")])
    p.op("v", lambda e: e.tensor_mul(ab[:, b_Pbm:b_Pbm + 128], PSd[:, 384:512], m_ui),
         reads=[K("PSd"), "msk_ui"], writes=[K("Pbm")])
    p.op("s", lambda e: e.activation(out=ab[:, b_Pkm:b_Pkm + 128], in_=PSq[:, 0:128], func=AF.Copy),
         reads=[K("PSq")], writes=[K("Pkm")])
    p.op("v", lambda e: e.tensor_mul(ab[:, b_Pkm:b_Pkm + 128], ab[:, b_Pkm:b_Pkm + 128], m_ui),
         reads=[K("Pkm"), "msk_ui"], writes=[K("Pkm")])

    if DEBUG_SUB == "w1":
        return
    for lv in range(6):
        btp = ab[:, b_BTs + lv * 128:b_BTs + (lv + 1) * 128]
        bp = ab[:, b_Bc + (lv % 2) * 128:b_Bc + (lv % 2) * 128 + 128]
        bn = ab[:, b_Bc + ((lv + 1) % 2) * 128:b_Bc + ((lv + 1) % 2) * 128 + 128]
        p.op("t", lambda e, bp=bp, btp=btp: e.matmul(PSq[:, 128:256], bp, btp, start=True, stop=True),
             reads=[K(f"BT{lv}"), K(f"B{lv}"), K("PSq")], writes=[K("PSq")])
        p.op("t", lambda e, btp=btp, bp=bp: e.matmul(PSq[:, 256:384], btp, bp, start=True, stop=True),
             reads=[K(f"BT{lv}"), K(f"B{lv}"), K("PSq")], writes=[K("PSq")])
        p.op("v", lambda e, lv=lv: e.tensor_copy(out=ab[:, b_BTs + (lv + 1) * 128:b_BTs + (lv + 2) * 128],
                                                 in_=PSq[:, 128:256]),
             reads=[K("PSq")], writes=[K(f"BT{lv + 1}")])
        p.op("v", lambda e, bn=bn: e.tensor_copy(out=bn, in_=PSq[:, 256:384]),
             reads=[K("PSq")], writes=[K(f"B{lv + 1}")])

    if DEBUG_SUB == "w2":
        return
    p.op("t", lambda e: e.matmul(PSq[:, 384:448], avw, stbf_hv, start=True, stop=False),
         reads=rd_pre + ["stbf", K("PSq")], writes=[K("PSq")])
    p.op("t", lambda e: e.matmul(PSq[:, 384:448], ab[:, b_KTm:b_KTm + 128], vch, start=False, stop=True),
         reads=[K("KTm"), "b_vt", K("PSq")], writes=[K("PSq")])
    p.op("v", lambda e: e.tensor_copy(out=ab[:, b_Wc:b_Wc + 64], in_=PSq[:, 384:448]),
         reads=[K("PSq")], writes=[K("W0")])
    for i in range(7):
        wcur = ab[:, b_Wc + (i % 2) * 64:b_Wc + (i % 2) * 64 + 64]
        wnext = ab[:, b_Wc + ((i + 1) % 2) * 64:b_Wc + ((i + 1) % 2) * 64 + 64]
        bti = ab[:, b_BTs + i * 128:b_BTs + (i + 1) * 128]
        p.op("t", lambda e, bti=bti, wcur=wcur: e.matmul(PSq[:, 448:512], bti, wcur, start=True, stop=True),
             reads=[K(f"BT{i}"), K(f"W{i}"), K("PSq")], writes=[K("PSq")])
        p.op("v", lambda e, wnext=wnext, wcur=wcur: e.tensor_add(wnext, PSq[:, 448:512], wcur),
             reads=[K("PSq"), K(f"W{i}")], writes=[K(f"W{i + 1}")])
    U = ab[:, b_Wc + 64:b_Wc + 128]

    if DEBUG_SUB == "w3":
        return
    pov = PSo[half:half + 64, 0:128]
    p.op("t", lambda e, pov=pov, half=half: e.matmul(pov, stbf_hv, rv, start=True, stop=False,
                                                     tile_position=(half, half)),
         reads=["stbf", K("PSo")] + rd_pre, writes=[K("PSo")])
    p.op("t", lambda e, pov=pov, half=half: e.matmul(pov, U, ab[:, b_Pbm:b_Pbm + 128],
                                                     start=False, stop=False, tile_position=(0, half)),
         reads=[K("W7"), K("Pbm"), K("PSo")], writes=[K("PSo")])
    p.op("t", lambda e, pov=pov, half=half: e.matmul(pov, vch, ab[:, b_Pkm:b_Pkm + 128],
                                                     start=False, stop=True, tile_position=(0, half)),
         reads=["b_vt", K("Pkm"), K("PSo")], writes=[K("PSo")])
    p.op("v", lambda e, pov=pov: e.tensor_copy(
        out=ab[half:half + 64, b_d1 + pc * 512 + tl: b_d1 + pc * 512 + tl + LCH], in_=pov),
        reads=[K("PSo")], writes=["b_d1"])

    if DEBUG_SUB == "w4":
        return
    p.op("t", lambda e: e.transpose(PBt[:, pbs:pbs + 64], bvw, sb_id64[half:half + 64, :]),
         reads=rd_pre + ["sb_id64", K("PBt")], writes=[K("PBt")])
    p.op("v", lambda e: e.tensor_copy(out=ab[:, b_Btok:b_Btok + 64], in_=PBt[:, pbs:pbs + 64]),
         reads=[K("PBt")], writes=[K("Btok")])
    p.op("t", lambda e: e.transpose(PBt[:, pbs + 64:pbs + 128], kv_, sb_id64[half:half + 64, :]),
         reads=rd_pre + ["sb_id64", K("PBt")], writes=[K("PBt")])
    p.op("v", lambda e: e.tensor_copy(out=ab[:, b_Ktok:b_Ktok + 64], in_=PBt[:, pbs + 64:pbs + 128]),
         reads=[K("PBt")], writes=[K("Ktok")])

    if DEBUG_SUB == "w5":
        return
    pstv = PSo[half:half + 64, 128:192]
    p.op("t", lambda e, pstv=pstv, half=half: e.matmul(pstv, ab[:, b_Btok:b_Btok + 64], U,
                                                       start=True, stop=False, tile_position=(0, half)),
         reads=[K("Btok"), K("W7"), K("PSo")], writes=[K("PSo")])
    p.op("t", lambda e, pstv=pstv, half=half: e.matmul(pstv, ab[:, b_Ktok:b_Ktok + 64], vch,
                                                       start=False, stop=True, tile_position=(0, half)),
         reads=[K("Ktok"), "b_vt", K("PSo")], writes=[K("PSo")])
    p.op("v", lambda e, pstv=pstv: e.tensor_add(stsc_hv, pstv, st_hv),
         reads=[K("PSo"), kst], writes=[kssc])
    p.op("v", lambda e: e.tensor_scalar_mul(st_hv, stsc_hv,
                                            av[half:half + 64, a_gcol + pc:a_gcol + pc + 1]),
         reads=[kssc, "a_gcol"], writes=[kst])


def emit_opost(ctx, L):
    p, PS = ctx["p"], ctx["PS"]
    ab, av = ctx["AB16"].t, ctx["AF32"].t
    sb_ind2, sb_ind2T = ctx["sb_ind2"], ctx["sb_ind2T"]
    par, po = ctx["sb_par"], ctx["par_off"]
    b_d1, b_var, b_rT, b_kT, b_vT, b_gg, b_kkn = (L[k] for k in
        ["b_d1", "b_var", "b_rT", "b_kT", "b_vT", "b_gg", "b_kkn"])
    a_scr, a_scr2, a_sig, a_gr, a_xm = (L[k] for k in ["a_scr", "a_scr2", "a_sig", "a_gr", "a_xm"])
    st2 = av[0:2, a_sig:a_sig + 4096]
    st2b = av[0:2, a_gr:a_gr + 4096]
    st3 = av[0:2, a_xm:a_xm + 4096]

    def blk8(base, c, n=512):
        return ab[:, base + c * 512: base + c * 512 + n]

    def parap(nm, c, n=1):
        o = po[nm]
        return par[:, o + c:o + c + n]

    for c in range(8):
        p.op("t", lambda e, c=c: e.matmul(PS[3][0:2, 0:512], sb_ind2[:], blk8(b_d1, c),
                                          start=True, stop=True),
             reads=["sb_ind2", "b_d1", ("PS", 3)], writes=[("PS", 3)])
        p.op("v", lambda e, c=c: e.tensor_copy(out=st2[:, c * 512:(c + 1) * 512],
                                               in_=PS[3][0:2, 0:512]),
             reads=[("PS", 3)], writes=["a_sig"])
        p.op("s", lambda e, c=c: e.activation(out=ab[:, b_var:b_var + 512], in_=blk8(b_d1, c),
                                              func=AF.Square),
             reads=["b_d1"], writes=["b_var"])
        p.op("t", lambda e, c=c: e.matmul(PS[3][0:2, 0:512], sb_ind2[:],
                                          ab[:, b_var:b_var + 512], start=True, stop=True),
             reads=["sb_ind2", "b_var", ("PS", 3)], writes=[("PS", 3)])
        p.op("v", lambda e, c=c: e.tensor_copy(out=st2b[:, c * 512:(c + 1) * 512],
                                               in_=PS[3][0:2, 0:512]),
             reads=[("PS", 3)], writes=["a_gr"])
    p.op("v", lambda e: e.tensor_scalar_mul(st2[:], st2[:], 1.0 / 64),
         reads=["a_sig"], writes=["a_sig"])
    p.op("s", lambda e: e.activation(out=st3[:], in_=st2[:], func=AF.Square),
         reads=["a_sig"], writes=["a_xm"])
    p.op("v", lambda e: e.tensor_scalar_mul(st2b[:], st2b[:], 1.0 / 64),
         reads=["a_gr"], writes=["a_gr"])
    p.op("v", lambda e: e.tensor_sub(st2b[:], st2b[:], st3[:]),
         reads=["a_gr", "a_xm"], writes=["a_gr"])
    p.op("v", lambda e: e.tensor_scalar_add(st2b[:], st2b[:], float(EPS_GN)),
         reads=["a_gr"], writes=["a_gr"])
    p.op("s", lambda e: e.activation(out=st2b[:], in_=st2b[:], func=AF.Ln),
         reads=["a_gr"], writes=["a_gr"])
    p.op("s", lambda e: e.activation(out=st2b[:], in_=st2b[:], func=AF.Exp, scale=-0.5),
         reads=["a_gr"], writes=["a_gr"])
    rko = po["r_k"]
    for c in range(8):
        p.op("v", lambda e, c=c: e.scalar_tensor_tensor(
            out=ab[:, b_var:b_var + 512], in0=blk8(b_rT, c), scalar=par[:, rko + c:rko + c + 1],
            in1=blk8(b_kT, c), op0=ALU.mult, op1=ALU.mult),
            reads=["b_rT", "b_kT", ("par", "r_k")], writes=["b_var"])
        p.op("t", lambda e, c=c: e.matmul(PS[3][0:2, 0:512], sb_ind2[:],
                                          ab[:, b_var:b_var + 512], start=True, stop=True),
             reads=["sb_ind2", "b_var", ("PS", 3)], writes=[("PS", 3)])
        p.op("v", lambda e, c=c: e.tensor_copy(out=st3[:, c * 512:(c + 1) * 512],
                                               in_=PS[3][0:2, 0:512]),
             reads=[("PS", 3)], writes=["a_xm"])
    for c in range(8):
        p.op("t", lambda e, c=c: e.matmul(PS[3][:, 0:512], ctx["sb_ind2Tf"][0:2, :],
                                          st2[:, c * 512:(c + 1) * 512], start=True, stop=True),
             reads=["a_sig", "sb_ind2Tf", ("PS", 3)], writes=[("PS", 3)])
        p.op("t", lambda e, c=c: e.matmul(PS[4][:, 0:512], ctx["sb_ind2Tf"][0:2, :],
                                          st2b[:, c * 512:(c + 1) * 512], start=True, stop=True),
             reads=["a_gr", "sb_ind2Tf", ("PS", 4)], writes=[("PS", 4)])
        p.op("v", lambda e, c=c: e.tensor_sub(av[:, a_scr:a_scr + 512], blk8(b_d1, c), PS[3][:, 0:512]),
             reads=["b_d1", ("PS", 3)], writes=["a_scr"])
        p.op("v", lambda e, c=c: e.tensor_mul(av[:, a_scr:a_scr + 512], av[:, a_scr:a_scr + 512],
                                              PS[4][:, 0:512]),
             reads=["a_scr", ("PS", 4)], writes=["a_scr"])
        p.op("v", lambda e, c=c: e.tensor_scalar(av[:, a_scr:a_scr + 512], av[:, a_scr:a_scr + 512],
                                                 parap("gn_w", c), parap("gn_b", c),
                                                 op0=ALU.mult, op1=ALU.add),
             reads=["a_scr", ("par", "gn_w"), ("par", "gn_b")], writes=["a_scr"])
        p.op("t", lambda e, c=c: e.matmul(PS[3][:, 0:512], ctx["sb_ind2Tf"][0:2, :],
                                          st3[:, c * 512:(c + 1) * 512], start=True, stop=True),
             reads=["a_xm", "sb_ind2Tf", ("PS", 3)], writes=[("PS", 3)])
        p.op("v", lambda e, c=c: e.tensor_mul(av[:, a_scr2:a_scr2 + 512], PS[3][:, 0:512],
                                              blk8(b_vT, c)),
             reads=[("PS", 3), "b_vT"], writes=["a_scr2"])
        p.op("v", lambda e, c=c: e.tensor_add(av[:, a_scr:a_scr + 512], av[:, a_scr:a_scr + 512],
                                              av[:, a_scr2:a_scr2 + 512]),
             reads=["a_scr", "a_scr2"], writes=["a_scr"])
        p.op("v", lambda e, c=c: e.tensor_mul(blk8(b_kkn, c), av[:, a_scr:a_scr + 512],
                                              blk8(b_gg, c)),
             reads=["a_scr", "b_gg"], writes=["b_kkn"])


def emit_l0tail(ctx, L):
    nc, p, di, ds, PS, PB = ctx["nc"], ctx["p"], ctx["di"], ctx["ds"], ctx["PS"], ctx["PB"]
    ab, av = ctx["AB16"].t, ctx["AF32"].t
    par, po = ctx["sb_par"], ctx["par_off"]
    sb_id = ctx["sb_id"]
    lnb = ctx["a_lnb"]
    b = L["b"]
    b_kkn, b_var, b_xfT, b_xa, b_d1 = L["b_kkn"], L["b_var"], L["b_xfT"], L["b_xa"], L["b_d1"]
    a_xm, a_scr, a_scr2, a_st = L["a_xm"], L["a_scr"], L["a_scr2"], L["a_st"]
    load_w = L["load_w"]

    def blk8(base, c, n=512):
        return ab[:, base + c * 512: base + c * 512 + n]

    def parap(nm, c, n=1):
        o = po[nm]
        return par[:, o + c:o + c + n]

    def xf_view(c, c0, n):
        return ab[:, b_xfT + c * 514 + c0: b_xfT + c * 514 + c0 + n]

    for q in range(4):
        wo, wk = load_w(di["W_o"][:, q * 256:(q + 1) * 256], C, 256, ("Wo", b, q))
        for ts in range(4):
            for c in range(8):
                p.op("t", lambda e, ts=ts, c=c, wo=wo, q=q: e.matmul(
                    PS[q % 2][:, 0:256],
                    blk8(b_kkn, c)[:, ts * 128:(ts + 1) * 128],
                    ab[:, wo + c * 256: wo + (c + 1) * 256],
                    start=(c == 0), stop=(c == 7)),
                    reads=[wk, "b_kkn", ("PS", q % 2)], writes=[("PS", q % 2)])
            p.op("v", lambda e, ts=ts, q=q: e.tensor_copy(
                out=av[:, a_xm + ts * 1024 + q * 256: a_xm + ts * 1024 + (q + 1) * 256],
                in_=PS[q % 2][:, 0:256]),
                reads=[("PS", q % 2)], writes=["a_xm"])
    for ts in range(4):
        gi = b * 4 + ts
        xm = av[:, a_xm + ts * 1024: a_xm + (ts + 1) * 1024]
        p.op("q", lambda e, gi=gi: e.dma_start(out=av[:, a_scr:a_scr + 1024],
                                               in_=ds["x0f"][gi * 128:(gi + 1) * 128, :]),
             reads=[("x0d", gi), "a_scr"], writes=["a_scr"], ch="lx0")
        p.op("v", lambda e, xm=xm: e.tensor_add(xm, xm, av[:, a_scr:a_scr + 1024]),
             reads=["a_xm", "a_scr"], writes=["a_xm"])
        ln_tok(ctx, "a_xm", xm, av[:, a_scr:a_scr + 1024], av[:, a_scr2:a_scr2 + 1024],
               "a_scr2", av[:, a_st:a_st + 16],
               wb=(ab[:, lnb + 4096:lnb + 5120], ab[:, lnb + 5120:lnb + 6144],
                   ("lnb", "ln_ffn_w"), ("lnb", "ln_ffn_b")), scratch_key="a_scr")
        p.op("v", lambda e: e.tensor_copy(out=ab[:, b_var:b_var + 1024],
                                          in_=av[:, a_scr2:a_scr2 + 1024]),
             reads=["a_scr2"], writes=["b_var"])
        for c in range(8):
            p.op("t", lambda e, c=c: e.transpose(PB[0][:, 0:128],
                                                 ab[:, b_var + c * 128:b_var + (c + 1) * 128],
                                                 sb_id[:]),
                 reads=["b_var", "sb_id", "PB0"], writes=["PB0"])
            p.op("v", lambda e, c=c, ts=ts: e.tensor_copy(
                out=xf_view(c, 1 + ts * 128, 128), in_=PB[0][:, 0:128]),
                reads=["PB0"], writes=["b_xfT"])
    for c in range(8):
        p.op("v", lambda e, c=c: e.tensor_sub(av[:, a_scr:a_scr + 512],
                                              xf_view(c, 0, 512), xf_view(c, 1, 512)),
             reads=["b_xfT"], writes=["a_scr"])
        p.op("v", lambda e, c=c: e.scalar_tensor_tensor(
            out=blk8(b_d1, c), in0=av[:, a_scr:a_scr + 512], scalar=parap("ffn_x_k", c),
            in1=xf_view(c, 1, 512), op0=ALU.mult, op1=ALU.add),
            reads=["a_scr", "b_xfT", ("par", "ffn_x_k")], writes=["b_d1"])
    p.op("v", lambda e: e.tensor_copy(
        out=ab[:, b_xfT:b_xfT + 8 * 514].rearrange("p (c w) -> p c w", w=514)[:, :, 0:1],
        in_=ab[:, b_xfT:b_xfT + 8 * 514].rearrange("p (c w) -> p c w", w=514)[:, :, 512:513]),
        reads=["b_xfT"], writes=["b_xfT"])

    for ts in range(4):
        gi = b * 4 + ts
        for q in range(16):
            w1o, w1k = load_w(di["W_ffn1"][:, q * 256:(q + 1) * 256], C, 256, ("f1", b, ts, q))
            for fc in range(2):
                fg = q * 2 + fc
                for c in range(8):
                    p.op("t", lambda e, fc=fc, c=c, w1o=w1o, ts=ts: e.matmul(
                        PS[2][:, 0:128],
                        ab[:, w1o + c * 256 + fc * 128: w1o + c * 256 + (fc + 1) * 128],
                        blk8(b_d1, c)[:, ts * 128:(ts + 1) * 128],
                        start=(c == 0), stop=(c == 7)),
                        reads=[w1k, "b_d1", ("PS", 2)], writes=[("PS", 2)])
                p.op("s", lambda e: e.activation(out=av[:, a_scr:a_scr + 128], in_=PS[2][:, 0:128],
                                                 func=AF.Relu),
                     reads=[("PS", 2)], writes=["a_scr"])
                p.op("s", lambda e, fg=fg: e.activation(
                    out=ab[:, b_xa + fg * 128: b_xa + (fg + 1) * 128],
                    in_=av[:, a_scr:a_scr + 128], func=AF.Square),
                    reads=["a_scr"], writes=["b_xa"])
        for q in range(8):
            w2o, w2k = load_w(di["W_ffn2"][q * 512:(q + 1) * 512, :], 512, C // 4, ("f2x", b, ts, q)) \
                if False else (None, None)
        first = [True, True]
        for fs in range(8):
            for half in range(2):
                w2o, w2k = load_w(di["W_ffn2"][fs * 512:(fs + 1) * 512, half * 512:(half + 1) * 512],
                                  512, 512, ("f2", b, ts, fs, half))
                for fc in range(4):
                    fg = fs * 4 + fc
                    p.op("t", lambda e, fg=fg, fc=fc, half=half, w2o=w2o, fs=fs: e.matmul(
                        PS[4 + half][:, 0:512],
                        ab[:, b_xa + fg * 128: b_xa + (fg + 1) * 128],
                        ab[:, w2o + fc * 512: w2o + (fc + 1) * 512],
                        start=(fg == 0), stop=(fg == 31)),
                        reads=[w2k, "b_xa", ("PS", 4 + half)], writes=[("PS", 4 + half)])
        xm = av[:, a_xm + ts * 1024: a_xm + (ts + 1) * 1024]
        for half in range(2):
            p.op("v", lambda e, half=half, xm=xm: e.tensor_add(
                av[:, a_scr2 + half * 512:a_scr2 + half * 512 + 512], PS[4 + half][:, 0:512],
                xm[:, half * 512:half * 512 + 512]),
                reads=[("PS", 4 + half), "a_xm"], writes=["a_scr2"])
        p.op("q", lambda e, gi=gi: e.dma_start(out=ds["sattn"][gi * 128:(gi + 1) * 128, :],
                                               in_=av[:, a_scr2:a_scr2 + 1024]),
             reads=["a_scr2"], writes=[("sad", gi)], ch="stsa")


def emit_l0ffn(ctx):
    nc, p, di, ds = ctx["nc"], ctx["p"], ctx["di"], ctx["ds"]
    AF32, AB16, PS, PB = ctx["AF32"], ctx["AB16"], ctx["PS"], ctx["PB"]
    av, ab = AF32.t, AB16.t
    sb_id, sb_idf = ctx["sb_id"], ctx["sb_idf"]
    logitsT = ctx["logitsT"]

    p.barrier()
    AF32.reset("keep")
    AB16.reset("keep")

    a_msk = AF32.alloc(2048)
    a_aux = AF32.alloc(2048)
    a_pos = AF32.alloc(2048)
    a_bis = AF32.alloc(32)
    a_ptt = AF32.alloc(16 * 16)
    a_tv = AF32.alloc(16)
    a_pay = AF32.alloc(2)
    a_sidx = AF32.alloc(8)
    a_scr = AF32.alloc(1024)
    a_isc = AF32.alloc(8)
    a_pay8 = AF32.alloc(16)

    a_scr2 = AF32.alloc(1024)
    i_idx = nc.alloc_sbuf_tensor("i_idx", [128, 8], I32)
    b_xin = AB16.alloc(1024)
    b_xinT = AB16.alloc(8 * 512)
    b_wA = AB16.alloc(2048)
    b_wA2 = AB16.alloc(2048)
    b_wB = AB16.alloc(1024)
    b_wB2 = AB16.alloc(1024)
    b_xin2 = AB16.alloc(1024)
    b_hf = AB16.alloc(32 * 512)
    b_h1 = AB16.alloc(512)

    lo = av[0:8, a_bis + 0:a_bis + 1]
    hi = av[0:8, a_bis + 1:a_bis + 2]
    mid = av[0:8, a_bis + 2:a_bis + 3]
    cnt = av[0:8, a_bis + 3:a_bis + 4]
    sel1 = av[0:8, a_bis + 4:a_bis + 5]
    sel1i = av[0:8, a_bis + 5:a_bis + 6]
    dd = av[0:8, a_bis + 6:a_bis + 7]
    vk = av[0:8, a_bis + 7:a_bis + 8]
    need = av[0:8, a_bis + 9:a_bis + 10]
    aff = logitsT

    for i in range(NT):
        p.op("t", lambda e, i=i: e.transpose(PS[4][:, 0:8], logitsT[0:8, i * 128:(i + 1) * 128],
                                             sb_idf[0:8, 0:8]),
             reads=[("lgT", i), "sb_idf", ("PS", 4)], writes=[("PS", 4)])
        p.op("v", lambda e: e.tensor_copy(out=av[:, a_scr:a_scr + 8], in_=PS[4][:, 0:8]),
             reads=[("PS", 4)], writes=["a_scr"])
        p.op("v", lambda e: e.tensor_reduce(out=av[:, a_scr + 8:a_scr + 9],
                                            in_=av[:, a_scr:a_scr + 8], axis=AX.X,
                                            op=ALU.max, negate=True),
             reads=["a_scr"], writes=["a_scr"])
        p.op("s", lambda e: e.activation(out=av[:, a_scr:a_scr + 8], in_=av[:, a_scr:a_scr + 8],
                                         func=AF.Exp, bias=av[:, a_scr + 8:a_scr + 9],
                                         accum_out=av[:, a_scr + 9:a_scr + 10]),
             reads=["a_scr"], writes=["a_scr"])
        p.op("s", lambda e: e.activation(out=av[:, a_scr + 10:a_scr + 11],
                                         in_=av[:, a_scr + 9:a_scr + 10], func=AF.Ln),
             reads=["a_scr"], writes=["a_scr"])
        p.op("s", lambda e: e.activation(out=av[:, a_scr + 8:a_scr + 9],
                                         in_=av[:, a_scr + 10:a_scr + 11], func=AF.Exp, scale=-1.0),
             reads=["a_scr"], writes=["a_scr"])
        p.op("v", lambda e: e.tensor_scalar_mul(av[:, a_scr:a_scr + 8],
                                                av[:, a_scr:a_scr + 8],
                                                av[:, a_scr + 8:a_scr + 9]),
             reads=["a_scr"], writes=["a_scr"])
        p.op("t", lambda e, i=i: e.transpose(PS[4][0:8, 0:128], av[:, a_scr:a_scr + 8], sb_idf[:]),
             reads=["a_scr", "sb_idf", ("PS", 4)], writes=[("PS", 4)])
        p.op("v", lambda e, i=i: e.tensor_copy(out=aff[0:8, i * 128:(i + 1) * 128],
                                               in_=PS[4][0:8, 0:128]),
             reads=[("PS", 4)], writes=["aff"])

    p.op("v", lambda e: e.memset(lo, 0.0), writes=["bis"])
    p.op("v", lambda e: e.memset(hi, 1.0), writes=["bis"])
    for it in range(40):
        p.op("v", lambda e: e.tensor_add(mid, lo, hi), reads=["bis"], writes=["bis"])
        p.op("v", lambda e: e.tensor_scalar_mul(mid, mid, 0.5), reads=["bis"], writes=["bis"])
        p.op("v", lambda e: e.tensor_single_scalar(out=av[0:8, a_msk:a_msk + 2048],
                                                   in_=aff[0:8, :], scalar=mid, op=ALU.is_gt),
             reads=["aff", "bis"], writes=["a_msk"])
        p.op("v", lambda e: e.tensor_reduce(out=cnt, in_=av[0:8, a_msk:a_msk + 2048],
                                            axis=AX.X, op=ALU.add),
             reads=["a_msk"], writes=["bis"])
        p.op("v", lambda e: e.tensor_single_scalar(out=sel1, in_=cnt, scalar=512.0, op=ALU.is_ge),
             reads=["bis"], writes=["bis"])
        p.op("v", lambda e: e.tensor_scalar(sel1i, sel1, -1.0, 1.0, op0=ALU.mult, op1=ALU.add),
             reads=["bis"], writes=["bis"])
        p.op("v", lambda e: e.tensor_sub(dd, mid, lo), reads=["bis"], writes=["bis"])
        p.op("v", lambda e: e.scalar_tensor_tensor(out=lo, in0=dd, scalar=sel1, in1=lo,
                                                   op0=ALU.mult, op1=ALU.add),
             reads=["bis"], writes=["bis"])
        p.op("v", lambda e: e.tensor_sub(dd, mid, hi), reads=["bis"], writes=["bis"])
        p.op("v", lambda e: e.scalar_tensor_tensor(out=hi, in0=dd, scalar=sel1i, in1=hi,
                                                   op0=ALU.mult, op1=ALU.add),
             reads=["bis"], writes=["bis"])
    p.op("v", lambda e: e.tensor_single_scalar(out=av[0:8, a_msk:a_msk + 2048], in_=aff[0:8, :],
                                               scalar=hi, op=ALU.is_le),
         reads=["aff", "bis"], writes=["a_msk"])
    p.op("v", lambda e: e.tensor_mul(av[0:8, a_aux:a_aux + 2048], aff[0:8, :],
                                     av[0:8, a_msk:a_msk + 2048]),
         reads=["aff", "a_msk"], writes=["a_aux"])
    p.op("v", lambda e: e.tensor_reduce(out=vk, in_=av[0:8, a_aux:a_aux + 2048],
                                        axis=AX.X, op=ALU.max),
         reads=["a_aux"], writes=["bis"])
    p.op("v", lambda e: e.tensor_single_scalar(out=av[0:8, a_msk:a_msk + 2048], in_=aff[0:8, :],
                                               scalar=vk, op=ALU.is_gt),
         reads=["aff", "bis"], writes=["a_msk"])
    p.op("v", lambda e: e.tensor_reduce(out=cnt, in_=av[0:8, a_msk:a_msk + 2048],
                                        axis=AX.X, op=ALU.add),
         reads=["a_msk"], writes=["bis"])
    p.op("v", lambda e: e.tensor_scalar(need, cnt, -1.0, 512.0, op0=ALU.mult, op1=ALU.add),
         reads=["bis"], writes=["bis"])
    p.op("v", lambda e: e.tensor_single_scalar(out=av[0:8, a_aux:a_aux + 2048], in_=aff[0:8, :],
                                               scalar=vk, op=ALU.is_equal),
         reads=["aff", "bis"], writes=["a_aux"])
    p.op("v", lambda e: e.memset(av[0:8, a_pos:a_pos + 1], 0.0), writes=["a_pos"])
    p.op("v", lambda e: e.tensor_tensor_scan(out=av[0:8, a_pos:a_pos + 2048],
                                             data0=av[0:8, a_aux:a_aux + 2048],
                                             data1=av[0:8, a_pos:a_pos + 1].to_broadcast([8, 2048]),
                                             initial=0.0, op0=ALU.add, op1=ALU.add),
         reads=["a_aux", "a_pos"], writes=["a_pos"])
    p.op("v", lambda e: e.tensor_single_scalar(out=av[0:8, a_pos:a_pos + 2048],
                                               in_=av[0:8, a_pos:a_pos + 2048],
                                               scalar=need, op=ALU.is_le),
         reads=["a_pos", "bis"], writes=["a_pos"])
    p.op("v", lambda e: e.tensor_mul(av[0:8, a_aux:a_aux + 2048], av[0:8, a_aux:a_aux + 2048],
                                     av[0:8, a_pos:a_pos + 2048]),
         reads=["a_aux", "a_pos"], writes=["a_aux"])
    p.op("v", lambda e: e.tensor_add(av[0:8, a_msk:a_msk + 2048], av[0:8, a_msk:a_msk + 2048],
                                     av[0:8, a_aux:a_aux + 2048]),
         reads=["a_msk", "a_aux"], writes=["a_msk"])
    p.op("v", lambda e: e.tensor_mul(av[0:8, a_aux:a_aux + 2048], aff[0:8, :],
                                     av[0:8, a_msk:a_msk + 2048]),
         reads=["aff", "a_msk"], writes=["a_aux"])
    p.op("v", lambda e: e.memset(av[0:8, a_pos:a_pos + 1], 0.0), writes=["a_pos"])
    p.op("v", lambda e: e.tensor_tensor_scan(out=av[0:8, a_pos:a_pos + 2048],
                                             data0=av[0:8, a_msk:a_msk + 2048],
                                             data1=av[0:8, a_pos:a_pos + 1].to_broadcast([8, 2048]),
                                             initial=0.0, op0=ALU.add, op1=ALU.add),
         reads=["a_msk", "a_pos"], writes=["a_pos"])
    p.op("v", lambda e: e.tensor_scalar_add(av[0:8, a_pos:a_pos + 2048],
                                            av[0:8, a_pos:a_pos + 2048], -1.0),
         reads=["a_pos"], writes=["a_pos"])
    p.op("v", lambda e: e.tensor_mul(av[0:8, a_pos:a_pos + 2048], av[0:8, a_pos:a_pos + 2048],
                                     av[0:8, a_msk:a_msk + 2048]),
         reads=["a_pos", "a_msk"], writes=["a_pos"])
    p.op("v", lambda e: e.scalar_tensor_tensor(out=av[0:8, a_scr:a_scr + 1024] if False else
                                               av[0:8, a_msk:a_msk + 2048],
                                               in0=av[0:8, a_msk:a_msk + 2048], scalar=-4096.0,
                                               in1=av[0:8, a_pos:a_pos + 2048],
                                               op0=ALU.mult, op1=ALU.bypass),
         reads=["a_msk"], writes=["a_msk"]) if False else None
    p.op("v", lambda e: e.tensor_scalar(av[0:8, a_aux + 0:a_aux + 0] if False else
                                        av[0:8, a_msk:a_msk + 2048],
                                        av[0:8, a_msk:a_msk + 2048], -4096.0, 4096.0,
                                        op0=ALU.mult, op1=ALU.add),
         reads=["a_msk"], writes=["a_msk"])
    p.op("v", lambda e: e.tensor_add(av[0:8, a_pos:a_pos + 2048], av[0:8, a_pos:a_pos + 2048],
                                     av[0:8, a_msk:a_msk + 2048]),
         reads=["a_pos", "a_msk"], writes=["a_pos"])

    p.op("g", lambda e: e.iota(av[:, a_tv:a_tv + 16], [[128, 16]], base=0,
                               channel_multiplier=1, allow_small_or_imprecise_dtypes=True),
         writes=["a_tv"])

    i_sc = av[:, a_isc:a_isc + 8].bitcast(I32)
    for i in range(NT):
        ip = i % 2
        pto = a_ptt + ip * 16
        p.op("t", lambda e, i=i, ip=ip: e.transpose(PS[4 + ip][:, 0:8],
                                                    av[0:8, a_pos + i * 128:a_pos + (i + 1) * 128],
                                                    sb_idf[0:8, 0:8]),
             reads=["a_pos", "sb_idf", ("PS", 4 + ip)], writes=[("PS", 4 + ip)])
        p.op("v", lambda e, i=i, ip=ip, pto=pto: e.tensor_copy(out=av[:, pto:pto + 8],
                                                               in_=PS[4 + ip][:, 0:8]),
             reads=[("PS", 4 + ip), ("a_ptt", ip)], writes=[("a_ptt", ip)])
        p.op("t", lambda e, i=i, ip=ip: e.transpose(PS[4 + ip][:, 8:16],
                                                    av[0:8, a_aux + i * 128:a_aux + (i + 1) * 128],
                                                    sb_idf[0:8, 0:8]),
             reads=["a_aux", "sb_idf", ("PS", 4 + ip)], writes=[("PS", 4 + ip)])
        p.op("v", lambda e, i=i, ip=ip, pto=pto: e.tensor_copy(out=av[:, pto + 8:pto + 16],
                                                               in_=PS[4 + ip][:, 8:16]),
             reads=[("PS", 4 + ip), ("a_ptt", ip)], writes=[("a_ptt", ip)])
        for ee in range(8):
            p.op("v", lambda e, ee=ee, ip=ip, pto=pto: e.tensor_copy(
                out=i_sc[:, ee:ee + 1], in_=av[:, pto + ee:pto + ee + 1]),
                reads=[("a_ptt", ip), ("isc", ee)], writes=[("isc", ee)])
            p.op("v", lambda e, i=i, ee=ee: e.tensor_copy(
                out=av[:, a_pay8 + 2 * ee:a_pay8 + 2 * ee + 1],
                in_=av[:, a_tv + i:a_tv + i + 1]),
                reads=["a_tv", ("pay", ee)], writes=[("pay", ee)])
            p.op("v", lambda e, ee=ee, ip=ip, pto=pto: e.tensor_copy(
                out=av[:, a_pay8 + 2 * ee + 1:a_pay8 + 2 * ee + 2],
                in_=av[:, pto + 8 + ee:pto + 9 + ee]),
                reads=[("a_ptt", ip), ("pay", ee)], writes=[("pay", ee)])
            p.op("g", lambda e, ee=ee: e.indirect_dma_start(
                out=ds["idxw%d" % ee], out_offset=IndirectOffsetOnAxis(i_sc[:, ee:ee + 1], 0),
                in_=av[:, a_pay8 + 2 * ee:a_pay8 + 2 * ee + 2], in_offset=None),
                reads=[("isc", ee), ("pay", ee)], writes=[("idxd", ee)], ch="sc" + str(ee % 4))

    p.op("v", lambda e: e.memset(av[:, a_scr:a_scr + 1024], 0.0),
         writes=[("a_scr", 0), ("a_scr", 1)])
    for i in range(NT):
        p.op("q", lambda e, i=i: e.dma_start(out=ds["g0a"][i * 128:(i + 1) * 128, :],
                                             in_=av[:, a_scr:a_scr + 512]),
             reads=[("a_scr", 0)], writes=[("g0z", i)], ch="g0z")
        p.op("q", lambda e, i=i: e.dma_start(out=ds["g0b"][i * 128:(i + 1) * 128, :],
                                             in_=av[:, a_scr:a_scr + 512]),
             reads=[("a_scr", 0)], writes=[("g0z", i)], ch="g0z")

    for ee in range(8):
        p.op("q", lambda e, ee=ee: e.dma_start(
            out=av[:, a_sidx:a_sidx + 8].rearrange("p (g x) -> p g x", x=2),
            in_=ds["idxw%d" % ee][0:512, :].rearrange("(g p) x -> p g x", p=128)),
            reads=[("idxd", ee), "a_sidx"], writes=["a_sidx"], ch="lsidx")
        for g in range(4):
            p.op("v", lambda e, g=g: e.tensor_copy(out=i_idx[:, g:g + 1],
                                                   in_=av[:, a_sidx + 2 * g:a_sidx + 2 * g + 1]),
                 reads=["a_sidx"], writes=["i_idx"])
        for g in range(4):
            gp = g % 2
            xo = b_xin if gp == 0 else b_xin2
            p.op("g", lambda e, g=g, xo=xo, gp=gp: e.indirect_dma_start(
                out=ab[:, xo:xo + 1024], out_offset=None,
                in_=ds["xn_bf"], in_offset=IndirectOffsetOnAxis(i_idx[:, g:g + 1], 0)),
                reads=["i_idx", ("b_xin", gp)] + [("xnbd", i) for i in range(NT)],
                writes=[("b_xin", gp)], ch="gxin" + str(gp))
            for c in range(8):
                p.op("t", lambda e, c=c, xo=xo, gp=gp: e.transpose(
                    PB[gp][:, 0:128], ab[:, xo + c * 128:xo + (c + 1) * 128], sb_id[:]),
                    reads=[("b_xin", gp), "sb_id", "PB%d" % gp], writes=["PB%d" % gp])
                p.op("v", lambda e, c=c, g=g, gp=gp: e.tensor_copy(
                    out=ab[:, b_xinT + c * 512 + g * 128: b_xinT + c * 512 + (g + 1) * 128],
                    in_=PB[gp][:, 0:128]),
                    reads=["PB%d" % gp], writes=["b_xinT"])
        for fs in range(32):
            fp = fs % 2
            wo = b_wA if fp == 0 else b_wA2
            so = a_scr + fp * 512
            p.op("g", lambda e, ee=ee, fs=fs, wo=wo, fp=fp: e.dma_start(
                out=ab[:, wo:wo + 1024].rearrange("p (c j) -> p c j", c=8),
                in_=di["w_gT"][ee, :, fs * 128:(fs + 1) * 128].rearrange("(c p) j -> p c j", p=128)),
                reads=[("b_wA", fp)], writes=[("b_wA", fp)], ch="fWA" + str(fp))
            p.op("g", lambda e, ee=ee, fs=fs, wo=wo, fp=fp: e.dma_start(
                out=ab[:, wo + 1024:wo + 2048].rearrange("p (c j) -> p c j", c=8),
                in_=di["w_uT"][ee, :, fs * 128:(fs + 1) * 128].rearrange("(c p) j -> p c j", p=128)),
                reads=[("b_wA", fp)], writes=[("b_wA", fp)], ch="fWA" + str(fp))
            for c in range(8):
                p.op("t", lambda e, c=c, wo=wo, fp=fp: e.matmul(
                    PS[fp][:, 0:512],
                    ab[:, wo + c * 128:wo + (c + 1) * 128],
                    ab[:, b_xinT + c * 512:b_xinT + (c + 1) * 512],
                    start=(c == 0), stop=(c == 7)),
                    reads=[("b_wA", fp), "b_xinT", ("PS", fp)], writes=[("PS", fp)])
            for c in range(8):
                p.op("t", lambda e, c=c, wo=wo, fp=fp: e.matmul(
                    PS[2 + fp][:, 0:512],
                    ab[:, wo + 1024 + c * 128:wo + 1024 + (c + 1) * 128],
                    ab[:, b_xinT + c * 512:b_xinT + (c + 1) * 512],
                    start=(c == 0), stop=(c == 7)),
                    reads=[("b_wA", fp), "b_xinT", ("PS", 2 + fp)], writes=[("PS", 2 + fp)])
            p.op("s", lambda e, fp=fp, so=so: e.activation(out=av[:, so:so + 512],
                                                           in_=PS[fp][:, 0:512], func=AF.Silu),
                 reads=[("PS", fp), ("a_scr", fp)], writes=[("a_scr", fp)])
            p.op("v", lambda e, fs=fs, fp=fp, so=so: e.tensor_mul(
                ab[:, b_hf + fs * 512: b_hf + (fs + 1) * 512],
                av[:, so:so + 512], PS[2 + fp][:, 0:512]),
                reads=[("a_scr", fp), ("PS", 2 + fp)], writes=["b_hf"])
        for half in range(2):
            for fs in range(32):
                fp = fs % 2
                wo = b_wB if fp == 0 else b_wB2
                p.op("g", lambda e, ee=ee, fs=fs, half=half, wo=wo, fp=fp: e.dma_start(
                    out=ab[:, wo:wo + 512],
                    in_=di["w_dn"][ee, fs * 128:(fs + 1) * 128, half * 512:(half + 1) * 512]),
                    reads=[("b_wB", fp)], writes=[("b_wB", fp)], ch="fWB" + str(fp))
                for g in range(4):
                    p.op("t", lambda e, fs=fs, g=g, wo=wo, fp=fp: e.matmul(
                        PS[g][:, 0:512],
                        ab[:, b_hf + fs * 512 + g * 128: b_hf + fs * 512 + (g + 1) * 128],
                        ab[:, wo:wo + 512],
                        start=(fs == 0), stop=(fs == 31)),
                        reads=[("b_wB", fp), "b_hf", ("PS", g)], writes=[("PS", g)])
            for g in range(4):
                gp = g % 2
                so = a_scr + gp * 512
                p.op("v", lambda e, g=g, half=half, so=so, gp=gp: e.tensor_scalar_mul(
                    av[:, so:so + 512],
                    PS[g][:, 0:512],
                    av[:, a_sidx + 2 * g + 1:a_sidx + 2 * g + 2]),
                    reads=[("PS", g), "a_sidx", ("a_scr", gp)], writes=[("a_scr", gp)])
                p.op("g", lambda e, g=g, half=half, so=so, gp=gp: e.indirect_dma_start(
                    out=ds["g0a" if half == 0 else "g0b"],
                    out_offset=IndirectOffsetOnAxis(i_idx[:, g:g + 1], 0),
                    in_=av[:, so:so + 512], in_offset=None, compute_op=ALU.add),
                    reads=["i_idx", ("a_scr", gp), ("g0d",)] + [("g0z", i) for i in range(NT)],
                    writes=[("g0d",)], ch="g0sc")

def emit_combine0(ctx):
    nc, p, di, ds = ctx["nc"], ctx["p"], ctx["di"], ctx["ds"]
    AF32, AB16 = ctx["AF32"], ctx["AB16"]
    av = AF32.t
    p.barrier()
    AF32.reset("keep")
    AB16.reset("keep")
    a_x = AF32.alloc(2048)
    a_s = AF32.alloc(2048)
    a_g = AF32.alloc(2048)
    for i in range(NT):
        pr = i % 2
        xo = a_x + pr * 1024
        so = a_s + pr * 1024
        go = a_g + pr * 1024
        p.op("q", lambda e, i=i, xo=xo: e.dma_start(out=av[:, xo:xo + 1024],
                                                    in_=di["X"][i * 128:(i + 1) * 128, :]),
             reads=[("c0x", pr)], writes=[("c0x", pr)], ch="c0x" + str(pr))
        p.op("q", lambda e, i=i, so=so: e.dma_start(out=av[:, so:so + 1024],
                                                    in_=ds["sattn"][i * 128:(i + 1) * 128, :]),
             reads=[("c0s", pr), ("sad", i)], writes=[("c0s", pr)], ch="c0s" + str(pr))
        p.op("q", lambda e, i=i, go=go: e.dma_start(out=av[:, go:go + 512],
                                                    in_=ds["g0a"][i * 128:(i + 1) * 128, :]),
             reads=[("c0g", pr), ("g0d",)], writes=[("c0g", pr)], ch="c0g" + str(pr))
        p.op("q", lambda e, i=i, go=go: e.dma_start(out=av[:, go + 512:go + 1024],
                                                    in_=ds["g0b"][i * 128:(i + 1) * 128, :]),
             reads=[("c0g", pr), ("g0d",)], writes=[("c0g", pr)], ch="c0g" + str(pr))
        p.op("s", lambda e, go=go: e.activation(out=av[:, go:go + 1024], in_=av[:, go:go + 1024],
                                                func=AF.Silu),
             reads=[("c0g", pr)], writes=[("c0g", pr)])
        p.op("v", lambda e, so=so, go=go: e.tensor_mul(av[:, so:so + 1024], av[:, so:so + 1024],
                                                       av[:, go:go + 1024]),
             reads=[("c0s", pr), ("c0g", pr)], writes=[("c0s", pr)])
        p.op("v", lambda e, xo=xo, so=so: e.tensor_add(av[:, xo:xo + 1024], av[:, xo:xo + 1024],
                                                       av[:, so:so + 1024]),
             reads=[("c0x", pr), ("c0s", pr)], writes=[("c0x", pr)])
        p.op("q", lambda e, i=i, xo=xo: e.dma_start(out=ds["x1"][i * 128:(i + 1) * 128, :],
                                                    in_=av[:, xo:xo + 1024]),
             reads=[("c0x", pr)], writes=[("x1d", i)], ch="stx1" + str(pr))


def emit_layer1(ctx):
    nc, p, di, ds, OUT = ctx["nc"], ctx["p"], ctx["di"], ctx["ds"], ctx["OUT"]
    AF32, AB16, PS, PB = ctx["AF32"], ctx["AB16"], ctx["PS"], ctx["PB"]
    av, ab = AF32.t, AB16.t
    sb_id, sb_idf, sb_onesb, sb_invf = ctx["sb_id"], ctx["sb_idf"], ctx["sb_onesb"], ctx["sb_invf"]

    p.barrier()
    AF32.reset("keep")
    AB16.reset("keep")

    meanrow = nc.dram_tensor("scr_meanrow", [1, C], F32, kind="Internal").ap()
    meanT_d = nc.dram_tensor("scr_meanT", [128, 8], F32, kind="Internal").ap()
    rwrow_d = nc.dram_tensor("scr_rwrow", [1, 8], F32, kind="Internal").ap()
    cw_d = nc.dram_tensor("scr_cw", [1, 8], F32, kind="Internal").ap()

    xn2T_d = nc.dram_tensor("scr_xn2T", [C, T], BF16, kind="Internal").ap()
    b_x2T = AB16.alloc(8 * 512)
    b_latT = AB16.alloc(2 * 2048)
    b_st = AB16.alloc(1024)
    b_rows = AB16.alloc(4 * 1024)
    b_b2 = AB16.alloc(1024)
    b_qg = AB16.alloc(65)
    b_kg = AB16.alloc(65)
    b_vg = AB16.alloc(65)
    b_qgT = AB16.alloc(512)
    b_kgT = AB16.alloc(512)
    b_pT = AB16.alloc(512)
    b_og = AB16.alloc(64)
    b_qg2 = AB16.alloc(65)
    b_kg2 = AB16.alloc(65)
    b_vg2 = AB16.alloc(65)
    b_og2 = AB16.alloc(64)
    b_hT = AB16.alloc(32 * 512)
    b_oT = b_hT
    b_g2 = AB16.alloc(16 * 1024)
    b_msk01 = AB16.alloc(512)
    b_wA = AB16.alloc(2048)
    b_wB = AB16.alloc(2048)
    b_gb = AB16.alloc(2048)

    a_x = AF32.alloc(1024)
    a_sq = AF32.alloc(1024)
    a_o = AF32.alloc(1024)
    a_st = AF32.alloc(16)
    a_sc = AF32.alloc(1024)
    a_cc = AF32.alloc(1024)
    a_tv = AF32.alloc(16)
    a_i32 = AF32.alloc(32)
    a_pi = AF32.alloc(1)
    a_mt = AF32.alloc(8)
    a_rw = AF32.alloc(32)
    a_cwb = AF32.alloc(12)
    i_fl = nc.alloc_sbuf_tensor("i_fl", [128, 32], I32)

    selW1 = nc.dram_tensor("scr_selW1", [8192, 1024], BF16, kind="Internal").ap()
    selW2 = nc.dram_tensor("scr_selW2", [8192, 1024], BF16, kind="Internal").ap()
    eidx_d = nc.dram_tensor("scr_eidx", [1, 2], F32, kind="Internal").ap()
    m12_d = nc.dram_tensor("scr_m12", [1, 18], F32, kind="Internal").ap()
    a_eid = AF32.alloc(4)
    a_gw = AF32.alloc(64)
    a_gwi = AF32.alloc(64)
    a_b1s = AF32.alloc(64)
    a_mb = AF32.alloc(25)
    if AB16.free_at % 2:
        AB16.alloc(1)
    b_gst = AB16.alloc(4096)
    i_gw = av[:, a_gwi:a_gwi + 64].bitcast(I32)
    sb_io8 = av[0:1, a_mb + 17:a_mb + 25]
    c_io8 = nc.inline_tensor(np.arange(8, dtype=np.float32).reshape(1, 8), "c_io8")
    p.op("q", lambda e: e.dma_start(out=sb_io8, in_=c_io8.ap()), writes=["a_mb"], ch="l1c")
    fsp = (np.arange(128).reshape(128, 1)
           + np.tile(np.arange(32) * 128, 2).reshape(1, 64)).astype(np.float32)
    c_fsp = nc.inline_tensor(fsp, "c_fsp")
    p.op("q", lambda e: e.dma_start(out=av[:, a_gw:a_gw + 64], in_=c_fsp.ap()),
         writes=["a_gw"], ch="l1c")

    maskg = np.zeros((128, 512), np.float32)
    for g in range(4):
        for t in range(8):
            maskg[t * 16:(t + 1) * 16, g * 128 + t * 16: g * 128 + (t + 1) * 16] = 1.0
    c_maskg = nc.inline_tensor(maskg.astype(ml_dtypes.bfloat16), "c_maskg")
    p.op("q", lambda e: e.dma_start(out=ab[:, b_msk01:b_msk01 + 512], in_=c_maskg.ap()),
         writes=["msk01"], ch="l1c")
    c_pi = nc.inline_tensor(np.full((128, 1), np.pi, np.float32), "c_pi")
    p.op("q", lambda e: e.dma_start(out=av[:, a_pi:a_pi + 1], in_=c_pi.ap()), writes=["a_pi"], ch="l1c")

    p.op("g", lambda e: e.dma_start(out=ab[0:1, b_rows:b_rows + 1024], in_=di["bq"]),
         writes=["rows"], ch="l1r")
    p.op("g", lambda e: e.dma_start(out=ab[0:1, b_rows + 1024:b_rows + 3072], in_=di["bup"]),
         writes=["rows"], ch="l1r")
    p.op("g", lambda e: e.dma_start(out=ab[0:1, b_rows + 3072:b_rows + 4096], in_=di["bout"]),
         writes=["rows"], ch="l1r")
    p.op("g", lambda e: e.dma_start(out=ab[:, b_gb:b_gb + 1024],
                                    in_=di["gamma"].to_broadcast([128, C])),
         writes=["b_gb"], ch="l1r")
    p.op("g", lambda e: e.dma_start(out=ab[:, b_gb + 1024:b_gb + 2048],
                                    in_=di["beta"].to_broadcast([128, C])),
         writes=["b_gb"], ch="l1r")
    sb_bd = nc.alloc_sbuf_tensor("sb_bd", [128, 2], F32)
    p.op("q", lambda e: e.dma_start(out=sb_bd[:], in_=di["bdown"]), writes=["sb_bd"], ch="l1c")
    sb_onesf = nc.alloc_sbuf_tensor("sb_onesf", [128, 1], F32)
    p.op("v", lambda e: e.memset(sb_onesf[:], 1.0), writes=["sb_onesf"])
    p.op("v", lambda e: e.memset(ab[:, b_vg + 64:b_vg + 65], 1.0), writes=[("b_vg1", 0)])
    p.op("v", lambda e: e.memset(ab[:, b_vg2 + 64:b_vg2 + 65], 1.0), writes=[("b_vg1", 1)])

    a_xq = AF32.alloc(4096)
    a_st2 = AF32.alloc(32)
    for i in range(NT):
        pr = i % 2
        kx = ("a_xq", pr * 2)
        ko = ("a_xq", pr * 2 + 1)
        xo = a_xq + pr * 2048
        oo = xo + 1024
        sto = a_st2 + pr * 16
        bsto = b_st if pr == 0 else b_gst
        kbst = "b_st" if pr == 0 else ("gst", 0)
        p.op("q", lambda e, i=i, xo=xo: e.dma_start(out=av[:, xo:xo + 1024],
                                                    in_=ds["x1"][i * 128:(i + 1) * 128, :]),
             reads=[kx, ("x1d", i)], writes=[kx], ch="l1x" + str(pr))
        ln_tok(ctx, kx, av[:, xo:xo + 1024], av[:, oo:oo + 1024],
               av[:, oo:oo + 1024], ko, av[:, sto:sto + 16],
               scratch_key=ko, stat_key=("a_st2", pr))
        p.op("q", lambda e, i=i, oo=oo: e.dma_start(out=ds["xn2f"][i * 128:(i + 1) * 128, :],
                                                    in_=av[:, oo:oo + 1024]),
             reads=[ko], writes=[("x2d", i)], ch="stx2" + str(pr))
        for half in range(2):
            p.op("t", lambda e, i=i, half=half, oo=oo: e.matmul(
                PS[4 + half][0:1, 0:512], sb_onesf[:],
                av[:, oo + half * 512:oo + half * 512 + 512],
                start=(i == 0), stop=(i == NT - 1)),
                reads=[ko, "sb_onesf", ("PS", 4 + half)], writes=[("PS", 4 + half)])
        p.op("v", lambda e, oo=oo, bsto=bsto: e.tensor_copy(out=ab[:, bsto:bsto + 1024],
                                                            in_=av[:, oo:oo + 1024]),
             reads=[ko, kbst], writes=[kbst])
        for c in range(8):
            p.op("t", lambda e, c=c, pr=pr, bsto=bsto: e.transpose(
                PB[pr][:, 0:128], ab[:, bsto + c * 128:bsto + (c + 1) * 128], sb_id[:]),
                reads=[kbst, "sb_id", "PB%d" % pr], writes=["PB%d" % pr])
            p.op("v", lambda e, c=c, i=i, pr=pr: e.tensor_copy(
                out=ab[:, b_x2T + c * 512 + pr * 128: b_x2T + c * 512 + pr * 128 + 128],
                in_=PB[pr][:, 0:128]),
                reads=["PB%d" % pr, ("bx2T", pr)], writes=[("bx2T", pr)])
            p.op("q", lambda e, c=c, i=i, pr=pr: e.dma_start(
                out=xn2T_d[c * 128:(c + 1) * 128, i * 128:(i + 1) * 128],
                in_=ab[:, b_x2T + c * 512 + pr * 128: b_x2T + c * 512 + pr * 128 + 128]),
                reads=[("bx2T", pr)], writes=[("x2Td", c, i)], ch="stx2T" + str(pr))
    for half in range(2):
        p.op("v", lambda e, half=half: e.tensor_scalar_mul(
            av[0:1, a_sq + half * 512:a_sq + half * 512 + 512],
            PS[4 + half][0:1, 0:512], 1.0 / T),
            reads=[("PS", 4 + half)], writes=["a_sq"])
    p.op("q", lambda e: e.dma_start(out=meanrow, in_=av[0:1, a_sq:a_sq + 1024]),
         reads=["a_sq"], writes=["meand"], ch="stmn")
    p.op("q", lambda e: e.dma_start(out=av[:, a_mt:a_mt + 8],
                                    in_=meanrow.rearrange("o (c p) -> (o p) c", p=128),
                                    allow_slow_non_contiguous=True),
         reads=["meand"], writes=["a_mt"], ch="lmn")
    sb_wr = nc.alloc_sbuf_tensor("sb_wr", [128, 64], F32)
    sb_br = nc.alloc_sbuf_tensor("sb_br", [8, 1], F32)
    for c in range(8):
        p.op("q", lambda e, c=c: e.dma_start(out=sb_wr[:, c * 8:(c + 1) * 8],
                                             in_=di["Wr1"][c * 128:(c + 1) * 128, :]),
             writes=["sb_wr"], ch="l1c")
    p.op("q", lambda e: e.dma_start(out=sb_br[:], in_=di["br1"]), writes=["sb_br"], ch="l1c")
    for c in range(8):
        p.op("t", lambda e, c=c: e.matmul(PS[4][0:8, 0:1], sb_wr[:, c * 8:(c + 1) * 8],
                                          av[:, a_mt + c:a_mt + c + 1],
                                          start=(c == 0), stop=(c == 7)),
             reads=["a_mt", "sb_wr", ("PS", 4)], writes=[("PS", 4)])
    p.op("v", lambda e: e.tensor_add(av[0:8, a_rw:a_rw + 1], PS[4][0:8, 0:1], sb_br[:]),
         reads=[("PS", 4), "sb_br"], writes=["a_rw"])
    p.op("q", lambda e: e.dma_start(out=rwrow_d.rearrange("o e -> (o e)").unsqueeze(1),
                                    in_=av[0:8, a_rw:a_rw + 1], allow_slow_non_contiguous=True),
         reads=["a_rw"], writes=["rwd"], ch="strw")
    p.op("q", lambda e: e.dma_start(out=av[0:1, a_rw + 8:a_rw + 16], in_=rwrow_d),
         reads=["rwd"], writes=["a_rw"], ch="lrw")
    rwr = av[0:1, a_rw + 8:a_rw + 16]
    p.op("v", lambda e: e.tensor_reduce(out=av[0:1, a_rw + 16:a_rw + 17], in_=rwr, axis=AX.X,
                                        op=ALU.max, negate=True),
         reads=["a_rw"], writes=["a_rw"])
    p.op("s", lambda e: e.activation(out=rwr, in_=rwr, func=AF.Exp,
                                     bias=av[0:1, a_rw + 16:a_rw + 17],
                                     accum_out=av[0:1, a_rw + 17:a_rw + 18]),
         reads=["a_rw"], writes=["a_rw"])
    p.op("s", lambda e: e.activation(out=av[0:1, a_rw + 18:a_rw + 19],
                                     in_=av[0:1, a_rw + 17:a_rw + 18], func=AF.Ln),
         reads=["a_rw"], writes=["a_rw"])
    p.op("s", lambda e: e.activation(out=av[0:1, a_rw + 16:a_rw + 17],
                                     in_=av[0:1, a_rw + 18:a_rw + 19], func=AF.Exp, scale=-1.0),
         reads=["a_rw"], writes=["a_rw"])
    p.op("v", lambda e: e.tensor_scalar_mul(rwr, rwr, av[0:1, a_rw + 16:a_rw + 17]),
         reads=["a_rw"], writes=["a_rw"])
    p.op("v", lambda e: e.max(out=av[0:1, a_rw + 20:a_rw + 28], in_=rwr),
         reads=["a_rw"], writes=["a_rw"])
    v1 = av[0:1, a_rw + 20:a_rw + 21]
    v2 = av[0:1, a_rw + 21:a_rw + 22]
    p.op("v", lambda e: e.tensor_sub(av[0:1, a_rw + 28:a_rw + 29], v1, v2),
         reads=["a_rw"], writes=["a_rw"])
    p.op("s", lambda e: e.activation(out=av[0:1, a_rw + 29:a_rw + 30],
                                     in_=av[0:1, a_rw + 28:a_rw + 29], func=AF.Sigmoid),
         reads=["a_rw"], writes=["a_rw"])
    p.op("v", lambda e: e.tensor_scalar(av[0:1, a_rw + 30:a_rw + 31],
                                        av[0:1, a_rw + 29:a_rw + 30], -1.0, 1.0,
                                        op0=ALU.mult, op1=ALU.add),
         reads=["a_rw"], writes=["a_rw"])
    cwrow = av[0:1, a_rw + 0:a_rw + 8]
    m1 = av[0:1, a_rw + 31:a_rw + 32]
    tmp8a = av[0:1, a_rw - 0 + 8:a_rw + 16]
    t8 = AF32.alloc(32)
    m1r = av[0:1, t8:t8 + 8]
    m2r = av[0:1, t8 + 8:t8 + 16]
    rkr = av[0:1, t8 + 16:t8 + 24]
    p.op("v", lambda e: e.tensor_single_scalar(out=m1r, in_=rwr, scalar=v1, op=ALU.is_equal),
         reads=["a_rw"], writes=["t8"])
    p.op("v", lambda e: e.memset(av[0:1, t8 + 24:t8 + 25], 0.0), writes=["t8"])
    p.op("v", lambda e: e.tensor_tensor_scan(out=rkr, data0=m1r,
                                             data1=av[0:1, t8 + 24:t8 + 25].to_broadcast([1, 8]),
                                             initial=0.0, op0=ALU.add, op1=ALU.add),
         reads=["t8"], writes=["t8"])
    p.op("v", lambda e: e.tensor_single_scalar(out=rkr, in_=rkr, scalar=1.0, op=ALU.is_le),
         reads=["t8"], writes=["t8"])
    p.op("v", lambda e: e.tensor_mul(m1r, m1r, rkr), reads=["t8"], writes=["t8"])
    p.op("v", lambda e: e.tensor_single_scalar(out=m2r, in_=rwr, scalar=v2, op=ALU.is_equal),
         reads=["a_rw", "t8"], writes=["t8"])
    p.op("v", lambda e: e.scalar_tensor_tensor(out=rkr, in0=m1r, scalar=-1.0, in1=m2r,
                                               op0=ALU.mult, op1=ALU.add),
         reads=["t8"], writes=["t8"])
    p.op("v", lambda e: e.tensor_single_scalar(out=rkr, in_=rkr, scalar=0.5, op=ALU.is_ge),
         reads=["t8"], writes=["t8"])
    p.op("v", lambda e: e.memset(av[0:1, t8 + 24:t8 + 25], 0.0), writes=["t8"])
    p.op("v", lambda e: e.tensor_tensor_scan(out=m2r, data0=rkr,
                                             data1=av[0:1, t8 + 24:t8 + 25].to_broadcast([1, 8]),
                                             initial=0.0, op0=ALU.add, op1=ALU.add),
         reads=["t8"], writes=["t8"])
    p.op("v", lambda e: e.tensor_single_scalar(out=m2r, in_=m2r, scalar=1.0, op=ALU.is_le),
         reads=["t8"], writes=["t8"])
    p.op("v", lambda e: e.tensor_mul(m2r, m2r, rkr), reads=["t8"], writes=["t8"])
    p.op("v", lambda e: e.tensor_scalar_mul(av[0:1, t8 + 24:t8 + 32], m1r,
                                            av[0:1, a_rw + 29:a_rw + 30]),
         reads=["t8", "a_rw"], writes=["t8"])
    p.op("v", lambda e: e.scalar_tensor_tensor(out=cwrow, in0=m2r,
                                               scalar=av[0:1, a_rw + 30:a_rw + 31],
                                               in1=av[0:1, t8 + 24:t8 + 32],
                                               op0=ALU.mult, op1=ALU.add),
         reads=["t8", "a_rw"], writes=["a_rw"])
    p.op("q", lambda e: e.dma_start(out=cw_d, in_=cwrow), reads=["a_rw"], writes=["cwd"], ch="stcw")
    p.op("q", lambda e: e.dma_start(out=av[:, a_cwb:a_cwb + 8], in_=cw_d.to_broadcast([128, 8])),
         reads=["cwd"], writes=["a_cwb"], ch="lcw")
    p.op("v", lambda e: e.tensor_mul(av[0:1, t8 + 16:t8 + 24], m1r, sb_io8),
         reads=["t8", "a_mb"], writes=["t8"])
    p.op("v", lambda e: e.tensor_reduce(out=av[0:1, a_eid:a_eid + 1],
                                        in_=av[0:1, t8 + 16:t8 + 24], axis=AX.X, op=ALU.add),
         reads=["t8"], writes=["a_eid"])
    p.op("v", lambda e: e.tensor_mul(av[0:1, t8 + 16:t8 + 24], m2r, sb_io8),
         reads=["t8", "a_mb"], writes=["t8"])
    p.op("v", lambda e: e.tensor_reduce(out=av[0:1, a_eid + 1:a_eid + 2],
                                        in_=av[0:1, t8 + 16:t8 + 24], axis=AX.X, op=ALU.add),
         reads=["t8"], writes=["a_eid"])
    p.op("q", lambda e: e.dma_start(out=eidx_d, in_=av[0:1, a_eid:a_eid + 2]),
         reads=["a_eid"], writes=["eidxd"], ch="steidx")
    p.op("q", lambda e: e.dma_start(out=m12_d[0:1, 0:16], in_=av[0:1, t8:t8 + 16]),
         reads=["t8"], writes=["m12d"], ch="stm12")
    p.op("q", lambda e: e.dma_start(out=m12_d[0:1, 16:18], in_=av[0:1, a_rw + 29:a_rw + 31]),
         reads=["a_rw"], writes=["m12d"], ch="stm12")
    p.op("q", lambda e: e.dma_start(out=av[:, a_eid + 2:a_eid + 4],
                                    in_=eidx_d.to_broadcast([128, 2])),
         reads=["eidxd", "a_eid"], writes=["a_eid"], ch="leidx")
    p.op("q", lambda e: e.dma_start(out=av[:, a_mb:a_mb + 16],
                                    in_=m12_d[0:1, 0:16].to_broadcast([128, 16])),
         reads=["m12d", "a_mb"], writes=["a_mb"], ch="lmb")
    p.op("q", lambda e: e.dma_start(out=av[:, a_cwb + 8:a_cwb + 10],
                                    in_=m12_d[0:1, 16:18].to_broadcast([128, 2])),
         reads=["m12d", "a_cwb"], writes=["a_cwb"], ch="lcw")
    for ee2 in range(2):
        p.op("v", lambda e, ee2=ee2: e.scalar_tensor_tensor(
            out=av[:, a_gw + ee2 * 32:a_gw + (ee2 + 1) * 32],
            in0=av[:, a_eid + 2 + ee2:a_eid + 3 + ee2].to_broadcast([128, 32]),
            scalar=4096.0, in1=av[:, a_gw + ee2 * 32:a_gw + (ee2 + 1) * 32],
            op0=ALU.mult, op1=ALU.add),
            reads=["a_eid", "a_gw"], writes=["a_gw"])
    p.op("v", lambda e: e.tensor_copy(out=i_gw, in_=av[:, a_gw:a_gw + 64]),
         reads=["a_gw", "a_gwi"], writes=["a_gwi"])
    spar = ctx["sb_par"]
    fb1o = ctx["par_off"]["fb1"]
    for ee2 in range(2):
        for e8 in range(8):
            if e8 == 0:
                p.op("v", lambda e, ee2=ee2, e8=e8: e.tensor_scalar_mul(
                    av[:, a_b1s + ee2 * 32:a_b1s + (ee2 + 1) * 32],
                    spar[:, fb1o + e8 * 32:fb1o + (e8 + 1) * 32],
                    av[:, a_mb + ee2 * 8 + e8:a_mb + ee2 * 8 + e8 + 1]),
                    reads=[("par", "fb1"), "a_mb", "a_b1s"], writes=["a_b1s"])
            else:
                p.op("v", lambda e, ee2=ee2, e8=e8: e.scalar_tensor_tensor(
                    out=av[:, a_b1s + ee2 * 32:a_b1s + (ee2 + 1) * 32],
                    in0=spar[:, fb1o + e8 * 32:fb1o + (e8 + 1) * 32],
                    scalar=av[:, a_mb + ee2 * 8 + e8:a_mb + ee2 * 8 + e8 + 1],
                    in1=av[:, a_b1s + ee2 * 32:a_b1s + (ee2 + 1) * 32],
                    op0=ALU.mult, op1=ALU.add),
                    reads=[("par", "fb1"), "a_mb", "a_b1s"], writes=["a_b1s"])
    gstv = [ab[:, b_gst + 2048 * j:b_gst + 2048 * (j + 1)].bitcast(F32) for j in range(2)]
    gsrcs = [(di["f_W1r"], selW1, "sW1"),
             (di["f_W2"].rearrange("e f c -> (e f) c"), selW2, "sW2")]
    gcnt = 0
    for gsrc, gdst, gdkey in gsrcs:
        for ee2 in range(2):
            for k in range(32):
                par2 = gcnt % 2
                p.op("g", lambda e, gsrc=gsrc, ee2=ee2, k=k, par2=par2: e.indirect_dma_start(
                    out=gstv[par2], out_offset=None, in_=gsrc,
                    in_offset=IndirectOffsetOnAxis(i_gw[:, ee2 * 32 + k:ee2 * 32 + k + 1], 0)),
                    reads=["a_gwi", ("gst", par2)], writes=[("gst", par2)], ch="gg" + str(par2))
                p.op("g", lambda e, gdst=gdst, ee2=ee2, k=k, par2=par2: e.dma_start(
                    out=gdst[(ee2 * 32 + k) * 128:(ee2 * 32 + k + 1) * 128, :],
                    in_=gstv[par2]),
                    reads=[("gst", par2)], writes=[(gdkey, ee2 * 32 + k)], ch="gs" + str(par2))
                gcnt += 1

    p.op("g", lambda e: e.iota(av[:, a_tv:a_tv + 16], [[128, 16]], base=0,
                               channel_multiplier=1, allow_small_or_imprecise_dtypes=True),
         writes=["a_tv"])
    for i in range(NT):
        fcol = av[:, a_sq:a_sq + 32]
        p.op("v", lambda e, i=i: e.tensor_mul(fcol, av[:, a_tv + i:a_tv + i + 1]
                                              .to_broadcast([128, 32]), sb_invf[:]),
             reads=["a_tv", "sb_invf"], writes=["a_sq"])
        for which, base_off, phase in [("sin", a_sc, 0.0), ("cos", a_cc, 0.25)]:
            u = av[:, a_sq + 32:a_sq + 64]
            p.op("v", lambda e, phase=phase: e.tensor_scalar(u, fcol, 1.0 / (2 * np.pi), phase,
                                                             op0=ALU.mult, op1=ALU.add),
                 reads=["a_sq"], writes=["a_sq"])
            p.op("v", lambda e: e.tensor_copy(out=i_fl[:], in_=u), reads=["a_sq"], writes=["i_fl"])
            p.op("v", lambda e: e.tensor_copy(out=av[:, a_sq + 64:a_sq + 96], in_=i_fl[:]),
                 reads=["i_fl"], writes=["a_sq"])
            p.op("v", lambda e: e.tensor_sub(u, u, av[:, a_sq + 64:a_sq + 96]),
                 reads=["a_sq"], writes=["a_sq"])
            p.op("v", lambda e: e.tensor_single_scalar(out=av[:, a_sq + 64:a_sq + 96], in_=u,
                                                       scalar=0.0, op=ALU.is_lt),
                 reads=["a_sq"], writes=["a_sq"])
            p.op("v", lambda e: e.tensor_add(u, u, av[:, a_sq + 64:a_sq + 96]),
                 reads=["a_sq"], writes=["a_sq"])
            p.op("s", lambda e, base_off=base_off, i=i: e.activation(
                out=av[:, base_off + i * 32: base_off + (i + 1) * 32], in_=u, func=AF.Sin,
                scale=float(-2 * np.pi), bias=av[:, a_pi:a_pi + 1]),
                reads=["a_sq", "a_pi"], writes=["a_sc" if base_off == a_sc else "a_cc"])

    for tb in range(4):
        for c in range(8):
            p.op("q", lambda e, c=c, tb=tb: e.dma_start(
                out=ab[:, b_x2T + c * 512: b_x2T + (c + 1) * 512],
                in_=xn2T_d[c * 128:(c + 1) * 128, tb * 512:(tb + 1) * 512]),
                reads=["b_x2T"] + [("x2Td", c, i) for i in range(NT)],
                writes=["b_x2T", ("bx2T", 0), ("bx2T", 1)], ch="lx2w")
        for lc in range(2):
            p.op("g", lambda e, lc=lc: e.dma_start(
                out=ab[:, b_wA:b_wA + 1024].rearrange("p (c j) -> p c j", c=8),
                in_=di["Wdown"][:, lc * 128:(lc + 1) * 128].rearrange("(c p) j -> p c j", p=128)),
                reads=["b_wA"], writes=["b_wA"], ch="chWA")
            for c in range(8):
                p.op("t", lambda e, lc=lc, tb=tb, c=c: e.matmul(
                    PS[0][:, 0:512],
                    ab[:, b_wA + c * 128:b_wA + (c + 1) * 128],
                    ab[:, b_x2T + c * 512: b_x2T + (c + 1) * 512],
                    start=(c == 0), stop=(c == 7)),
                    reads=["b_wA", "b_x2T", ("PS", 0)], writes=[("PS", 0)])
            p.op("v", lambda e, lc=lc, tb=tb: e.tensor_scalar_add(
                ab[:, b_latT + lc * 2048 + tb * 512: b_latT + lc * 2048 + (tb + 1) * 512],
                PS[0][:, 0:512], sb_bd[:, lc:lc + 1]),
                reads=[("PS", 0), "sb_bd"], writes=["b_latT"])

    for tb in range(4):
        for c in range(8):
            p.op("q", lambda e, c=c, tb=tb: e.dma_start(
                out=ab[:, b_x2T + c * 512: b_x2T + (c + 1) * 512],
                in_=xn2T_d[c * 128:(c + 1) * 128, tb * 512:(tb + 1) * 512]),
                reads=["b_x2T"] + [("x2Td", c, i) for i in range(NT)],
                writes=["b_x2T", ("bx2T", 0), ("bx2T", 1)], ch="lx2w")
        for qr in range(4):
            qp = qr % 2
            wqo = b_wB if qp == 0 else b_wA
            wkey = "b_wB" if qp == 0 else "b_wA"
            wch = "chWB" if qp == 0 else "chWA"
            p.op("g", lambda e, qr=qr, wqo=wqo: e.dma_start(
                out=ab[:, wqo:wqo + 2048].rearrange("p (c j) -> p c j", c=8),
                in_=di["Wq"][:, qr * 256:(qr + 1) * 256].rearrange("(c p) j -> p c j", p=128)),
                reads=[wkey], writes=[wkey], ch=wch)
            for ii in range(4):
                bank = (qr * 4 + ii) % 2
                for c in range(8):
                    p.op("t", lambda e, ii=ii, c=c, wqo=wqo, bank=bank: e.matmul(
                        PS[bank][:, 0:256],
                        ab[:, b_x2T + c * 512 + ii * 128: b_x2T + c * 512 + ii * 128 + 128],
                        ab[:, wqo + c * 256:wqo + (c + 1) * 256],
                        start=(c == 0), stop=(c == 7)),
                        reads=[wkey, "b_x2T", ("PS", bank)], writes=[("PS", bank)])
                p.op("v", lambda e, qr=qr, ii=ii, bank=bank: e.tensor_copy(
                    out=av[:, a_xq + ii * 1024 + qr * 256:a_xq + ii * 1024 + (qr + 1) * 256],
                    in_=PS[bank][:, 0:256]),
                    reads=[("PS", bank), ("a_xq", ii)], writes=[("a_xq", ii)])
        for ii in range(4):
            i = tb * 4 + ii
            xo = a_xq + ii * 1024
            emit_rowadd(ctx, av[:, xo:xo + 1024], ab[0:1, b_rows:b_rows + 1024],
                        ("a_xq", ii), "rows", PS, p)
            emit_rope(ctx, p, av, xo, a_sc, a_cc, i, ("a_xq", ii))
            p.op("v", lambda e, xo=xo: e.tensor_copy(out=ab[:, b_st:b_st + 1024],
                                                     in_=av[:, xo:xo + 1024]),
                 reads=[("a_xq", ii), "b_st"], writes=["b_st"])
            p.op("q", lambda e, i=i: e.dma_start(out=ds["q_tok"][i * 128:(i + 1) * 128, :],
                                                 in_=ab[:, b_st:b_st + 1024]),
                 reads=["b_st"], writes=[("qd", i)], ch="stq")
        for part, dst, dkey in [(0, "k_tok", "kd"), (1, "v_tok", "vd")]:
            for half in range(2):
                qp = half
                wqo = b_wB if qp == 0 else b_wA
                wkey = "b_wB" if qp == 0 else "b_wA"
                wch = "chWB" if qp == 0 else "chWA"
                p.op("g", lambda e, part=part, half=half, wqo=wqo: e.dma_start(
                    out=ab[:, wqo:wqo + 1024].rearrange("p (c j) -> p c j", c=2),
                    in_=di["Wup"][:, part * 1024 + half * 512: part * 1024 + (half + 1) * 512]
                    .rearrange("(c p) j -> p c j", p=128)),
                    reads=[wkey], writes=[wkey], ch=wch)
                for ii in range(4):
                    i = tb * 4 + ii
                    bank = ii % 2
                    for lc in range(2):
                        p.op("t", lambda e, i=i, lc=lc, wqo=wqo, bank=bank: e.matmul(
                            PS[bank][:, 0:512],
                            ab[:, b_latT + lc * 2048 + i * 128: b_latT + lc * 2048 + (i + 1) * 128],
                            ab[:, wqo + lc * 512:wqo + (lc + 1) * 512],
                            start=(lc == 0), stop=(lc == 1)),
                            reads=[wkey, "b_latT", ("PS", bank)], writes=[("PS", bank)])
                    p.op("v", lambda e, half=half, ii=ii, bank=bank: e.tensor_copy(
                        out=av[:, a_xq + ii * 1024 + half * 512:a_xq + ii * 1024 + half * 512 + 512],
                        in_=PS[bank][:, 0:512]),
                        reads=[("PS", bank), ("a_xq", ii)], writes=[("a_xq", ii)])
            for ii in range(4):
                i = tb * 4 + ii
                xo = a_xq + ii * 1024
                emit_rowadd(ctx, av[:, xo:xo + 1024],
                            ab[0:1, b_rows + 1024 + part * 1024:b_rows + 1024 + (part + 1) * 1024],
                            ("a_xq", ii), "rows", PS, p)
                if part == 0:
                    emit_rope(ctx, p, av, xo, a_sc, a_cc, i, ("a_xq", ii))
                p.op("v", lambda e, xo=xo, ii=ii: e.tensor_copy(out=ab[:, b_st:b_st + 1024],
                                                                in_=av[:, xo:xo + 1024]),
                     reads=[("a_xq", ii), "b_st"], writes=["b_st"])
                p.op("q", lambda e, i=i, dst=dst, dkey=dkey: e.dma_start(
                    out=ds[dst][i * 128:(i + 1) * 128, :], in_=ab[:, b_st:b_st + 1024]),
                    reads=["b_st"], writes=[(dkey, i)], ch="st" + dst)

    a_ax = AF32.alloc(1024)
    a_asq = AF32.alloc(16)
    for gb in range(64):
        par = gb % 2
        qgo = b_qg if par == 0 else b_qg2
        kgo = b_kg if par == 0 else b_kg2
        vgo = b_vg if par == 0 else b_vg2
        ogo = b_og if par == 0 else b_og2
        qkb = 0 if par == 0 else 2
        avb = 1 if par == 0 else 3
        aoo = a_ax + par * 512
        sqo = a_asq + par * 8
        t0 = gb * 32
        for gg in range(4):
            tg = t0 + gg * 8
            p.op("q", lambda e, tg=tg, qgo=qgo, par=par: e.dma_start(
                out=ab[:, qgo:qgo + 64],
                in_=ds["q_tok"][tg:tg + 8, :].rearrange("t (h d) -> (t h) d", d=64)),
                reads=[("b_qg", par)] + [("qd", tg // 128)], writes=[("b_qg", par)],
                ch="lqg" + str(par))
            p.op("q", lambda e, tg=tg, kgo=kgo, par=par: e.dma_start(
                out=ab[:, kgo:kgo + 64],
                in_=ds["k_tok"][tg:tg + 8, :].rearrange("t (h d) -> (t h) d", d=64)),
                reads=[("b_kg", par)] + [("kd", tg // 128)], writes=[("b_kg", par)],
                ch="lkg" + str(par))
            p.op("t", lambda e, gg=gg, qgo=qgo, par=par: e.transpose(
                PB[0][0:64, gg * 128:(gg + 1) * 128], ab[:, qgo:qgo + 64], sb_id[:]),
                reads=[("b_qg", par), "sb_id", "PB0"], writes=["PB0"])
            p.op("v", lambda e, gg=gg, par=par: e.tensor_copy(
                out=ab[0:64, b_qgT + gg * 128:b_qgT + (gg + 1) * 128],
                in_=PB[0][0:64, gg * 128:(gg + 1) * 128]),
                reads=["PB0"], writes=["b_qgT"])
            p.op("t", lambda e, gg=gg, kgo=kgo, par=par: e.transpose(
                PB[1][0:64, gg * 128:(gg + 1) * 128], ab[:, kgo:kgo + 64], sb_id[:]),
                reads=[("b_kg", par), "sb_id", "PB1"], writes=["PB1"])
            p.op("v", lambda e, gg=gg, par=par: e.tensor_copy(
                out=ab[0:64, b_kgT + gg * 128:b_kgT + (gg + 1) * 128],
                in_=PB[1][0:64, gg * 128:(gg + 1) * 128]),
                reads=["PB1"], writes=["b_kgT"])
            p.op("t", lambda e, gg=gg, qkb=qkb: e.matmul(
                PS[qkb][:, gg * 128:(gg + 1) * 128],
                ab[0:64, b_kgT + gg * 128:b_kgT + (gg + 1) * 128],
                ab[0:64, b_qgT + gg * 128:b_qgT + (gg + 1) * 128],
                start=True, stop=True),
                reads=["b_kgT", "b_qgT", ("PS", qkb)], writes=[("PS", qkb)])
        p.op("s", lambda e, qkb=qkb, aoo=aoo, par=par: e.activation(
            out=av[:, aoo:aoo + 512], in_=PS[qkb][:, 0:512], func=AF.Exp, scale=0.125),
             reads=[("PS", qkb), ("a_o", par)], writes=[("a_o", par)])
        p.op("v", lambda e, aoo=aoo, par=par: e.tensor_mul(
            ab[:, b_pT:b_pT + 512], av[:, aoo:aoo + 512], ab[:, b_msk01:b_msk01 + 512]),
             reads=[("a_o", par), "msk01", "b_pT"], writes=["b_pT"])
        for gg in range(4):
            tg = t0 + gg * 8
            p.op("q", lambda e, tg=tg, vgo=vgo, par=par: e.dma_start(
                out=ab[:, vgo:vgo + 64],
                in_=ds["v_tok"][tg:tg + 8, :].rearrange("t (h d) -> (t h) d", d=64)),
                reads=[("b_vg", par)] + [("vd", tg // 128)], writes=[("b_vg", par)],
                ch="lvg" + str(par))
            p.op("t", lambda e, gg=gg, vgo=vgo, avb=avb, par=par: e.matmul(
                PS[avb][:, 0:65],
                ab[:, b_pT + gg * 128:b_pT + (gg + 1) * 128],
                ab[:, vgo:vgo + 65],
                start=True, stop=True),
                reads=["b_pT", ("b_vg", par), ("b_vg1", par), ("PS", avb)],
                writes=[("PS", avb)])
            p.op("v", lambda e, avb=avb, sqo=sqo, par=par: e.tensor_copy(
                out=av[:, sqo + 2:sqo + 3], in_=PS[avb][:, 64:65]),
                reads=[("PS", avb), ("a_sq", par)], writes=[("a_sq", par)])
            p.op("s", lambda e, sqo=sqo, par=par: e.activation(
                out=av[:, sqo:sqo + 1], in_=av[:, sqo + 2:sqo + 3], func=AF.Ln),
                reads=[("a_sq", par)], writes=[("a_sq", par)])
            p.op("s", lambda e, sqo=sqo, par=par: e.activation(
                out=av[:, sqo + 1:sqo + 2], in_=av[:, sqo:sqo + 1], func=AF.Exp, scale=-1.0),
                reads=[("a_sq", par)], writes=[("a_sq", par)])
            p.op("v", lambda e, avb=avb, ogo=ogo, sqo=sqo, par=par: e.tensor_scalar_mul(
                ab[:, ogo:ogo + 64], PS[avb][:, 0:64], av[:, sqo + 1:sqo + 2]),
                reads=[("PS", avb), ("a_sq", par), ("b_og", par)], writes=[("b_og", par)])
            p.op("q", lambda e, tg=tg, ogo=ogo, par=par: e.dma_start(
                out=ds["o_grp"][tg * 16:(tg + 8) * 16, :], in_=ab[:, ogo:ogo + 64]),
                reads=[("b_og", par)], writes=[("ogd", tg // 128)], ch="stog" + str(par))

    for i in range(NT):
        p.op("q", lambda e, i=i: e.dma_start(
            out=ab[:, b_st:b_st + 1024],
            in_=ds["o_grp"][i * 2048:(i + 1) * 2048, :].rearrange("(t h) d -> t (h d)", h=16)),
            reads=["b_st", ("ogd", i)], writes=["b_st"], ch="log")
        for c in range(8):
            p.op("t", lambda e, c=c: e.transpose(PB[0][:, 0:128],
                                                 ab[:, b_st + c * 128:b_st + (c + 1) * 128],
                                                 sb_id[:]),
                 reads=["b_st", "sb_id", "PB0"], writes=["PB0"])
            p.op("v", lambda e, c=c: e.tensor_copy(out=ab[:, b_oT + c * 512:b_oT + c * 512 + 128],
                                                   in_=PB[0][:, 0:128]),
                 reads=["PB0"], writes=["b_hT"])
        for qr in range(4):
            p.op("g", lambda e, qr=qr: e.dma_start(
                out=ab[:, b_wB:b_wB + 2048].rearrange("p (c j) -> p c j", c=8),
                in_=di["Wout"][:, qr * 256:(qr + 1) * 256].rearrange("(c p) j -> p c j", p=128)),
                reads=["b_wB"], writes=["b_wB"], ch="chWB")
            for c in range(8):
                p.op("t", lambda e, c=c: e.matmul(
                    PS[0][:, 0:256],
                    ab[:, b_oT + c * 512:b_oT + c * 512 + 128],
                    ab[:, b_wB + c * 256:b_wB + (c + 1) * 256],
                    start=(c == 0), stop=(c == 7)),
                    reads=["b_wB", "b_hT", ("PS", 0)], writes=[("PS", 0)])
            p.op("v", lambda e, qr=qr: e.tensor_copy(out=av[:, a_x + qr * 256:a_x + (qr + 1) * 256],
                                                     in_=PS[0][:, 0:256]),
                 reads=[("PS", 0)], writes=["a_x"])
        emit_rowadd(ctx, av[:, a_x:a_x + 1024], ab[0:1, b_rows + 3072:b_rows + 4096],
                    "a_x", "rows", PS, p)
        p.op("q", lambda e, i=i: e.dma_start(out=ds["s2"][i * 128:(i + 1) * 128, :],
                                             in_=av[:, a_x:a_x + 1024]),
             reads=["a_x"], writes=[("s2d", i)], ch="sts2")

    p.barrier()
    for ee in range(2):
        for tb in range(4):
            for c in range(8):
                p.op("q", lambda e, c=c, tb=tb: e.dma_start(
                    out=ab[:, b_x2T + c * 512: b_x2T + (c + 1) * 512],
                    in_=xn2T_d[c * 128:(c + 1) * 128, tb * 512:(tb + 1) * 512]),
                    reads=["b_x2T"] + [("x2Td", c, i) for i in range(NT)],
                    writes=["b_x2T", ("bx2T", 0), ("bx2T", 1)], ch="lx2w")
            for fs in range(32):
                wpar = fs % 2
                wo = b_wA + wpar * 1024
                p.op("q", lambda e, ee=ee, fs=fs, wo=wo: e.dma_start(
                    out=ab[:, wo:wo + 1024],
                    in_=selW1[(ee * 32 + fs) * 128:(ee * 32 + fs + 1) * 128, :]),
                    reads=[("b_wA", wpar), ("sW1", ee * 32 + fs)],
                    writes=[("b_wA", wpar)], ch="chWA" + str(wpar))
                for c in range(8):
                    p.op("t", lambda e, fs=fs, c=c, wo=wo, wpar=wpar: e.matmul(
                        PS[2 + wpar][:, 0:512],
                        ab[:, wo + c * 128:wo + (c + 1) * 128],
                        ab[:, b_x2T + c * 512:(b_x2T + (c + 1) * 512)],
                        start=(c == 0), stop=(c == 7)),
                        reads=[("b_wA", wpar), "b_x2T", ("PS", 2 + wpar)],
                        writes=[("PS", 2 + wpar)])
                p.op("s", lambda e, ee=ee, fs=fs, wpar=wpar: e.activation(
                    out=ab[:, b_hT + fs * 512:b_hT + (fs + 1) * 512], in_=PS[2 + wpar][:, 0:512],
                    func=AF.Gelu,
                    bias=av[:, a_b1s + ee * 32 + fs:a_b1s + ee * 32 + fs + 1]),
                    reads=[("PS", 2 + wpar), "a_b1s"], writes=["b_hT"])
            for fs in range(32):
                wpar = fs % 2
                wo = b_wB + wpar * 1024
                p.op("q", lambda e, ee=ee, fs=fs, wo=wo: e.dma_start(
                    out=ab[:, wo:wo + 1024],
                    in_=selW2[(ee * 32 + fs) * 128:(ee * 32 + fs + 1) * 128, :]),
                    reads=[("b_wB", wpar), ("sW2", ee * 32 + fs)],
                    writes=[("b_wB", wpar)], ch="chWB" + str(wpar))
                for tsu in range(4):
                    for half in range(2):
                        p.op("t", lambda e, fs=fs, tsu=tsu, half=half, wo=wo: e.matmul(
                            PS[tsu * 2 + half][:, 0:512] if tsu * 2 + half < 6 else
                            PB[tsu * 2 + half - 6].ap().bitcast(F32)[:, 0:512],
                            ab[:, b_hT + fs * 512 + tsu * 128: b_hT + fs * 512 + (tsu + 1) * 128],
                            ab[:, wo + half * 512:wo + (half + 1) * 512],
                            start=(fs == 0), stop=(fs == 31)),
                            reads=[("b_wB", wpar), "b_hT",
                                   ("PS", tsu * 2 + half) if tsu * 2 + half < 6 else "PB%d" % (tsu * 2 + half - 6)],
                            writes=[("PS", tsu * 2 + half) if tsu * 2 + half < 6 else "PB%d" % (tsu * 2 + half - 6)])
            for tsu in range(4):
                gi = tb * 4 + tsu
                for half in range(2):
                    pbank = (PS[tsu * 2 + half][:, 0:512] if tsu * 2 + half < 6 else
                             PB[tsu * 2 + half - 6].ap().bitcast(F32)[:, 0:512])
                    psxkey = ("PS", tsu * 2 + half) if tsu * 2 + half < 6 else "PB%d" % (tsu * 2 + half - 6)
                    if ee == 0:
                        p.op("v", lambda e, pbank=pbank, gi=gi, half=half: e.tensor_scalar_mul(
                            ab[:, b_g2 + gi * 1024 + half * 512: b_g2 + gi * 1024 + (half + 1) * 512],
                            pbank, av[:, a_cwb + 8:a_cwb + 9]),
                            reads=[psxkey, "a_cwb"], writes=["b_g2"])
                    else:
                        p.op("v", lambda e, pbank=pbank, gi=gi, half=half: e.scalar_tensor_tensor(
                            out=ab[:, b_g2 + gi * 1024 + half * 512: b_g2 + gi * 1024 + (half + 1) * 512],
                            in0=pbank, scalar=av[:, a_cwb + 9:a_cwb + 10],
                            in1=ab[:, b_g2 + gi * 1024 + half * 512: b_g2 + gi * 1024 + (half + 1) * 512],
                            op0=ALU.mult, op1=ALU.add),
                            reads=[psxkey, "a_cwb", "b_g2"], writes=["b_g2"])

    a_b2w = AF32.alloc(1024)
    p.op("q", lambda e: e.dma_start(out=av[0:8, a_mb + 16:a_mb + 17],
                                    in_=cw_d.rearrange("o e -> (o e)").unsqueeze(1),
                                    allow_slow_non_contiguous=True),
         reads=["cwd", "a_mb"], writes=["a_mb"], ch="lcwT")
    p.op("q", lambda e: e.dma_start(out=av[0:8, a_b2w:a_b2w + 1024],
                                    in_=di["f_b2"].rearrange("o (e c) -> (o e) c", e=8)),
         reads=["a_b2w"], writes=["a_b2w"], ch="lb2f")
    p.op("v", lambda e: e.tensor_scalar_mul(av[0:8, a_b2w:a_b2w + 1024],
                                            av[0:8, a_b2w:a_b2w + 1024],
                                            av[0:8, a_mb + 16:a_mb + 17]),
         reads=["a_b2w", "a_mb"], writes=["a_b2w"])
    p.op("v", lambda e: e.memset(av[0:8, a_gw:a_gw + 1], 1.0),
         reads=["a_gw"], writes=["a_gw"])
    for half in range(2):
        p.op("t", lambda e, half=half: e.matmul(PS[half][0:1, 0:512], av[0:8, a_gw:a_gw + 1],
                                                av[0:8, a_b2w + half * 512:a_b2w + (half + 1) * 512],
                                                start=True, stop=True),
             reads=["a_gw", "a_b2w", ("PS", half)], writes=[("PS", half)])
    for half in range(2):
        p.op("v", lambda e, half=half: e.tensor_copy(
            out=av[0:1, a_b2w + half * 512:a_b2w + (half + 1) * 512],
            in_=PS[half][0:1, 0:512]),
            reads=[("PS", half), "a_b2w"], writes=["a_b2w"])

    for i in range(NT):
        p.op("q", lambda e, i=i: e.dma_start(out=av[:, a_o:a_o + 1024],
                                             in_=ds["xn2f"][i * 128:(i + 1) * 128, :]),
             reads=["a_o", ("x2d", i)], writes=["a_o"], ch="lx2")
        p.op("v", lambda e, i=i: e.tensor_add(av[:, a_x:a_x + 1024],
                                              ab[:, b_g2 + i * 1024:b_g2 + (i + 1) * 1024],
                                              av[:, a_o:a_o + 1024]),
             reads=["b_g2", "a_o"], writes=["a_x"])
        emit_rowadd(ctx, av[:, a_x:a_x + 1024], av[0:1, a_b2w:a_b2w + 1024].bitcast(F32)
                    if False else av[0:1, a_b2w:a_b2w + 1024], "a_x", "a_b2w", PS, p, f32=True)
        ln_tok(ctx, "a_x", av[:, a_x:a_x + 1024], av[:, a_sq:a_sq + 1024],
               av[:, a_o:a_o + 1024], "a_o", av[:, a_st:a_st + 16],
               wb=(ab[:, b_gb:b_gb + 1024], ab[:, b_gb + 1024:b_gb + 2048],
                   "b_gb", "b_gb"))
        p.op("s", lambda e: e.activation(out=av[:, a_o:a_o + 1024], in_=av[:, a_o:a_o + 1024],
                                         func=AF.Silu),
             reads=["a_o"], writes=["a_o"])
        p.op("q", lambda e, i=i: e.dma_start(out=av[:, a_x:a_x + 1024],
                                             in_=ds["s2"][i * 128:(i + 1) * 128, :]),
             reads=["a_x", ("s2d", i)], writes=["a_x"], ch="ls2")
        p.op("v", lambda e: e.tensor_mul(av[:, a_o:a_o + 1024], av[:, a_o:a_o + 1024],
                                         av[:, a_x:a_x + 1024]),
             reads=["a_o", "a_x"], writes=["a_o"])
        p.op("q", lambda e, i=i: e.dma_start(out=av[:, a_x:a_x + 1024],
                                             in_=ds["x1"][i * 128:(i + 1) * 128, :]),
             reads=["a_x", ("x1d", i)], writes=["a_x"], ch="lx1b")
        p.op("v", lambda e: e.tensor_add(av[:, a_o:a_o + 1024], av[:, a_o:a_o + 1024],
                                         av[:, a_x:a_x + 1024]),
             reads=["a_o", "a_x"], writes=["a_o"])
        p.op("q", lambda e, i=i: e.dma_start(out=OUT[i * 128:(i + 1) * 128, :],
                                             in_=av[:, a_o:a_o + 1024]),
             reads=["a_o"], writes=[("outd", i)], ch="stout")


def emit_rowadd(ctx, dst_ap, row_ap, dst_key, row_key, PS, p, f32=False):
    ones = ctx["sb_ones"] if f32 else ctx["sb_onesb"]
    okey = "sb_ones" if f32 else "sb_onesb"
    for half in range(2):
        p.op("t", lambda e, half=half, ones=ones: e.matmul(
            PS[3][:, 0:512], ones[0:1, :],
            row_ap[0:1, half * 512:(half + 1) * 512], start=True, stop=True),
            reads=[row_key, okey, ("PS", 3)], writes=[("PS", 3)])
        p.op("v", lambda e, half=half: e.tensor_add(
            dst_ap[:, half * 512:(half + 1) * 512], dst_ap[:, half * 512:(half + 1) * 512],
            PS[3][:, 0:512]),
            reads=[dst_key, ("PS", 3)], writes=[dst_key])


def emit_rope(ctx, p, av, a_x, a_sc, a_cc, i, key):
    AF32 = ctx["AF32"]
    ALU_ = ALU
    if "_rope_scr" not in ctx:
        ctx["_rope_scr"] = AF32.alloc(1024)
        ctx["_rope_scr2"] = AF32.alloc(1024)
    a_r1, a_r2 = ctx["_rope_scr"], ctx["_rope_scr2"]
    x3 = av[:, a_x:a_x + 1024].rearrange("p (h d) -> p h d", d=64)
    r3 = av[:, a_r1:a_r1 + 1024].rearrange("p (h d) -> p h d", d=64)
    q3 = av[:, a_r2:a_r2 + 1024].rearrange("p (h d) -> p h d", d=64)
    sin = av[:, a_sc + i * 32:a_sc + (i + 1) * 32].unsqueeze(1).to_broadcast([128, 16, 32])
    cos = av[:, a_cc + i * 32:a_cc + (i + 1) * 32].unsqueeze(1).to_broadcast([128, 16, 32])
    p.op("v", lambda e: e.tensor_mul(r3[:, :, 0:32], x3[:, :, 0:32], cos),
         reads=[key, "a_cc"], writes=["_rope"])
    p.op("v", lambda e: e.tensor_mul(r3[:, :, 32:64], x3[:, :, 0:32], sin),
         reads=[key, "a_sc"], writes=["_rope"])
    p.op("v", lambda e: e.tensor_mul(q3[:, :, 0:32], x3[:, :, 32:64], sin),
         reads=[key, "a_sc"], writes=["_rope2"])
    p.op("v", lambda e: e.tensor_mul(q3[:, :, 32:64], x3[:, :, 32:64], cos),
         reads=[key, "a_cc"], writes=["_rope2"])
    p.op("v", lambda e: e.tensor_sub(x3[:, :, 0:32], r3[:, :, 0:32], q3[:, :, 0:32]),
         reads=["_rope", "_rope2", key], writes=[key])
    p.op("v", lambda e: e.tensor_add(x3[:, :, 32:64], r3[:, :, 32:64], q3[:, :, 32:64]),
         reads=["_rope", "_rope2", key], writes=[key])



def _prep_vec(v):
    return np.ascontiguousarray(np.asarray(v, np.float32).reshape(NPC, 128).T)


def make_in_maps(inputs):
    d = {k: np.asarray(v) for k, v in inputs.items()}
    base = {}
    pfx = "L0_attn_"
    for nm in ["x_r", "x_w", "x_k", "x_v", "x_a", "x_g", "w0",
               "a0", "k_k", "k_a", "ffn_x_k", "gn_w", "gn_b"]:
        base["p_" + nm] = _prep_vec(d[pfx + nm])
    for nm in ["ln_pre_w", "ln_pre_b", "ln_attn_w", "ln_attn_b", "ln_ffn_w",
               "ln_ffn_b"]:
        base["row_" + nm] = np.asarray(d[pfx + nm], np.float32).reshape(1, C)
    base["r_k"] = np.ascontiguousarray(
        d[pfx + "r_k"].reshape(8, 2, 64).transpose(1, 2, 0).reshape(128, 8))
    for nm in ["W_r", "W_k", "W_v", "W_o", "w1", "w2", "a1", "a2", "g1", "g2",
               "W_ffn1", "W_ffn2"]:
        base[nm] = np.ascontiguousarray(d[pfx + nm], np.float32)
    base["gate_wT"] = np.ascontiguousarray(d["L0_ffn_gate_w"].T.astype(np.float32))
    base["w_gT"] = np.ascontiguousarray(d["L0_ffn_w_gate"].transpose(0, 2, 1))
    base["w_uT"] = np.ascontiguousarray(d["L0_ffn_w_up"].transpose(0, 2, 1))
    base["w_dn"] = np.ascontiguousarray(d["L0_ffn_w_down"].transpose(0, 2, 1))
    base["Wq"] = np.ascontiguousarray(d["L1_attn_Wq"], np.float32)
    base["Wdown"] = np.ascontiguousarray(d["L1_attn_Wdown"], np.float32)
    perm = ([h * 128 + j for h in range(16) for j in range(64)] +
            [h * 128 + 64 + j for h in range(16) for j in range(64)])
    base["Wup"] = np.ascontiguousarray(d["L1_attn_Wup"][:, perm], np.float32)
    base["Wout"] = np.ascontiguousarray(d["L1_attn_Wout"], np.float32)
    base["bq"] = np.ascontiguousarray(d["L1_attn_bq"].reshape(1, C), np.float32)
    base["bdown"] = np.ascontiguousarray(d["L1_attn_bdown"].reshape(2, 128).T, np.float32)
    base["bup"] = np.ascontiguousarray(
        np.asarray(d["L1_attn_bup"], np.float32)[perm].reshape(1, 2 * C))
    base["bout"] = np.ascontiguousarray(d["L1_attn_bout"].reshape(1, C), np.float32)
    base["Wr1"] = np.ascontiguousarray(d["L1_ffn_Wr"], np.float32)
    base["br1"] = np.ascontiguousarray(d["L1_ffn_br"].reshape(8, 1), np.float32)
    base["f_W1r"] = np.ascontiguousarray(
        d["L1_ffn_W1"].reshape(8, 8, 128, 32, 128).transpose(0, 3, 2, 1, 4)
        .reshape(32768, 1024), np.float32)
    base["f_b1"] = np.ascontiguousarray(
        d["L1_ffn_b1"].reshape(8, 32, 128).transpose(2, 0, 1).reshape(128, 8 * 32))
    base["f_W2"] = np.ascontiguousarray(d["L1_ffn_W2"], np.float32)
    base["f_b2"] = np.ascontiguousarray(d["L1_ffn_b2"].reshape(1, 8 * C), np.float32)
    base["gamma"] = np.ascontiguousarray(d["L1_ffn_gamma"].reshape(1, C), np.float32)
    base["beta"] = np.ascontiguousarray(d["L1_ffn_beta"].reshape(1, C), np.float32)
    x = np.asarray(d["L0_attn_x"], np.float32)
    maps = []
    for b in range(8):
        m = dict(base)
        m["X"] = np.ascontiguousarray(x[b])
        maps.append(m)
    return maps


def kernel(**inputs):
    nc = build_program()
    in_maps = make_in_maps(inputs)
    res = run_bass_kernel_spmd(nc, in_maps, core_ids=list(range(8)))
    return np.stack([r["OUT"] for r in res.results], 0).astype(np.float32)


# version 08 (final)
# speedup vs baseline: 1.9716x; 1.0257x over previous
import numpy as np
import ml_dtypes
import concourse.bass as bass
import concourse.mybir as mybir
from concourse.bass_utils import run_bass_kernel_spmd
from concourse.bass_core_rust import IndirectOffsetOnAxis

F32 = mybir.dt.float32
BF16 = mybir.dt.bfloat16
I32 = mybir.dt.int32
AF = mybir.ActivationFunctionType
ALU = mybir.AluOpType
AX = mybir.AxisListType

T, C, H, N = 2048, 1024, 16, 64
NT = T // 128
NPC = C // 128
LCH = 128
NCH = T // LCH
BLK = 512
NBLK = T // BLK
LAM = float(np.exp(-0.5))
EPS_LN = 1e-5
EPS_GN = 0.00064

DEBUG_SUB = None

ENGINES = ("t", "v", "s", "g", "q")
ENG_MAP = {"t": "tensor", "v": "vector", "s": "scalar", "g": "gpsimd", "q": "sync"}


class Prog:

    def __init__(self, nc):
        self.nc = nc
        self.ops = {e: [] for e in ENGINES}
        self.cnt = {e: 0 for e in ENGINES}
        self.chcnt = {}
        self.last_w = {}
        self.readers = {}

    def _dep(self, stamp, deps):
        kind, ident, val = stamp
        key = (kind, ident)
        if kind == "c":
            val = self.chcnt[ident]
        deps[key] = max(deps.get(key, 0), val)

    def op(self, e, emit, reads=(), writes=(), ch=None):
        deps = dict(getattr(self, "bar", {}))
        for k in reads:
            w = self.last_w.get(k)
            if w is not None:
                self._dep(w, deps)
        for k in writes:
            w = self.last_w.get(k)
            if w is not None:
                self._dep(w, deps)
            for r in self.readers.get(k, ()):
                self._dep(r, deps)
        if ch is None:
            self.cnt[e] += 1
            stamp = ("e", e, self.cnt[e])
        else:
            self.chcnt[ch] = self.chcnt.get(ch, 0) + 16
            stamp = ("c", ch, self.chcnt[ch])
        self.ops[e].append((emit, deps, ch))
        for k in reads:
            self.readers.setdefault(k, []).append(stamp)
        for k in writes:
            self.last_w[k] = stamp
            self.readers[k] = []
        return stamp

    def barrier(self):
        self.bar = {("e", e): self.cnt[e] for e in ENGINES if self.cnt[e]}
        self.bar.update({("c", c): v for c, v in self.chcnt.items()})

    def replay(self):
        nc = self.nc
        import contextlib
        stack = contextlib.ExitStack()
        esems = {e: stack.enter_context(nc.semaphore(f"pe_{e}")) for e in ENGINES}
        csems = {c: stack.enter_context(nc.semaphore(f"pc_{i}"))
                 for i, c in enumerate(self.chcnt)}

        def semof(key):
            kind, ident = key
            return esems[ident] if kind == "e" else csems[ident]

        with nc.Block() as block:
            for e in ENGINES:
                if not self.ops[e]:
                    continue

                def body(eng, e=e, oplist=self.ops[e]):
                    waited = {}
                    for emit, deps, ch in oplist:
                        for key, v in sorted(deps.items(), key=lambda kv: str(kv[0])):
                            if waited.get(key, 0) < v:
                                eng.wait_ge(semof(key), v)
                                waited[key] = v
                        inst = emit(eng)
                        inst.then_inc(esems[e] if ch is None else csems[ch],
                                      1 if ch is None else 16)
                getattr(block, ENG_MAP[e])(body)
        with nc.Block() as block:
            def fin(eng):
                for e in ENGINES:
                    if self.cnt[e]:
                        eng.wait_ge(esems[e], self.cnt[e])
                for c, v in self.chcnt.items():
                    eng.wait_ge(csems[c], v)
            block.sync(fin)
        stack.close()


class Arena:

    def __init__(self, nc, name, cols, dtype):
        self.t = nc.alloc_sbuf_tensor(name, [128, cols], dtype)
        self.name = name
        self.cols = cols
        self.free_at = 0
        self.marks = {}

    def alloc(self, n):
        a = self.free_at
        assert a + n <= self.cols, f"{self.name} overflow: {a}+{n} > {self.cols}"
        self.free_at += n
        return a

    def mark(self, tag):
        self.marks[tag] = self.free_at

    def reset(self, tag):
        self.free_at = self.marks[tag]


def ceil_div(a, b):
    return (a + b - 1) // b



def build_program(debug_stop=None):
    nc = bass.Bass("TRN2", target_bir_lowering=False)
    p = Prog(nc)

    di = {}

    def inp(name, shape, dtype=F32):
        di[name] = nc.dram_tensor(name, list(shape), dtype, kind="ExternalInput").ap()
        return di[name]

    inp("X", [T, C])
    for nm in ["x_r", "x_w", "x_k", "x_v", "x_a", "x_g", "w0",
               "a0", "k_k", "k_a", "ffn_x_k", "gn_w", "gn_b"]:
        inp("p_" + nm, [128, NPC])
    for nm in ["ln_pre_w", "ln_pre_b", "ln_attn_w", "ln_attn_b", "ln_ffn_w",
               "ln_ffn_b"]:
        inp("row_" + nm, [1, C])
    inp("r_k", [128, 8])
    for nm, shp in [("W_r", [C, C]), ("W_k", [C, C]), ("W_v", [C, C]),
                    ("W_o", [C, C]), ("w1", [C, 64]), ("w2", [64, C]),
                    ("a1", [C, 64]), ("a2", [64, C]), ("g1", [C, 128]),
                    ("g2", [128, C]), ("W_ffn1", [C, 4 * C]), ("W_ffn2", [4 * C, C])]:
        inp(nm, shp)
    inp("gate_wT", [C, 8])
    inp("w_gT", [8, C, 4096])
    inp("w_uT", [8, C, 4096])
    inp("w_dn", [8, 4096, C])
    for nm, shp in [("Wq", [C, C]), ("Wdown", [C, 256]), ("Wup", [256, 2 * C]),
                    ("Wout", [C, C]), ("Wr1", [C, 8])]:
        inp(nm, shp)
    inp("bq", [1, C])
    inp("bdown", [128, 2])
    inp("bup", [1, 2 * C])
    inp("bout", [1, C])
    inp("br1", [8, 1])
    inp("f_W1r", [32768, 1024])
    inp("f_b1", [128, 8 * 32])
    inp("f_W2", [8, 4096, C])
    inp("f_b2", [1, 8 * C])
    inp("gamma", [1, C])
    inp("beta", [1, C])

    OUT = nc.dram_tensor("OUT", [T, C], F32, kind="ExternalOutput").ap()

    ds = {}
    for nm, shape, dt in [("xn_bf", [T, C], BF16), ("xnaT", [C, T], BF16),
                          ("x0f", [T, C], F32), ("xmid", [T, C], F32),
                          ("sattn", [T, C], F32),
                          ("g0a", [T, C // 2], F32), ("g0b", [T, C // 2], F32),
                          ("x1", [T, C], F32), ("xn2f", [T, C], F32),
                          ("wm", [8, T], F32),
                          ("idxw0", [4097, 2], F32), ("idxw1", [4097, 2], F32),
                          ("idxw2", [4097, 2], F32), ("idxw3", [4097, 2], F32),
                          ("idxw4", [4097, 2], F32), ("idxw5", [4097, 2], F32),
                          ("idxw6", [4097, 2], F32), ("idxw7", [4097, 2], F32),
                          ("q_tok", [T, C], BF16), ("k_tok", [T, C], BF16),
                          ("v_tok", [T, C], BF16), ("o_grp", [T * 16, N], BF16),
                          ("s2", [T, C], F32)]:
        ds[nm] = nc.dram_tensor("scr_" + nm, list(shape), dt, kind="Internal").ap()

    ident_b = nc.inline_tensor(np.eye(128, dtype=ml_dtypes.bfloat16), "ident_b")
    id64_2 = nc.inline_tensor(
        np.concatenate([np.eye(64), np.eye(64)], 0).astype(ml_dtypes.bfloat16), "id64_2")
    mus_np = (np.arange(128)[:, None] < np.arange(128)[None, :]).astype(np.float32)
    mui_np = (np.arange(128)[:, None] <= np.arange(128)[None, :]).astype(np.float32)
    c_mus = nc.inline_tensor(mus_np.astype(ml_dtypes.bfloat16), "c_mus")
    c_mui = nc.inline_tensor(mui_np.astype(ml_dtypes.bfloat16), "c_mui")
    c_mls = nc.inline_tensor(mus_np.T.copy().astype(ml_dtypes.bfloat16), "c_mls")
    c_idf = nc.inline_tensor(np.eye(128, dtype=np.float32), "c_idf")
    ones_col = nc.inline_tensor(np.ones((1, 128), np.float32), "ones_col")
    ones_col_b = nc.inline_tensor(np.ones((1, 128), ml_dtypes.bfloat16), "ones_col_b")
    ind2 = np.zeros((128, 2), np.float32)
    ind2[0:64, 0] = 1.0
    ind2[64:128, 1] = 1.0
    c_ind2 = nc.inline_tensor(ind2.astype(ml_dtypes.bfloat16), "c_ind2")
    ind2T_ext = np.zeros((128, 128), np.float32)
    for rb in (0, 32, 64):
        ind2T_ext[rb + 0, 0:64] = 1.0
        ind2T_ext[rb + 1, 64:128] = 1.0
    c_ind2T = nc.inline_tensor(ind2T_ext.astype(ml_dtypes.bfloat16), "c_ind2T")
    c_ind2Tf = nc.inline_tensor(ind2T_ext, "c_ind2Tf")
    invf = (1.0 / 10000.0 ** (np.arange(0, 64, 2) / 64.0)).astype(np.float32)
    c_invf = nc.inline_tensor(np.broadcast_to(invf, (128, 32)).copy(), "c_invf")

    AF32 = Arena(nc, "AF32", 18000, F32)
    AB16 = Arena(nc, "AB16", 66944, BF16)
    sb_par = nc.alloc_sbuf_tensor("sb_par", [128, 32 * 20], F32)
    sb_id = nc.alloc_sbuf_tensor("sb_id", [128, 128], BF16)
    sb_id64 = nc.alloc_sbuf_tensor("sb_id64", [128, 64], BF16)
    sb_msk = nc.alloc_sbuf_tensor("sb_msk", [128, 3 * 128], BF16)
    sb_ones = nc.alloc_sbuf_tensor("sb_ones", [1, 128], F32)
    sb_onesb = nc.alloc_sbuf_tensor("sb_onesb", [1, 128], BF16)
    sb_ind2 = nc.alloc_sbuf_tensor("sb_ind2", [128, 2], BF16)
    sb_ind2T = nc.alloc_sbuf_tensor("sb_ind2T", [128, 128], BF16)
    sb_ind2Tf = nc.alloc_sbuf_tensor("sb_ind2Tf", [128, 128], F32)
    sb_invf = nc.alloc_sbuf_tensor("sb_invf", [128, 32], F32)
    sb_idf = nc.alloc_sbuf_tensor("sb_idf", [128, 128], F32)
    sb_gw = nc.alloc_sbuf_tensor("sb_gw", [128, 64], F32)

    PS = [nc.alloc_psum_tensor(f"PS{i}", [128, 512], F32) for i in range(6)]
    PB = [nc.alloc_psum_tensor(f"PBb{i}", [128, 1024], BF16) for i in range(2)]

    ctx = dict(nc=nc, p=p, di=di, ds=ds, OUT=OUT, AF32=AF32, AB16=AB16,
               sb_idf=sb_idf, sb_gw=sb_gw,
               sb_par=sb_par, sb_id=sb_id, sb_id64=sb_id64, sb_msk=sb_msk,
               sb_ones=sb_ones, sb_onesb=sb_onesb, sb_ind2=sb_ind2,
               sb_ind2T=sb_ind2T, sb_ind2Tf=sb_ind2Tf, sb_invf=sb_invf, PS=PS, PB=PB)

    p.op("q", lambda e: e.dma_start(out=sb_id[:], in_=ident_b.ap()), writes=["sb_id"], ch="lc")
    p.op("q", lambda e: e.dma_start(out=sb_id64[:], in_=id64_2.ap()), writes=["sb_id64"], ch="lc")
    p.op("q", lambda e: e.dma_start(out=sb_msk[:, 0:128], in_=c_mus.ap()), writes=["msk_us"], ch="lc")
    p.op("q", lambda e: e.dma_start(out=sb_msk[:, 128:256], in_=c_mui.ap()), writes=["msk_ui"], ch="lc")
    p.op("q", lambda e: e.dma_start(out=sb_msk[:, 256:384], in_=c_mls.ap()), writes=["msk_ls"], ch="lc")
    p.op("q", lambda e: e.dma_start(out=sb_ones[:], in_=ones_col.ap()), writes=["sb_ones"], ch="lc")
    p.op("q", lambda e: e.dma_start(out=sb_onesb[:], in_=ones_col_b.ap()), writes=["sb_onesb"], ch="lc")
    p.op("q", lambda e: e.dma_start(out=sb_ind2[:], in_=c_ind2.ap()), writes=["sb_ind2"], ch="lc")
    p.op("q", lambda e: e.dma_start(out=sb_ind2T[:], in_=c_ind2T.ap()), writes=["sb_ind2T"], ch="lc")
    p.op("q", lambda e: e.dma_start(out=sb_ind2Tf[:], in_=c_ind2Tf.ap()), writes=["sb_ind2Tf"], ch="lc")
    p.op("q", lambda e: e.dma_start(out=sb_invf[:], in_=c_invf.ap()), writes=["sb_invf"], ch="lc")
    p.op("q", lambda e: e.dma_start(out=sb_idf[:], in_=c_idf.ap()), writes=["sb_idf"], ch="lc")

    emit_phase01(ctx)
    if debug_stop == "p1":
        p.replay()
        nc.finalize()
        return nc
    emit_phase2345(ctx)
    if debug_stop == "p5":
        p.replay()
        nc.finalize()
        return nc
    emit_l0ffn(ctx)
    emit_combine0(ctx)
    if debug_stop == "p6":
        p.replay()
        nc.finalize()
        return nc
    emit_layer1(ctx)
    p.replay()
    nc.finalize()
    return nc



def ln_tok(ctx, key_in, xin_ap, scratch_a, out_ap, key_out, stat_a, wb=None,
           eps=EPS_LN, tag="", scratch_key="a_sq", stat_key="a_st"):
    p = ctx["p"]
    kS = scratch_key
    p.op("s", lambda e: e.activation(out=scratch_a, in_=xin_ap, func=AF.Square),
         reads=[key_in], writes=[kS])
    p.op("v", lambda e: e.tensor_reduce(out=stat_a[:, 0:1], in_=xin_ap, axis=AX.X, op=ALU.add),
         reads=[key_in], writes=[(stat_key, 0)])
    p.op("v", lambda e: e.tensor_reduce(out=stat_a[:, 1:2], in_=scratch_a, axis=AX.X, op=ALU.add),
         reads=[kS], writes=[(stat_key, 1)])
    p.op("v", lambda e: e.tensor_scalar_mul(stat_a[:, 2:3], stat_a[:, 0:1], 1.0 / C),
         reads=[(stat_key, 0)], writes=[(stat_key, 2)])
    p.op("s", lambda e: e.activation(out=stat_a[:, 3:4], in_=stat_a[:, 2:3], func=AF.Square),
         reads=[(stat_key, 2)], writes=[(stat_key, 3)])
    p.op("v", lambda e: e.tensor_scalar(stat_a[:, 4:5], stat_a[:, 1:2],
                                        1.0 / C, stat_a[:, 3:4],
                                        op0=ALU.mult, op1=ALU.subtract),
         reads=[(stat_key, 1), (stat_key, 3)], writes=[(stat_key, 4)])
    p.op("v", lambda e: e.tensor_scalar_add(stat_a[:, 7:8], stat_a[:, 4:5], float(eps)),
         reads=[(stat_key, 4)], writes=[(stat_key, 7)])
    p.op("s", lambda e: e.activation(out=stat_a[:, 5:6], in_=stat_a[:, 7:8], func=AF.Ln),
         reads=[(stat_key, 7)], writes=[(stat_key, 5)])
    p.op("s", lambda e: e.activation(out=stat_a[:, 6:7], in_=stat_a[:, 5:6],
                                     func=AF.Exp, scale=-0.5),
         reads=[(stat_key, 5)], writes=[(stat_key, 6)])
    p.op("v", lambda e: e.tensor_scalar(out_ap, xin_ap, stat_a[:, 2:3], stat_a[:, 6:7],
                                        op0=ALU.subtract, op1=ALU.mult),
         reads=[key_in, (stat_key, 2), (stat_key, 6)], writes=[key_out])
    if wb is not None:
        w_ap, b_ap, keyw, keyb = wb
        p.op("v", lambda e: e.tensor_mul(out_ap, out_ap, w_ap),
             reads=[key_out, keyw], writes=[key_out])
        p.op("v", lambda e: e.tensor_add(out_ap, out_ap, b_ap),
             reads=[key_out, keyb], writes=[key_out])


def bcast_row(ctx, row_ap, key_row, psum_idx, out_ap, key_out, n):
    p, PS, sb_ones = ctx["p"], ctx["PS"], ctx["sb_ones"]
    ps = PS[psum_idx]
    assert n <= 512
    p.op("t", lambda e: e.matmul(ps[:, 0:n], sb_ones[:], row_ap, start=True, stop=True),
         reads=[key_row, "sb_ones"], writes=[("PSb", psum_idx)])
    p.op("v", lambda e: e.tensor_copy(out=out_ap, in_=ps[:, 0:n]),
         reads=[("PSb", psum_idx)], writes=[key_out])



def emit_phase01(ctx):
    nc, p, di, ds = ctx["nc"], ctx["p"], ctx["di"], ctx["ds"]
    AF32, AB16, PS, PB = ctx["AF32"], ctx["AB16"], ctx["PS"], ctx["PB"]
    sb_id = ctx["sb_id"]
    av, ab = AF32.t, AB16.t

    sb_par = ctx["sb_par"]
    ctx["par_off"] = {}
    off = 0
    for nm in ["x_r", "x_w", "x_k", "x_v", "x_a", "x_g", "w0",
               "a0", "k_k", "k_a", "ffn_x_k", "gn_w", "gn_b"]:
        ctx["par_off"][nm] = off
        p.op("q", lambda e, nm=nm, o=off: e.dma_start(out=sb_par[:, o:o + NPC], in_=di["p_" + nm]),
             writes=[("par", nm)], ch="lpar")
        off += NPC
    ctx["par_off"]["r_k"] = off
    p.op("q", lambda e, o=off: e.dma_start(out=sb_par[:, o:o + 8], in_=di["r_k"]),
         writes=[("par", "r_k")], ch="lpar")
    off += 8
    ctx["par_off"]["fb1"] = off
    p.op("q", lambda e, o=off: e.dma_start(out=sb_par[:, o:o + 256], in_=di["f_b1"]),
         writes=[("par", "fb1")], ch="lpar")
    off += 256
    ctx["par_off"]["ka1m"] = off
    oka = ctx["par_off"]["k_a"]
    p.op("v", lambda e, o=off, oka=oka: e.tensor_scalar(sb_par[:, o:o + NPC],
                                                        sb_par[:, oka:oka + NPC],
                                                        -1.0, 1.0, op0=ALU.mult, op1=ALU.add),
         reads=[("par", "k_a")], writes=[("par", "ka1m")])
    off += NPC

    a_lnb = AB16.alloc(6 * 1024)
    ctx["a_lnb"] = a_lnb
    for i, nm in enumerate(["ln_pre_w", "ln_pre_b", "ln_attn_w", "ln_attn_b",
                            "ln_ffn_w", "ln_ffn_b"]):
        p.op("g", lambda e, nm=nm, i=i: e.dma_start(
            out=ab[:, a_lnb + i * 1024:a_lnb + (i + 1) * 1024],
            in_=di["row_" + nm].to_broadcast([128, C])),
            writes=[("lnb", nm)], ch="llnb")

    sb_gw = ctx["sb_gw"]
    for c in range(NPC):
        p.op("q", lambda e, c=c: e.dma_start(out=sb_gw[:, c * 8:(c + 1) * 8],
                                             in_=di["gate_wT"][c * 128:(c + 1) * 128, :]),
             writes=[("gwT", c)], ch="lgw")

    a_logits = AF32.alloc(T)
    logitsT = av[:, a_logits:a_logits + T]
    ctx["logitsT"] = logitsT
    AF32.mark("keep")
    AB16.mark("keep")

    a_x = AF32.alloc(1024)
    a_sq = AF32.alloc(1024)
    a_xn = AF32.alloc(1024)
    a_x0 = AF32.alloc(1024)
    a_st = AF32.alloc(16)
    b_xn = AB16.alloc(1024)
    b_tr = AB16.alloc(128)
    a_tr = AF32.alloc(128)
    lnb = ctx["a_lnb"]
    sb_idf = ctx["sb_idf"]

    for i in range(NT):
        p.op("q", lambda e, i=i: e.dma_start(out=av[:, a_x:a_x + 1024],
                                             in_=di["X"][i * 128:(i + 1) * 128, :]),
             writes=["a_x"], ch="ldx")
        ln_tok(ctx, "a_x", av[:, a_x:a_x + 1024], av[:, a_sq:a_sq + 1024],
               av[:, a_xn:a_xn + 1024], "a_xn", av[:, a_st:a_st + 16], tag="p1")
        p.op("v", lambda e: e.tensor_copy(out=ab[:, b_xn:b_xn + 1024], in_=av[:, a_xn:a_xn + 1024]),
             reads=["a_xn"], writes=["b_xn"])
        p.op("q", lambda e, i=i: e.dma_start(out=ds["xn_bf"][i * 128:(i + 1) * 128, :],
                                             in_=ab[:, b_xn:b_xn + 1024]),
             reads=["b_xn"], writes=[("xnbd", i)], ch="stxn")
        for c in range(NPC):
            p.op("t", lambda e, c=c: e.transpose(PS[4][:, 0:128],
                                                 av[:, a_xn + c * 128:a_xn + (c + 1) * 128],
                                                 sb_idf[:]),
                 reads=["a_xn", "sb_idf", ("PS", 4)], writes=[("PS", 4)])
            p.op("v", lambda e: e.tensor_copy(out=av[:, a_tr:a_tr + 128], in_=PS[4][:, 0:128]),
                 reads=[("PS", 4)], writes=["a_tr"])
            p.op("t", lambda e, c=c: e.matmul(PS[5][0:8, 0:128],
                                              sb_gw[:, c * 8:(c + 1) * 8],
                                              av[:, a_tr:a_tr + 128],
                                              start=(c == 0), stop=(c == NPC - 1)),
                 reads=[("gwT", c), "a_tr", "PS5"], writes=["PS5"])
        p.op("v", lambda e, i=i: e.tensor_copy(out=logitsT[0:8, i * 128:(i + 1) * 128],
                                               in_=PS[5][0:8, 0:128]),
             reads=["PS5"], writes=[("lgT", i)])

        ln_tok(ctx, "a_xn", av[:, a_xn:a_xn + 1024], av[:, a_sq:a_sq + 1024],
               av[:, a_x0:a_x0 + 1024], "a_x0", av[:, a_st:a_st + 16],
               wb=(ab[:, lnb:lnb + 1024], ab[:, lnb + 1024:lnb + 2048],
                   ("lnb", "ln_pre_w"), ("lnb", "ln_pre_b")), tag="p1")
        p.op("q", lambda e, i=i: e.dma_start(out=ds["x0f"][i * 128:(i + 1) * 128, :],
                                             in_=av[:, a_x0:a_x0 + 1024]),
             reads=["a_x0"], writes=[("x0d", i)], ch="stx0")
        ln_tok(ctx, "a_x0", av[:, a_x0:a_x0 + 1024], av[:, a_sq:a_sq + 1024],
               av[:, a_xn:a_xn + 1024], "a_xn", av[:, a_st:a_st + 16],
               wb=(ab[:, lnb + 2048:lnb + 3072], ab[:, lnb + 3072:lnb + 4096],
                   ("lnb", "ln_attn_w"), ("lnb", "ln_attn_b")), tag="p1")
        p.op("v", lambda e: e.tensor_copy(out=ab[:, b_xn:b_xn + 1024], in_=av[:, a_xn:a_xn + 1024]),
             reads=["a_xn"], writes=["b_xn"])
        for c in range(NPC):
            p.op("t", lambda e, c=c: e.transpose(PB[0][:, 0:128],
                                                 ab[:, b_xn + c * 128:b_xn + (c + 1) * 128],
                                                 sb_id[:]),
                 reads=["b_xn", "sb_id"], writes=["PB0"])
            p.op("v", lambda e: e.tensor_copy(out=ab[:, b_tr:b_tr + 128], in_=PB[0][:, 0:128]),
                 reads=["PB0"], writes=["b_tr"])
            p.op("q", lambda e, c=c, i=i: e.dma_start(
                out=ds["xnaT"][c * 128:(c + 1) * 128, i * 128:(i + 1) * 128],
                in_=ab[:, b_tr:b_tr + 128]),
                reads=["b_tr"], writes=[("xnaTd", c, i)], ch="stxT")


def emit_phase2345(ctx):
    nc, p, di, ds = ctx["nc"], ctx["p"], ctx["di"], ctx["ds"]
    AF32, AB16, PS, PB = ctx["AF32"], ctx["AB16"], ctx["PS"], ctx["PB"]
    av, ab = AF32.t, AB16.t
    par, po = ctx["sb_par"], ctx["par_off"]
    sb_id, sb_id64 = ctx["sb_id"], ctx["sb_id64"]
    sb_msk, sb_ind2, sb_ind2T = ctx["sb_msk"], ctx["sb_ind2"], ctx["sb_ind2T"]
    lnb = ctx["a_lnb"]

    p.barrier()
    AF32.reset("keep")
    AB16.reset("keep")

    def parap(nm, c, n=1):
        o = po[nm]
        return par[:, o + c:o + c + n]

    b_xa = AB16.alloc(8 * 514)
    b_d1 = AB16.alloc(8 * 512)
    b_var = AB16.alloc(8 * 512)
    b_wA = AB16.alloc(2048)
    b_wB = AB16.alloc(2048)
    b_sm = AB16.alloc(3072)
    o_w1, o_a1, o_g1 = b_sm, b_sm + 512, b_sm + 1024
    o_lo2 = b_sm + 2048
    b_rT = AB16.alloc(8 * 512)
    b_kT = AB16.alloc(8 * 512)
    b_aS = AB16.alloc(8 * 512)
    b_kkn = AB16.alloc(8 * 512)
    b_gg = AB16.alloc(8 * 512)
    b_vt = AB16.alloc(4 * 1024)
    b_vT = AB16.alloc(8 * 512)
    b_der = AB16.alloc(4 * 1024)
    b_xfT = AB16.alloc(8 * 514)
    b_mid = b_wB
    b_BTs = [AB16.alloc(7 * 128) for _ in range(2)]
    b_Bc = [AB16.alloc(2 * 128) for _ in range(2)]
    b_KTm = [AB16.alloc(128) for _ in range(2)]
    b_Pbm = [AB16.alloc(128) for _ in range(2)]
    b_Pkm = [AB16.alloc(128) for _ in range(2)]
    b_Wc = [AB16.alloc(2 * 64) for _ in range(2)]
    b_Btok = [AB16.alloc(64) for _ in range(2)]
    b_Ktok = [AB16.alloc(64) for _ in range(2)]
    b_stbf = AB16.alloc(8 * 64)

    a_sig = AF32.alloc(8 * 512)
    a_gsc = AF32.alloc(516)
    a_gr = AF32.alloc(8 * 512)
    a_xm = AF32.alloc(4 * 1024)
    a_scr = AF32.alloc(1024)
    a_scr2 = AF32.alloc(1024)
    a_st = AF32.alloc(16)
    a_state = AF32.alloc(8 * 64)
    a_stsc = AF32.alloc(8 * 64)
    a_gcol = AF32.alloc(8)

    wbufs = [(b_wA, "chWA", "b_wA"), (b_wB, "chWB", "b_wB")]
    nwb = [0]

    def load_w(dram_ap, rows, cols, tag):
        boff, chn, keyn = wbufs[nwb[0] % 2]
        nwb[0] += 1
        nch = rows // 128
        assert nch * cols <= 2048
        src = dram_ap.rearrange("(c p) j -> p c j", p=128)
        p.op("g", lambda e, boff=boff, nch=nch, cols=cols, src=src: e.dma_start(
            out=ab[:, boff:boff + nch * cols].rearrange("p (c j) -> p c j", c=nch), in_=src),
            reads=[keyn], writes=[keyn], ch=chn)
        return boff, keyn

    p.op("g", lambda e: e.dma_start(out=ab[:, o_w1:o_w1 + 512].rearrange("p (c j) -> p c j", c=8),
                                    in_=di["w1"].rearrange("(c p) j -> p c j", p=128)),
         writes=["sm_w1"], ch="lsm")
    p.op("g", lambda e: e.dma_start(out=ab[:, o_a1:o_a1 + 512].rearrange("p (c j) -> p c j", c=8),
                                    in_=di["a1"].rearrange("(c p) j -> p c j", p=128)),
         writes=["sm_a1"], ch="lsm")
    p.op("g", lambda e: e.dma_start(out=ab[:, o_g1:o_g1 + 1024].rearrange("p (c j) -> p c j", c=8),
                                    in_=di["g1"].rearrange("(c p) j -> p c j", p=128)),
         writes=["sm_g1"], ch="lsm")

    p.op("v", lambda e: e.memset(av[:, a_state:a_state + 512], 0.0),
         writes=[("state", pcx, hx) for pcx in range(8) for hx in range(2)])
    p.op("v", lambda e: e.memset(ab[:, b_stbf:b_stbf + 512], 0.0), writes=["stbf"])
    p.op("v", lambda e: e.memset(ab[:, b_xfT:b_xfT + 8 * 514]
                                 .rearrange("p (c w) -> p c w", w=514)[:, :, 0:1], 0.0),
         writes=["b_xfT"])

    def xa_view(c, c0, n):
        return ab[:, b_xa + c * 514 + c0: b_xa + c * 514 + c0 + n]

    def blk8(base, c, n=512):
        return ab[:, base + c * 512: base + c * 512 + n]

    def proj_var(xi_nm):
        for c in range(8):
            p.op("v", lambda e, c=c, xi_nm=xi_nm: e.scalar_tensor_tensor(
                out=blk8(b_var, c), in0=blk8(b_d1, c), scalar=parap(xi_nm, c),
                in1=xa_view(c, 1, 512), op0=ALU.mult, op1=ALU.add),
                reads=["b_d1", "b_xa", ("par", xi_nm)], writes=["b_var"])

    def mm_feature_qtr(dram_w, out_base, outkey, tag):
        for q in range(4):
            wo, wk = load_w(dram_w[:, q * 256:(q + 1) * 256], C, 256, (tag, q))
            for jj in range(2):
                j = q * 2 + jj
                for c in range(8):
                    p.op("t", lambda e, wo=wo, jj=jj, c=c: e.matmul(
                        PS[0][:, 0:512],
                        ab[:, wo + c * 256 + jj * 128: wo + c * 256 + (jj + 1) * 128],
                        blk8(b_var, c), start=(c == 0), stop=(c == 7)),
                        reads=[wk, "b_var", ("PS", 0)], writes=[("PS", 0)])
                p.op("v", lambda e, j=j: e.tensor_copy(out=blk8(out_base, j), in_=PS[0][:, 0:512]),
                     reads=[("PS", 0)], writes=[outkey])

    nblk = 1 if DEBUG_SUB else NBLK
    for b in range(nblk):
        t0 = b * BLK
        if b == 0:
            for c in range(8):
                p.op("v", lambda e, c=c: e.memset(xa_view(c, 0, 1), 0.0), writes=["b_xa"])
        else:
            for c in range(8):
                p.op("q", lambda e, c=c, t0=t0: e.dma_start(
                    out=xa_view(c, 0, 1), in_=ds["xnaT"][c * 128:(c + 1) * 128, t0 - 1:t0],
                    allow_slow_non_contiguous=True),
                    reads=["b_xa"], writes=["b_xa"], ch="lxa")
        for c in range(8):
            p.op("q", lambda e, c=c, t0=t0: e.dma_start(
                out=xa_view(c, 1, 512), in_=ds["xnaT"][c * 128:(c + 1) * 128, t0:t0 + 512]),
                reads=["b_xa"] + [("xnaTd", c, i) for i in range(4 * b, 4 * b + 4)],
                writes=["b_xa"], ch="lxa")
        for c in range(8):
            p.op("v", lambda e, c=c: e.tensor_sub(blk8(b_d1, c), xa_view(c, 0, 512), xa_view(c, 1, 512)),
                 reads=["b_xa"], writes=["b_d1"])

        p.op("g", lambda e: e.dma_start(out=ab[0:64, o_lo2:o_lo2 + 1024], in_=di["w2"]),
             reads=["sm_lo2"], writes=["sm_lo2"], ch="lsm2")
        proj_var("x_w")
        for c in range(8):
            p.op("t", lambda e, c=c: e.matmul(PS[2][0:64, 0:512],
                                              ab[:, o_w1 + c * 64:o_w1 + (c + 1) * 64],
                                              blk8(b_var, c), start=(c == 0), stop=(c == 7)),
                 reads=["sm_w1", "b_var", ("PS", 2)], writes=[("PS", 2)])
        p.op("s", lambda e: e.activation(out=ab[0:64, b_mid:b_mid + 512], in_=PS[2][0:64, 0:512],
                                         func=AF.Tanh),
             reads=[("PS", 2)], writes=["b_wB"])
        for j in range(8):
            p.op("t", lambda e, j=j: e.matmul(PS[2][:, 0:512],
                                              ab[0:64, o_lo2 + j * 128:o_lo2 + (j + 1) * 128],
                                              ab[0:64, b_mid:b_mid + 512],
                                              start=True, stop=True),
                 reads=["sm_lo2", "b_wB", ("PS", 2)], writes=[("PS", 2)])
            p.op("s", lambda e, j=j: e.activation(out=av[:, a_sig + j * 512:a_sig + (j + 1) * 512],
                                                  in_=PS[2][:, 0:512], func=AF.Sigmoid,
                                                  bias=parap("w0", j)),
                 reads=[("PS", 2), ("par", "w0")], writes=["a_sig"])

        p.op("g", lambda e: e.dma_start(out=ab[0:64, o_lo2:o_lo2 + 1024], in_=di["a2"]),
             reads=["sm_lo2"], writes=["sm_lo2"], ch="lsm2")
        proj_var("x_a")
        for c in range(8):
            p.op("t", lambda e, c=c: e.matmul(PS[2][0:64, 0:512],
                                              ab[:, o_a1 + c * 64:o_a1 + (c + 1) * 64],
                                              blk8(b_var, c), start=(c == 0), stop=(c == 7)),
                 reads=["sm_a1", "b_var", ("PS", 2)], writes=[("PS", 2)])
        p.op("v", lambda e: e.tensor_copy(out=ab[0:64, b_mid:b_mid + 512], in_=PS[2][0:64, 0:512]),
             reads=[("PS", 2)], writes=["b_wB"])
        for j in range(8):
            p.op("t", lambda e, j=j: e.matmul(PS[2][:, 0:512],
                                              ab[0:64, o_lo2 + j * 128:o_lo2 + (j + 1) * 128],
                                              ab[0:64, b_mid:b_mid + 512],
                                              start=True, stop=True),
                 reads=["sm_lo2", "b_wB", ("PS", 2)], writes=[("PS", 2)])
            p.op("s", lambda e, j=j: e.activation(out=blk8(b_aS, j), in_=PS[2][:, 0:512],
                                                  func=AF.Sigmoid, bias=parap("a0", j)),
                 reads=[("PS", 2), ("par", "a0")], writes=["b_aS"])

        p.op("g", lambda e: e.dma_start(out=ab[:, o_lo2:o_lo2 + 1024], in_=di["g2"]),
             reads=["sm_lo2"], writes=["sm_lo2"], ch="lsm2")
        proj_var("x_g")
        for c in range(8):
            p.op("t", lambda e, c=c: e.matmul(PS[2][:, 0:512],
                                              ab[:, o_g1 + c * 128:o_g1 + (c + 1) * 128],
                                              blk8(b_var, c), start=(c == 0), stop=(c == 7)),
                 reads=["sm_g1", "b_var", ("PS", 2)], writes=[("PS", 2)])
        p.op("s", lambda e: e.activation(out=ab[:, b_mid:b_mid + 512], in_=PS[2][:, 0:512],
                                         func=AF.Sigmoid),
             reads=[("PS", 2)], writes=["b_wB"])
        for j in range(8):
            p.op("t", lambda e, j=j: e.matmul(PS[2][:, 0:512],
                                              ab[:, o_lo2 + j * 128:o_lo2 + (j + 1) * 128],
                                              ab[:, b_mid:b_mid + 512],
                                              start=True, stop=True),
                 reads=["sm_lo2", "b_wB", ("PS", 2)], writes=[("PS", 2)])
            p.op("v", lambda e, j=j: e.tensor_copy(out=blk8(b_gg, j), in_=PS[2][:, 0:512]),
                 reads=[("PS", 2)], writes=["b_gg"])

        proj_var("x_k")
        mm_feature_qtr(di["W_k"], b_kT, "b_kT", ("Wk", b))
        for c in range(8):
            p.op("v", lambda e, c=c: e.tensor_scalar_mul(av[:, a_scr:a_scr + 512],
                                                         blk8(b_kT, c), parap("k_k", c)),
                 reads=["b_kT", ("par", "k_k")], writes=["a_scr"])
            p.op("v", lambda e, c=c: e.tensor_copy(out=blk8(b_kkn, c), in_=av[:, a_scr:a_scr + 512]),
                 reads=["a_scr"], writes=["b_kkn"])
            p.op("s", lambda e: e.activation(out=ab[:, b_var:b_var + 512],
                                             in_=av[:, a_scr:a_scr + 512], func=AF.Square),
                 reads=["a_scr"], writes=["b_var"])
            p.op("t", lambda e, c=c: e.matmul(PS[3][0:2, 0:512], sb_ind2[:],
                                              ab[:, b_var:b_var + 512], start=True, stop=True),
                 reads=["sb_ind2", "b_var", ("PS", 3)], writes=[("PS", 3)])
            p.op("v", lambda e, c=c: e.tensor_copy(out=av[0:2, a_xm + c * 512:a_xm + (c + 1) * 512],
                                                   in_=PS[3][0:2, 0:512]),
                 reads=[("PS", 3)], writes=["a_xm"])
        p.op("v", lambda e: e.tensor_scalar_max(av[0:2, a_xm:a_xm + 4096],
                                                av[0:2, a_xm:a_xm + 4096], 1e-24),
             reads=["a_xm"], writes=["a_xm"])
        p.op("s", lambda e: e.activation(out=av[0:2, a_xm:a_xm + 4096],
                                         in_=av[0:2, a_xm:a_xm + 4096], func=AF.Ln),
             reads=["a_xm"], writes=["a_xm"])
        p.op("s", lambda e: e.activation(out=av[0:2, a_xm:a_xm + 4096],
                                         in_=av[0:2, a_xm:a_xm + 4096], func=AF.Exp, scale=-0.5),
             reads=["a_xm"], writes=["a_xm"])
        for c in range(8):
            p.op("t", lambda e, c=c: e.matmul(PS[3][:, 0:512], ctx["sb_ind2Tf"][0:2, :],
                                              av[0:2, a_xm + c * 512:a_xm + (c + 1) * 512],
                                              start=True, stop=True),
                 reads=["a_xm", "sb_ind2Tf", ("PS", 3)], writes=[("PS", 3)])
            p.op("v", lambda e, c=c: e.tensor_mul(blk8(b_kkn, c), blk8(b_kkn, c), PS[3][:, 0:512]),
                 reads=["b_kkn", ("PS", 3)], writes=["b_kkn"])
        for c in range(8):
            p.op("v", lambda e, c=c: e.tensor_scalar(av[:, a_scr:a_scr + 512], blk8(b_aS, c),
                                                     parap("k_a", c), parap("ka1m", c),
                                                     op0=ALU.mult, op1=ALU.add),
                 reads=["b_aS", ("par", "k_a"), ("par", "ka1m")], writes=["a_scr"])
            p.op("v", lambda e, c=c: e.tensor_mul(blk8(b_kT, c), blk8(b_kT, c), av[:, a_scr:a_scr + 512]),
                 reads=["b_kT", "a_scr"], writes=["b_kT"])

        proj_var("x_r")
        mm_feature_qtr(di["W_r"], b_rT, "b_rT", ("Wr", b))

        proj_var("x_v")
        for q in range(4):
            wo, wk = load_w(di["W_v"][:, q * 256:(q + 1) * 256], C, 256, ("Wv", b, q))
            for ts in range(4):
                for hh in range(2):
                    jcol = q * 256 + hh * 128
                    for c in range(8):
                        p.op("t", lambda e, ts=ts, hh=hh, c=c, wo=wo: e.matmul(
                            PS[0][:, 0:128 + 0] if False else PS[0][:, hh * 128: hh * 128 + 128],
                            blk8(b_var, c)[:, ts * 128:(ts + 1) * 128],
                            ab[:, wo + c * 256 + hh * 128: wo + c * 256 + (hh + 1) * 128],
                            start=(c == 0), stop=(c == 7)),
                            reads=[wk, "b_var", ("PS", 0)], writes=[("PS", 0)])
                    p.op("v", lambda e, ts=ts, jcol=jcol, hh=hh: e.tensor_copy(
                        out=ab[:, b_vt + ts * 1024 + jcol: b_vt + ts * 1024 + jcol + 128],
                        in_=PS[0][:, hh * 128: hh * 128 + 128]),
                        reads=[("PS", 0)], writes=["b_vt"])
        for ts in range(4):
            for c in range(8):
                p.op("t", lambda e, ts=ts, c=c: e.transpose(
                    PB[0][:, 0:128],
                    ab[:, b_vt + ts * 1024 + c * 128: b_vt + ts * 1024 + (c + 1) * 128],
                    sb_id[:]),
                    reads=["b_vt", "sb_id"], writes=["PB0"])
                p.op("v", lambda e, ts=ts, c=c: e.tensor_copy(
                    out=blk8(b_vT, c)[:, ts * 128:(ts + 1) * 128], in_=PB[0][:, 0:128]),
                    reads=["PB0"], writes=["b_vT"])

        for c in range(8):
            p.op("v", lambda e: e.memset(av[:, a_gsc:a_gsc + 1], 0.0), writes=["a_gsc"])
            p.op("v", lambda e, c=c: e.tensor_tensor_scan(
                out=av[:, a_gsc + 1:a_gsc + 513], data0=av[:, a_sig + c * 512:a_sig + (c + 1) * 512],
                data1=av[:, a_gsc:a_gsc + 1].to_broadcast([128, 512]),
                initial=0.0, op0=ALU.add, op1=ALU.add),
                reads=["a_sig", "a_gsc"], writes=["a_gsc"])
            p.op("v", lambda e, c=c: e.tensor_tensor(
                out=av[:, a_gr + c * 512:a_gr + (c + 1) * 512].rearrange("p (k l) -> p k l", l=LCH),
                in0=av[:, a_gsc + 1:a_gsc + 513].rearrange("p (k l) -> p k l", l=LCH),
                in1=av[:, a_gsc:a_gsc + 512].rearrange("p (k l) -> p k l", l=LCH)[:, :, 0:1]
                .to_broadcast([128, 4, LCH]),
                op=ALU.subtract),
                reads=["a_gsc"], writes=["a_gr"])

        if DEBUG_SUB == "p2":
            continue
        WSUB = DEBUG_SUB if DEBUG_SUB in ("w1", "w2", "w3", "w4", "w5") else None
        gr3 = av[:, a_gr:a_gr + 4096].rearrange("p (c t) -> p c t", t=512)
        sg3 = av[:, a_sig:a_sig + 4096].rearrange("p (c t) -> p c t", t=512)
        kkn3 = ab[:, b_kkn:b_kkn + 4096].rearrange("p (c t) -> p c t", t=512)
        aS3 = ab[:, b_aS:b_aS + 4096].rearrange("p (c t) -> p c t", t=512)
        rT3 = ab[:, b_rT:b_rT + 4096].rearrange("p (c t) -> p c t", t=512)
        kT3 = ab[:, b_kT:b_kT + 4096].rearrange("p (c t) -> p c t", t=512)
        o_rt, o_kt, o_at, o_bt = b_der, b_der + 1024, b_der + 2048, b_der + 3072
        for lc in range(4):
            tl = lc * LCH
            sl = lambda base: ab[:, base:base + 1024].rearrange("p (c t) -> p c t", t=128)
            p.op("s", lambda e, tl=tl: e.activation(
                out=av[:, a_gcol:a_gcol + 8],
                in_=gr3[:, :, tl + LCH - 1:tl + LCH], func=AF.Exp, scale=-LAM),
                reads=["a_gr"], writes=["a_gcol"])
            p.op("s", lambda e, tl=tl: e.activation(out=av[:, a_scr:a_scr + 1024]
                                                    .rearrange("p (c t) -> p c t", t=128),
                                                    in_=gr3[:, :, tl:tl + LCH],
                                                    func=AF.Exp, scale=-LAM),
                 reads=["a_gr"], writes=["a_scr"])
            p.op("v", lambda e, tl=tl: e.tensor_mul(sl(o_rt), rT3[:, :, tl:tl + LCH],
                                                    av[:, a_scr:a_scr + 1024]
                                                    .rearrange("p (c t) -> p c t", t=128)),
                 reads=["b_rT", "a_scr"], writes=["b_der"])
            p.op("v", lambda e, tl=tl: e.scalar_tensor_tensor(
                out=av[:, a_scr2:a_scr2 + 1024].rearrange("p (c t) -> p c t", t=128),
                in0=gr3[:, :, tl:tl + LCH], scalar=-1.0, in1=sg3[:, :, tl:tl + LCH],
                op0=ALU.mult, op1=ALU.add),
                reads=["a_gr", "a_sig"], writes=["a_scr2"])
            p.op("s", lambda e: e.activation(out=av[:, a_scr2:a_scr2 + 1024],
                                             in_=av[:, a_scr2:a_scr2 + 1024],
                                             func=AF.Exp, scale=LAM),
                 reads=["a_scr2"], writes=["a_scr2"])
            p.op("v", lambda e, tl=tl: e.scalar_tensor_tensor(
                out=sl(o_at), in0=kkn3[:, :, tl:tl + LCH], scalar=-1.0,
                in1=av[:, a_scr2:a_scr2 + 1024].rearrange("p (c t) -> p c t", t=128),
                op0=ALU.mult, op1=ALU.mult),
                reads=["b_kkn", "a_scr2"], writes=["b_der"])
            p.op("s", lambda e, tl=tl: e.activation(out=av[:, a_scr:a_scr + 1024]
                                                    .rearrange("p (c t) -> p c t", t=128),
                                                    in_=gr3[:, :, tl:tl + LCH],
                                                    func=AF.Exp, scale=LAM),
                 reads=["a_gr", "b_der"], writes=["a_scr"])
            p.op("v", lambda e, tl=tl: e.tensor_mul(sl(o_kt), kT3[:, :, tl:tl + LCH],
                                                    av[:, a_scr:a_scr + 1024]
                                                    .rearrange("p (c t) -> p c t", t=128)),
                 reads=["b_kT", "a_scr"], writes=["b_der"])
            p.op("v", lambda e, tl=tl: e.tensor_mul(
                av[:, a_scr2:a_scr2 + 1024].rearrange("p (c t) -> p c t", t=128),
                kkn3[:, :, tl:tl + LCH], aS3[:, :, tl:tl + LCH]),
                reads=["b_kkn", "b_aS"], writes=["a_scr2"])
            p.op("v", lambda e: e.tensor_mul(ab[:, o_bt:o_bt + 1024],
                                             av[:, a_scr2:a_scr2 + 1024]
                                             .bitcast(F32) if False else av[:, a_scr2:a_scr2 + 1024],
                                             av[:, a_scr:a_scr + 1024]),
                 reads=["a_scr2", "a_scr"], writes=["b_der"])
            for pc in range(8):
                for hh in range(2):
                    half = hh * 64
                    h = pc * 2 + hh
                    rv = ab[half:half + 64, o_rt + pc * 128:o_rt + (pc + 1) * 128]
                    kv_ = ab[half:half + 64, o_kt + pc * 128:o_kt + (pc + 1) * 128]
                    avw = ab[half:half + 64, o_at + pc * 128:o_at + (pc + 1) * 128]
                    bvw = ab[half:half + 64, o_bt + pc * 128:o_bt + (pc + 1) * 128]
                    rd_pre = ["b_der"]
                    vch = ab[:, b_vt + lc * 1024 + h * 64: b_vt + lc * 1024 + (h + 1) * 64]
                    st_hv = av[half:half + 64, a_state + pc * 64:a_state + (pc + 1) * 64]
                    stbf_hv = ab[half:half + 64, b_stbf + pc * 64:b_stbf + (pc + 1) * 64]
                    stsc_hv = av[half:half + 64, a_stsc + pc * 64:a_stsc + (pc + 1) * 64]
                    emit_wkv_head(ctx, locals())
            p.op("v", lambda e: e.tensor_copy(out=ab[:, b_stbf:b_stbf + 512],
                                              in_=av[:, a_state:a_state + 512]),
                 reads=[("state", pcx, hx) for pcx in range(8) for hx in range(2)],
                 writes=["stbf"])

        if DEBUG_SUB in ("p3", "w0", "w1", "w2", "w3", "w4", "w5"):
            continue
        emit_opost(ctx, locals())
        if DEBUG_SUB == "p4a":
            continue
        emit_l0tail(ctx, locals())


def emit_wkv_head(ctx, L):
    p, PS, PB = ctx["p"], ctx["PS"], ctx["PB"]
    ab, av = ctx["AB16"].t, ctx["AF32"].t
    sb_msk, sb_id64 = ctx["sb_msk"], ctx["sb_id64"]
    rv, kv_, avw, bvw = L["rv"], L["kv_"], L["avw"], L["bvw"]
    rd_pre, vch = L["rd_pre"], L["vch"]
    st_hv, stbf_hv, stsc_hv = L["st_hv"], L["stbf_hv"], L["stsc_hv"]
    half, pc = L["half"], L["pc"]
    hh = L["hh"]
    par = (pc * 2 + hh) % 2
    b_BTs = L["b_BTs"][par]
    b_Bc = L["b_Bc"][par]
    b_KTm = L["b_KTm"][par]
    b_Pbm = L["b_Pbm"][par]
    b_Pkm = L["b_Pkm"][par]
    b_Wc = L["b_Wc"][par]
    b_Btok = L["b_Btok"][par]
    b_Ktok = L["b_Ktok"][par]
    b_d1, a_gcol = L["b_d1"], L["a_gcol"]
    tl = L["tl"]
    m_us, m_ui, m_ls = sb_msk[:, 0:128], sb_msk[:, 128:256], sb_msk[:, 256:384]
    if DEBUG_SUB == "w0":
        return

    PSd = PS[0 + par]
    PSq = PS[2 + par]
    PSo = PS[4 + par]
    PBt = PB[0]
    pbs = par * 256
    kst = ("state", pc, hh)
    kssc = ("stsc", pc, hh)
    kmap = {"PSd": ("PS", par), "PSq": ("PS", 2 + par), "PSo": ("PS", 4 + par),
            "PBt": "PB0"}

    def K(nm):
        return kmap.get(nm, (nm, par))

    p.op("t", lambda e: e.matmul(PSd[:, 0:128], bvw, avw, start=True, stop=True),
         reads=rd_pre + [K("PSd")], writes=[K("PSd")])
    p.op("t", lambda e: e.matmul(PSd[:, 128:256], avw, bvw, start=True, stop=True),
         reads=rd_pre + [K("PSd")], writes=[K("PSd")])
    p.op("t", lambda e: e.matmul(PSd[:, 256:384], kv_, avw, start=True, stop=True),
         reads=rd_pre + [K("PSd")], writes=[K("PSd")])
    p.op("t", lambda e: e.matmul(PSd[:, 384:512], bvw, rv, start=True, stop=True),
         reads=rd_pre + [K("PSd")], writes=[K("PSd")])
    p.op("t", lambda e: e.matmul(PSq[:, 0:128], kv_, rv, start=True, stop=True),
         reads=rd_pre + [K("PSq")], writes=[K("PSq")])
    p.op("v", lambda e: e.tensor_mul(ab[:, b_BTs:b_BTs + 128], PSd[:, 0:128], m_us),
         reads=[K("PSd"), "msk_us"], writes=[K("BT0")])
    p.op("v", lambda e: e.tensor_mul(ab[:, b_Bc:b_Bc + 128], PSd[:, 128:256], m_ls),
         reads=[K("PSd"), "msk_ls"], writes=[K("B0")])
    p.op("v", lambda e: e.tensor_mul(ab[:, b_KTm:b_KTm + 128], PSd[:, 256:384], m_us),
         reads=[K("PSd"), "msk_us"], writes=[K("KTm")])
    p.op("v", lambda e: e.tensor_mul(ab[:, b_Pbm:b_Pbm + 128], PSd[:, 384:512], m_ui),
         reads=[K("PSd"), "msk_ui"], writes=[K("Pbm")])
    p.op("s", lambda e: e.activation(out=ab[:, b_Pkm:b_Pkm + 128], in_=PSq[:, 0:128], func=AF.Copy),
         reads=[K("PSq")], writes=[K("Pkm")])
    p.op("v", lambda e: e.tensor_mul(ab[:, b_Pkm:b_Pkm + 128], ab[:, b_Pkm:b_Pkm + 128], m_ui),
         reads=[K("Pkm"), "msk_ui"], writes=[K("Pkm")])

    if DEBUG_SUB == "w1":
        return
    for lv in range(6):
        btp = ab[:, b_BTs + lv * 128:b_BTs + (lv + 1) * 128]
        bp = ab[:, b_Bc + (lv % 2) * 128:b_Bc + (lv % 2) * 128 + 128]
        bn = ab[:, b_Bc + ((lv + 1) % 2) * 128:b_Bc + ((lv + 1) % 2) * 128 + 128]
        p.op("t", lambda e, bp=bp, btp=btp: e.matmul(PSq[:, 128:256], bp, btp, start=True, stop=True),
             reads=[K(f"BT{lv}"), K(f"B{lv}"), K("PSq")], writes=[K("PSq")])
        p.op("t", lambda e, btp=btp, bp=bp: e.matmul(PSq[:, 256:384], btp, bp, start=True, stop=True),
             reads=[K(f"BT{lv}"), K(f"B{lv}"), K("PSq")], writes=[K("PSq")])
        p.op("v", lambda e, lv=lv: e.tensor_copy(out=ab[:, b_BTs + (lv + 1) * 128:b_BTs + (lv + 2) * 128],
                                                 in_=PSq[:, 128:256]),
             reads=[K("PSq")], writes=[K(f"BT{lv + 1}")])
        p.op("v", lambda e, bn=bn: e.tensor_copy(out=bn, in_=PSq[:, 256:384]),
             reads=[K("PSq")], writes=[K(f"B{lv + 1}")])

    if DEBUG_SUB == "w2":
        return
    p.op("t", lambda e: e.matmul(PSq[:, 384:448], avw, stbf_hv, start=True, stop=False),
         reads=rd_pre + ["stbf", K("PSq")], writes=[K("PSq")])
    p.op("t", lambda e: e.matmul(PSq[:, 384:448], ab[:, b_KTm:b_KTm + 128], vch, start=False, stop=True),
         reads=[K("KTm"), "b_vt", K("PSq")], writes=[K("PSq")])
    p.op("v", lambda e: e.tensor_copy(out=ab[:, b_Wc:b_Wc + 64], in_=PSq[:, 384:448]),
         reads=[K("PSq")], writes=[K("W0")])
    for i in range(7):
        wcur = ab[:, b_Wc + (i % 2) * 64:b_Wc + (i % 2) * 64 + 64]
        wnext = ab[:, b_Wc + ((i + 1) % 2) * 64:b_Wc + ((i + 1) % 2) * 64 + 64]
        bti = ab[:, b_BTs + i * 128:b_BTs + (i + 1) * 128]
        p.op("t", lambda e, bti=bti, wcur=wcur: e.matmul(PSq[:, 448:512], bti, wcur, start=True, stop=True),
             reads=[K(f"BT{i}"), K(f"W{i}"), K("PSq")], writes=[K("PSq")])
        p.op("v", lambda e, wnext=wnext, wcur=wcur: e.tensor_add(wnext, PSq[:, 448:512], wcur),
             reads=[K("PSq"), K(f"W{i}")], writes=[K(f"W{i + 1}")])
    U = ab[:, b_Wc + 64:b_Wc + 128]

    if DEBUG_SUB == "w3":
        return
    pov = PSo[half:half + 64, 0:128]
    p.op("t", lambda e, pov=pov, half=half: e.matmul(pov, stbf_hv, rv, start=True, stop=False,
                                                     tile_position=(half, half)),
         reads=["stbf", K("PSo")] + rd_pre, writes=[K("PSo")])
    p.op("t", lambda e, pov=pov, half=half: e.matmul(pov, U, ab[:, b_Pbm:b_Pbm + 128],
                                                     start=False, stop=False, tile_position=(0, half)),
         reads=[K("W7"), K("Pbm"), K("PSo")], writes=[K("PSo")])
    p.op("t", lambda e, pov=pov, half=half: e.matmul(pov, vch, ab[:, b_Pkm:b_Pkm + 128],
                                                     start=False, stop=True, tile_position=(0, half)),
         reads=["b_vt", K("Pkm"), K("PSo")], writes=[K("PSo")])
    p.op("v", lambda e, pov=pov: e.tensor_copy(
        out=ab[half:half + 64, b_d1 + pc * 512 + tl: b_d1 + pc * 512 + tl + LCH], in_=pov),
        reads=[K("PSo")], writes=["b_d1"])

    if DEBUG_SUB == "w4":
        return
    p.op("t", lambda e: e.transpose(PBt[:, pbs:pbs + 64], bvw, sb_id64[half:half + 64, :]),
         reads=rd_pre + ["sb_id64", K("PBt")], writes=[K("PBt")])
    p.op("v", lambda e: e.tensor_copy(out=ab[:, b_Btok:b_Btok + 64], in_=PBt[:, pbs:pbs + 64]),
         reads=[K("PBt")], writes=[K("Btok")])
    p.op("t", lambda e: e.transpose(PBt[:, pbs + 64:pbs + 128], kv_, sb_id64[half:half + 64, :]),
         reads=rd_pre + ["sb_id64", K("PBt")], writes=[K("PBt")])
    p.op("v", lambda e: e.tensor_copy(out=ab[:, b_Ktok:b_Ktok + 64], in_=PBt[:, pbs + 64:pbs + 128]),
         reads=[K("PBt")], writes=[K("Ktok")])

    if DEBUG_SUB == "w5":
        return
    pstv = PSo[half:half + 64, 128:192]
    p.op("t", lambda e, pstv=pstv, half=half: e.matmul(pstv, ab[:, b_Btok:b_Btok + 64], U,
                                                       start=True, stop=False, tile_position=(0, half)),
         reads=[K("Btok"), K("W7"), K("PSo")], writes=[K("PSo")])
    p.op("t", lambda e, pstv=pstv, half=half: e.matmul(pstv, ab[:, b_Ktok:b_Ktok + 64], vch,
                                                       start=False, stop=True, tile_position=(0, half)),
         reads=[K("Ktok"), "b_vt", K("PSo")], writes=[K("PSo")])
    p.op("v", lambda e, pstv=pstv: e.tensor_add(stsc_hv, pstv, st_hv),
         reads=[K("PSo"), kst], writes=[kssc])
    p.op("v", lambda e: e.tensor_scalar_mul(st_hv, stsc_hv,
                                            av[half:half + 64, a_gcol + pc:a_gcol + pc + 1]),
         reads=[kssc, "a_gcol"], writes=[kst])


def emit_opost(ctx, L):
    p, PS = ctx["p"], ctx["PS"]
    ab, av = ctx["AB16"].t, ctx["AF32"].t
    sb_ind2, sb_ind2T = ctx["sb_ind2"], ctx["sb_ind2T"]
    par, po = ctx["sb_par"], ctx["par_off"]
    b_d1, b_var, b_rT, b_kT, b_vT, b_gg, b_kkn = (L[k] for k in
        ["b_d1", "b_var", "b_rT", "b_kT", "b_vT", "b_gg", "b_kkn"])
    a_scr, a_scr2, a_sig, a_gr, a_xm = (L[k] for k in ["a_scr", "a_scr2", "a_sig", "a_gr", "a_xm"])
    st2 = av[0:2, a_sig:a_sig + 4096]
    st2b = av[0:2, a_gr:a_gr + 4096]
    st3 = av[0:2, a_xm:a_xm + 4096]

    def blk8(base, c, n=512):
        return ab[:, base + c * 512: base + c * 512 + n]

    def parap(nm, c, n=1):
        o = po[nm]
        return par[:, o + c:o + c + n]

    for c in range(8):
        p.op("t", lambda e, c=c: e.matmul(PS[3][0:2, 0:512], sb_ind2[:], blk8(b_d1, c),
                                          start=True, stop=True),
             reads=["sb_ind2", "b_d1", ("PS", 3)], writes=[("PS", 3)])
        p.op("v", lambda e, c=c: e.tensor_copy(out=st2[:, c * 512:(c + 1) * 512],
                                               in_=PS[3][0:2, 0:512]),
             reads=[("PS", 3)], writes=["a_sig"])
        p.op("s", lambda e, c=c: e.activation(out=ab[:, b_var:b_var + 512], in_=blk8(b_d1, c),
                                              func=AF.Square),
             reads=["b_d1"], writes=["b_var"])
        p.op("t", lambda e, c=c: e.matmul(PS[3][0:2, 0:512], sb_ind2[:],
                                          ab[:, b_var:b_var + 512], start=True, stop=True),
             reads=["sb_ind2", "b_var", ("PS", 3)], writes=[("PS", 3)])
        p.op("v", lambda e, c=c: e.tensor_copy(out=st2b[:, c * 512:(c + 1) * 512],
                                               in_=PS[3][0:2, 0:512]),
             reads=[("PS", 3)], writes=["a_gr"])
    p.op("v", lambda e: e.tensor_scalar_mul(st2[:], st2[:], 1.0 / 64),
         reads=["a_sig"], writes=["a_sig"])
    p.op("s", lambda e: e.activation(out=st3[:], in_=st2[:], func=AF.Square),
         reads=["a_sig"], writes=["a_xm"])
    p.op("v", lambda e: e.tensor_scalar_mul(st2b[:], st2b[:], 1.0 / 64),
         reads=["a_gr"], writes=["a_gr"])
    p.op("v", lambda e: e.tensor_sub(st2b[:], st2b[:], st3[:]),
         reads=["a_gr", "a_xm"], writes=["a_gr"])
    p.op("v", lambda e: e.tensor_scalar_add(st2b[:], st2b[:], float(EPS_GN)),
         reads=["a_gr"], writes=["a_gr"])
    p.op("s", lambda e: e.activation(out=st2b[:], in_=st2b[:], func=AF.Ln),
         reads=["a_gr"], writes=["a_gr"])
    p.op("s", lambda e: e.activation(out=st2b[:], in_=st2b[:], func=AF.Exp, scale=-0.5),
         reads=["a_gr"], writes=["a_gr"])
    rko = po["r_k"]
    for c in range(8):
        p.op("v", lambda e, c=c: e.scalar_tensor_tensor(
            out=ab[:, b_var:b_var + 512], in0=blk8(b_rT, c), scalar=par[:, rko + c:rko + c + 1],
            in1=blk8(b_kT, c), op0=ALU.mult, op1=ALU.mult),
            reads=["b_rT", "b_kT", ("par", "r_k")], writes=["b_var"])
        p.op("t", lambda e, c=c: e.matmul(PS[3][0:2, 0:512], sb_ind2[:],
                                          ab[:, b_var:b_var + 512], start=True, stop=True),
             reads=["sb_ind2", "b_var", ("PS", 3)], writes=[("PS", 3)])
        p.op("v", lambda e, c=c: e.tensor_copy(out=st3[:, c * 512:(c + 1) * 512],
                                               in_=PS[3][0:2, 0:512]),
             reads=[("PS", 3)], writes=["a_xm"])
    for c in range(8):
        p.op("t", lambda e, c=c: e.matmul(PS[3][:, 0:512], ctx["sb_ind2Tf"][0:2, :],
                                          st2[:, c * 512:(c + 1) * 512], start=True, stop=True),
             reads=["a_sig", "sb_ind2Tf", ("PS", 3)], writes=[("PS", 3)])
        p.op("t", lambda e, c=c: e.matmul(PS[4][:, 0:512], ctx["sb_ind2Tf"][0:2, :],
                                          st2b[:, c * 512:(c + 1) * 512], start=True, stop=True),
             reads=["a_gr", "sb_ind2Tf", ("PS", 4)], writes=[("PS", 4)])
        p.op("v", lambda e, c=c: e.tensor_sub(av[:, a_scr:a_scr + 512], blk8(b_d1, c), PS[3][:, 0:512]),
             reads=["b_d1", ("PS", 3)], writes=["a_scr"])
        p.op("v", lambda e, c=c: e.tensor_mul(av[:, a_scr:a_scr + 512], av[:, a_scr:a_scr + 512],
                                              PS[4][:, 0:512]),
             reads=["a_scr", ("PS", 4)], writes=["a_scr"])
        p.op("v", lambda e, c=c: e.tensor_scalar(av[:, a_scr:a_scr + 512], av[:, a_scr:a_scr + 512],
                                                 parap("gn_w", c), parap("gn_b", c),
                                                 op0=ALU.mult, op1=ALU.add),
             reads=["a_scr", ("par", "gn_w"), ("par", "gn_b")], writes=["a_scr"])
        p.op("t", lambda e, c=c: e.matmul(PS[3][:, 0:512], ctx["sb_ind2Tf"][0:2, :],
                                          st3[:, c * 512:(c + 1) * 512], start=True, stop=True),
             reads=["a_xm", "sb_ind2Tf", ("PS", 3)], writes=[("PS", 3)])
        p.op("v", lambda e, c=c: e.tensor_mul(av[:, a_scr2:a_scr2 + 512], PS[3][:, 0:512],
                                              blk8(b_vT, c)),
             reads=[("PS", 3), "b_vT"], writes=["a_scr2"])
        p.op("v", lambda e, c=c: e.tensor_add(av[:, a_scr:a_scr + 512], av[:, a_scr:a_scr + 512],
                                              av[:, a_scr2:a_scr2 + 512]),
             reads=["a_scr", "a_scr2"], writes=["a_scr"])
        p.op("v", lambda e, c=c: e.tensor_mul(blk8(b_kkn, c), av[:, a_scr:a_scr + 512],
                                              blk8(b_gg, c)),
             reads=["a_scr", "b_gg"], writes=["b_kkn"])


def emit_l0tail(ctx, L):
    nc, p, di, ds, PS, PB = ctx["nc"], ctx["p"], ctx["di"], ctx["ds"], ctx["PS"], ctx["PB"]
    ab, av = ctx["AB16"].t, ctx["AF32"].t
    par, po = ctx["sb_par"], ctx["par_off"]
    sb_id = ctx["sb_id"]
    lnb = ctx["a_lnb"]
    b = L["b"]
    b_kkn, b_var, b_xfT, b_xa, b_d1 = L["b_kkn"], L["b_var"], L["b_xfT"], L["b_xa"], L["b_d1"]
    a_xm, a_scr, a_scr2, a_st = L["a_xm"], L["a_scr"], L["a_scr2"], L["a_st"]
    load_w = L["load_w"]

    def blk8(base, c, n=512):
        return ab[:, base + c * 512: base + c * 512 + n]

    def parap(nm, c, n=1):
        o = po[nm]
        return par[:, o + c:o + c + n]

    def xf_view(c, c0, n):
        return ab[:, b_xfT + c * 514 + c0: b_xfT + c * 514 + c0 + n]

    for q in range(4):
        wo, wk = load_w(di["W_o"][:, q * 256:(q + 1) * 256], C, 256, ("Wo", b, q))
        for ts in range(4):
            for c in range(8):
                p.op("t", lambda e, ts=ts, c=c, wo=wo, q=q: e.matmul(
                    PS[q % 2][:, 0:256],
                    blk8(b_kkn, c)[:, ts * 128:(ts + 1) * 128],
                    ab[:, wo + c * 256: wo + (c + 1) * 256],
                    start=(c == 0), stop=(c == 7)),
                    reads=[wk, "b_kkn", ("PS", q % 2)], writes=[("PS", q % 2)])
            p.op("v", lambda e, ts=ts, q=q: e.tensor_copy(
                out=av[:, a_xm + ts * 1024 + q * 256: a_xm + ts * 1024 + (q + 1) * 256],
                in_=PS[q % 2][:, 0:256]),
                reads=[("PS", q % 2)], writes=["a_xm"])
    for ts in range(4):
        gi = b * 4 + ts
        xm = av[:, a_xm + ts * 1024: a_xm + (ts + 1) * 1024]
        p.op("q", lambda e, gi=gi: e.dma_start(out=av[:, a_scr:a_scr + 1024],
                                               in_=ds["x0f"][gi * 128:(gi + 1) * 128, :]),
             reads=[("x0d", gi), "a_scr"], writes=["a_scr"], ch="lx0")
        p.op("v", lambda e, xm=xm: e.tensor_add(xm, xm, av[:, a_scr:a_scr + 1024]),
             reads=["a_xm", "a_scr"], writes=["a_xm"])
        ln_tok(ctx, "a_xm", xm, av[:, a_scr:a_scr + 1024], av[:, a_scr2:a_scr2 + 1024],
               "a_scr2", av[:, a_st:a_st + 16],
               wb=(ab[:, lnb + 4096:lnb + 5120], ab[:, lnb + 5120:lnb + 6144],
                   ("lnb", "ln_ffn_w"), ("lnb", "ln_ffn_b")), scratch_key="a_scr")
        p.op("v", lambda e: e.tensor_copy(out=ab[:, b_var:b_var + 1024],
                                          in_=av[:, a_scr2:a_scr2 + 1024]),
             reads=["a_scr2"], writes=["b_var"])
        for c in range(8):
            p.op("t", lambda e, c=c: e.transpose(PB[0][:, 0:128],
                                                 ab[:, b_var + c * 128:b_var + (c + 1) * 128],
                                                 sb_id[:]),
                 reads=["b_var", "sb_id", "PB0"], writes=["PB0"])
            p.op("v", lambda e, c=c, ts=ts: e.tensor_copy(
                out=xf_view(c, 1 + ts * 128, 128), in_=PB[0][:, 0:128]),
                reads=["PB0"], writes=["b_xfT"])
    for c in range(8):
        p.op("v", lambda e, c=c: e.tensor_sub(av[:, a_scr:a_scr + 512],
                                              xf_view(c, 0, 512), xf_view(c, 1, 512)),
             reads=["b_xfT"], writes=["a_scr"])
        p.op("v", lambda e, c=c: e.scalar_tensor_tensor(
            out=blk8(b_d1, c), in0=av[:, a_scr:a_scr + 512], scalar=parap("ffn_x_k", c),
            in1=xf_view(c, 1, 512), op0=ALU.mult, op1=ALU.add),
            reads=["a_scr", "b_xfT", ("par", "ffn_x_k")], writes=["b_d1"])
    p.op("v", lambda e: e.tensor_copy(
        out=ab[:, b_xfT:b_xfT + 8 * 514].rearrange("p (c w) -> p c w", w=514)[:, :, 0:1],
        in_=ab[:, b_xfT:b_xfT + 8 * 514].rearrange("p (c w) -> p c w", w=514)[:, :, 512:513]),
        reads=["b_xfT"], writes=["b_xfT"])

    for ts in range(4):
        gi = b * 4 + ts
        for q in range(16):
            w1o, w1k = load_w(di["W_ffn1"][:, q * 256:(q + 1) * 256], C, 256, ("f1", b, ts, q))
            for fc in range(2):
                fg = q * 2 + fc
                for c in range(8):
                    p.op("t", lambda e, fc=fc, c=c, w1o=w1o, ts=ts: e.matmul(
                        PS[2][:, 0:128],
                        ab[:, w1o + c * 256 + fc * 128: w1o + c * 256 + (fc + 1) * 128],
                        blk8(b_d1, c)[:, ts * 128:(ts + 1) * 128],
                        start=(c == 0), stop=(c == 7)),
                        reads=[w1k, "b_d1", ("PS", 2)], writes=[("PS", 2)])
                p.op("s", lambda e: e.activation(out=av[:, a_scr:a_scr + 128], in_=PS[2][:, 0:128],
                                                 func=AF.Relu),
                     reads=[("PS", 2)], writes=["a_scr"])
                p.op("s", lambda e, fg=fg: e.activation(
                    out=ab[:, b_xa + fg * 128: b_xa + (fg + 1) * 128],
                    in_=av[:, a_scr:a_scr + 128], func=AF.Square),
                    reads=["a_scr"], writes=["b_xa"])
        for q in range(8):
            w2o, w2k = load_w(di["W_ffn2"][q * 512:(q + 1) * 512, :], 512, C // 4, ("f2x", b, ts, q)) \
                if False else (None, None)
        first = [True, True]
        for fs in range(8):
            for half in range(2):
                w2o, w2k = load_w(di["W_ffn2"][fs * 512:(fs + 1) * 512, half * 512:(half + 1) * 512],
                                  512, 512, ("f2", b, ts, fs, half))
                for fc in range(4):
                    fg = fs * 4 + fc
                    p.op("t", lambda e, fg=fg, fc=fc, half=half, w2o=w2o, fs=fs: e.matmul(
                        PS[4 + half][:, 0:512],
                        ab[:, b_xa + fg * 128: b_xa + (fg + 1) * 128],
                        ab[:, w2o + fc * 512: w2o + (fc + 1) * 512],
                        start=(fg == 0), stop=(fg == 31)),
                        reads=[w2k, "b_xa", ("PS", 4 + half)], writes=[("PS", 4 + half)])
        xm = av[:, a_xm + ts * 1024: a_xm + (ts + 1) * 1024]
        for half in range(2):
            p.op("v", lambda e, half=half, xm=xm: e.tensor_add(
                av[:, a_scr2 + half * 512:a_scr2 + half * 512 + 512], PS[4 + half][:, 0:512],
                xm[:, half * 512:half * 512 + 512]),
                reads=[("PS", 4 + half), "a_xm"], writes=["a_scr2"])
        p.op("q", lambda e, gi=gi: e.dma_start(out=ds["sattn"][gi * 128:(gi + 1) * 128, :],
                                               in_=av[:, a_scr2:a_scr2 + 1024]),
             reads=["a_scr2"], writes=[("sad", gi)], ch="stsa")


def emit_l0ffn(ctx):
    nc, p, di, ds = ctx["nc"], ctx["p"], ctx["di"], ctx["ds"]
    AF32, AB16, PS, PB = ctx["AF32"], ctx["AB16"], ctx["PS"], ctx["PB"]
    av, ab = AF32.t, AB16.t
    sb_id, sb_idf = ctx["sb_id"], ctx["sb_idf"]
    logitsT = ctx["logitsT"]

    p.barrier()
    AF32.reset("keep")
    AB16.reset("keep")

    a_msk = AF32.alloc(2048)
    a_aux = AF32.alloc(2048)
    a_pos = AF32.alloc(2048)
    a_bis = AF32.alloc(32)
    a_ptt = AF32.alloc(16 * 16)
    a_tv = AF32.alloc(16)
    a_pay = AF32.alloc(2)
    a_sidx = AF32.alloc(8)
    a_scr = AF32.alloc(1024)
    a_isc = AF32.alloc(8)
    a_pay8 = AF32.alloc(16)

    a_scr2 = AF32.alloc(1024)
    i_idx = nc.alloc_sbuf_tensor("i_idx", [128, 8], I32)
    b_xin = AB16.alloc(1024)
    b_xinT = AB16.alloc(8 * 512)
    b_wA = AB16.alloc(2048)
    b_wA2 = AB16.alloc(2048)
    b_wB = AB16.alloc(1024)
    b_wB2 = AB16.alloc(1024)
    b_xin2 = AB16.alloc(1024)
    b_hf = AB16.alloc(32 * 512)
    b_h1 = AB16.alloc(512)

    lo = av[0:8, a_bis + 0:a_bis + 1]
    hi = av[0:8, a_bis + 1:a_bis + 2]
    mid = av[0:8, a_bis + 2:a_bis + 3]
    cnt = av[0:8, a_bis + 3:a_bis + 4]
    sel1 = av[0:8, a_bis + 4:a_bis + 5]
    sel1i = av[0:8, a_bis + 5:a_bis + 6]
    dd = av[0:8, a_bis + 6:a_bis + 7]
    vk = av[0:8, a_bis + 7:a_bis + 8]
    need = av[0:8, a_bis + 9:a_bis + 10]
    aff = logitsT

    for i in range(NT):
        p.op("t", lambda e, i=i: e.transpose(PS[4][:, 0:8], logitsT[0:8, i * 128:(i + 1) * 128],
                                             sb_idf[0:8, 0:8]),
             reads=[("lgT", i), "sb_idf", ("PS", 4)], writes=[("PS", 4)])
        p.op("v", lambda e: e.tensor_copy(out=av[:, a_scr:a_scr + 8], in_=PS[4][:, 0:8]),
             reads=[("PS", 4)], writes=["a_scr"])
        p.op("v", lambda e: e.tensor_reduce(out=av[:, a_scr + 8:a_scr + 9],
                                            in_=av[:, a_scr:a_scr + 8], axis=AX.X,
                                            op=ALU.max, negate=True),
             reads=["a_scr"], writes=["a_scr"])
        p.op("s", lambda e: e.activation(out=av[:, a_scr:a_scr + 8], in_=av[:, a_scr:a_scr + 8],
                                         func=AF.Exp, bias=av[:, a_scr + 8:a_scr + 9],
                                         accum_out=av[:, a_scr + 9:a_scr + 10]),
             reads=["a_scr"], writes=["a_scr"])
        p.op("s", lambda e: e.activation(out=av[:, a_scr + 10:a_scr + 11],
                                         in_=av[:, a_scr + 9:a_scr + 10], func=AF.Ln),
             reads=["a_scr"], writes=["a_scr"])
        p.op("s", lambda e: e.activation(out=av[:, a_scr + 8:a_scr + 9],
                                         in_=av[:, a_scr + 10:a_scr + 11], func=AF.Exp, scale=-1.0),
             reads=["a_scr"], writes=["a_scr"])
        p.op("v", lambda e: e.tensor_scalar_mul(av[:, a_scr:a_scr + 8],
                                                av[:, a_scr:a_scr + 8],
                                                av[:, a_scr + 8:a_scr + 9]),
             reads=["a_scr"], writes=["a_scr"])
        p.op("t", lambda e, i=i: e.transpose(PS[4][0:8, 0:128], av[:, a_scr:a_scr + 8], sb_idf[:]),
             reads=["a_scr", "sb_idf", ("PS", 4)], writes=[("PS", 4)])
        p.op("v", lambda e, i=i: e.tensor_copy(out=aff[0:8, i * 128:(i + 1) * 128],
                                               in_=PS[4][0:8, 0:128]),
             reads=[("PS", 4)], writes=["aff"])

    p.op("v", lambda e: e.memset(lo, 0.0), writes=["bis"])
    p.op("v", lambda e: e.memset(hi, 1.0), writes=["bis"])
    for it in range(40):
        p.op("v", lambda e: e.tensor_add(mid, lo, hi), reads=["bis"], writes=["bis"])
        p.op("v", lambda e: e.tensor_scalar_mul(mid, mid, 0.5), reads=["bis"], writes=["bis"])
        p.op("v", lambda e: e.tensor_single_scalar(out=av[0:8, a_msk:a_msk + 2048],
                                                   in_=aff[0:8, :], scalar=mid, op=ALU.is_gt),
             reads=["aff", "bis"], writes=["a_msk"])
        p.op("v", lambda e: e.tensor_reduce(out=cnt, in_=av[0:8, a_msk:a_msk + 2048],
                                            axis=AX.X, op=ALU.add),
             reads=["a_msk"], writes=["bis"])
        p.op("v", lambda e: e.tensor_single_scalar(out=sel1, in_=cnt, scalar=512.0, op=ALU.is_ge),
             reads=["bis"], writes=["bis"])
        p.op("v", lambda e: e.tensor_scalar(sel1i, sel1, -1.0, 1.0, op0=ALU.mult, op1=ALU.add),
             reads=["bis"], writes=["bis"])
        p.op("v", lambda e: e.tensor_sub(dd, mid, lo), reads=["bis"], writes=["bis"])
        p.op("v", lambda e: e.scalar_tensor_tensor(out=lo, in0=dd, scalar=sel1, in1=lo,
                                                   op0=ALU.mult, op1=ALU.add),
             reads=["bis"], writes=["bis"])
        p.op("v", lambda e: e.tensor_sub(dd, mid, hi), reads=["bis"], writes=["bis"])
        p.op("v", lambda e: e.scalar_tensor_tensor(out=hi, in0=dd, scalar=sel1i, in1=hi,
                                                   op0=ALU.mult, op1=ALU.add),
             reads=["bis"], writes=["bis"])
    p.op("v", lambda e: e.tensor_single_scalar(out=av[0:8, a_msk:a_msk + 2048], in_=aff[0:8, :],
                                               scalar=hi, op=ALU.is_le),
         reads=["aff", "bis"], writes=["a_msk"])
    p.op("v", lambda e: e.tensor_mul(av[0:8, a_aux:a_aux + 2048], aff[0:8, :],
                                     av[0:8, a_msk:a_msk + 2048]),
         reads=["aff", "a_msk"], writes=["a_aux"])
    p.op("v", lambda e: e.tensor_reduce(out=vk, in_=av[0:8, a_aux:a_aux + 2048],
                                        axis=AX.X, op=ALU.max),
         reads=["a_aux"], writes=["bis"])
    p.op("v", lambda e: e.tensor_single_scalar(out=av[0:8, a_msk:a_msk + 2048], in_=aff[0:8, :],
                                               scalar=vk, op=ALU.is_gt),
         reads=["aff", "bis"], writes=["a_msk"])
    p.op("v", lambda e: e.tensor_reduce(out=cnt, in_=av[0:8, a_msk:a_msk + 2048],
                                        axis=AX.X, op=ALU.add),
         reads=["a_msk"], writes=["bis"])
    p.op("v", lambda e: e.tensor_scalar(need, cnt, -1.0, 512.0, op0=ALU.mult, op1=ALU.add),
         reads=["bis"], writes=["bis"])
    p.op("v", lambda e: e.tensor_single_scalar(out=av[0:8, a_aux:a_aux + 2048], in_=aff[0:8, :],
                                               scalar=vk, op=ALU.is_equal),
         reads=["aff", "bis"], writes=["a_aux"])
    p.op("v", lambda e: e.memset(av[0:8, a_pos:a_pos + 1], 0.0), writes=["a_pos"])
    p.op("v", lambda e: e.tensor_tensor_scan(out=av[0:8, a_pos:a_pos + 2048],
                                             data0=av[0:8, a_aux:a_aux + 2048],
                                             data1=av[0:8, a_pos:a_pos + 1].to_broadcast([8, 2048]),
                                             initial=0.0, op0=ALU.add, op1=ALU.add),
         reads=["a_aux", "a_pos"], writes=["a_pos"])
    p.op("v", lambda e: e.tensor_single_scalar(out=av[0:8, a_pos:a_pos + 2048],
                                               in_=av[0:8, a_pos:a_pos + 2048],
                                               scalar=need, op=ALU.is_le),
         reads=["a_pos", "bis"], writes=["a_pos"])
    p.op("v", lambda e: e.tensor_mul(av[0:8, a_aux:a_aux + 2048], av[0:8, a_aux:a_aux + 2048],
                                     av[0:8, a_pos:a_pos + 2048]),
         reads=["a_aux", "a_pos"], writes=["a_aux"])
    p.op("v", lambda e: e.tensor_add(av[0:8, a_msk:a_msk + 2048], av[0:8, a_msk:a_msk + 2048],
                                     av[0:8, a_aux:a_aux + 2048]),
         reads=["a_msk", "a_aux"], writes=["a_msk"])
    p.op("v", lambda e: e.tensor_mul(av[0:8, a_aux:a_aux + 2048], aff[0:8, :],
                                     av[0:8, a_msk:a_msk + 2048]),
         reads=["aff", "a_msk"], writes=["a_aux"])
    p.op("v", lambda e: e.memset(av[0:8, a_pos:a_pos + 1], 0.0), writes=["a_pos"])
    p.op("v", lambda e: e.tensor_tensor_scan(out=av[0:8, a_pos:a_pos + 2048],
                                             data0=av[0:8, a_msk:a_msk + 2048],
                                             data1=av[0:8, a_pos:a_pos + 1].to_broadcast([8, 2048]),
                                             initial=0.0, op0=ALU.add, op1=ALU.add),
         reads=["a_msk", "a_pos"], writes=["a_pos"])
    p.op("v", lambda e: e.tensor_scalar_add(av[0:8, a_pos:a_pos + 2048],
                                            av[0:8, a_pos:a_pos + 2048], -1.0),
         reads=["a_pos"], writes=["a_pos"])
    p.op("v", lambda e: e.tensor_mul(av[0:8, a_pos:a_pos + 2048], av[0:8, a_pos:a_pos + 2048],
                                     av[0:8, a_msk:a_msk + 2048]),
         reads=["a_pos", "a_msk"], writes=["a_pos"])
    p.op("v", lambda e: e.scalar_tensor_tensor(out=av[0:8, a_scr:a_scr + 1024] if False else
                                               av[0:8, a_msk:a_msk + 2048],
                                               in0=av[0:8, a_msk:a_msk + 2048], scalar=-4096.0,
                                               in1=av[0:8, a_pos:a_pos + 2048],
                                               op0=ALU.mult, op1=ALU.bypass),
         reads=["a_msk"], writes=["a_msk"]) if False else None
    p.op("v", lambda e: e.tensor_scalar(av[0:8, a_aux + 0:a_aux + 0] if False else
                                        av[0:8, a_msk:a_msk + 2048],
                                        av[0:8, a_msk:a_msk + 2048], -4096.0, 4096.0,
                                        op0=ALU.mult, op1=ALU.add),
         reads=["a_msk"], writes=["a_msk"])
    p.op("v", lambda e: e.tensor_add(av[0:8, a_pos:a_pos + 2048], av[0:8, a_pos:a_pos + 2048],
                                     av[0:8, a_msk:a_msk + 2048]),
         reads=["a_pos", "a_msk"], writes=["a_pos"])

    p.op("g", lambda e: e.iota(av[:, a_tv:a_tv + 16], [[128, 16]], base=0,
                               channel_multiplier=1, allow_small_or_imprecise_dtypes=True),
         writes=["a_tv"])

    i_sc = av[:, a_isc:a_isc + 8].bitcast(I32)
    for i in range(NT):
        ip = i % 2
        pto = a_ptt + ip * 16
        p.op("t", lambda e, i=i, ip=ip: e.transpose(PS[4 + ip][:, 0:8],
                                                    av[0:8, a_pos + i * 128:a_pos + (i + 1) * 128],
                                                    sb_idf[0:8, 0:8]),
             reads=["a_pos", "sb_idf", ("PS", 4 + ip)], writes=[("PS", 4 + ip)])
        p.op("v", lambda e, i=i, ip=ip, pto=pto: e.tensor_copy(out=av[:, pto:pto + 8],
                                                               in_=PS[4 + ip][:, 0:8]),
             reads=[("PS", 4 + ip), ("a_ptt", ip)], writes=[("a_ptt", ip)])
        p.op("t", lambda e, i=i, ip=ip: e.transpose(PS[4 + ip][:, 8:16],
                                                    av[0:8, a_aux + i * 128:a_aux + (i + 1) * 128],
                                                    sb_idf[0:8, 0:8]),
             reads=["a_aux", "sb_idf", ("PS", 4 + ip)], writes=[("PS", 4 + ip)])
        p.op("v", lambda e, i=i, ip=ip, pto=pto: e.tensor_copy(out=av[:, pto + 8:pto + 16],
                                                               in_=PS[4 + ip][:, 8:16]),
             reads=[("PS", 4 + ip), ("a_ptt", ip)], writes=[("a_ptt", ip)])
        for ee in range(8):
            p.op("v", lambda e, ee=ee, ip=ip, pto=pto: e.tensor_copy(
                out=i_sc[:, ee:ee + 1], in_=av[:, pto + ee:pto + ee + 1]),
                reads=[("a_ptt", ip), ("isc", ee)], writes=[("isc", ee)])
            p.op("v", lambda e, i=i, ee=ee: e.tensor_copy(
                out=av[:, a_pay8 + 2 * ee:a_pay8 + 2 * ee + 1],
                in_=av[:, a_tv + i:a_tv + i + 1]),
                reads=["a_tv", ("pay", ee)], writes=[("pay", ee)])
            p.op("v", lambda e, ee=ee, ip=ip, pto=pto: e.tensor_copy(
                out=av[:, a_pay8 + 2 * ee + 1:a_pay8 + 2 * ee + 2],
                in_=av[:, pto + 8 + ee:pto + 9 + ee]),
                reads=[("a_ptt", ip), ("pay", ee)], writes=[("pay", ee)])
            p.op("g", lambda e, ee=ee: e.indirect_dma_start(
                out=ds["idxw%d" % ee], out_offset=IndirectOffsetOnAxis(i_sc[:, ee:ee + 1], 0),
                in_=av[:, a_pay8 + 2 * ee:a_pay8 + 2 * ee + 2], in_offset=None),
                reads=[("isc", ee), ("pay", ee)], writes=[("idxd", ee)], ch="sc" + str(ee % 4))

    p.op("v", lambda e: e.memset(av[:, a_scr:a_scr + 1024], 0.0),
         writes=[("a_scr", 0), ("a_scr", 1)])
    for i in range(NT):
        p.op("q", lambda e, i=i: e.dma_start(out=ds["g0a"][i * 128:(i + 1) * 128, :],
                                             in_=av[:, a_scr:a_scr + 512]),
             reads=[("a_scr", 0)], writes=[("g0z", i)], ch="g0z")
        p.op("q", lambda e, i=i: e.dma_start(out=ds["g0b"][i * 128:(i + 1) * 128, :],
                                             in_=av[:, a_scr:a_scr + 512]),
             reads=[("a_scr", 0)], writes=[("g0z", i)], ch="g0z")

    for ee in range(8):
        p.op("q", lambda e, ee=ee: e.dma_start(
            out=av[:, a_sidx:a_sidx + 8].rearrange("p (g x) -> p g x", x=2),
            in_=ds["idxw%d" % ee][0:512, :].rearrange("(g p) x -> p g x", p=128)),
            reads=[("idxd", ee), "a_sidx"], writes=["a_sidx"], ch="lsidx")
        for g in range(4):
            p.op("v", lambda e, g=g: e.tensor_copy(out=i_idx[:, g:g + 1],
                                                   in_=av[:, a_sidx + 2 * g:a_sidx + 2 * g + 1]),
                 reads=["a_sidx"], writes=["i_idx"])
        for g in range(4):
            gp = g % 2
            xo = b_xin if gp == 0 else b_xin2
            p.op("g", lambda e, g=g, xo=xo, gp=gp: e.indirect_dma_start(
                out=ab[:, xo:xo + 1024], out_offset=None,
                in_=ds["xn_bf"], in_offset=IndirectOffsetOnAxis(i_idx[:, g:g + 1], 0)),
                reads=["i_idx", ("b_xin", gp)] + [("xnbd", i) for i in range(NT)],
                writes=[("b_xin", gp)], ch="gxin" + str(gp))
            for c in range(8):
                p.op("t", lambda e, c=c, xo=xo, gp=gp: e.transpose(
                    PB[gp][:, 0:128], ab[:, xo + c * 128:xo + (c + 1) * 128], sb_id[:]),
                    reads=[("b_xin", gp), "sb_id", "PB%d" % gp], writes=["PB%d" % gp])
                p.op("v", lambda e, c=c, g=g, gp=gp: e.tensor_copy(
                    out=ab[:, b_xinT + c * 512 + g * 128: b_xinT + c * 512 + (g + 1) * 128],
                    in_=PB[gp][:, 0:128]),
                    reads=["PB%d" % gp], writes=["b_xinT"])
        for fs in range(32):
            fp = fs % 2
            wo = b_wA if fp == 0 else b_wA2
            so = a_scr + fp * 512
            p.op("g", lambda e, ee=ee, fs=fs, wo=wo, fp=fp: e.dma_start(
                out=ab[:, wo:wo + 1024].rearrange("p (c j) -> p c j", c=8),
                in_=di["w_gT"][ee, :, fs * 128:(fs + 1) * 128].rearrange("(c p) j -> p c j", p=128)),
                reads=[("b_wA", fp)], writes=[("b_wA", fp)], ch="fWA" + str(fp))
            p.op("g", lambda e, ee=ee, fs=fs, wo=wo, fp=fp: e.dma_start(
                out=ab[:, wo + 1024:wo + 2048].rearrange("p (c j) -> p c j", c=8),
                in_=di["w_uT"][ee, :, fs * 128:(fs + 1) * 128].rearrange("(c p) j -> p c j", p=128)),
                reads=[("b_wA", fp)], writes=[("b_wA", fp)], ch="fWA" + str(fp))
            for c in range(8):
                p.op("t", lambda e, c=c, wo=wo, fp=fp: e.matmul(
                    PS[fp][:, 0:512],
                    ab[:, wo + c * 128:wo + (c + 1) * 128],
                    ab[:, b_xinT + c * 512:b_xinT + (c + 1) * 512],
                    start=(c == 0), stop=(c == 7)),
                    reads=[("b_wA", fp), "b_xinT", ("PS", fp)], writes=[("PS", fp)])
            for c in range(8):
                p.op("t", lambda e, c=c, wo=wo, fp=fp: e.matmul(
                    PS[2 + fp][:, 0:512],
                    ab[:, wo + 1024 + c * 128:wo + 1024 + (c + 1) * 128],
                    ab[:, b_xinT + c * 512:b_xinT + (c + 1) * 512],
                    start=(c == 0), stop=(c == 7)),
                    reads=[("b_wA", fp), "b_xinT", ("PS", 2 + fp)], writes=[("PS", 2 + fp)])
            p.op("s", lambda e, fp=fp, so=so: e.activation(out=av[:, so:so + 512],
                                                           in_=PS[fp][:, 0:512], func=AF.Silu),
                 reads=[("PS", fp), ("a_scr", fp)], writes=[("a_scr", fp)])
            p.op("v", lambda e, fs=fs, fp=fp, so=so: e.tensor_mul(
                ab[:, b_hf + fs * 512: b_hf + (fs + 1) * 512],
                av[:, so:so + 512], PS[2 + fp][:, 0:512]),
                reads=[("a_scr", fp), ("PS", 2 + fp)], writes=["b_hf"])
        for half in range(2):
            for fs in range(32):
                fp = fs % 2
                wo = b_wB if fp == 0 else b_wB2
                p.op("g", lambda e, ee=ee, fs=fs, half=half, wo=wo, fp=fp: e.dma_start(
                    out=ab[:, wo:wo + 512],
                    in_=di["w_dn"][ee, fs * 128:(fs + 1) * 128, half * 512:(half + 1) * 512]),
                    reads=[("b_wB", fp)], writes=[("b_wB", fp)], ch="fWB" + str(fp))
                for g in range(4):
                    p.op("t", lambda e, fs=fs, g=g, wo=wo, fp=fp: e.matmul(
                        PS[g][:, 0:512],
                        ab[:, b_hf + fs * 512 + g * 128: b_hf + fs * 512 + (g + 1) * 128],
                        ab[:, wo:wo + 512],
                        start=(fs == 0), stop=(fs == 31)),
                        reads=[("b_wB", fp), "b_hf", ("PS", g)], writes=[("PS", g)])
            for g in range(4):
                gp = g % 2
                so = a_scr + gp * 512
                p.op("v", lambda e, g=g, half=half, so=so, gp=gp: e.tensor_scalar_mul(
                    av[:, so:so + 512],
                    PS[g][:, 0:512],
                    av[:, a_sidx + 2 * g + 1:a_sidx + 2 * g + 2]),
                    reads=[("PS", g), "a_sidx", ("a_scr", gp)], writes=[("a_scr", gp)])
                p.op("g", lambda e, g=g, half=half, so=so, gp=gp: e.indirect_dma_start(
                    out=ds["g0a" if half == 0 else "g0b"],
                    out_offset=IndirectOffsetOnAxis(i_idx[:, g:g + 1], 0),
                    in_=av[:, so:so + 512], in_offset=None, compute_op=ALU.add),
                    reads=["i_idx", ("a_scr", gp), ("g0d",)] + [("g0z", i) for i in range(NT)],
                    writes=[("g0d",)], ch="g0sc")

def emit_combine0(ctx):
    nc, p, di, ds = ctx["nc"], ctx["p"], ctx["di"], ctx["ds"]
    AF32, AB16 = ctx["AF32"], ctx["AB16"]
    av = AF32.t
    p.barrier()
    AF32.reset("keep")
    AB16.reset("keep")
    a_x = AF32.alloc(2048)
    a_s = AF32.alloc(2048)
    a_g = AF32.alloc(2048)
    for i in range(NT):
        pr = i % 2
        xo = a_x + pr * 1024
        so = a_s + pr * 1024
        go = a_g + pr * 1024
        p.op("q", lambda e, i=i, xo=xo: e.dma_start(out=av[:, xo:xo + 1024],
                                                    in_=di["X"][i * 128:(i + 1) * 128, :]),
             reads=[("c0x", pr)], writes=[("c0x", pr)], ch="c0x" + str(pr))
        p.op("q", lambda e, i=i, so=so: e.dma_start(out=av[:, so:so + 1024],
                                                    in_=ds["sattn"][i * 128:(i + 1) * 128, :]),
             reads=[("c0s", pr), ("sad", i)], writes=[("c0s", pr)], ch="c0s" + str(pr))
        p.op("q", lambda e, i=i, go=go: e.dma_start(out=av[:, go:go + 512],
                                                    in_=ds["g0a"][i * 128:(i + 1) * 128, :]),
             reads=[("c0g", pr), ("g0d",)], writes=[("c0g", pr)], ch="c0g" + str(pr))
        p.op("q", lambda e, i=i, go=go: e.dma_start(out=av[:, go + 512:go + 1024],
                                                    in_=ds["g0b"][i * 128:(i + 1) * 128, :]),
             reads=[("c0g", pr), ("g0d",)], writes=[("c0g", pr)], ch="c0g" + str(pr))
        p.op("s", lambda e, go=go: e.activation(out=av[:, go:go + 1024], in_=av[:, go:go + 1024],
                                                func=AF.Silu),
             reads=[("c0g", pr)], writes=[("c0g", pr)])
        p.op("v", lambda e, so=so, go=go: e.tensor_mul(av[:, so:so + 1024], av[:, so:so + 1024],
                                                       av[:, go:go + 1024]),
             reads=[("c0s", pr), ("c0g", pr)], writes=[("c0s", pr)])
        p.op("v", lambda e, xo=xo, so=so: e.tensor_add(av[:, xo:xo + 1024], av[:, xo:xo + 1024],
                                                       av[:, so:so + 1024]),
             reads=[("c0x", pr), ("c0s", pr)], writes=[("c0x", pr)])
        p.op("q", lambda e, i=i, xo=xo: e.dma_start(out=ds["x1"][i * 128:(i + 1) * 128, :],
                                                    in_=av[:, xo:xo + 1024]),
             reads=[("c0x", pr)], writes=[("x1d", i)], ch="stx1" + str(pr))


def emit_layer1(ctx):
    nc, p, di, ds, OUT = ctx["nc"], ctx["p"], ctx["di"], ctx["ds"], ctx["OUT"]
    AF32, AB16, PS, PB = ctx["AF32"], ctx["AB16"], ctx["PS"], ctx["PB"]
    av, ab = AF32.t, AB16.t
    sb_id, sb_idf, sb_onesb, sb_invf = ctx["sb_id"], ctx["sb_idf"], ctx["sb_onesb"], ctx["sb_invf"]

    p.barrier()
    AF32.reset("keep")
    AB16.reset("keep")

    meanrow = nc.dram_tensor("scr_meanrow", [1, C], F32, kind="Internal").ap()
    meanT_d = nc.dram_tensor("scr_meanT", [128, 8], F32, kind="Internal").ap()
    rwrow_d = nc.dram_tensor("scr_rwrow", [1, 8], F32, kind="Internal").ap()
    cw_d = nc.dram_tensor("scr_cw", [1, 8], F32, kind="Internal").ap()

    xn2T_d = nc.dram_tensor("scr_xn2T", [C, T], BF16, kind="Internal").ap()
    b_x2T = AB16.alloc(8 * 512)
    b_latT = AB16.alloc(2 * 2048)
    b_st = AB16.alloc(1024)
    b_rows = AB16.alloc(4 * 1024)
    b_b2 = AB16.alloc(1024)
    b_qg = AB16.alloc(65)
    b_kg = AB16.alloc(65)
    b_vg = AB16.alloc(65)
    b_qgT = AB16.alloc(512)
    b_kgT = AB16.alloc(512)
    b_pT = AB16.alloc(512)
    b_og = AB16.alloc(64)
    b_qg2 = AB16.alloc(65)
    b_kg2 = AB16.alloc(65)
    b_vg2 = AB16.alloc(65)
    b_og2 = AB16.alloc(64)
    b_hT = AB16.alloc(32 * 512)
    b_oT = b_hT
    b_g2 = AB16.alloc(16 * 1024)
    b_msk01 = AB16.alloc(512)
    b_wA = AB16.alloc(2048)
    b_wB = AB16.alloc(2048)
    b_gb = AB16.alloc(2048)

    a_x = AF32.alloc(1024)
    a_sq = AF32.alloc(1024)
    a_o = AF32.alloc(1024)
    a_st = AF32.alloc(16)
    a_sc = AF32.alloc(1024)
    a_cc = AF32.alloc(1024)
    a_tv = AF32.alloc(16)
    a_i32 = AF32.alloc(32)
    a_pi = AF32.alloc(1)
    a_mt = AF32.alloc(8)
    a_rw = AF32.alloc(32)
    a_cwb = AF32.alloc(12)
    i_fl = nc.alloc_sbuf_tensor("i_fl", [128, 32], I32)

    selW1 = nc.dram_tensor("scr_selW1", [8192, 1024], BF16, kind="Internal").ap()
    selW2 = nc.dram_tensor("scr_selW2", [8192, 1024], BF16, kind="Internal").ap()
    eidx_d = nc.dram_tensor("scr_eidx", [1, 2], F32, kind="Internal").ap()
    m12_d = nc.dram_tensor("scr_m12", [1, 18], F32, kind="Internal").ap()
    a_eid = AF32.alloc(4)
    a_gw = AF32.alloc(64)
    a_gwi = AF32.alloc(64)
    a_b1s = AF32.alloc(64)
    a_mb = AF32.alloc(25)
    if AB16.free_at % 2:
        AB16.alloc(1)
    b_gst = AB16.alloc(4096)
    i_gw = av[:, a_gwi:a_gwi + 64].bitcast(I32)
    sb_io8 = av[0:1, a_mb + 17:a_mb + 25]
    c_io8 = nc.inline_tensor(np.arange(8, dtype=np.float32).reshape(1, 8), "c_io8")
    p.op("q", lambda e: e.dma_start(out=sb_io8, in_=c_io8.ap()), writes=["a_mb"], ch="l1c")
    fsp = (np.arange(128).reshape(128, 1)
           + np.tile(np.arange(32) * 128, 2).reshape(1, 64)).astype(np.float32)
    c_fsp = nc.inline_tensor(fsp, "c_fsp")
    p.op("q", lambda e: e.dma_start(out=av[:, a_gw:a_gw + 64], in_=c_fsp.ap()),
         writes=["a_gw"], ch="l1c")

    maskg = np.zeros((128, 512), np.float32)
    for g in range(4):
        for t in range(8):
            maskg[t * 16:(t + 1) * 16, g * 128 + t * 16: g * 128 + (t + 1) * 16] = 1.0
    c_maskg = nc.inline_tensor(maskg.astype(ml_dtypes.bfloat16), "c_maskg")
    p.op("q", lambda e: e.dma_start(out=ab[:, b_msk01:b_msk01 + 512], in_=c_maskg.ap()),
         writes=["msk01"], ch="l1c")
    c_pi = nc.inline_tensor(np.full((128, 1), np.pi, np.float32), "c_pi")
    p.op("q", lambda e: e.dma_start(out=av[:, a_pi:a_pi + 1], in_=c_pi.ap()), writes=["a_pi"], ch="l1c")

    p.op("g", lambda e: e.dma_start(out=ab[0:1, b_rows:b_rows + 1024], in_=di["bq"]),
         writes=["rows"], ch="l1r")
    p.op("g", lambda e: e.dma_start(out=ab[0:1, b_rows + 1024:b_rows + 3072], in_=di["bup"]),
         writes=["rows"], ch="l1r")
    p.op("g", lambda e: e.dma_start(out=ab[0:1, b_rows + 3072:b_rows + 4096], in_=di["bout"]),
         writes=["rows"], ch="l1r")
    p.op("g", lambda e: e.dma_start(out=ab[:, b_gb:b_gb + 1024],
                                    in_=di["gamma"].to_broadcast([128, C])),
         writes=["b_gb"], ch="l1r")
    p.op("g", lambda e: e.dma_start(out=ab[:, b_gb + 1024:b_gb + 2048],
                                    in_=di["beta"].to_broadcast([128, C])),
         writes=["b_gb"], ch="l1r")
    sb_bd = nc.alloc_sbuf_tensor("sb_bd", [128, 2], F32)
    p.op("q", lambda e: e.dma_start(out=sb_bd[:], in_=di["bdown"]), writes=["sb_bd"], ch="l1c")
    sb_onesf = nc.alloc_sbuf_tensor("sb_onesf", [128, 1], F32)
    p.op("v", lambda e: e.memset(sb_onesf[:], 1.0), writes=["sb_onesf"])
    p.op("v", lambda e: e.memset(ab[:, b_vg + 64:b_vg + 65], 1.0), writes=[("b_vg1", 0)])
    p.op("v", lambda e: e.memset(ab[:, b_vg2 + 64:b_vg2 + 65], 1.0), writes=[("b_vg1", 1)])

    a_xq = AF32.alloc(4096)
    a_st2 = AF32.alloc(32)
    for i in range(NT):
        pr = i % 2
        kx = ("a_xq", pr * 2)
        ko = ("a_xq", pr * 2 + 1)
        xo = a_xq + pr * 2048
        oo = xo + 1024
        sto = a_st2 + pr * 16
        bsto = b_st if pr == 0 else b_gst
        kbst = "b_st" if pr == 0 else ("gst", 0)
        p.op("q", lambda e, i=i, xo=xo: e.dma_start(out=av[:, xo:xo + 1024],
                                                    in_=ds["x1"][i * 128:(i + 1) * 128, :]),
             reads=[kx, ("x1d", i)], writes=[kx], ch="l1x" + str(pr))
        ln_tok(ctx, kx, av[:, xo:xo + 1024], av[:, oo:oo + 1024],
               av[:, oo:oo + 1024], ko, av[:, sto:sto + 16],
               scratch_key=ko, stat_key=("a_st2", pr))
        p.op("q", lambda e, i=i, oo=oo: e.dma_start(out=ds["xn2f"][i * 128:(i + 1) * 128, :],
                                                    in_=av[:, oo:oo + 1024]),
             reads=[ko], writes=[("x2d", i)], ch="stx2" + str(pr))
        for half in range(2):
            p.op("t", lambda e, i=i, half=half, oo=oo: e.matmul(
                PS[4 + half][0:1, 0:512], sb_onesf[:],
                av[:, oo + half * 512:oo + half * 512 + 512],
                start=(i == 0), stop=(i == NT - 1)),
                reads=[ko, "sb_onesf", ("PS", 4 + half)], writes=[("PS", 4 + half)])
        p.op("v", lambda e, oo=oo, bsto=bsto: e.tensor_copy(out=ab[:, bsto:bsto + 1024],
                                                            in_=av[:, oo:oo + 1024]),
             reads=[ko, kbst], writes=[kbst])
        for c in range(8):
            p.op("t", lambda e, c=c, pr=pr, bsto=bsto: e.transpose(
                PB[pr][:, 0:128], ab[:, bsto + c * 128:bsto + (c + 1) * 128], sb_id[:]),
                reads=[kbst, "sb_id", "PB%d" % pr], writes=["PB%d" % pr])
            p.op("v", lambda e, c=c, i=i, pr=pr: e.tensor_copy(
                out=ab[:, b_x2T + c * 512 + pr * 128: b_x2T + c * 512 + pr * 128 + 128],
                in_=PB[pr][:, 0:128]),
                reads=["PB%d" % pr, ("bx2T", pr)], writes=[("bx2T", pr)])
            p.op("q", lambda e, c=c, i=i, pr=pr: e.dma_start(
                out=xn2T_d[c * 128:(c + 1) * 128, i * 128:(i + 1) * 128],
                in_=ab[:, b_x2T + c * 512 + pr * 128: b_x2T + c * 512 + pr * 128 + 128]),
                reads=[("bx2T", pr)], writes=[("x2Td", c, i)], ch="stx2T" + str(pr))
    for half in range(2):
        p.op("v", lambda e, half=half: e.tensor_scalar_mul(
            av[0:1, a_sq + half * 512:a_sq + half * 512 + 512],
            PS[4 + half][0:1, 0:512], 1.0 / T),
            reads=[("PS", 4 + half)], writes=["a_sq"])
    p.op("q", lambda e: e.dma_start(out=meanrow, in_=av[0:1, a_sq:a_sq + 1024]),
         reads=["a_sq"], writes=["meand"], ch="stmn")
    p.op("q", lambda e: e.dma_start(out=av[:, a_mt:a_mt + 8],
                                    in_=meanrow.rearrange("o (c p) -> (o p) c", p=128),
                                    allow_slow_non_contiguous=True),
         reads=["meand"], writes=["a_mt"], ch="lmn")
    sb_wr = nc.alloc_sbuf_tensor("sb_wr", [128, 64], F32)
    sb_br = nc.alloc_sbuf_tensor("sb_br", [8, 1], F32)
    for c in range(8):
        p.op("q", lambda e, c=c: e.dma_start(out=sb_wr[:, c * 8:(c + 1) * 8],
                                             in_=di["Wr1"][c * 128:(c + 1) * 128, :]),
             writes=["sb_wr"], ch="l1c")
    p.op("q", lambda e: e.dma_start(out=sb_br[:], in_=di["br1"]), writes=["sb_br"], ch="l1c")
    for c in range(8):
        p.op("t", lambda e, c=c: e.matmul(PS[4][0:8, 0:1], sb_wr[:, c * 8:(c + 1) * 8],
                                          av[:, a_mt + c:a_mt + c + 1],
                                          start=(c == 0), stop=(c == 7)),
             reads=["a_mt", "sb_wr", ("PS", 4)], writes=[("PS", 4)])
    p.op("v", lambda e: e.tensor_add(av[0:8, a_rw:a_rw + 1], PS[4][0:8, 0:1], sb_br[:]),
         reads=[("PS", 4), "sb_br"], writes=["a_rw"])
    p.op("q", lambda e: e.dma_start(out=rwrow_d.rearrange("o e -> (o e)").unsqueeze(1),
                                    in_=av[0:8, a_rw:a_rw + 1], allow_slow_non_contiguous=True),
         reads=["a_rw"], writes=["rwd"], ch="strw")
    p.op("q", lambda e: e.dma_start(out=av[0:1, a_rw + 8:a_rw + 16], in_=rwrow_d),
         reads=["rwd"], writes=["a_rw"], ch="lrw")
    rwr = av[0:1, a_rw + 8:a_rw + 16]
    p.op("v", lambda e: e.tensor_reduce(out=av[0:1, a_rw + 16:a_rw + 17], in_=rwr, axis=AX.X,
                                        op=ALU.max, negate=True),
         reads=["a_rw"], writes=["a_rw"])
    p.op("s", lambda e: e.activation(out=rwr, in_=rwr, func=AF.Exp,
                                     bias=av[0:1, a_rw + 16:a_rw + 17],
                                     accum_out=av[0:1, a_rw + 17:a_rw + 18]),
         reads=["a_rw"], writes=["a_rw"])
    p.op("s", lambda e: e.activation(out=av[0:1, a_rw + 18:a_rw + 19],
                                     in_=av[0:1, a_rw + 17:a_rw + 18], func=AF.Ln),
         reads=["a_rw"], writes=["a_rw"])
    p.op("s", lambda e: e.activation(out=av[0:1, a_rw + 16:a_rw + 17],
                                     in_=av[0:1, a_rw + 18:a_rw + 19], func=AF.Exp, scale=-1.0),
         reads=["a_rw"], writes=["a_rw"])
    p.op("v", lambda e: e.tensor_scalar_mul(rwr, rwr, av[0:1, a_rw + 16:a_rw + 17]),
         reads=["a_rw"], writes=["a_rw"])
    p.op("v", lambda e: e.max(out=av[0:1, a_rw + 20:a_rw + 28], in_=rwr),
         reads=["a_rw"], writes=["a_rw"])
    v1 = av[0:1, a_rw + 20:a_rw + 21]
    v2 = av[0:1, a_rw + 21:a_rw + 22]
    p.op("v", lambda e: e.tensor_sub(av[0:1, a_rw + 28:a_rw + 29], v1, v2),
         reads=["a_rw"], writes=["a_rw"])
    p.op("s", lambda e: e.activation(out=av[0:1, a_rw + 29:a_rw + 30],
                                     in_=av[0:1, a_rw + 28:a_rw + 29], func=AF.Sigmoid),
         reads=["a_rw"], writes=["a_rw"])
    p.op("v", lambda e: e.tensor_scalar(av[0:1, a_rw + 30:a_rw + 31],
                                        av[0:1, a_rw + 29:a_rw + 30], -1.0, 1.0,
                                        op0=ALU.mult, op1=ALU.add),
         reads=["a_rw"], writes=["a_rw"])
    cwrow = av[0:1, a_rw + 0:a_rw + 8]
    m1 = av[0:1, a_rw + 31:a_rw + 32]
    tmp8a = av[0:1, a_rw - 0 + 8:a_rw + 16]
    t8 = AF32.alloc(32)
    m1r = av[0:1, t8:t8 + 8]
    m2r = av[0:1, t8 + 8:t8 + 16]
    rkr = av[0:1, t8 + 16:t8 + 24]
    p.op("v", lambda e: e.tensor_single_scalar(out=m1r, in_=rwr, scalar=v1, op=ALU.is_equal),
         reads=["a_rw"], writes=["t8"])
    p.op("v", lambda e: e.memset(av[0:1, t8 + 24:t8 + 25], 0.0), writes=["t8"])
    p.op("v", lambda e: e.tensor_tensor_scan(out=rkr, data0=m1r,
                                             data1=av[0:1, t8 + 24:t8 + 25].to_broadcast([1, 8]),
                                             initial=0.0, op0=ALU.add, op1=ALU.add),
         reads=["t8"], writes=["t8"])
    p.op("v", lambda e: e.tensor_single_scalar(out=rkr, in_=rkr, scalar=1.0, op=ALU.is_le),
         reads=["t8"], writes=["t8"])
    p.op("v", lambda e: e.tensor_mul(m1r, m1r, rkr), reads=["t8"], writes=["t8"])
    p.op("v", lambda e: e.tensor_single_scalar(out=m2r, in_=rwr, scalar=v2, op=ALU.is_equal),
         reads=["a_rw", "t8"], writes=["t8"])
    p.op("v", lambda e: e.scalar_tensor_tensor(out=rkr, in0=m1r, scalar=-1.0, in1=m2r,
                                               op0=ALU.mult, op1=ALU.add),
         reads=["t8"], writes=["t8"])
    p.op("v", lambda e: e.tensor_single_scalar(out=rkr, in_=rkr, scalar=0.5, op=ALU.is_ge),
         reads=["t8"], writes=["t8"])
    p.op("v", lambda e: e.memset(av[0:1, t8 + 24:t8 + 25], 0.0), writes=["t8"])
    p.op("v", lambda e: e.tensor_tensor_scan(out=m2r, data0=rkr,
                                             data1=av[0:1, t8 + 24:t8 + 25].to_broadcast([1, 8]),
                                             initial=0.0, op0=ALU.add, op1=ALU.add),
         reads=["t8"], writes=["t8"])
    p.op("v", lambda e: e.tensor_single_scalar(out=m2r, in_=m2r, scalar=1.0, op=ALU.is_le),
         reads=["t8"], writes=["t8"])
    p.op("v", lambda e: e.tensor_mul(m2r, m2r, rkr), reads=["t8"], writes=["t8"])
    p.op("v", lambda e: e.tensor_scalar_mul(av[0:1, t8 + 24:t8 + 32], m1r,
                                            av[0:1, a_rw + 29:a_rw + 30]),
         reads=["t8", "a_rw"], writes=["t8"])
    p.op("v", lambda e: e.scalar_tensor_tensor(out=cwrow, in0=m2r,
                                               scalar=av[0:1, a_rw + 30:a_rw + 31],
                                               in1=av[0:1, t8 + 24:t8 + 32],
                                               op0=ALU.mult, op1=ALU.add),
         reads=["t8", "a_rw"], writes=["a_rw"])
    p.op("q", lambda e: e.dma_start(out=cw_d, in_=cwrow), reads=["a_rw"], writes=["cwd"], ch="stcw")
    p.op("q", lambda e: e.dma_start(out=av[:, a_cwb:a_cwb + 8], in_=cw_d.to_broadcast([128, 8])),
         reads=["cwd"], writes=["a_cwb"], ch="lcw")
    p.op("v", lambda e: e.tensor_mul(av[0:1, t8 + 16:t8 + 24], m1r, sb_io8),
         reads=["t8", "a_mb"], writes=["t8"])
    p.op("v", lambda e: e.tensor_reduce(out=av[0:1, a_eid:a_eid + 1],
                                        in_=av[0:1, t8 + 16:t8 + 24], axis=AX.X, op=ALU.add),
         reads=["t8"], writes=["a_eid"])
    p.op("v", lambda e: e.tensor_mul(av[0:1, t8 + 16:t8 + 24], m2r, sb_io8),
         reads=["t8", "a_mb"], writes=["t8"])
    p.op("v", lambda e: e.tensor_reduce(out=av[0:1, a_eid + 1:a_eid + 2],
                                        in_=av[0:1, t8 + 16:t8 + 24], axis=AX.X, op=ALU.add),
         reads=["t8"], writes=["a_eid"])
    p.op("q", lambda e: e.dma_start(out=eidx_d, in_=av[0:1, a_eid:a_eid + 2]),
         reads=["a_eid"], writes=["eidxd"], ch="steidx")
    p.op("q", lambda e: e.dma_start(out=m12_d[0:1, 0:16], in_=av[0:1, t8:t8 + 16]),
         reads=["t8"], writes=["m12d"], ch="stm12")
    p.op("q", lambda e: e.dma_start(out=m12_d[0:1, 16:18], in_=av[0:1, a_rw + 29:a_rw + 31]),
         reads=["a_rw"], writes=["m12d"], ch="stm12")
    p.op("q", lambda e: e.dma_start(out=av[:, a_eid + 2:a_eid + 4],
                                    in_=eidx_d.to_broadcast([128, 2])),
         reads=["eidxd", "a_eid"], writes=["a_eid"], ch="leidx")
    p.op("q", lambda e: e.dma_start(out=av[:, a_mb:a_mb + 16],
                                    in_=m12_d[0:1, 0:16].to_broadcast([128, 16])),
         reads=["m12d", "a_mb"], writes=["a_mb"], ch="lmb")
    p.op("q", lambda e: e.dma_start(out=av[:, a_cwb + 8:a_cwb + 10],
                                    in_=m12_d[0:1, 16:18].to_broadcast([128, 2])),
         reads=["m12d", "a_cwb"], writes=["a_cwb"], ch="lcw")
    for ee2 in range(2):
        p.op("v", lambda e, ee2=ee2: e.scalar_tensor_tensor(
            out=av[:, a_gw + ee2 * 32:a_gw + (ee2 + 1) * 32],
            in0=av[:, a_eid + 2 + ee2:a_eid + 3 + ee2].to_broadcast([128, 32]),
            scalar=4096.0, in1=av[:, a_gw + ee2 * 32:a_gw + (ee2 + 1) * 32],
            op0=ALU.mult, op1=ALU.add),
            reads=["a_eid", "a_gw"], writes=["a_gw"])
    p.op("v", lambda e: e.tensor_copy(out=i_gw, in_=av[:, a_gw:a_gw + 64]),
         reads=["a_gw", "a_gwi"], writes=["a_gwi"])
    spar = ctx["sb_par"]
    fb1o = ctx["par_off"]["fb1"]
    for ee2 in range(2):
        for e8 in range(8):
            if e8 == 0:
                p.op("v", lambda e, ee2=ee2, e8=e8: e.tensor_scalar_mul(
                    av[:, a_b1s + ee2 * 32:a_b1s + (ee2 + 1) * 32],
                    spar[:, fb1o + e8 * 32:fb1o + (e8 + 1) * 32],
                    av[:, a_mb + ee2 * 8 + e8:a_mb + ee2 * 8 + e8 + 1]),
                    reads=[("par", "fb1"), "a_mb", "a_b1s"], writes=["a_b1s"])
            else:
                p.op("v", lambda e, ee2=ee2, e8=e8: e.scalar_tensor_tensor(
                    out=av[:, a_b1s + ee2 * 32:a_b1s + (ee2 + 1) * 32],
                    in0=spar[:, fb1o + e8 * 32:fb1o + (e8 + 1) * 32],
                    scalar=av[:, a_mb + ee2 * 8 + e8:a_mb + ee2 * 8 + e8 + 1],
                    in1=av[:, a_b1s + ee2 * 32:a_b1s + (ee2 + 1) * 32],
                    op0=ALU.mult, op1=ALU.add),
                    reads=[("par", "fb1"), "a_mb", "a_b1s"], writes=["a_b1s"])
    gstv = [ab[:, b_gst + 2048 * j:b_gst + 2048 * (j + 1)].bitcast(F32) for j in range(2)]
    gsrcs = [(di["f_W1r"], selW1, "sW1"),
             (di["f_W2"].rearrange("e f c -> (e f) c"), selW2, "sW2")]
    gcnt = 0
    for gsrc, gdst, gdkey in gsrcs:
        for ee2 in range(2):
            for k in range(32):
                par2 = gcnt % 2
                p.op("g", lambda e, gsrc=gsrc, ee2=ee2, k=k, par2=par2: e.indirect_dma_start(
                    out=gstv[par2], out_offset=None, in_=gsrc,
                    in_offset=IndirectOffsetOnAxis(i_gw[:, ee2 * 32 + k:ee2 * 32 + k + 1], 0)),
                    reads=["a_gwi", ("gst", par2)], writes=[("gst", par2)], ch="gg" + str(par2))
                p.op("g", lambda e, gdst=gdst, ee2=ee2, k=k, par2=par2: e.dma_start(
                    out=gdst[(ee2 * 32 + k) * 128:(ee2 * 32 + k + 1) * 128, :],
                    in_=gstv[par2]),
                    reads=[("gst", par2)], writes=[(gdkey, ee2 * 32 + k)], ch="gs" + str(par2))
                gcnt += 1

    p.op("g", lambda e: e.iota(av[:, a_tv:a_tv + 16], [[128, 16]], base=0,
                               channel_multiplier=1, allow_small_or_imprecise_dtypes=True),
         writes=["a_tv"])
    for i in range(NT):
        fcol = av[:, a_sq:a_sq + 32]
        p.op("v", lambda e, i=i: e.tensor_mul(fcol, av[:, a_tv + i:a_tv + i + 1]
                                              .to_broadcast([128, 32]), sb_invf[:]),
             reads=["a_tv", "sb_invf"], writes=["a_sq"])
        for which, base_off, phase in [("sin", a_sc, 0.0), ("cos", a_cc, 0.25)]:
            u = av[:, a_sq + 32:a_sq + 64]
            p.op("v", lambda e, phase=phase: e.tensor_scalar(u, fcol, 1.0 / (2 * np.pi), phase,
                                                             op0=ALU.mult, op1=ALU.add),
                 reads=["a_sq"], writes=["a_sq"])
            p.op("v", lambda e: e.tensor_copy(out=i_fl[:], in_=u), reads=["a_sq"], writes=["i_fl"])
            p.op("v", lambda e: e.tensor_copy(out=av[:, a_sq + 64:a_sq + 96], in_=i_fl[:]),
                 reads=["i_fl"], writes=["a_sq"])
            p.op("v", lambda e: e.tensor_sub(u, u, av[:, a_sq + 64:a_sq + 96]),
                 reads=["a_sq"], writes=["a_sq"])
            p.op("v", lambda e: e.tensor_single_scalar(out=av[:, a_sq + 64:a_sq + 96], in_=u,
                                                       scalar=0.0, op=ALU.is_lt),
                 reads=["a_sq"], writes=["a_sq"])
            p.op("v", lambda e: e.tensor_add(u, u, av[:, a_sq + 64:a_sq + 96]),
                 reads=["a_sq"], writes=["a_sq"])
            p.op("s", lambda e, base_off=base_off, i=i: e.activation(
                out=av[:, base_off + i * 32: base_off + (i + 1) * 32], in_=u, func=AF.Sin,
                scale=float(-2 * np.pi), bias=av[:, a_pi:a_pi + 1]),
                reads=["a_sq", "a_pi"], writes=["a_sc" if base_off == a_sc else "a_cc"])

    for tb in range(4):
        for c in range(8):
            p.op("q", lambda e, c=c, tb=tb: e.dma_start(
                out=ab[:, b_x2T + c * 512: b_x2T + (c + 1) * 512],
                in_=xn2T_d[c * 128:(c + 1) * 128, tb * 512:(tb + 1) * 512]),
                reads=["b_x2T"] + [("x2Td", c, i) for i in range(NT)],
                writes=["b_x2T", ("bx2T", 0), ("bx2T", 1)], ch="lx2w")
        for lc in range(2):
            p.op("g", lambda e, lc=lc: e.dma_start(
                out=ab[:, b_wA:b_wA + 1024].rearrange("p (c j) -> p c j", c=8),
                in_=di["Wdown"][:, lc * 128:(lc + 1) * 128].rearrange("(c p) j -> p c j", p=128)),
                reads=["b_wA"], writes=["b_wA"], ch="chWA")
            for c in range(8):
                p.op("t", lambda e, lc=lc, tb=tb, c=c: e.matmul(
                    PS[0][:, 0:512],
                    ab[:, b_wA + c * 128:b_wA + (c + 1) * 128],
                    ab[:, b_x2T + c * 512: b_x2T + (c + 1) * 512],
                    start=(c == 0), stop=(c == 7)),
                    reads=["b_wA", "b_x2T", ("PS", 0)], writes=[("PS", 0)])
            p.op("v", lambda e, lc=lc, tb=tb: e.tensor_scalar_add(
                ab[:, b_latT + lc * 2048 + tb * 512: b_latT + lc * 2048 + (tb + 1) * 512],
                PS[0][:, 0:512], sb_bd[:, lc:lc + 1]),
                reads=[("PS", 0), "sb_bd"], writes=["b_latT"])

    for tb in range(4):
        for c in range(8):
            p.op("q", lambda e, c=c, tb=tb: e.dma_start(
                out=ab[:, b_x2T + c * 512: b_x2T + (c + 1) * 512],
                in_=xn2T_d[c * 128:(c + 1) * 128, tb * 512:(tb + 1) * 512]),
                reads=["b_x2T"] + [("x2Td", c, i) for i in range(NT)],
                writes=["b_x2T", ("bx2T", 0), ("bx2T", 1)], ch="lx2w")
        for qr in range(4):
            qp = qr % 2
            wqo = b_wB if qp == 0 else b_wA
            wkey = "b_wB" if qp == 0 else "b_wA"
            wch = "chWB" if qp == 0 else "chWA"
            p.op("g", lambda e, qr=qr, wqo=wqo: e.dma_start(
                out=ab[:, wqo:wqo + 2048].rearrange("p (c j) -> p c j", c=8),
                in_=di["Wq"][:, qr * 256:(qr + 1) * 256].rearrange("(c p) j -> p c j", p=128)),
                reads=[wkey], writes=[wkey], ch=wch)
            for ii in range(4):
                bank = (qr * 4 + ii) % 2
                for c in range(8):
                    p.op("t", lambda e, ii=ii, c=c, wqo=wqo, bank=bank: e.matmul(
                        PS[bank][:, 0:256],
                        ab[:, b_x2T + c * 512 + ii * 128: b_x2T + c * 512 + ii * 128 + 128],
                        ab[:, wqo + c * 256:wqo + (c + 1) * 256],
                        start=(c == 0), stop=(c == 7)),
                        reads=[wkey, "b_x2T", ("PS", bank)], writes=[("PS", bank)])
                p.op("v", lambda e, qr=qr, ii=ii, bank=bank: e.tensor_copy(
                    out=av[:, a_xq + ii * 1024 + qr * 256:a_xq + ii * 1024 + (qr + 1) * 256],
                    in_=PS[bank][:, 0:256]),
                    reads=[("PS", bank), ("a_xq", ii)], writes=[("a_xq", ii)])
        for ii in range(4):
            i = tb * 4 + ii
            xo = a_xq + ii * 1024
            emit_rowadd(ctx, av[:, xo:xo + 1024], ab[0:1, b_rows:b_rows + 1024],
                        ("a_xq", ii), "rows", PS, p)
            emit_rope(ctx, p, av, xo, a_sc, a_cc, i, ("a_xq", ii))
            p.op("v", lambda e, xo=xo: e.tensor_copy(out=ab[:, b_st:b_st + 1024],
                                                     in_=av[:, xo:xo + 1024]),
                 reads=[("a_xq", ii), "b_st"], writes=["b_st"])
            p.op("q", lambda e, i=i: e.dma_start(out=ds["q_tok"][i * 128:(i + 1) * 128, :],
                                                 in_=ab[:, b_st:b_st + 1024]),
                 reads=["b_st"], writes=[("qd", i)], ch="stq")
        for part, dst, dkey in [(0, "k_tok", "kd"), (1, "v_tok", "vd")]:
            for half in range(2):
                qp = half
                wqo = b_wB if qp == 0 else b_wA
                wkey = "b_wB" if qp == 0 else "b_wA"
                wch = "chWB" if qp == 0 else "chWA"
                p.op("g", lambda e, part=part, half=half, wqo=wqo: e.dma_start(
                    out=ab[:, wqo:wqo + 1024].rearrange("p (c j) -> p c j", c=2),
                    in_=di["Wup"][:, part * 1024 + half * 512: part * 1024 + (half + 1) * 512]
                    .rearrange("(c p) j -> p c j", p=128)),
                    reads=[wkey], writes=[wkey], ch=wch)
                for ii in range(4):
                    i = tb * 4 + ii
                    bank = ii % 2
                    for lc in range(2):
                        p.op("t", lambda e, i=i, lc=lc, wqo=wqo, bank=bank: e.matmul(
                            PS[bank][:, 0:512],
                            ab[:, b_latT + lc * 2048 + i * 128: b_latT + lc * 2048 + (i + 1) * 128],
                            ab[:, wqo + lc * 512:wqo + (lc + 1) * 512],
                            start=(lc == 0), stop=(lc == 1)),
                            reads=[wkey, "b_latT", ("PS", bank)], writes=[("PS", bank)])
                    p.op("v", lambda e, half=half, ii=ii, bank=bank: e.tensor_copy(
                        out=av[:, a_xq + ii * 1024 + half * 512:a_xq + ii * 1024 + half * 512 + 512],
                        in_=PS[bank][:, 0:512]),
                        reads=[("PS", bank), ("a_xq", ii)], writes=[("a_xq", ii)])
            for ii in range(4):
                i = tb * 4 + ii
                xo = a_xq + ii * 1024
                emit_rowadd(ctx, av[:, xo:xo + 1024],
                            ab[0:1, b_rows + 1024 + part * 1024:b_rows + 1024 + (part + 1) * 1024],
                            ("a_xq", ii), "rows", PS, p)
                if part == 0:
                    emit_rope(ctx, p, av, xo, a_sc, a_cc, i, ("a_xq", ii))
                p.op("v", lambda e, xo=xo, ii=ii: e.tensor_copy(out=ab[:, b_st:b_st + 1024],
                                                                in_=av[:, xo:xo + 1024]),
                     reads=[("a_xq", ii), "b_st"], writes=["b_st"])
                p.op("q", lambda e, i=i, dst=dst, dkey=dkey: e.dma_start(
                    out=ds[dst][i * 128:(i + 1) * 128, :], in_=ab[:, b_st:b_st + 1024]),
                    reads=["b_st"], writes=[(dkey, i)], ch="st" + dst)

    a_ax = AF32.alloc(1024)
    a_asq = AF32.alloc(16)
    for gb in range(64):
        par = gb % 2
        qgo = b_qg if par == 0 else b_qg2
        kgo = b_kg if par == 0 else b_kg2
        vgo = b_vg if par == 0 else b_vg2
        ogo = b_og if par == 0 else b_og2
        qkb = 0 if par == 0 else 2
        avb = 1 if par == 0 else 3
        aoo = a_ax + par * 512
        sqo = a_asq + par * 8
        t0 = gb * 32
        for gg in range(4):
            tg = t0 + gg * 8
            p.op("q", lambda e, tg=tg, qgo=qgo, par=par: e.dma_start(
                out=ab[:, qgo:qgo + 64],
                in_=ds["q_tok"][tg:tg + 8, :].rearrange("t (h d) -> (t h) d", d=64)),
                reads=[("b_qg", par)] + [("qd", tg // 128)], writes=[("b_qg", par)],
                ch="lqg" + str(par))
            p.op("q", lambda e, tg=tg, kgo=kgo, par=par: e.dma_start(
                out=ab[:, kgo:kgo + 64],
                in_=ds["k_tok"][tg:tg + 8, :].rearrange("t (h d) -> (t h) d", d=64)),
                reads=[("b_kg", par)] + [("kd", tg // 128)], writes=[("b_kg", par)],
                ch="lkg" + str(par))
            p.op("t", lambda e, gg=gg, qgo=qgo, par=par: e.transpose(
                PB[0][0:64, gg * 128:(gg + 1) * 128], ab[:, qgo:qgo + 64], sb_id[:]),
                reads=[("b_qg", par), "sb_id", "PB0"], writes=["PB0"])
            p.op("v", lambda e, gg=gg, par=par: e.tensor_copy(
                out=ab[0:64, b_qgT + gg * 128:b_qgT + (gg + 1) * 128],
                in_=PB[0][0:64, gg * 128:(gg + 1) * 128]),
                reads=["PB0"], writes=["b_qgT"])
            p.op("t", lambda e, gg=gg, kgo=kgo, par=par: e.transpose(
                PB[1][0:64, gg * 128:(gg + 1) * 128], ab[:, kgo:kgo + 64], sb_id[:]),
                reads=[("b_kg", par), "sb_id", "PB1"], writes=["PB1"])
            p.op("v", lambda e, gg=gg, par=par: e.tensor_copy(
                out=ab[0:64, b_kgT + gg * 128:b_kgT + (gg + 1) * 128],
                in_=PB[1][0:64, gg * 128:(gg + 1) * 128]),
                reads=["PB1"], writes=["b_kgT"])
            p.op("t", lambda e, gg=gg, qkb=qkb: e.matmul(
                PS[qkb][:, gg * 128:(gg + 1) * 128],
                ab[0:64, b_kgT + gg * 128:b_kgT + (gg + 1) * 128],
                ab[0:64, b_qgT + gg * 128:b_qgT + (gg + 1) * 128],
                start=True, stop=True),
                reads=["b_kgT", "b_qgT", ("PS", qkb)], writes=[("PS", qkb)])
        p.op("s", lambda e, qkb=qkb, aoo=aoo, par=par: e.activation(
            out=av[:, aoo:aoo + 512], in_=PS[qkb][:, 0:512], func=AF.Exp, scale=0.125),
             reads=[("PS", qkb), ("a_o", par)], writes=[("a_o", par)])
        p.op("v", lambda e, aoo=aoo, par=par: e.tensor_mul(
            ab[:, b_pT:b_pT + 512], av[:, aoo:aoo + 512], ab[:, b_msk01:b_msk01 + 512]),
             reads=[("a_o", par), "msk01", "b_pT"], writes=["b_pT"])
        for gg in range(4):
            tg = t0 + gg * 8
            p.op("q", lambda e, tg=tg, vgo=vgo, par=par: e.dma_start(
                out=ab[:, vgo:vgo + 64],
                in_=ds["v_tok"][tg:tg + 8, :].rearrange("t (h d) -> (t h) d", d=64)),
                reads=[("b_vg", par)] + [("vd", tg // 128)], writes=[("b_vg", par)],
                ch="lvg" + str(par))
            p.op("t", lambda e, gg=gg, vgo=vgo, avb=avb, par=par: e.matmul(
                PS[avb][:, 0:65],
                ab[:, b_pT + gg * 128:b_pT + (gg + 1) * 128],
                ab[:, vgo:vgo + 65],
                start=True, stop=True),
                reads=["b_pT", ("b_vg", par), ("b_vg1", par), ("PS", avb)],
                writes=[("PS", avb)])
            p.op("v", lambda e, avb=avb, sqo=sqo, par=par: e.tensor_copy(
                out=av[:, sqo + 2:sqo + 3], in_=PS[avb][:, 64:65]),
                reads=[("PS", avb), ("a_sq", par)], writes=[("a_sq", par)])
            p.op("s", lambda e, sqo=sqo, par=par: e.activation(
                out=av[:, sqo:sqo + 1], in_=av[:, sqo + 2:sqo + 3], func=AF.Ln),
                reads=[("a_sq", par)], writes=[("a_sq", par)])
            p.op("s", lambda e, sqo=sqo, par=par: e.activation(
                out=av[:, sqo + 1:sqo + 2], in_=av[:, sqo:sqo + 1], func=AF.Exp, scale=-1.0),
                reads=[("a_sq", par)], writes=[("a_sq", par)])
            p.op("v", lambda e, avb=avb, ogo=ogo, sqo=sqo, par=par: e.tensor_scalar_mul(
                ab[:, ogo:ogo + 64], PS[avb][:, 0:64], av[:, sqo + 1:sqo + 2]),
                reads=[("PS", avb), ("a_sq", par), ("b_og", par)], writes=[("b_og", par)])
            p.op("q", lambda e, tg=tg, ogo=ogo, par=par: e.dma_start(
                out=ds["o_grp"][tg * 16:(tg + 8) * 16, :], in_=ab[:, ogo:ogo + 64]),
                reads=[("b_og", par)], writes=[("ogd", tg // 128)], ch="stog" + str(par))

    for tbo in range(4):
        for ii in range(4):
            i = tbo * 4 + ii
            sp = ii % 2
            bsto = b_st if sp == 0 else b_gst
            kbst = "b_st" if sp == 0 else ("gst", 0)
            p.op("q", lambda e, i=i, bsto=bsto, sp=sp: e.dma_start(
                out=ab[:, bsto:bsto + 1024],
                in_=ds["o_grp"][i * 2048:(i + 1) * 2048, :].rearrange("(t h) d -> t (h d)", h=16)),
                reads=[kbst, ("ogd", i)], writes=[kbst], ch="log" + str(sp))
            for c in range(8):
                p.op("t", lambda e, c=c, bsto=bsto, sp=sp: e.transpose(
                    PB[sp][:, 0:128], ab[:, bsto + c * 128:bsto + (c + 1) * 128], sb_id[:]),
                    reads=[kbst, "sb_id", "PB%d" % sp], writes=["PB%d" % sp])
                p.op("v", lambda e, c=c, ii=ii, sp=sp: e.tensor_copy(
                    out=ab[:, b_oT + c * 512 + ii * 128:b_oT + c * 512 + (ii + 1) * 128],
                    in_=PB[sp][:, 0:128]),
                    reads=["PB%d" % sp, "b_hT"], writes=["b_hT"])
        for qr in range(4):
            qp = qr % 2
            wqo = b_wB if qp == 0 else b_wA
            wkey = "b_wB" if qp == 0 else "b_wA"
            wch = "chWB" if qp == 0 else "chWA"
            p.op("g", lambda e, qr=qr, wqo=wqo: e.dma_start(
                out=ab[:, wqo:wqo + 2048].rearrange("p (c j) -> p c j", c=8),
                in_=di["Wout"][:, qr * 256:(qr + 1) * 256].rearrange("(c p) j -> p c j", p=128)),
                reads=[wkey], writes=[wkey], ch=wch)
            for ii in range(4):
                bank = (qr * 4 + ii) % 2
                for c in range(8):
                    p.op("t", lambda e, c=c, ii=ii, wqo=wqo, bank=bank: e.matmul(
                        PS[bank][:, 0:256],
                        ab[:, b_oT + c * 512 + ii * 128:b_oT + c * 512 + (ii + 1) * 128],
                        ab[:, wqo + c * 256:wqo + (c + 1) * 256],
                        start=(c == 0), stop=(c == 7)),
                        reads=[wkey, "b_hT", ("PS", bank)], writes=[("PS", bank)])
                p.op("v", lambda e, qr=qr, ii=ii, bank=bank: e.tensor_copy(
                    out=av[:, a_xq + ii * 1024 + qr * 256:a_xq + ii * 1024 + (qr + 1) * 256],
                    in_=PS[bank][:, 0:256]),
                    reads=[("PS", bank), ("a_xq", ii)], writes=[("a_xq", ii)])
        for ii in range(4):
            i = tbo * 4 + ii
            xo = a_xq + ii * 1024
            emit_rowadd(ctx, av[:, xo:xo + 1024], ab[0:1, b_rows + 3072:b_rows + 4096],
                        ("a_xq", ii), "rows", PS, p)
            p.op("q", lambda e, i=i, xo=xo: e.dma_start(out=ds["s2"][i * 128:(i + 1) * 128, :],
                                                        in_=av[:, xo:xo + 1024]),
                 reads=[("a_xq", ii)], writes=[("s2d", i)], ch="sts2")

    p.barrier()
    for ee in range(2):
        for tb in range(4):
            for c in range(8):
                p.op("q", lambda e, c=c, tb=tb: e.dma_start(
                    out=ab[:, b_x2T + c * 512: b_x2T + (c + 1) * 512],
                    in_=xn2T_d[c * 128:(c + 1) * 128, tb * 512:(tb + 1) * 512]),
                    reads=["b_x2T"] + [("x2Td", c, i) for i in range(NT)],
                    writes=["b_x2T", ("bx2T", 0), ("bx2T", 1)], ch="lx2w")
            for fs in range(32):
                wpar = fs % 2
                wo = b_wA + wpar * 1024
                p.op("q", lambda e, ee=ee, fs=fs, wo=wo: e.dma_start(
                    out=ab[:, wo:wo + 1024],
                    in_=selW1[(ee * 32 + fs) * 128:(ee * 32 + fs + 1) * 128, :]),
                    reads=[("b_wA", wpar), ("sW1", ee * 32 + fs)],
                    writes=[("b_wA", wpar)], ch="chWA" + str(wpar))
                for c in range(8):
                    p.op("t", lambda e, fs=fs, c=c, wo=wo, wpar=wpar: e.matmul(
                        PS[2 + wpar][:, 0:512],
                        ab[:, wo + c * 128:wo + (c + 1) * 128],
                        ab[:, b_x2T + c * 512:(b_x2T + (c + 1) * 512)],
                        start=(c == 0), stop=(c == 7)),
                        reads=[("b_wA", wpar), "b_x2T", ("PS", 2 + wpar)],
                        writes=[("PS", 2 + wpar)])
                p.op("s", lambda e, ee=ee, fs=fs, wpar=wpar: e.activation(
                    out=ab[:, b_hT + fs * 512:b_hT + (fs + 1) * 512], in_=PS[2 + wpar][:, 0:512],
                    func=AF.Gelu,
                    bias=av[:, a_b1s + ee * 32 + fs:a_b1s + ee * 32 + fs + 1]),
                    reads=[("PS", 2 + wpar), "a_b1s"], writes=["b_hT"])
            for fs in range(32):
                wpar = fs % 2
                wo = b_wB + wpar * 1024
                p.op("q", lambda e, ee=ee, fs=fs, wo=wo: e.dma_start(
                    out=ab[:, wo:wo + 1024],
                    in_=selW2[(ee * 32 + fs) * 128:(ee * 32 + fs + 1) * 128, :]),
                    reads=[("b_wB", wpar), ("sW2", ee * 32 + fs)],
                    writes=[("b_wB", wpar)], ch="chWB" + str(wpar))
                for tsu in range(4):
                    for half in range(2):
                        p.op("t", lambda e, fs=fs, tsu=tsu, half=half, wo=wo: e.matmul(
                            PS[tsu * 2 + half][:, 0:512] if tsu * 2 + half < 6 else
                            PB[tsu * 2 + half - 6].ap().bitcast(F32)[:, 0:512],
                            ab[:, b_hT + fs * 512 + tsu * 128: b_hT + fs * 512 + (tsu + 1) * 128],
                            ab[:, wo + half * 512:wo + (half + 1) * 512],
                            start=(fs == 0), stop=(fs == 31)),
                            reads=[("b_wB", wpar), "b_hT",
                                   ("PS", tsu * 2 + half) if tsu * 2 + half < 6 else "PB%d" % (tsu * 2 + half - 6)],
                            writes=[("PS", tsu * 2 + half) if tsu * 2 + half < 6 else "PB%d" % (tsu * 2 + half - 6)])
            for tsu in range(4):
                gi = tb * 4 + tsu
                for half in range(2):
                    pbank = (PS[tsu * 2 + half][:, 0:512] if tsu * 2 + half < 6 else
                             PB[tsu * 2 + half - 6].ap().bitcast(F32)[:, 0:512])
                    psxkey = ("PS", tsu * 2 + half) if tsu * 2 + half < 6 else "PB%d" % (tsu * 2 + half - 6)
                    if ee == 0:
                        p.op("v", lambda e, pbank=pbank, gi=gi, half=half: e.tensor_scalar_mul(
                            ab[:, b_g2 + gi * 1024 + half * 512: b_g2 + gi * 1024 + (half + 1) * 512],
                            pbank, av[:, a_cwb + 8:a_cwb + 9]),
                            reads=[psxkey, "a_cwb"], writes=["b_g2"])
                    else:
                        p.op("v", lambda e, pbank=pbank, gi=gi, half=half: e.scalar_tensor_tensor(
                            out=ab[:, b_g2 + gi * 1024 + half * 512: b_g2 + gi * 1024 + (half + 1) * 512],
                            in0=pbank, scalar=av[:, a_cwb + 9:a_cwb + 10],
                            in1=ab[:, b_g2 + gi * 1024 + half * 512: b_g2 + gi * 1024 + (half + 1) * 512],
                            op0=ALU.mult, op1=ALU.add),
                            reads=[psxkey, "a_cwb", "b_g2"], writes=["b_g2"])

    a_b2w = AF32.alloc(1024)
    p.op("q", lambda e: e.dma_start(out=av[0:8, a_mb + 16:a_mb + 17],
                                    in_=cw_d.rearrange("o e -> (o e)").unsqueeze(1),
                                    allow_slow_non_contiguous=True),
         reads=["cwd", "a_mb"], writes=["a_mb"], ch="lcwT")
    p.op("q", lambda e: e.dma_start(out=av[0:8, a_b2w:a_b2w + 1024],
                                    in_=di["f_b2"].rearrange("o (e c) -> (o e) c", e=8)),
         reads=["a_b2w"], writes=["a_b2w"], ch="lb2f")
    p.op("v", lambda e: e.tensor_scalar_mul(av[0:8, a_b2w:a_b2w + 1024],
                                            av[0:8, a_b2w:a_b2w + 1024],
                                            av[0:8, a_mb + 16:a_mb + 17]),
         reads=["a_b2w", "a_mb"], writes=["a_b2w"])
    p.op("v", lambda e: e.memset(av[0:8, a_gw:a_gw + 1], 1.0),
         reads=["a_gw"], writes=["a_gw"])
    for half in range(2):
        p.op("t", lambda e, half=half: e.matmul(PS[half][0:1, 0:512], av[0:8, a_gw:a_gw + 1],
                                                av[0:8, a_b2w + half * 512:a_b2w + (half + 1) * 512],
                                                start=True, stop=True),
             reads=["a_gw", "a_b2w", ("PS", half)], writes=[("PS", half)])
    for half in range(2):
        p.op("v", lambda e, half=half: e.tensor_copy(
            out=av[0:1, a_b2w + half * 512:a_b2w + (half + 1) * 512],
            in_=PS[half][0:1, 0:512]),
            reads=[("PS", half), "a_b2w"], writes=["a_b2w"])

    for i in range(NT):
        p.op("q", lambda e, i=i: e.dma_start(out=av[:, a_o:a_o + 1024],
                                             in_=ds["xn2f"][i * 128:(i + 1) * 128, :]),
             reads=["a_o", ("x2d", i)], writes=["a_o"], ch="lx2")
        p.op("v", lambda e, i=i: e.tensor_add(av[:, a_x:a_x + 1024],
                                              ab[:, b_g2 + i * 1024:b_g2 + (i + 1) * 1024],
                                              av[:, a_o:a_o + 1024]),
             reads=["b_g2", "a_o"], writes=["a_x"])
        emit_rowadd(ctx, av[:, a_x:a_x + 1024], av[0:1, a_b2w:a_b2w + 1024].bitcast(F32)
                    if False else av[0:1, a_b2w:a_b2w + 1024], "a_x", "a_b2w", PS, p, f32=True)
        ln_tok(ctx, "a_x", av[:, a_x:a_x + 1024], av[:, a_sq:a_sq + 1024],
               av[:, a_o:a_o + 1024], "a_o", av[:, a_st:a_st + 16],
               wb=(ab[:, b_gb:b_gb + 1024], ab[:, b_gb + 1024:b_gb + 2048],
                   "b_gb", "b_gb"))
        p.op("s", lambda e: e.activation(out=av[:, a_o:a_o + 1024], in_=av[:, a_o:a_o + 1024],
                                         func=AF.Silu),
             reads=["a_o"], writes=["a_o"])
        p.op("q", lambda e, i=i: e.dma_start(out=av[:, a_x:a_x + 1024],
                                             in_=ds["s2"][i * 128:(i + 1) * 128, :]),
             reads=["a_x", ("s2d", i)], writes=["a_x"], ch="ls2")
        p.op("v", lambda e: e.tensor_mul(av[:, a_o:a_o + 1024], av[:, a_o:a_o + 1024],
                                         av[:, a_x:a_x + 1024]),
             reads=["a_o", "a_x"], writes=["a_o"])
        p.op("q", lambda e, i=i: e.dma_start(out=av[:, a_x:a_x + 1024],
                                             in_=ds["x1"][i * 128:(i + 1) * 128, :]),
             reads=["a_x", ("x1d", i)], writes=["a_x"], ch="lx1b")
        p.op("v", lambda e: e.tensor_add(av[:, a_o:a_o + 1024], av[:, a_o:a_o + 1024],
                                         av[:, a_x:a_x + 1024]),
             reads=["a_o", "a_x"], writes=["a_o"])
        p.op("q", lambda e, i=i: e.dma_start(out=OUT[i * 128:(i + 1) * 128, :],
                                             in_=av[:, a_o:a_o + 1024]),
             reads=["a_o"], writes=[("outd", i)], ch="stout")


def emit_rowadd(ctx, dst_ap, row_ap, dst_key, row_key, PS, p, f32=False):
    ones = ctx["sb_ones"] if f32 else ctx["sb_onesb"]
    okey = "sb_ones" if f32 else "sb_onesb"
    for half in range(2):
        p.op("t", lambda e, half=half, ones=ones: e.matmul(
            PS[3][:, 0:512], ones[0:1, :],
            row_ap[0:1, half * 512:(half + 1) * 512], start=True, stop=True),
            reads=[row_key, okey, ("PS", 3)], writes=[("PS", 3)])
        p.op("v", lambda e, half=half: e.tensor_add(
            dst_ap[:, half * 512:(half + 1) * 512], dst_ap[:, half * 512:(half + 1) * 512],
            PS[3][:, 0:512]),
            reads=[dst_key, ("PS", 3)], writes=[dst_key])


def emit_rope(ctx, p, av, a_x, a_sc, a_cc, i, key):
    AF32 = ctx["AF32"]
    ALU_ = ALU
    if "_rope_scr" not in ctx:
        ctx["_rope_scr"] = AF32.alloc(1024)
        ctx["_rope_scr2"] = AF32.alloc(1024)
    a_r1, a_r2 = ctx["_rope_scr"], ctx["_rope_scr2"]
    x3 = av[:, a_x:a_x + 1024].rearrange("p (h d) -> p h d", d=64)
    r3 = av[:, a_r1:a_r1 + 1024].rearrange("p (h d) -> p h d", d=64)
    q3 = av[:, a_r2:a_r2 + 1024].rearrange("p (h d) -> p h d", d=64)
    sin = av[:, a_sc + i * 32:a_sc + (i + 1) * 32].unsqueeze(1).to_broadcast([128, 16, 32])
    cos = av[:, a_cc + i * 32:a_cc + (i + 1) * 32].unsqueeze(1).to_broadcast([128, 16, 32])
    p.op("v", lambda e: e.tensor_mul(r3[:, :, 0:32], x3[:, :, 0:32], cos),
         reads=[key, "a_cc"], writes=["_rope"])
    p.op("v", lambda e: e.tensor_mul(r3[:, :, 32:64], x3[:, :, 0:32], sin),
         reads=[key, "a_sc"], writes=["_rope"])
    p.op("v", lambda e: e.tensor_mul(q3[:, :, 0:32], x3[:, :, 32:64], sin),
         reads=[key, "a_sc"], writes=["_rope2"])
    p.op("v", lambda e: e.tensor_mul(q3[:, :, 32:64], x3[:, :, 32:64], cos),
         reads=[key, "a_cc"], writes=["_rope2"])
    p.op("v", lambda e: e.tensor_sub(x3[:, :, 0:32], r3[:, :, 0:32], q3[:, :, 0:32]),
         reads=["_rope", "_rope2", key], writes=[key])
    p.op("v", lambda e: e.tensor_add(x3[:, :, 32:64], r3[:, :, 32:64], q3[:, :, 32:64]),
         reads=["_rope", "_rope2", key], writes=[key])



def _prep_vec(v):
    return np.ascontiguousarray(np.asarray(v, np.float32).reshape(NPC, 128).T)


def make_in_maps(inputs):
    d = {k: np.asarray(v) for k, v in inputs.items()}
    d.pop("L0_attn_v_first", None)
    base = {}
    pfx = "L0_attn_"
    for nm in ["x_r", "x_w", "x_k", "x_v", "x_a", "x_g", "w0",
               "a0", "k_k", "k_a", "ffn_x_k", "gn_w", "gn_b"]:
        base["p_" + nm] = _prep_vec(d[pfx + nm])
    for nm in ["ln_pre_w", "ln_pre_b", "ln_attn_w", "ln_attn_b", "ln_ffn_w",
               "ln_ffn_b"]:
        base["row_" + nm] = np.asarray(d[pfx + nm], np.float32).reshape(1, C)
    base["r_k"] = np.ascontiguousarray(
        d[pfx + "r_k"].reshape(8, 2, 64).transpose(1, 2, 0).reshape(128, 8))
    for nm in ["W_r", "W_k", "W_v", "W_o", "w1", "w2", "a1", "a2", "g1", "g2",
               "W_ffn1", "W_ffn2"]:
        base[nm] = np.ascontiguousarray(d[pfx + nm], np.float32)
    base["gate_wT"] = np.ascontiguousarray(d["L0_ffn_gate_w"].T.astype(np.float32))
    base["w_gT"] = np.ascontiguousarray(d["L0_ffn_w_gate"].transpose(0, 2, 1))
    base["w_uT"] = np.ascontiguousarray(d["L0_ffn_w_up"].transpose(0, 2, 1))
    base["w_dn"] = np.ascontiguousarray(d["L0_ffn_w_down"].transpose(0, 2, 1))
    base["Wq"] = np.ascontiguousarray(d["L1_attn_Wq"], np.float32)
    base["Wdown"] = np.ascontiguousarray(d["L1_attn_Wdown"], np.float32)
    perm = ([h * 128 + j for h in range(16) for j in range(64)] +
            [h * 128 + 64 + j for h in range(16) for j in range(64)])
    base["Wup"] = np.ascontiguousarray(d["L1_attn_Wup"][:, perm], np.float32)
    base["Wout"] = np.ascontiguousarray(d["L1_attn_Wout"], np.float32)
    base["bq"] = np.ascontiguousarray(d["L1_attn_bq"].reshape(1, C), np.float32)
    base["bdown"] = np.ascontiguousarray(d["L1_attn_bdown"].reshape(2, 128).T, np.float32)
    base["bup"] = np.ascontiguousarray(
        np.asarray(d["L1_attn_bup"], np.float32)[perm].reshape(1, 2 * C))
    base["bout"] = np.ascontiguousarray(d["L1_attn_bout"].reshape(1, C), np.float32)
    base["Wr1"] = np.ascontiguousarray(d["L1_ffn_Wr"], np.float32)
    base["br1"] = np.ascontiguousarray(d["L1_ffn_br"].reshape(8, 1), np.float32)
    base["f_W1r"] = np.ascontiguousarray(
        d["L1_ffn_W1"].reshape(8, 8, 128, 32, 128).transpose(0, 3, 2, 1, 4)
        .reshape(32768, 1024), np.float32)
    base["f_b1"] = np.ascontiguousarray(
        d["L1_ffn_b1"].reshape(8, 32, 128).transpose(2, 0, 1).reshape(128, 8 * 32))
    base["f_W2"] = np.ascontiguousarray(d["L1_ffn_W2"], np.float32)
    base["f_b2"] = np.ascontiguousarray(d["L1_ffn_b2"].reshape(1, 8 * C), np.float32)
    base["gamma"] = np.ascontiguousarray(d["L1_ffn_gamma"].reshape(1, C), np.float32)
    base["beta"] = np.ascontiguousarray(d["L1_ffn_beta"].reshape(1, C), np.float32)
    x = np.asarray(d["L0_attn_x"], np.float32)
    maps = []
    for b in range(8):
        m = dict(base)
        m["X"] = np.ascontiguousarray(x[b])
        maps.append(m)
    return maps


def kernel(**inputs):
    nc = build_program()
    in_maps = make_in_maps(inputs)
    res = run_bass_kernel_spmd(nc, in_maps, core_ids=list(range(8)))
    return np.stack([r["OUT"] for r in res.results], 0).astype(np.float32)
